# version 02
# baseline (speedup 1.0000x reference)
import contextlib
import numpy as np
import ml_dtypes
import concourse.bass as bass
import concourse.mybir as mybir
from concourse.bass_utils import run_bass_kernel_spmd

F32 = mybir.dt.float32
BF16 = mybir.dt.bfloat16
AF = mybir.ActivationFunctionType
ALU = mybir.AluOpType
AX = mybir.AxisListType
BF = ml_dtypes.bfloat16

NCORE = 8
CONSTS = {"eps5": 0, "eps6": 1, "invd": 2}
NCONST = 3
B = 2
N = 8192
D = 1024
TPC = 2048
NT = TPC // 128
DC = D // 128
H = 16
HD = 64
INV_D = 1.0 / 1024.0


class Sched:
    ENGINES = ("g", "v", "s", "t", "y")

    def __init__(self):
        self.ops = []
        self.cnt = {}

    def op(self, eng, emit, wait=(), inc=None):
        if inc is None:
            by = 16 if eng in ("g", "y") else 1
            inc = (eng, by)
        sem_key, by = inc
        self.cnt[sem_key] = self.cnt.get(sem_key, 0) + by
        self.ops.append((eng, emit, tuple(wait), sem_key, by))
        return self.cnt[sem_key]

    def mark(self, key):
        return self.cnt.get(key, 0)

    def marks(self, *keys):
        return tuple((k, self.mark(k)) for k in keys if self.mark(k) > 0)

    def nop(self, eng, wait):
        self.ops.append((eng, lambda e: None, tuple(wait), eng, 0))

    def replay(self, eng_key, eng_obj, sems):
        waited = {}
        for eng, emit, waits, sem_key, by in self.ops:
            if eng != eng_key:
                continue
            for wk, wv in waits:
                if wv <= 0:
                    continue
                if waited.get(wk, -1) >= wv:
                    continue
                eng_obj.wait_ge(sems[wk], wv)
                waited[wk] = wv
            ins = emit(eng_obj)
            if ins is not None:
                ins.then_inc(sems[sem_key], by)

    def attach(self, block, sems):
        block.gpsimd(lambda e: self.replay("g", e, sems))
        block.vector(lambda e: self.replay("v", e, sems))
        block.scalar(lambda e: self.replay("s", e, sems))
        block.tensor(lambda e: self.replay("t", e, sems))
        block.sync(lambda e: self.replay("y", e, sems))


def _ln_stats(sch, xin, mu_c, rstd_c, sq, eps_col, invd_col, wait_in=()):
    F = 1.0 / xin.shape[-1]
    sch.op("v", lambda v: v.reduce_sum(mu_c, xin, axis=AX.X), wait=wait_in)
    sch.op("v", lambda v: v.tensor_scalar_mul(mu_c, mu_c, F),
           wait=[("v", sch.mark("v"))])
    sch.op("v", lambda v: v.tensor_scalar(sq, xin, mu_c, None, ALU.subtract),
           wait=[("v", sch.mark("v"))])
    sch.op("s", lambda s: s.activation(sq, sq, AF.Square),
           wait=[("v", sch.mark("v"))])
    sch.op("v", lambda v: v.reduce_sum(rstd_c, sq, axis=AX.X),
           wait=[("s", sch.mark("s"))])
    sch.op("s", lambda s: s.activation(rstd_c, rstd_c, AF.Sqrt,
                                       bias=eps_col, scale=invd_col),
           wait=[("v", sch.mark("v"))])
    sch.op("v", lambda v: v.reciprocal(rstd_c, rstd_c),
           wait=[("s", sch.mark("s"))])


def build_u1():
    nc = bass.Bass("TRN2", target_bir_lowering=False)
    x = nc.dram_tensor("x_tok", [TPC, D], F32, kind="ExternalInput").ap()
    lng = nc.dram_tensor("lng", [128, D], F32, kind="ExternalInput").ap()
    lnb = nc.dram_tensor("lnb", [128, D], F32, kind="ExternalInput").ap()
    wkv = nc.dram_tensor("wkv", [D, 2 * D], F32, kind="ExternalInput").ap()
    ident = nc.dram_tensor("ident", [128, 128], BF16, kind="ExternalInput").ap()
    onesb = nc.dram_tensor("ones_bf", [128, 1], BF16, kind="ExternalInput").ap()
    cvals = nc.dram_tensor("cvals", [128, NCONST], F32, kind="ExternalInput").ap()
    kvT_s = nc.dram_tensor("kvT_s", [2 * D, TPC], BF16, kind="ExternalOutput").ap()
    vtok_s = nc.dram_tensor("vtok_s", [TPC, D], BF16, kind="ExternalOutput").ap()
    sqq_s = nc.dram_tensor("sqq_s", [H, TPC], F32, kind="ExternalOutput").ap()

    with contextlib.ExitStack() as ctx:
        def sb(name, shape, dtype):
            return ctx.enter_context(nc.sbuf_tensor(name, shape, dtype))

        def psum(name, shape, dtype):
            return ctx.enter_context(nc.psum_tensor(name, shape, dtype))

        def sem(name):
            return ctx.enter_context(nc.semaphore(name))

        xt = sb("xt", [128, 2 * D], F32)
        lngt = sb("lngt", [128, D], F32)
        lnbt = sb("lnbt", [128, D], F32)
        identb = sb("identb", [128, 128], BF16)
        onesbt = sb("onesbt", [128, 1], BF16)
        cv = sb("cv", [128, NCONST], F32)
        mu = sb("mu", [128, NT], F32)
        rstd = sb("rstd", [128, NT], F32)
        sq = sb("sq", [128, D], F32)
        usc = sb("usc", [128, D], F32)
        hbuf = sb("hbuf", [128, NT * D], BF16)
        hT = sb("hT", [128, DC * TPC], BF16)
        wkvb = sb("wkvb", [128, DC * 2 * D], BF16)
        kvo = sb("kvo", [128, 2 * TPC], BF16)
        sqo = sb("sqo", [128, TPC], BF16)
        sqqsb = sb("sqqsb", [1, 2 * TPC], F32)
        vtb = sb("vtb", [128, 2 * D], BF16)
        ps_t = psum("ps_t", [128, 128], BF16)
        ps_kv = psum("ps_kv", [128, 512], F32)
        ps_v = psum("ps_v", [128, 512], F32)
        ps_sq = psum("ps_sq", [1, 512], F32)
        qG, qV, qS, qT, qY = sem("qG"), sem("qV"), sem("qS"), sem("qT"), sem("qY")
        block = ctx.enter_context(nc.Block())
        qX0, qX1 = sem("qX0"), sem("qX1")
        qK0, qK1 = sem("qK0"), sem("qK1")
        qW0, qW1 = sem("qW0"), sem("qW1")
        qQ0, qQ1 = sem("qQ0"), sem("qQ1")
        sems = {"g": qG, "v": qV, "s": qS, "t": qT, "y": qY,
                "x0": qX0, "x1": qX1, "k0": qK0, "k1": qK1,
                "w0": qW0, "w1": qW1, "sq0": qQ0, "sq1": qQ1}
        sch = Sched()

        sch.op("g", lambda g: g.dma_start(out=lngt[:], in_=lng[:]))
        sch.op("g", lambda g: g.dma_start(out=lnbt[:], in_=lnb[:]))
        sch.op("g", lambda g: g.dma_start(out=identb[:], in_=ident[:]))
        sch.op("g", lambda g: g.dma_start(out=onesbt[:], in_=onesb[:]))
        sch.op("g", lambda g: g.dma_start(out=cv[:], in_=cvals[:]))
        for c in range(DC):
            sch.op("g", lambda g, c=c: g.dma_start(
                out=wkvb[:, c * 2 * D:(c + 1) * 2 * D],
                in_=wkv[c * 128:(c + 1) * 128, :]))
        lall = sch.mark("g")

        def ldx(t, wait=()):
            s_ = t % 2
            sch.op("g", lambda g, t=t, s_=s_: g.dma_start(
                out=xt[:, s_ * D:(s_ + 1) * D],
                in_=x[t * 128:(t + 1) * 128, :]), wait=wait,
                inc=("x%d" % s_, 16))

        ldx(0)
        ldx(1)
        for t in range(NT):
            s_ = t % 2
            xin = xt[:, s_ * D:(s_ + 1) * D]
            _ln_stats(sch, xin, mu[:, t:t + 1], rstd[:, t:t + 1], sq[:],
                      cv[:, 0:1], cv[:, 2:3],
                      wait_in=[("x%d" % s_, 16 * (t // 2 + 1)), ("g", lall)])
            sch.op("v", lambda v, xin=xin, t=t: v.tensor_scalar(
                usc[:], xin, mu[:, t:t + 1], rstd[:, t:t + 1],
                ALU.subtract, ALU.mult), wait=[("v", sch.mark("v"))])
            if t + 2 < NT:
                ldx(t + 2, wait=[("v", sch.mark("v"))])
            sch.op("v", lambda v: v.tensor_mul(sq[:], usc[:], lngt[:]),
                   wait=[("v", sch.mark("v"))])
            sch.op("v", lambda v, t=t: v.tensor_add(
                hbuf[:, t * D:(t + 1) * D], sq[:], lnbt[:]),
                wait=[("v", sch.mark("v"))])

        for t in range(NT):
            for c in range(DC):
                sch.op("t", lambda tt, t=t, c=c: tt.transpose(
                    ps_t[:], hbuf[:, t * D + c * 128: t * D + (c + 1) * 128],
                    identb[:]),
                    wait=[("v", sch.mark("v")), ("g", lall)])
                sch.op("v", lambda v, t=t, c=c: v.tensor_copy(
                    hT[:, c * TPC + t * 128: c * TPC + (t + 1) * 128], ps_t[:]),
                    wait=[("t", sch.mark("t"))])

        for o in range(2 * DC):
            slot = o % 2
            w = [("v", sch.mark("v")), ("g", lall)]
            if o >= 2:
                w.append(("k%d" % slot, 16 * (o // 2)))
            if o >= 2 and (o - 2) < 8:
                w.append(("s", sch.mark("s")))
            for f in range(4):
                for c in range(DC):
                    sch.op("t", lambda tt, o=o, f=f, c=c: tt.matmul(
                        ps_kv[:],
                        wkvb[:, c * 2 * D + o * 128: c * 2 * D + (o + 1) * 128],
                        hT[:, c * TPC + f * 512: c * TPC + (f + 1) * 512],
                        start=(c == 0), stop=(c == DC - 1)),
                        wait=w if (f == 0 and c == 0) else ())
                sch.op("v", lambda v, slot=slot, f=f: v.tensor_copy(
                    kvo[:, slot * TPC + f * 512: slot * TPC + (f + 1) * 512],
                    ps_kv[:]), wait=[("t", sch.mark("t"))])
                if f < 3:
                    sch.nop("t", [("v", sch.mark("v"))])
            evm = sch.mark("v")
            sch.op("y", lambda y, o=o, slot=slot: y.dma_start(
                out=kvT_s[o * 128:(o + 1) * 128, :],
                in_=kvo[:, slot * TPC:(slot + 1) * TPC]), wait=[("v", evm)],
                inc=("k%d" % slot, 16))
            if o < 8:
                sch.op("s", lambda s, slot=slot: s.activation(
                    sqo[:], kvo[:, slot * TPC:(slot + 1) * TPC], AF.Square),
                    wait=[("v", evm), ("t", sch.mark("t"))])
                sqm = sch.mark("s")
                for hh in (2 * o, 2 * o + 1):
                    base = (hh % 2) * 64
                    hs = hh % 2
                    for f in range(4):
                        wv = [("s", sqm)]
                        if f == 0 and hh >= 2:
                            wv.append(("sq%d" % hs, 16 * (hh // 2)))
                        sch.op("t", lambda tt, base=base, f=f: tt.matmul(
                            ps_sq[:], onesbt[base:base + 64, 0:1],
                            sqo[base:base + 64, f * 512:(f + 1) * 512],
                            start=True, stop=True), wait=wv)
                        sch.op("v", lambda v, hs=hs, f=f: v.tensor_copy(
                            sqqsb[0:1, hs * TPC + f * 512: hs * TPC + (f + 1) * 512],
                            ps_sq[:]), wait=[("t", sch.mark("t"))])
                        sch.nop("t", [("v", sch.mark("v"))])
                    sch.op("y", lambda y, hh=hh, hs=hs: y.dma_start(
                        out=sqq_s[hh:hh + 1, :],
                        in_=sqqsb[0:1, hs * TPC:(hs + 1) * TPC]),
                        wait=[("v", sch.mark("v"))], inc=("sq%d" % hs, 16))
        for t in range(NT):
            slot = t % 2
            w = [("v", sch.mark("v"))]
            if t >= 2:
                w.append(("w%d" % slot, 16 * (t // 2)))
            for f in range(2):
                for c in range(DC):
                    sch.op("t", lambda tt, t=t, f=f, c=c: tt.matmul(
                        ps_v[:],
                        hT[:, c * TPC + t * 128: c * TPC + (t + 1) * 128],
                        wkvb[:, c * 2 * D + D + f * 512: c * 2 * D + D + (f + 1) * 512],
                        start=(c == 0), stop=(c == DC - 1)),
                        wait=w if (f == 0 and c == 0) else ())
                sch.op("v", lambda v, slot=slot, f=f: v.tensor_copy(
                    vtb[:, slot * D + f * 512: slot * D + (f + 1) * 512],
                    ps_v[:]), wait=[("t", sch.mark("t"))])
                if f == 0:
                    sch.nop("t", [("v", sch.mark("v"))])
            sch.op("y", lambda y, t=t, slot=slot: y.dma_start(
                out=vtok_s[t * 128:(t + 1) * 128, :],
                in_=vtb[:, slot * D:(slot + 1) * D]),
                wait=[("v", sch.mark("v"))], inc=("w%d" % slot, 16))
        sch.nop("y", [(k, sch.mark(k)) for k in
                      ("k0", "k1", "w0", "w1", "sq0", "sq1")])

        sch.attach(block, sems)
    return nc


def _fm_stats(sch, xb, c_off, Ftok, ps_sx, ps_sxx, ones1, sqsc,
              st_sx, st_sxx, wait_in=()):
    for f in range(Ftok // 512):
        for c in range(DC):
            sl = slice(c_off(c) + f * 512, c_off(c) + (f + 1) * 512)
            w = list(wait_in) + [("t", sch.mark("t"))]
            sch.op("s", lambda s, sl=sl: s.activation(
                sqsc[:, 0:512], xb[:, sl], AF.Square), wait=w)
            sch.op("t", lambda t, sl=sl, c=c: t.matmul(
                ps_sx[:], ones1[:, 0:1], xb[:, sl],
                start=(c == 0), stop=(c == DC - 1)),
                wait=[("s", sch.mark("s"))])
            sch.op("t", lambda t, c=c: t.matmul(
                ps_sxx[:], ones1[:, 0:1], sqsc[:, 0:512],
                start=(c == 0), stop=(c == DC - 1)))
        sch.op("v", lambda v, f=f: v.tensor_copy(
            st_sx[0:1, f * 512:(f + 1) * 512], ps_sx[:]),
            wait=[("t", sch.mark("t"))])
        sch.op("v", lambda v, f=f: v.tensor_copy(
            st_sxx[0:1, f * 512:(f + 1) * 512], ps_sxx[:]))
        sch.nop("t", [("v", sch.mark("v"))])


def _fm_rows(sch, Ftok, st_sx, st_sxx, murow, rsrow, varsc, eps_ap, invd):
    sch.op("v", lambda v: v.tensor_scalar_mul(
        murow[0:1, :], st_sx[0:1, :], invd), wait=[("v", sch.mark("v"))])
    sch.op("s", lambda s: s.activation(
        varsc[0:1, :], murow[0:1, :], AF.Square), wait=[("v", sch.mark("v"))])
    sch.op("v", lambda v: v.tensor_scalar(
        rsrow[0:1, :], st_sxx[0:1, :], invd, None, ALU.mult),
        wait=[("s", sch.mark("s"))])
    sch.op("v", lambda v: v.tensor_sub(
        rsrow[0:1, :], rsrow[0:1, :], varsc[0:1, :]),
        wait=[("v", sch.mark("v"))])
    sch.op("s", lambda s: s.activation(
        rsrow[0:1, :], rsrow[0:1, :], AF.Sqrt, bias=eps_ap),
        wait=[("v", sch.mark("v"))])
    sch.op("v", lambda v: v.reciprocal(rsrow[0:1, :], rsrow[0:1, :]),
        wait=[("s", sch.mark("s"))])


def _fm_bcast(sch, Ftok, row, out_bc, onesf1, ps_bc):
    for f in range(Ftok // 512):
        sch.op("t", lambda t, f=f: t.matmul(
            ps_bc[:], onesf1[0:1, :], row[0:1, f * 512:(f + 1) * 512],
            start=True, stop=True), wait=[("v", sch.mark("v"))])
        sch.op("v", lambda v, f=f: v.tensor_copy(
            out_bc[:, f * 512:(f + 1) * 512], ps_bc[:]),
            wait=[("t", sch.mark("t"))])
        sch.nop("t", [("v", sch.mark("v"))])


def build_u3():
    nc = bass.Bass("TRN2", target_bir_lowering=False)
    x1T = nc.dram_tensor("x1T_s", [D, TPC], F32, kind="ExternalInput").ap()
    wrT = nc.dram_tensor("wrT", [D, D], F32, kind="ExternalInput").ap()
    wiT = nc.dram_tensor("wiT", [D, D], F32, kind="ExternalInput").ap()
    Gr = nc.dram_tensor("Gr", [D, D], F32, kind="ExternalInput").ap()
    Gi = nc.dram_tensor("Gi", [D, D], F32, kind="ExternalInput").ap()
    Gin = nc.dram_tensor("Gin", [D, D], F32, kind="ExternalInput").ap()
    Fr = nc.dram_tensor("Fr", [D, D], F32, kind="ExternalInput").ap()
    Fin = nc.dram_tensor("Fin", [D, D], F32, kind="ExternalInput").ap()
    biasc = nc.dram_tensor("biasc", [128, DC], F32, kind="ExternalInput").ap()
    onesb = nc.dram_tensor("ones_bf", [128, 1], BF16, kind="ExternalInput").ap()
    onesf = nc.dram_tensor("ones_f", [1, 128], F32, kind="ExternalInput").ap()
    cvals = nc.dram_tensor("cvals", [128, NCONST], F32, kind="ExternalInput").ap()
    x2T_s = nc.dram_tensor("x2T_s", [D, TPC], F32, kind="ExternalOutput").ap()

    with contextlib.ExitStack() as ctx:
        def sb(name, shape, dtype):
            return ctx.enter_context(nc.sbuf_tensor(name, shape, dtype))

        def psum(name, shape, dtype):
            return ctx.enter_context(nc.psum_tensor(name, shape, dtype))

        def sem(name):
            return ctx.enter_context(nc.semaphore(name))

        Msb = sb("Msb", [128, DC * D], BF16)
        cv = sb("cv", [128, NCONST], F32)
        onesbt = sb("onesbt", [128, 1], BF16)
        onesft = sb("onesft", [1, 128], F32)
        bic = sb("bic", [128, DC], F32)
        sems = {k: sem("q" + k) for k in
                ("g", "v", "s", "t", "y", "o0", "o1", "x0", "x1")}
        block = None
        sch = Sched()

        with contextlib.ExitStack() as mctx:
            def msb(name, shape, dtype):
                return mctx.enter_context(nc.sbuf_tensor(name, shape, dtype))

            wrb = msb("wrb", [128, DC * D], BF16)
            wib = msb("wib", [128, DC * D], BF16)
            Grb = msb("Grb", [128, DC * D], BF16)
            Gib = msb("Gib", [128, DC * D], BF16)
            Ginb = msb("Ginb", [128, DC * D], BF16)
            Trb = msb("Trb", [128, DC * D], BF16)
            Tib = msb("Tib", [128, DC * D], BF16)
            ps_m = psum("ps_m", [128, 512], F32)

            for c in range(DC):
                for dst, srcap in ((wrb, wrT), (wib, wiT), (Grb, Gr),
                                   (Gib, Gi), (Ginb, Gin)):
                    sch.op("g", lambda g, dst=dst, srcap=srcap, c=c: g.dma_start(
                        out=dst[:, c * D:(c + 1) * D],
                        in_=srcap[c * 128:(c + 1) * 128, :]))
            sch.op("g", lambda g: g.dma_start(out=cv[:], in_=cvals[:]))
            sch.op("g", lambda g: g.dma_start(out=onesbt[:], in_=onesb[:]))
            sch.op("g", lambda g: g.dma_start(out=onesft[:], in_=onesf[:]))
            sch.op("g", lambda g: g.dma_start(out=bic[:], in_=biasc[:]))
            lA = sch.mark("g")

            def mm_pair(dst, lhs1, rhs1, lhs2, rhs2, wait0):
                for k in range(DC):
                    for f in range(D // 512):
                        for c in range(DC):
                            sch.op("t", lambda t, k=k, f=f, c=c, lhs1=lhs1, rhs1=rhs1: t.matmul(
                                ps_m[:],
                                lhs1[:, c * D + k * 128:c * D + (k + 1) * 128],
                                rhs1[:, c * D + f * 512:c * D + (f + 1) * 512],
                                start=(c == 0), stop=False),
                                wait=wait0 if (k == 0 and f == 0 and c == 0) else ())
                        for c in range(DC):
                            sch.op("t", lambda t, k=k, f=f, c=c, lhs2=lhs2, rhs2=rhs2: t.matmul(
                                ps_m[:],
                                lhs2[:, c * D + k * 128:c * D + (k + 1) * 128],
                                rhs2[:, c * D + f * 512:c * D + (f + 1) * 512],
                                start=False, stop=(c == DC - 1)))
                        sch.op("v", lambda v, dst=dst, k=k, f=f: v.tensor_copy(
                            dst[:, k * D + f * 512:k * D + (f + 1) * 512],
                            ps_m[:]), wait=[("t", sch.mark("t"))])
                        sch.nop("t", [("v", sch.mark("v"))])

            mm_pair(Trb, wrb, Grb, wib, Ginb, [("g", lA)])
            mm_pair(Tib, wrb, Gib, wib, Grb, [])
            wrF = wrb
            wiF = wib
            for c in range(DC):
                sch.op("g", lambda g, c=c: g.dma_start(
                    out=wrF[:, c * D:(c + 1) * D],
                    in_=Fr[c * 128:(c + 1) * 128, :]),
                    wait=[("v", sch.mark("v"))])
                sch.op("g", lambda g, c=c: g.dma_start(
                    out=wiF[:, c * D:(c + 1) * D],
                    in_=Fin[c * 128:(c + 1) * 128, :]))
            lF = sch.mark("g")
            mm_pair(Msb, wrF, Trb, wiF, Tib, [("g", lF)])
        xb = sb("xb", [128, DC * TPC], BF16)
        ub = sb("ub", [128, DC * TPC], BF16)
        yb = sb("yb", [128, DC * TPC], BF16)
        sqsc = sb("sqsc", [128, 512], BF16)
        strow = sb("strow", [1, 2 * TPC], F32)
        varsc = sb("varsc", [1, TPC], F32)
        mubc = sb("mubc", [128, TPC], F32)
        rsbc = sb("rsbc", [128, TPC], F32)
        tsc = sb("tsc", [128, TPC], F32)
        xo = sb("xo", [128, 2 * TPC], F32)
        xr = sb("xr", [128, 2 * TPC], F32)
        ps_sx = psum("ps_sx", [1, 512], F32)
        ps_sxx = psum("ps_sxx", [1, 512], F32)
        ps_bc = psum("ps_bc", [128, 512], F32)
        ps_y = psum("ps_y", [128, 512], F32)

        st_sx = strow[0:1, 0:TPC]
        st_sxx = strow[0:1, TPC:2 * TPC]
        murow = st_sx
        rsrow = st_sxx

        phA = [("v", sch.mark("v")), ("t", sch.mark("t"))]
        for c in range(DC):
            sch.op("g", lambda g, c=c: g.dma_start(
                out=xb[:, c * TPC:(c + 1) * TPC],
                in_=x1T[c * 128:(c + 1) * 128, :]), wait=phA)
        lB = sch.mark("g")
        _fm_stats(sch, xb, lambda c: c * TPC, TPC, ps_sx, ps_sxx, onesbt,
                  sqsc, st_sx, st_sxx, wait_in=[("g", lB)])
        _fm_rows(sch, TPC, st_sx, st_sxx, murow, rsrow, varsc,
                 cv[0:1, 0:1], INV_D)
        _fm_bcast(sch, TPC, murow, mubc, onesft, ps_bc)
        _fm_bcast(sch, TPC, rsrow, rsbc, onesft, ps_bc)
        for c in range(DC):
            sch.op("v", lambda v, c=c: v.tensor_sub(
                tsc[:], xb[:, c * TPC:(c + 1) * TPC], mubc[:]),
                wait=[("v", sch.mark("v"))])
            sch.op("v", lambda v, c=c: v.tensor_mul(
                ub[:, c * TPC:(c + 1) * TPC], tsc[:], rsbc[:]),
                wait=[("v", sch.mark("v"))])
        vu = sch.mark("v")
        for o in range(DC):
            for f in range(4):
                for c in range(DC):
                    sch.op("t", lambda t, o=o, f=f, c=c: t.matmul(
                        ps_y[:],
                        Msb[:, c * D + o * 128:c * D + (o + 1) * 128],
                        ub[:, c * TPC + f * 512:c * TPC + (f + 1) * 512],
                        start=(c == 0), stop=(c == DC - 1)),
                        wait=[("v", vu)] if (f == 0 and c == 0) else ())
                sch.op("v", lambda v, o=o, f=f: v.tensor_scalar(
                    yb[:, o * TPC + f * 512:o * TPC + (f + 1) * 512],
                    ps_y[:], bic[:, o:o + 1], None, ALU.add),
                    wait=[("t", sch.mark("t"))])
                sch.nop("t", [("v", sch.mark("v"))])
        _fm_stats(sch, yb, lambda c: c * TPC, TPC, ps_sx, ps_sxx, onesbt,
                  sqsc, st_sx, st_sxx, wait_in=[("v", sch.mark("v"))])
        _fm_rows(sch, TPC, st_sx, st_sxx, murow, rsrow, varsc,
                 cv[0:1, 0:1], INV_D)
        _fm_bcast(sch, TPC, murow, mubc, onesft, ps_bc)
        _fm_bcast(sch, TPC, rsrow, rsbc, onesft, ps_bc)
        xrm = [0] * DC
        for c in range(2):
            sch.op("g", lambda g, c=c: g.dma_start(
                out=xr[:, (c % 2) * TPC:((c % 2) + 1) * TPC],
                in_=x1T[c * 128:(c + 1) * 128, :]), inc=("x%d" % (c % 2), 16))
        for c in range(DC):
            slot = c % 2
            w = [("v", sch.mark("v")), ("x%d" % slot, 16 * (c // 2 + 1))]
            if c >= 2:
                w.append(("o%d" % slot, 16 * (c // 2)))
            sch.op("v", lambda v, c=c: v.tensor_sub(
                tsc[:], yb[:, c * TPC:(c + 1) * TPC], mubc[:]), wait=w)
            sch.op("v", lambda v: v.tensor_mul(tsc[:], tsc[:], rsbc[:]),
                   wait=[("v", sch.mark("v"))])
            sch.op("v", lambda v, c=c, slot=slot: v.tensor_add(
                xo[:, slot * TPC:(slot + 1) * TPC], tsc[:],
                xr[:, slot * TPC:(slot + 1) * TPC]),
                wait=[("v", sch.mark("v"))])
            vdone = sch.mark("v")
            if c + 2 < DC:
                sch.op("g", lambda g, c=c, slot=slot: g.dma_start(
                    out=xr[:, slot * TPC:(slot + 1) * TPC],
                    in_=x1T[(c + 2) * 128:(c + 3) * 128, :]),
                    wait=[("v", vdone)], inc=("x%d" % slot, 16))
            sch.op("y", lambda y, c=c, slot=slot: y.dma_start(
                out=x2T_s[c * 128:(c + 1) * 128, :],
                in_=xo[:, slot * TPC:(slot + 1) * TPC]),
                wait=[("v", vdone)], inc=("o%d" % slot, 16))
        sch.nop("y", [("o0", sch.mark("o0")), ("o1", sch.mark("o1"))])

        block = ctx.enter_context(nc.Block())
        sch.attach(block, sems)
    return nc






def build_u4():
    nc = bass.Bass("TRN2", target_bir_lowering=False)
    x2T = nc.dram_tensor("x2T_s", [D, TPC], F32, kind="ExternalInput").ap()
    WqT = nc.dram_tensor("WqT", [D, D], F32, kind="ExternalInput").ap()
    WkT = nc.dram_tensor("WkT", [D, D], F32, kind="ExternalInput").ap()
    WvT = nc.dram_tensor("WvT", [D, D], F32, kind="ExternalInput").ap()
    nwc = nc.dram_tensor("nwc", [128, DC], F32, kind="ExternalInput").ap()
    bqc = nc.dram_tensor("bqc", [128, DC], F32, kind="ExternalInput").ap()
    bkc = nc.dram_tensor("bkc", [128, DC], F32, kind="ExternalInput").ap()
    bvr = nc.dram_tensor("bvr", [128, D], F32, kind="ExternalInput").ap()
    onesb = nc.dram_tensor("ones_bf", [128, 1], BF16, kind="ExternalInput").ap()
    onesf = nc.dram_tensor("ones_f", [1, 128], F32, kind="ExternalInput").ap()
    cvals = nc.dram_tensor("cvals", [128, NCONST], F32, kind="ExternalInput").ap()
    qT_s = nc.dram_tensor("qT_s", [D, TPC], BF16, kind="ExternalOutput").ap()
    kT_s = nc.dram_tensor("kT_s", [D, TPC], BF16, kind="ExternalOutput").ap()
    vtok1_s = nc.dram_tensor("vtok1_s", [TPC, D], BF16, kind="ExternalOutput").ap()
    xnT_s = nc.dram_tensor("xnT_s", [D, TPC], BF16, kind="ExternalOutput").ap()

    with contextlib.ExitStack() as ctx:
        def sb(name, shape, dtype):
            return ctx.enter_context(nc.sbuf_tensor(name, shape, dtype))

        def psum(name, shape, dtype):
            return ctx.enter_context(nc.psum_tensor(name, shape, dtype))

        def sem(name):
            return ctx.enter_context(nc.semaphore(name))

        xb = sb("xb", [128, DC * TPC], BF16)
        xnb = sb("xnb", [128, DC * TPC], BF16)
        wqb = sb("wqb", [128, DC * D], BF16)
        wkb = sb("wkb", [128, DC * D], BF16)
        wvb = sb("wvb", [128, DC * D], BF16)
        nwt = sb("nwt", [128, DC], F32)
        bqt = sb("bqt", [128, DC], F32)
        bkt = sb("bkt", [128, DC], F32)
        bvt = sb("bvt", [128, D], F32)
        cv = sb("cv", [128, NCONST], F32)
        onesbt = sb("onesbt", [128, 1], BF16)
        onesft = sb("onesft", [1, 128], F32)
        sqsc = sb("sqsc", [128, 512], BF16)
        strow = sb("strow", [1, 2 * TPC], F32)
        varsc = sb("varsc", [1, TPC], F32)
        mubc = sb("mubc", [128, TPC], F32)
        rsbc = sb("rsbc", [128, TPC], F32)
        tsc = sb("tsc", [128, TPC], F32)
        qo = sb("qo", [128, 2 * TPC], BF16)
        ko = sb("ko", [128, 2 * TPC], BF16)
        vo = sb("vo", [128, 2 * D], BF16)
        ps_sx = psum("ps_sx", [1, 512], F32)
        ps_sxx = psum("ps_sxx", [1, 512], F32)
        ps_bc = psum("ps_bc", [128, 512], F32)
        ps_q = psum("ps_q", [128, 512], F32)
        ps_k = psum("ps_k", [128, 512], F32)
        ps_v = psum("ps_v", [128, 512], F32)
        sems = {k: sem("q" + k) for k in
                ("g", "v", "s", "t", "y", "q0", "q1", "k0", "k1",
                 "w0", "w1", "n0", "n1")}
        sch = Sched()

        for c in range(DC):
            sch.op("g", lambda g, c=c: g.dma_start(
                out=xb[:, c * TPC:(c + 1) * TPC],
                in_=x2T[c * 128:(c + 1) * 128, :]))
            for dst, srcap in ((wqb, WqT), (wkb, WkT), (wvb, WvT)):
                sch.op("g", lambda g, dst=dst, srcap=srcap, c=c: g.dma_start(
                    out=dst[:, c * D:(c + 1) * D],
                    in_=srcap[c * 128:(c + 1) * 128, :]))
        for dst, srcap in ((nwt, nwc), (bqt, bqc), (bkt, bkc), (bvt, bvr),
                           (cv, cvals), (onesbt, onesb), (onesft, onesf)):
            sch.op("g", lambda g, dst=dst, srcap=srcap: g.dma_start(
                out=dst[:], in_=srcap[:]))
        lall = sch.mark("g")

        st_sx = strow[0:1, 0:TPC]
        st_sxx = strow[0:1, TPC:2 * TPC]

        _fm_stats(sch, xb, lambda c: c * TPC, TPC, ps_sx, ps_sxx, onesbt,
                  sqsc, st_sx, st_sxx, wait_in=[("g", lall)])
        sch.op("v", lambda v: v.tensor_scalar_mul(
            st_sx[0:1, :], st_sx[0:1, :], INV_D), wait=[("v", sch.mark("v"))])
        sch.op("s", lambda s: s.activation(
            varsc[0:1, :], st_sx[0:1, :], AF.Square),
            wait=[("v", sch.mark("v"))])
        sch.op("v", lambda v: v.tensor_scalar(
            st_sxx[0:1, :], st_sxx[0:1, :], INV_D, None, ALU.mult),
            wait=[("s", sch.mark("s"))])
        sch.op("v", lambda v: v.tensor_sub(
            varsc[0:1, :], st_sxx[0:1, :], varsc[0:1, :]),
            wait=[("v", sch.mark("v"))])
        sch.op("s", lambda s: s.activation(
            st_sxx[0:1, :], varsc[0:1, :], AF.Sqrt, bias=cv[0:1, 0:1]),
            wait=[("v", sch.mark("v"))])
        sch.op("v", lambda v: v.reciprocal(st_sxx[0:1, :], st_sxx[0:1, :]),
               wait=[("s", sch.mark("s"))])
        sch.op("s", lambda s: s.activation(
            varsc[0:1, :], st_sxx[0:1, :], AF.Square),
            wait=[("v", sch.mark("v"))])
        sch.op("v", lambda v: v.tensor_scalar(
            varsc[0:1, :], varsc[0:1, :], -1e-5, 1.0, ALU.mult, ALU.add),
            wait=[("s", sch.mark("s"))])
        sch.op("s", lambda s: s.activation(
            varsc[0:1, :], varsc[0:1, :], AF.Sqrt, bias=cv[0:1, 1:2]),
            wait=[("v", sch.mark("v"))])
        sch.op("v", lambda v: v.reciprocal(varsc[0:1, :], varsc[0:1, :]),
               wait=[("s", sch.mark("s"))])
        sch.op("v", lambda v: v.tensor_mul(
            st_sxx[0:1, :], st_sxx[0:1, :], varsc[0:1, :]),
            wait=[("v", sch.mark("v"))])
        _fm_bcast(sch, TPC, st_sx, mubc, onesft, ps_bc)
        _fm_bcast(sch, TPC, st_sxx, rsbc, onesft, ps_bc)
        for c in range(DC):
            slot = c % 2
            sch.op("v", lambda v, c=c: v.tensor_sub(
                tsc[:], xb[:, c * TPC:(c + 1) * TPC], mubc[:]),
                wait=[("v", sch.mark("v"))])
            sch.op("v", lambda v, c=c: v.scalar_tensor_tensor(
                xnb[:, c * TPC:(c + 1) * TPC], tsc[:],
                nwt[:, c:c + 1], rsbc[:], ALU.mult, ALU.mult),
                wait=[("v", sch.mark("v"))])
            sch.op("y", lambda y, c=c: y.dma_start(
                out=xnT_s[c * 128:(c + 1) * 128, :],
                in_=xnb[:, c * TPC:(c + 1) * TPC]),
                wait=[("v", sch.mark("v"))], inc=("n%d" % slot, 16))
        vxn = sch.mark("v")

        for o in range(DC):
            slot = o % 2
            for kind, wb, ps, bt, outap, semk in (
                    ("q", wqb, ps_q, bqt, qT_s, "q"),
                    ("k", wkb, ps_k, bkt, kT_s, "k")):
                ob = qo if kind == "q" else ko
                w = [("v", sch.mark("v"))]
                if o >= 2:
                    w.append(("%s%d" % (semk, slot), 16 * (o // 2)))
                for f in range(4):
                    for c in range(DC):
                        sch.op("t", lambda t, wb=wb, ps=ps, o=o, f=f, c=c: t.matmul(
                            ps[:],
                            wb[:, c * D + o * 128:c * D + (o + 1) * 128],
                            xnb[:, c * TPC + f * 512:c * TPC + (f + 1) * 512],
                            start=(c == 0), stop=(c == DC - 1)),
                            wait=w if (f == 0 and c == 0) else ())
                    if kind == "q":
                        sch.op("v", lambda v, ob=ob, ps=ps, bt=bt, slot=slot, f=f, o=o: v.tensor_scalar(
                            ob[:, slot * TPC + f * 512:slot * TPC + (f + 1) * 512],
                            ps[:], bt[:, o:o + 1], 0.125, ALU.add, ALU.mult),
                            wait=[("t", sch.mark("t"))])
                    else:
                        sch.op("v", lambda v, ob=ob, ps=ps, bt=bt, slot=slot, f=f, o=o: v.tensor_scalar(
                            ob[:, slot * TPC + f * 512:slot * TPC + (f + 1) * 512],
                            ps[:], bt[:, o:o + 1], None, ALU.add),
                            wait=[("t", sch.mark("t"))])
                    if f < 3:
                        sch.nop("t", [("v", sch.mark("v"))])
                sch.op("y", lambda y, ob=ob, outap=outap, o=o, slot=slot: y.dma_start(
                    out=outap[o * 128:(o + 1) * 128, :],
                    in_=ob[:, slot * TPC:(slot + 1) * TPC]),
                    wait=[("v", sch.mark("v"))], inc=("%s%d" % (semk, slot), 16))
        for t in range(NT):
            slot = t % 2
            w = [("v", sch.mark("v"))]
            if t >= 2:
                w.append(("w%d" % slot, 16 * (t // 2)))
            for f in range(2):
                for c in range(DC):
                    sch.op("t", lambda tt, t=t, f=f, c=c: tt.matmul(
                        ps_v[:],
                        xnb[:, c * TPC + t * 128:c * TPC + (t + 1) * 128],
                        wvb[:, c * D + f * 512:c * D + (f + 1) * 512],
                        start=(c == 0), stop=(c == DC - 1)),
                        wait=w if (f == 0 and c == 0) else ())
                sch.op("v", lambda v, slot=slot, f=f: v.tensor_add(
                    vo[:, slot * D + f * 512:slot * D + (f + 1) * 512],
                    ps_v[:], bvt[:, f * 512:(f + 1) * 512]),
                    wait=[("t", sch.mark("t"))])
                if f == 0:
                    sch.nop("t", [("v", sch.mark("v"))])
            sch.op("y", lambda y, t=t, slot=slot: y.dma_start(
                out=vtok1_s[t * 128:(t + 1) * 128, :],
                in_=vo[:, slot * D:(slot + 1) * D]),
                wait=[("v", sch.mark("v"))], inc=("w%d" % slot, 16))
        sch.nop("y", [(k, sch.mark(k)) for k in
                      ("q0", "q1", "k0", "k1", "w0", "w1", "n0", "n1")])

        block = ctx.enter_context(nc.Block())
        sch.attach(block, sems)
    return nc


def build_u5():
    nc = bass.Bass("TRN2", target_bir_lowering=False)
    qTh = nc.dram_tensor("qTh", [4 * HD, N], BF16, kind="ExternalInput").ap()
    kTh = nc.dram_tensor("kTh", [4 * HD, N], BF16, kind="ExternalInput").ap()
    vth = nc.dram_tensor("vth", [N, 4 * HD], BF16, kind="ExternalInput").ap()
    msk = nc.dram_tensor("msk", [128, 4 * 512], F32, kind="ExternalInput").ap()
    onesb = nc.dram_tensor("ones_bf", [128, 1], BF16, kind="ExternalInput").ap()
    onesf = nc.dram_tensor("ones_f", [1, 128], F32, kind="ExternalInput").ap()
    oT_s = nc.dram_tensor("oT_s", [4 * HD, N], BF16, kind="ExternalOutput").ap()

    NQT = N // 512
    with contextlib.ExitStack() as ctx:
        def sb(name, shape, dtype):
            return ctx.enter_context(nc.sbuf_tensor(name, shape, dtype))

        def psum(name, shape, dtype):
            return ctx.enter_context(nc.psum_tensor(name, shape, dtype))

        def sem(name):
            return ctx.enter_context(nc.semaphore(name))

        qb = sb("qb", [128, 2 * N], BF16)
        kb = sb("kb", [128, 2 * N], BF16)
        vb = sb("vb", [128, (N // 128) * 256], BF16)
        ob = sb("ob", [128, 2 * N], BF16)
        mskb = sb("mskb", [128, 4 * 512], F32)
        onesbt = sb("onesbt", [128, 1], BF16)
        onesft = sb("onesft", [1, 128], F32)
        esc = sb("esc", [128, 2 * 512], BF16)
        zrow = sb("zrow", [1, 512], F32)
        zbsb = sb("zbsb", [64, 512], F32)
        ps_s = psum("ps_s", [128, 2 * 512], F32)
        ps_o = psum("ps_o", [64, 512], F32)
        ps_z = psum("ps_z", [1, 512], F32)
        ps_zb = psum("ps_zb", [64, 512], F32)
        sems = {k: sem("q" + k) for k in ("g", "v", "s", "t", "y")}
        sch = Sched()

        for ch in range(2):
            sch.op("g", lambda g, ch=ch: g.dma_start(
                out=qb[:, ch * N:(ch + 1) * N], in_=qTh[ch * 128:(ch + 1) * 128, :]))
            sch.op("g", lambda g, ch=ch: g.dma_start(
                out=kb[:, ch * N:(ch + 1) * N], in_=kTh[ch * 128:(ch + 1) * 128, :]))
        for jt in range(N // 128):
            sch.op("g", lambda g, jt=jt: g.dma_start(
                out=vb[:, jt * 256:(jt + 1) * 256],
                in_=vth[jt * 128:(jt + 1) * 128, :]))
        sch.op("g", lambda g: g.dma_start(out=mskb[:], in_=msk[:]))
        sch.op("g", lambda g: g.dma_start(out=onesbt[:], in_=onesb[:]))
        sch.op("g", lambda g: g.dma_start(out=onesft[:], in_=onesf[:]))
        lall = sch.mark("g")

        for hl in range(4):
            ch = hl // 2
            base = (hl % 2) * 64
            for qt in range(NQT):
                jbl = (qt + 1) * 4
                for jb in range(jbl):
                    sslot = jb % 2
                    pss = ps_s[:, sslot * 512:(sslot + 1) * 512]
                    esl = esc[:, sslot * 512:(sslot + 1) * 512]
                    w = [("g", lall), ("s", sch.mark("s"))]
                    sch.op("t", lambda t, ch=ch, base=base, qt=qt, jb=jb, pss=pss: t.matmul(
                        pss,
                        kb[base:base + 64, ch * N + jb * 128:ch * N + (jb + 1) * 128],
                        qb[base:base + 64, ch * N + qt * 512:ch * N + (qt + 1) * 512],
                        start=True, stop=True), wait=w)
                    p = jb - qt * 4
                    if p >= 0:
                        sch.op("v", lambda v, pss=pss, p=p: v.tensor_add(
                            pss, pss, mskb[:, p * 512:(p + 1) * 512]),
                            wait=[("t", sch.mark("t"))])
                        wexp = [("v", sch.mark("v")), ("t", sch.mark("t"))]
                    else:
                        wexp = [("t", sch.mark("t"))]
                    sch.op("s", lambda s, esl=esl, pss=pss: s.activation(
                        esl, pss, AF.Exp), wait=wexp)
                    sm = sch.mark("s")
                    sch.op("t", lambda t, jb=jb, hl=hl, esl=esl, jbl=jbl: t.matmul(
                        ps_o[:], vb[:, jb * 256 + hl * 64:jb * 256 + (hl + 1) * 64],
                        esl, start=(jb == 0), stop=(jb == jbl - 1)),
                        wait=[("s", sm)])
                    sch.op("t", lambda t, jb=jb, esl=esl, jbl=jbl: t.matmul(
                        ps_z[:], onesbt[:, 0:1], esl,
                        start=(jb == 0), stop=(jb == jbl - 1)))
                sch.op("v", lambda v: v.tensor_copy(zrow[0:1, :], ps_z[:]),
                       wait=[("t", sch.mark("t"))])
                sch.op("v", lambda v: v.reciprocal(zrow[0:1, :], zrow[0:1, :]),
                       wait=[("v", sch.mark("v"))])
                sch.op("t", lambda t: t.matmul(
                    ps_zb[:], onesft[0:1, 0:64], zrow[0:1, :],
                    start=True, stop=True), wait=[("v", sch.mark("v"))])
                sch.op("v", lambda v: v.tensor_copy(zbsb[:], ps_zb[:]),
                       wait=[("t", sch.mark("t"))])
                sch.op("v", lambda v, ch=ch, base=base, qt=qt: v.tensor_mul(
                    ob[base:base + 64, ch * N + qt * 512:ch * N + (qt + 1) * 512],
                    ps_o[:], zbsb[:]), wait=[("v", sch.mark("v"))])
                sch.nop("t", [("v", sch.mark("v"))])
        for ch in range(2):
            sch.op("y", lambda y, ch=ch: y.dma_start(
                out=oT_s[ch * 128:(ch + 1) * 128, :],
                in_=ob[:, ch * N:(ch + 1) * N]),
                wait=[("v", sch.mark("v"))])
        sch.nop("y", [("y", sch.mark("y"))])

        block = ctx.enter_context(nc.Block())
        sch.attach(block, sems)
    return nc


def build_u6a():
    nc = bass.Bass("TRN2", target_bir_lowering=False)
    oT = nc.dram_tensor("oT_sl", [D, TPC], BF16, kind="ExternalInput").ap()
    xnT = nc.dram_tensor("xnT_sl", [D, TPC], BF16, kind="ExternalInput").ap()
    x2T = nc.dram_tensor("x2T_sl", [D, TPC], F32, kind="ExternalInput").ap()
    WoT = nc.dram_tensor("WoT", [D, D], F32, kind="ExternalInput").ap()
    boc = nc.dram_tensor("boc", [128, DC], F32, kind="ExternalInput").ap()
    onesb = nc.dram_tensor("ones_bf", [128, 1], BF16, kind="ExternalInput").ap()
    onesf = nc.dram_tensor("ones_f", [1, 128], F32, kind="ExternalInput").ap()
    cvals = nc.dram_tensor("cvals", [128, NCONST], F32, kind="ExternalInput").ap()
    x3T_s = nc.dram_tensor("x3T_s", [D, TPC], F32, kind="ExternalOutput").ap()

    with contextlib.ExitStack() as ctx:
        def sb(name, shape, dtype):
            return ctx.enter_context(nc.sbuf_tensor(name, shape, dtype))

        def psum(name, shape, dtype):
            return ctx.enter_context(nc.psum_tensor(name, shape, dtype))

        def sem(name):
            return ctx.enter_context(nc.semaphore(name))

        ob = sb("ob", [128, DC * TPC], BF16)
        wob = sb("wob", [128, DC * D], BF16)
        blk = sb("blk", [128, DC * TPC], BF16)
        xns = sb("xns", [128, 2 * TPC], BF16)
        xrs = sb("xrs", [128, 2 * TPC], F32)
        xo = sb("xo", [128, 2 * TPC], F32)
        boct = sb("boct", [128, DC], F32)
        cv = sb("cv", [128, NCONST], F32)
        onesbt = sb("onesbt", [128, 1], BF16)
        onesft = sb("onesft", [1, 128], F32)
        sqsc = sb("sqsc", [128, 512], BF16)
        strow = sb("strow", [1, 2 * TPC], F32)
        varsc = sb("varsc", [1, TPC], F32)
        mubc = sb("mubc", [128, TPC], F32)
        rsbc = sb("rsbc", [128, TPC], F32)
        tsc = sb("tsc", [128, TPC], F32)
        ps_a = psum("ps_a", [128, 512], F32)
        ps_sx = psum("ps_sx", [1, 512], F32)
        ps_sxx = psum("ps_sxx", [1, 512], F32)
        ps_bc = psum("ps_bc", [128, 512], F32)
        sems = {k: sem("q" + k) for k in
                ("g", "v", "s", "t", "y", "n0", "n1", "x0", "x1", "o0", "o1")}
        sch = Sched()

        for c in range(DC):
            sch.op("g", lambda g, c=c: g.dma_start(
                out=ob[:, c * TPC:(c + 1) * TPC],
                in_=oT[c * 128:(c + 1) * 128, :]))
            sch.op("g", lambda g, c=c: g.dma_start(
                out=wob[:, c * D:(c + 1) * D],
                in_=WoT[c * 128:(c + 1) * 128, :]))
        for dst, srcap in ((boct, boc), (cv, cvals), (onesbt, onesb),
                           (onesft, onesf)):
            sch.op("g", lambda g, dst=dst, srcap=srcap: g.dma_start(
                out=dst[:], in_=srcap[:]))
        lall = sch.mark("g")
        for c in range(2):
            sch.op("g", lambda g, c=c: g.dma_start(
                out=xns[:, (c % 2) * TPC:((c % 2) + 1) * TPC],
                in_=xnT[c * 128:(c + 1) * 128, :]), inc=("n%d" % (c % 2), 16))
        for o in range(DC):
            slot = o % 2
            w = [("g", lall), ("n%d" % slot, 16 * (o // 2 + 1)),
                 ("v", sch.mark("v"))]
            for f in range(4):
                for c in range(DC):
                    sch.op("t", lambda t, o=o, f=f, c=c: t.matmul(
                        ps_a[:],
                        wob[:, c * D + o * 128:c * D + (o + 1) * 128],
                        ob[:, c * TPC + f * 512:c * TPC + (f + 1) * 512],
                        start=(c == 0), stop=(c == DC - 1)),
                        wait=w if (f == 0 and c == 0) else ())
                sch.op("v", lambda v, o=o, f=f, slot=slot: v.tensor_scalar(
                    tsc[:, f * 512:(f + 1) * 512], ps_a[:], boct[:, o:o + 1],
                    None, ALU.add), wait=[("t", sch.mark("t"))])
                sch.op("v", lambda v, o=o, f=f, slot=slot: v.tensor_add(
                    blk[:, o * TPC + f * 512:o * TPC + (f + 1) * 512],
                    tsc[:, f * 512:(f + 1) * 512],
                    xns[:, slot * TPC + f * 512:slot * TPC + (f + 1) * 512]),
                    wait=[("v", sch.mark("v"))])
                if f < 3:
                    sch.nop("t", [("v", sch.mark("v"))])
            vdone = sch.mark("v")
            if o + 2 < DC:
                sch.op("g", lambda g, o=o, slot=slot: g.dma_start(
                    out=xns[:, slot * TPC:(slot + 1) * TPC],
                    in_=xnT[(o + 2) * 128:(o + 3) * 128, :]),
                    wait=[("v", vdone)], inc=("n%d" % slot, 16))
        st_sx = strow[0:1, 0:TPC]
        st_sxx = strow[0:1, TPC:2 * TPC]
        _fm_stats(sch, blk, lambda c: c * TPC, TPC, ps_sx, ps_sxx, onesbt,
                  sqsc, st_sx, st_sxx, wait_in=[("v", sch.mark("v"))])
        _fm_rows(sch, TPC, st_sx, st_sxx, st_sx, st_sxx, varsc,
                 cv[0:1, 0:1], INV_D)
        _fm_bcast(sch, TPC, st_sx, mubc, onesft, ps_bc)
        _fm_bcast(sch, TPC, st_sxx, rsbc, onesft, ps_bc)
        for c in range(2):
            sch.op("g", lambda g, c=c: g.dma_start(
                out=xrs[:, (c % 2) * TPC:((c % 2) + 1) * TPC],
                in_=x2T[c * 128:(c + 1) * 128, :]), inc=("x%d" % (c % 2), 16))
        for c in range(DC):
            slot = c % 2
            w = [("v", sch.mark("v")), ("x%d" % slot, 16 * (c // 2 + 1))]
            if c >= 2:
                w.append(("o%d" % slot, 16 * (c // 2)))
            sch.op("v", lambda v, c=c: v.tensor_sub(
                tsc[:], blk[:, c * TPC:(c + 1) * TPC], mubc[:]), wait=w)
            sch.op("v", lambda v: v.tensor_mul(tsc[:], tsc[:], rsbc[:]),
                   wait=[("v", sch.mark("v"))])
            sch.op("v", lambda v, c=c, slot=slot: v.tensor_add(
                xo[:, slot * TPC:(slot + 1) * TPC], tsc[:],
                xrs[:, slot * TPC:(slot + 1) * TPC]),
                wait=[("v", sch.mark("v"))])
            vdone = sch.mark("v")
            if c + 2 < DC:
                sch.op("g", lambda g, c=c, slot=slot: g.dma_start(
                    out=xrs[:, slot * TPC:(slot + 1) * TPC],
                    in_=x2T[(c + 2) * 128:(c + 3) * 128, :]),
                    wait=[("v", vdone)], inc=("x%d" % slot, 16))
            sch.op("y", lambda y, c=c, slot=slot: y.dma_start(
                out=x3T_s[c * 128:(c + 1) * 128, :],
                in_=xo[:, slot * TPC:(slot + 1) * TPC]),
                wait=[("v", vdone)], inc=("o%d" % slot, 16))
        sch.nop("y", [("o0", sch.mark("o0")), ("o1", sch.mark("o1"))])

        block = ctx.enter_context(nc.Block())
        sch.attach(block, sems)
    return nc


W6 = 2560
PAD = 256


def build_u6b1():
    nc = bass.Bass("TRN2", target_bir_lowering=False)
    x3p = nc.dram_tensor("x3T_pad", [D, W6], F32, kind="ExternalInput").ap()
    mk0 = nc.dram_tensor("mk0c", [128, DC], F32, kind="ExternalInput").ap()
    mk1 = nc.dram_tensor("mk1c", [128, DC], F32, kind="ExternalInput").ap()
    mk2 = nc.dram_tensor("mk2c", [128, DC], F32, kind="ExternalInput").ap()
    onesb = nc.dram_tensor("ones_bf", [128, 1], BF16, kind="ExternalInput").ap()
    onesf = nc.dram_tensor("ones_f", [1, 128], F32, kind="ExternalInput").ap()
    cvals = nc.dram_tensor("cvals", [128, NCONST], F32, kind="ExternalInput").ap()
    u3T_s = nc.dram_tensor("u3T_s", [D, TPC], BF16, kind="ExternalOutput").ap()
    xmT_s = nc.dram_tensor("xmT_s", [D, TPC], BF16, kind="ExternalOutput").ap()

    with contextlib.ExitStack() as ctx:
        def sb(name, shape, dtype):
            return ctx.enter_context(nc.sbuf_tensor(name, shape, dtype))

        def psum(name, shape, dtype):
            return ctx.enter_context(nc.psum_tensor(name, shape, dtype))

        def sem(name):
            return ctx.enter_context(nc.semaphore(name))

        xb = sb("xb", [128, DC * W6], BF16)
        mks = [sb("mk%d" % i, [128, DC], F32) for i in range(3)]
        cv = sb("cv", [128, NCONST], F32)
        onesbt = sb("onesbt", [128, 1], BF16)
        onesft = sb("onesft", [1, 128], F32)
        sqsc = sb("sqsc", [128, 512], BF16)
        strow = sb("strow", [1, 2 * W6], F32)
        varsc = sb("varsc", [1, W6], F32)
        mubc = sb("mubc", [128, W6], F32)
        rsbc = sb("rsbc", [128, W6], F32)
        tsc = sb("tsc", [128, W6], F32)
        t2 = sb("t2", [128, TPC], F32)
        xmo = sb("xmo", [128, 2 * TPC], BF16)
        ps_sx = psum("ps_sx", [1, 512], F32)
        ps_sxx = psum("ps_sxx", [1, 512], F32)
        ps_bc = psum("ps_bc", [128, 512], F32)
        sems = {k: sem("q" + k) for k in
                ("g", "v", "s", "t", "y", "o0", "o1", "u0", "u1")}
        sch = Sched()

        for c in range(DC):
            sch.op("g", lambda g, c=c: g.dma_start(
                out=xb[:, c * W6:(c + 1) * W6],
                in_=x3p[c * 128:(c + 1) * 128, :]))
        for dst, srcap in ((mks[0], mk0), (mks[1], mk1), (mks[2], mk2),
                           (cv, cvals), (onesbt, onesb), (onesft, onesf)):
            sch.op("g", lambda g, dst=dst, srcap=srcap: g.dma_start(
                out=dst[:], in_=srcap[:]))
        lall = sch.mark("g")

        st_sx = strow[0:1, 0:W6]
        st_sxx = strow[0:1, W6:2 * W6]
        _fm_stats(sch, xb, lambda c: c * W6, W6, ps_sx, ps_sxx, onesbt,
                  sqsc, st_sx, st_sxx, wait_in=[("g", lall)])
        _fm_rows(sch, W6, st_sx, st_sxx, st_sx, st_sxx, varsc,
                 cv[0:1, 0:1], INV_D)
        _fm_bcast(sch, W6, st_sx, mubc, onesft, ps_bc)
        _fm_bcast(sch, W6, st_sxx, rsbc, onesft, ps_bc)
        for c in range(DC):
            slot = c % 2
            sch.op("v", lambda v, c=c: v.tensor_sub(
                tsc[:], xb[:, c * W6:(c + 1) * W6], mubc[:]),
                wait=[("v", sch.mark("v"))])
            sch.op("v", lambda v, c=c: v.tensor_mul(
                xb[:, c * W6:(c + 1) * W6], tsc[:], rsbc[:]),
                wait=[("v", sch.mark("v"))])
            vu = sch.mark("v")
            sch.op("y", lambda y, c=c: y.dma_start(
                out=u3T_s[c * 128:(c + 1) * 128, :],
                in_=xb[:, c * W6 + PAD:c * W6 + PAD + TPC]),
                wait=[("v", vu)], inc=("u%d" % slot, 16))
            own = lambda c, off: xb[:, c * W6 + PAD + off:c * W6 + PAD + TPC + off]
            sch.op("v", lambda v, c=c, own=own: v.tensor_scalar(
                t2[:], own(c, -1), mks[0][:, c:c + 1], None, ALU.mult),
                wait=[("v", vu)])
            sch.op("v", lambda v, c=c, own=own: v.scalar_tensor_tensor(
                t2[:], own(c, 0), mks[1][:, c:c + 1], t2[:],
                ALU.mult, ALU.add), wait=[("v", sch.mark("v"))])
            w = [("v", sch.mark("v"))]
            if c >= 2:
                w.append(("o%d" % slot, 16 * (c // 2)))
            sch.op("v", lambda v, c=c, slot=slot, own=own: v.scalar_tensor_tensor(
                xmo[:, slot * TPC:(slot + 1) * TPC], own(c, 1),
                mks[2][:, c:c + 1], t2[:], ALU.mult, ALU.add), wait=w)
            sch.op("y", lambda y, c=c, slot=slot: y.dma_start(
                out=xmT_s[c * 128:(c + 1) * 128, :],
                in_=xmo[:, slot * TPC:(slot + 1) * TPC]),
                wait=[("v", sch.mark("v"))], inc=("o%d" % slot, 16))
        sch.nop("y", [(k, sch.mark(k)) for k in ("o0", "o1", "u0", "u1")])

        block = ctx.enter_context(nc.Block())
        sch.attach(block, sems)
    return nc


def build_u6b2():
    nc = bass.Bass("TRN2", target_bir_lowering=False)
    xmT = nc.dram_tensor("xmT_sl", [D, TPC], BF16, kind="ExternalInput").ap()
    u3T = nc.dram_tensor("u3T_sl", [D, TPC], BF16, kind="ExternalInput").ap()
    x3T = nc.dram_tensor("x3T_sl", [D, TPC], F32, kind="ExternalInput").ap()
    W1T = nc.dram_tensor("W1T", [9 * D, 256], F32, kind="ExternalInput").ap()
    W2T = nc.dram_tensor("W2T", [9 * 256, D], F32, kind="ExternalInput").ap()
    b1c = nc.dram_tensor("b1c", [128, 18], F32, kind="ExternalInput").ap()
    b2r = nc.dram_tensor("b2r", [9, D], F32, kind="ExternalInput").ap()
    Wo1 = nc.dram_tensor("Wo1", [D, 256], F32, kind="ExternalInput").ap()
    bo1c = nc.dram_tensor("bo1c", [128, 2], F32, kind="ExternalInput").ap()
    Wo2 = nc.dram_tensor("Wo2", [256, 9], F32, kind="ExternalInput").ap()
    bo2c = nc.dram_tensor("bo2c", [128, 1], F32, kind="ExternalInput").ap()
    gamc = nc.dram_tensor("gamc", [128, DC], F32, kind="ExternalInput").ap()
    betc = nc.dram_tensor("betc", [128, DC], F32, kind="ExternalInput").ap()
    identf = nc.dram_tensor("ident_f", [128, 128], F32, kind="ExternalInput").ap()
    selbc = nc.dram_tensor("selb", [9, 9 * 128], BF16, kind="ExternalInput").ap()
    onesb = nc.dram_tensor("ones_bf", [128, 1], BF16, kind="ExternalInput").ap()
    onesf = nc.dram_tensor("ones_f", [1, 128], F32, kind="ExternalInput").ap()
    cvals = nc.dram_tensor("cvals", [128, NCONST], F32, kind="ExternalInput").ap()
    x4_s = nc.dram_tensor("x4_s", [TPC, D], F32, kind="ExternalOutput").ap()

    HF = TPC // 4
    NF = HF // 512
    with contextlib.ExitStack() as ctx:
        def sb(name, shape, dtype):
            return ctx.enter_context(nc.sbuf_tensor(name, shape, dtype))

        def psum(name, shape, dtype):
            return ctx.enter_context(nc.psum_tensor(name, shape, dtype))

        def sem(name):
            return ctx.enter_context(nc.semaphore(name))

        zb = sb("zb", [128, DC * TPC], BF16)
        cv = sb("cv", [128, NCONST], F32)
        onesbt = sb("onesbt", [128, 1], BF16)
        onesft = sb("onesft", [1, 128], F32)
        identft = sb("identft", [128, 128], F32)
        gamt = sb("gamt", [128, DC], F32)
        bett = sb("bett", [128, DC], F32)
        sems = {k: sem("q" + k) for k in
                ("g", "v", "s", "t", "y", "o0", "o1", "x0", "x1", "u0", "u1")}
        sch = Sched()
        for dst, srcap in ((cv, cvals), (onesbt, onesb), (onesft, onesf),
                           (identft, identf), (gamt, gamc), (bett, betc)):
            sch.op("g", lambda g, dst=dst, srcap=srcap: g.dma_start(
                out=dst[:], in_=srcap[:]))

        with contextlib.ExitStack() as mctx:
            def msb(name, shape, dtype):
                return mctx.enter_context(nc.sbuf_tensor(name, shape, dtype))

            def mpsum(name, shape, dtype):
                return mctx.enter_context(nc.psum_tensor(name, shape, dtype))

            xmb = msb("xmb", [128, DC * TPC], BF16)
            w1b = msb("w1b", [128, 9 * DC * 256], BF16)
            w2b = msb("w2b", [128, 18 * D], BF16)
            wo1b = msb("wo1b", [128, DC * 256], BF16)
            wo2b = msb("wo2b", [128, 2 * 9], BF16)
            b1t = msb("b1t", [128, 18], F32)
            b2b = msb("b2b", [9, D], BF16)
            bo1t = msb("bo1t", [128, 2], F32)
            bo2t = msb("bo2t", [128, 1], F32)
            tb = msb("tb", [128, 2 * TPC], BF16)
            s9 = msb("s9", [9, TPC], F32)
            stok = msb("stok", [128, NT * 9], F32)
            mxc = msb("mxc", [128, NT], F32)
            ohtok = msb("ohtok", [128, NT * 9], F32)
            ohT = msb("ohT", [9, TPC], F32)
            ohTb = msb("ohTb", [9, TPC], BF16)
            ohbc = msb("ohbc", [128, HF], BF16)
            selbt = msb("selbt", [9, 9 * 128], BF16)
            hb = msb("hb", [128, 2 * HF], BF16)
            hmall = msb("hmall", [128, 18 * HF], BF16)
            u3s = msb("u3s", [128, 2 * HF], BF16)
            ps_h = mpsum("ps_h", [128, 512], F32)
            ps_c = mpsum("ps_c", [128, 512], F32)
            ps_t9 = mpsum("ps_t9", [9, 512], F32)
            ps_tp = mpsum("ps_tp", [128, NT * 9], F32)
            ps_oh = mpsum("ps_oh", [9, 512], F32)
            ps_b = mpsum("ps_b", [128, 512], F32)

            for c in range(DC):
                sch.op("g", lambda g, c=c: g.dma_start(
                    out=xmb[:, c * TPC:(c + 1) * TPC],
                    in_=xmT[c * 128:(c + 1) * 128, :]))
                sch.op("g", lambda g, c=c: g.dma_start(
                    out=wo1b[:, c * 256:(c + 1) * 256],
                    in_=Wo1[c * 128:(c + 1) * 128, :]))
                for p in range(9):
                    sch.op("g", lambda g, p=p, c=c: g.dma_start(
                        out=w1b[:, (p * DC + c) * 256:(p * DC + c + 1) * 256],
                        in_=W1T[p * D + c * 128:p * D + (c + 1) * 128, :]))
            for p in range(9):
                for mc in range(2):
                    sch.op("g", lambda g, p=p, mc=mc: g.dma_start(
                        out=w2b[:, (p * 2 + mc) * D:(p * 2 + mc + 1) * D],
                        in_=W2T[p * 256 + mc * 128:p * 256 + (mc + 1) * 128, :]))
            for mc in range(2):
                sch.op("g", lambda g, mc=mc: g.dma_start(
                    out=wo2b[:, mc * 9:(mc + 1) * 9],
                    in_=Wo2[mc * 128:(mc + 1) * 128, :]))
            for dst, srcap in ((b1t, b1c), (b2b, b2r), (bo1t, bo1c),
                               (bo2t, bo2c), (selbt, selbc)):
                sch.op("g", lambda g, dst=dst, srcap=srcap: g.dma_start(
                    out=dst[:], in_=srcap[:]))
            lall = sch.mark("g")

            for mc in range(2):
                for f in range(4):
                    for c in range(DC):
                        sch.op("t", lambda t, mc=mc, f=f, c=c: t.matmul(
                            ps_h[:],
                            wo1b[:, c * 256 + mc * 128:c * 256 + (mc + 1) * 128],
                            xmb[:, c * TPC + f * 512:c * TPC + (f + 1) * 512],
                            start=(c == 0), stop=(c == DC - 1)),
                            wait=[("g", lall), ("s", sch.mark("s"))] if (f == 0 and c == 0) else ())
                    sch.op("s", lambda s, mc=mc, f=f: s.activation(
                        tb[:, mc * TPC + f * 512:mc * TPC + (f + 1) * 512],
                        ps_h[:], AF.Tanh, bias=bo1t[:, mc:mc + 1]),
                        wait=[("t", sch.mark("t"))])
                    sch.nop("t", [("s", sch.mark("s"))])
            for f in range(4):
                for mc in range(2):
                    sch.op("t", lambda t, mc=mc, f=f: t.matmul(
                        ps_t9[:],
                        wo2b[:, mc * 9:(mc + 1) * 9],
                        tb[:, mc * TPC + f * 512:mc * TPC + (f + 1) * 512],
                        start=(mc == 0), stop=(mc == 1)),
                        wait=[("s", sch.mark("s"))] if mc == 0 else ())
                sch.op("v", lambda v, f=f: v.tensor_scalar(
                    s9[0:9, f * 512:(f + 1) * 512], ps_t9[:],
                    bo2t[0:9, 0:1], None, ALU.add),
                    wait=[("t", sch.mark("t"))])
                sch.nop("t", [("v", sch.mark("v"))])
            for t_ in range(NT):
                sch.op("t", lambda tt, t_=t_: tt.transpose(
                    ps_tp[:, t_ * 9:(t_ + 1) * 9],
                    s9[0:9, t_ * 128:(t_ + 1) * 128], identft[0:9, 0:9]),
                    wait=[("v", sch.mark("v"))])
            sch.op("v", lambda v: v.tensor_copy(stok[:], ps_tp[:]),
                   wait=[("t", sch.mark("t"))])
            for t_ in range(NT):
                sch.op("v", lambda v, t_=t_: v.reduce_max(
                    mxc[:, t_:t_ + 1], stok[:, t_ * 9:(t_ + 1) * 9],
                    axis=AX.X), wait=[("v", sch.mark("v"))])
                sch.op("v", lambda v, t_=t_: v.tensor_scalar(
                    ohtok[:, t_ * 9:(t_ + 1) * 9], stok[:, t_ * 9:(t_ + 1) * 9],
                    mxc[:, t_:t_ + 1], None, ALU.is_ge),
                    wait=[("v", sch.mark("v"))])
            for t_ in range(NT):
                sch.op("t", lambda tt, t_=t_: tt.transpose(
                    ps_oh[0:9, (t_ % 4) * 128:((t_ % 4) + 1) * 128],
                    ohtok[:, t_ * 9:(t_ + 1) * 9], identft[:]),
                    wait=[("v", sch.mark("v"))])
                if t_ % 4 == 3:
                    sch.op("v", lambda v, t_=t_: v.tensor_copy(
                        ohT[0:9, (t_ - 3) * 128:(t_ + 1) * 128], ps_oh[0:9, :]),
                        wait=[("t", sch.mark("t"))])
                    sch.nop("t", [("v", sch.mark("v"))])
            sch.op("v", lambda v: v.tensor_copy(ohTb[0:9, :], ohT[0:9, :]),
                   wait=[("v", sch.mark("v"))])

            for hf in range(4):
                for p in range(9):
                    for f in range(NF):
                        sch.op("t", lambda t, p=p, hf=hf, f=f: t.matmul(
                            ps_b[:], selbt[0:9, p * 128:(p + 1) * 128],
                            ohTb[0:9, hf * HF + f * 512:hf * HF + (f + 1) * 512],
                            start=True, stop=True),
                            wait=[("v", sch.mark("v"))])
                        sch.op("v", lambda v, f=f: v.tensor_copy(
                            ohbc[:, f * 512:(f + 1) * 512], ps_b[:]),
                            wait=[("t", sch.mark("t"))])
                        sch.nop("t", [("v", sch.mark("v"))])
                    for mc in range(2):
                        for f in range(NF):
                            for c in range(DC):
                                sch.op("t", lambda t, p=p, mc=mc, hf=hf, f=f, c=c: t.matmul(
                                    ps_h[:],
                                    w1b[:, (p * DC + c) * 256 + mc * 128:(p * DC + c) * 256 + (mc + 1) * 128],
                                    xmb[:, c * TPC + hf * HF + f * 512:c * TPC + hf * HF + (f + 1) * 512],
                                    start=(c == 0), stop=(c == DC - 1)),
                                    wait=[("v", sch.mark("v"))] if (f == 0 and c == 0) else ())
                            sch.op("v", lambda v, p=p, mc=mc, f=f: v.tensor_scalar(
                                hb[:, mc * HF + f * 512:mc * HF + (f + 1) * 512],
                                ps_h[:], b1t[:, p * 2 + mc:p * 2 + mc + 1],
                                None, ALU.add), wait=[("t", sch.mark("t"))])
                            sch.op("v", lambda v, p=p, mc=mc, f=f: v.tensor_mul(
                                hmall[:, (p * 2 + mc) * HF + f * 512:(p * 2 + mc) * HF + (f + 1) * 512],
                                hb[:, mc * HF + f * 512:mc * HF + (f + 1) * 512],
                                ohbc[:, f * 512:(f + 1) * 512]),
                                wait=[("v", sch.mark("v"))])
                            sch.nop("t", [("v", sch.mark("v"))])
                for o in range(2):
                    sch.op("g", lambda g, o=o, hf=hf: g.dma_start(
                        out=u3s[:, (o % 2) * HF:((o % 2) + 1) * HF],
                        in_=u3T[o * 128:(o + 1) * 128, hf * HF:(hf + 1) * HF]),
                        wait=[("v", sch.mark("v"))], inc=("u%d" % (o % 2), 16))
                for o in range(DC):
                    slot = o % 2
                    w = [("v", sch.mark("v")),
                         ("u%d" % slot, 16 * (hf * 4 + o // 2 + 1))]
                    for f in range(NF):
                        sch.op("t", lambda t, o=o, f=f, hf=hf: t.matmul(
                            ps_c[:], b2b[0:9, o * 128:(o + 1) * 128],
                            ohTb[0:9, hf * HF + f * 512:hf * HF + (f + 1) * 512],
                            start=True, stop=False),
                            wait=w if f == 0 else ())
                        for pm in range(18):
                            sch.op("t", lambda t, pm=pm, o=o, f=f: t.matmul(
                                ps_c[:],
                                w2b[:, pm * D + o * 128:pm * D + (o + 1) * 128],
                                hmall[:, pm * HF + f * 512:pm * HF + (f + 1) * 512],
                                start=False, stop=(pm == 17)))
                        sch.op("v", lambda v, o=o, f=f, hf=hf, slot=slot: v.tensor_add(
                            zb[:, o * TPC + hf * HF + f * 512:o * TPC + hf * HF + (f + 1) * 512],
                            ps_c[:],
                            u3s[:, slot * HF + f * 512:slot * HF + (f + 1) * 512]),
                            wait=[("t", sch.mark("t"))])
                        sch.nop("t", [("v", sch.mark("v"))])
                    vdone = sch.mark("v")
                    if o + 2 < DC:
                        sch.op("g", lambda g, o=o, hf=hf, slot=slot: g.dma_start(
                            out=u3s[:, slot * HF:(slot + 1) * HF],
                            in_=u3T[(o + 2) * 128:(o + 3) * 128, hf * HF:(hf + 1) * HF]),
                            wait=[("v", vdone)], inc=("u%d" % slot, 16))
        fb = sb("fb", [128, DC * TPC], BF16)
        sqsc = sb("sqsc", [128, 512], BF16)
        strow = sb("strow", [1, 2 * TPC], F32)
        varsc = sb("varsc", [1, TPC], F32)
        mubc = sb("mubc", [128, TPC], F32)
        rsbc = sb("rsbc", [128, TPC], F32)
        tsc = sb("tsc", [128, TPC], F32)
        x3s = sb("x3s", [128, 2 * TPC], F32)
        x4c = sb("x4c", [128, TPC], F32)
        xto = sb("xto", [128, 2 * 128], F32)
        ps_sx = psum("ps_sx", [1, 512], F32)
        ps_sxx = psum("ps_sxx", [1, 512], F32)
        ps_bc = psum("ps_bc", [128, 512], F32)
        ps_f = psum("ps_f", [128, 128], F32)

        st_sx = strow[0:1, 0:TPC]
        st_sxx = strow[0:1, TPC:2 * TPC]
        phA = [("v", sch.mark("v")), ("t", sch.mark("t")), ("g", sch.mark("g"))]
        _fm_stats(sch, zb, lambda c: c * TPC, TPC, ps_sx, ps_sxx, onesbt,
                  sqsc, st_sx, st_sxx, wait_in=phA)
        _fm_rows(sch, TPC, st_sx, st_sxx, st_sx, st_sxx, varsc,
                 cv[0:1, 0:1], INV_D)
        _fm_bcast(sch, TPC, st_sx, mubc, onesft, ps_bc)
        _fm_bcast(sch, TPC, st_sxx, rsbc, onesft, ps_bc)
        for c in range(DC):
            sch.op("v", lambda v, c=c: v.tensor_sub(
                tsc[:], zb[:, c * TPC:(c + 1) * TPC], mubc[:]),
                wait=[("v", sch.mark("v"))])
            sch.op("v", lambda v, c=c: v.scalar_tensor_tensor(
                tsc[:], tsc[:], gamt[:, c:c + 1], rsbc[:],
                ALU.mult, ALU.mult), wait=[("v", sch.mark("v"))])
            sch.op("v", lambda v, c=c: v.tensor_scalar(
                fb[:, c * TPC:(c + 1) * TPC], tsc[:], bett[:, c:c + 1],
                None, ALU.add), wait=[("v", sch.mark("v"))])
        _fm_stats(sch, fb, lambda c: c * TPC, TPC, ps_sx, ps_sxx, onesbt,
                  sqsc, st_sx, st_sxx, wait_in=[("v", sch.mark("v"))])
        _fm_rows(sch, TPC, st_sx, st_sxx, st_sx, st_sxx, varsc,
                 cv[0:1, 0:1], INV_D)
        _fm_bcast(sch, TPC, st_sx, mubc, onesft, ps_bc)
        _fm_bcast(sch, TPC, st_sxx, rsbc, onesft, ps_bc)
        for c in range(2):
            sch.op("g", lambda g, c=c: g.dma_start(
                out=x3s[:, (c % 2) * TPC:((c % 2) + 1) * TPC],
                in_=x3T[c * 128:(c + 1) * 128, :]),
                wait=[("v", sch.mark("v")), ("t", sch.mark("t"))],
                inc=("x%d" % (c % 2), 16))
        nyo = 0
        for c in range(DC):
            slot = c % 2
            w = [("v", sch.mark("v")), ("x%d" % slot, 16 * (c // 2 + 1))]
            sch.op("v", lambda v, c=c: v.tensor_sub(
                tsc[:], fb[:, c * TPC:(c + 1) * TPC], mubc[:]), wait=w)
            sch.op("v", lambda v: v.tensor_mul(tsc[:], tsc[:], rsbc[:]),
                   wait=[("v", sch.mark("v"))])
            sch.op("v", lambda v, c=c, slot=slot: v.tensor_add(
                x4c[:], tsc[:], x3s[:, slot * TPC:(slot + 1) * TPC]),
                wait=[("v", sch.mark("v")), ("t", sch.mark("t"))])
            vdone = sch.mark("v")
            if c + 2 < DC:
                sch.op("g", lambda g, c=c, slot=slot: g.dma_start(
                    out=x3s[:, slot * TPC:(slot + 1) * TPC],
                    in_=x3T[(c + 2) * 128:(c + 3) * 128, :]),
                    wait=[("v", vdone)], inc=("x%d" % slot, 16))
            for t_ in range(NT):
                ts_ = t_ % 2
                wtp = [("v", vdone)]
                if nyo >= 2:
                    wtp.append(("o%d" % ts_, 16 * (nyo // 2)))
                sch.op("t", lambda tt, t_=t_: tt.transpose(
                    ps_f[:], x4c[:, t_ * 128:(t_ + 1) * 128], identft[:]),
                    wait=wtp)
                sch.op("v", lambda v, ts_=ts_: v.tensor_copy(
                    xto[:, ts_ * 128:(ts_ + 1) * 128], ps_f[:]),
                    wait=[("t", sch.mark("t"))])
                sch.op("y", lambda y, c=c, t_=t_, ts_=ts_: y.dma_start(
                    out=x4_s[t_ * 128:(t_ + 1) * 128, c * 128:(c + 1) * 128],
                    in_=xto[:, ts_ * 128:(ts_ + 1) * 128]),
                    wait=[("v", sch.mark("v"))], inc=("o%d" % ts_, 16))
                nyo += 1
                sch.nop("t", [("v", sch.mark("v"))])
        sch.nop("y", [("o0", sch.mark("o0")), ("o1", sch.mark("o1"))])

        block = ctx.enter_context(nc.Block())
        sch.attach(block, sems)
    return nc


def build_u2():
    nc = bass.Bass("TRN2", target_bir_lowering=False)
    kqT = nc.dram_tensor("kqT_own", [D, TPC], BF16, kind="ExternalInput").ap()
    kTf = nc.dram_tensor("kTf", [D, N], BF16, kind="ExternalInput").ap()
    vtf = nc.dram_tensor("vtf", [N, D], BF16, kind="ExternalInput").ap()
    sqqc = nc.dram_tensor("sqqc", [128, 256], F32, kind="ExternalInput").ap()
    sqqf = nc.dram_tensor("sqqf", [H, N], F32, kind="ExternalInput").ap()
    wmix = nc.dram_tensor("wmix", [128, 128], F32, kind="ExternalInput").ap()
    wo0 = nc.dram_tensor("wo0", [D, D], F32, kind="ExternalInput").ap()
    x0T = nc.dram_tensor("x0T_s", [D, TPC], F32, kind="ExternalInput").ap()
    ident = nc.dram_tensor("ident", [128, 128], BF16, kind="ExternalInput").ap()
    identf = nc.dram_tensor("ident_f", [128, 128], F32, kind="ExternalInput").ap()
    onesb = nc.dram_tensor("ones_bf", [128, 1], BF16, kind="ExternalInput").ap()
    onesf = nc.dram_tensor("ones_f", [1, 128], F32, kind="ExternalInput").ap()
    cvals = nc.dram_tensor("cvals", [128, NCONST], F32, kind="ExternalInput").ap()
    x1T_s = nc.dram_tensor("x1T_s", [D, TPC], F32, kind="ExternalOutput").ap()
    scr = nc.dram_tensor("scr", [2, 128, H * 512], BF16, kind="Internal").ap()

    JC = N // 512
    QT = NT
    with contextlib.ExitStack() as ctx:
        def sb(name, shape, dtype):
            return ctx.enter_context(nc.sbuf_tensor(name, shape, dtype))

        def psum(name, shape, dtype):
            return ctx.enter_context(nc.psum_tensor(name, shape, dtype))

        def sem(name):
            return ctx.enter_context(nc.semaphore(name))

        attb = sb("attb", [128, DC * TPC], BF16)
        cv = sb("cv", [128, NCONST], F32)
        onesbt = sb("onesbt", [128, 1], BF16)
        onesft = sb("onesft", [1, 128], F32)
        identb = sb("identb", [128, 128], BF16)
        identft = sb("identft", [128, 128], F32)
        sems = {k: sem("q" + k) for k in
                ("g", "v", "s", "t", "y",
                 "kj0", "kj1", "vj0", "vj1", "ws0", "ws1", "rs0", "rs1",
                 "x0", "x1", "o0", "o1", "sq")}
        sch = Sched()
        for dst, srcap in ((cv, cvals), (onesbt, onesb), (onesft, onesf),
                           (identb, ident), (identft, identf)):
            sch.op("g", lambda g, dst=dst, srcap=srcap: g.dma_start(
                out=dst[:], in_=srcap[:]))

        with contextlib.ExitStack() as mctx:
            def msb(name, shape, dtype):
                return mctx.enter_context(nc.sbuf_tensor(name, shape, dtype))

            def mpsum(name, shape, dtype):
                return mctx.enter_context(nc.psum_tensor(name, shape, dtype))

            kqb = msb("kqb", [128, DC * TPC], BF16)
            wmb = msb("wmb", [128, 128], BF16)
            wob = msb("wob", [128, DC * D], BF16)
            sqt = msb("sqt", [128, 256], F32)
            dcol = msb("dcol", [H, 1], F32)
            drow = msb("drow", [1, H], F32)
            dbc = msb("dbc", [128, H], F32)
            negc = msb("negc", [128, 256], F32)
            zp = msb("zp", [128, 4096], F32)
            ztot = msb("ztot", [128, 256], F32)
            p2b = msb("p2b", [128, 256], F32)
            o1p = msb("o1p", [128, DC * 128], F32)
            o1b = msb("o1b", [128, DC * 128], BF16)
            ps_s = mpsum("ps_s", [128, 2 * 512], F32)
            ps_m = mpsum("ps_m", [128, 512], F32)
            ps_tp = mpsum("ps_tp", [128, 1024], BF16)
            ps_av = mpsum("ps_av", [64, 16 * 128], F32)

            for c in range(DC):
                sch.op("g", lambda g, c=c: g.dma_start(
                    out=kqb[:, c * TPC:(c + 1) * TPC],
                    in_=kqT[c * 128:(c + 1) * 128, :]))
                sch.op("g", lambda g, c=c: g.dma_start(
                    out=wob[:, c * D:(c + 1) * D],
                    in_=wo0[c * 128:(c + 1) * 128, :]))
            sch.op("g", lambda g: g.dma_start(out=wmb[:], in_=wmix[:]))
            sch.op("g", lambda g: g.dma_start(out=sqt[:], in_=sqqc[:]))
            sqf_ctx = contextlib.ExitStack()
            sqfb = sqf_ctx.enter_context(nc.sbuf_tensor("sqfb", [H, N // 4], F32))
            dp4 = sqf_ctx.enter_context(nc.sbuf_tensor("dp4", [H, 4], F32))
            for part in range(4):
                sch.op("g", lambda g, part=part: g.dma_start(
                    out=sqfb[:], in_=sqqf[:, part * (N // 4):(part + 1) * (N // 4)]),
                    wait=[("v", sch.mark("v"))] if part > 0 else [],
                    inc=("sq", 16))
                sch.op("v", lambda v, part=part: v.reduce_max(
                    dp4[0:H, part:part + 1], sqfb[0:H, :], axis=AX.X),
                    wait=[("sq", 16 * (part + 1))])
            lall = sch.mark("g")

            sch.op("v", lambda v: v.reduce_max(dcol[0:H, 0:1], dp4[0:H, :],
                                               axis=AX.X), wait=[("v", sch.mark("v"))])
            sch.op("v", lambda v: v.tensor_scalar(
                dcol[0:H, 0:1], dcol[0:H, 0:1], 0.25, -80.0,
                ALU.mult, ALU.add), wait=[("v", sch.mark("v"))])
            sch.op("v", lambda v: v.tensor_scalar_max(
                dcol[0:H, 0:1], dcol[0:H, 0:1], 0.0),
                wait=[("v", sch.mark("v"))])
            sch.op("t", lambda t: t.transpose(
                ps_m[0:1, 0:H], dcol[0:H, 0:1], identft[0:H, 0:H]),
                wait=[("v", sch.mark("v")), ("g", lall)])
            sch.op("v", lambda v: v.tensor_copy(drow[0:1, :], ps_m[0:1, 0:H]),
                   wait=[("t", sch.mark("t"))])
            sch.op("t", lambda t: t.matmul(
                ps_m[:, 0:H], onesft[0:1, :], drow[0:1, :],
                start=True, stop=True), wait=[("v", sch.mark("v"))])
            sch.op("v", lambda v: v.tensor_copy(dbc[:], ps_m[:, 0:H]),
                   wait=[("t", sch.mark("t"))])
            for h in range(H):
                sch.op("v", lambda v, h=h: v.tensor_scalar(
                    negc[:, h * 16:(h + 1) * 16], sqt[:, h * 16:(h + 1) * 16],
                    dbc[:, h:h + 1], -1.0, ALU.add, ALU.mult),
                    wait=[("v", sch.mark("v"))])
            vnegc = sch.mark("v")
            sqf_ctx.close()
            kb = msb("kb", [128, 2 * DC * 512], BF16)
            vb = msb("vb", [128, 2 * 4 * D], BF16)
            Abuf = msb("Abuf", [128, H * 512], BF16)
            Ast = msb("Ast", [128, H * 512], BF16)
            Amx = msb("Amx", [128, H * 512], BF16)
            TTb = msb("TTb", [128, 4 * 2048], BF16)

            kcons = {}
            for jc in range(JC):
                slot = jc % 2
                w = [("g", lall), ("v", vnegc)]
                if jc >= 2:
                    w.append(("t", kcons[jc - 2]))
                for c in range(DC):
                    sch.op("s", lambda s, jc=jc, slot=slot, c=c: s.dma_start(
                        out=kb[:, slot * 4096 + c * 512:slot * 4096 + (c + 1) * 512],
                        in_=kTf[c * 128:(c + 1) * 128, jc * 512:(jc + 1) * 512]),
                        wait=w if c == 0 else (), inc=("kj%d" % slot, 16))
                kw = ("kj%d" % slot, 16 * DC * (jc // 2 + 1))
                for qt in range(QT):
                    for h in range(H):
                        sslot = (qt * H + h) % 2
                        psl = ps_s[:, sslot * 512:(sslot + 1) * 512]
                        ch, base = h // 2, (h % 2) * 64
                        sch.op("t", lambda t, psl=psl, ch=ch, base=base, qt=qt, slot=slot: t.matmul(
                            psl,
                            kqb[base:base + 64, ch * TPC + qt * 128:ch * TPC + (qt + 1) * 128],
                            kb[base:base + 64, slot * 4096 + ch * 512:slot * 4096 + (ch + 1) * 512],
                            start=True, stop=True),
                            wait=[kw, ("s", max(0, sch.mark("s") - 1)), ("v", vnegc)])
                        sch.op("s", lambda s, psl=psl, h=h, qt=qt, jc=jc: s.activation(
                            Abuf[:, h * 512:(h + 1) * 512], psl, AF.Exp,
                            bias=negc[:, h * 16 + qt:h * 16 + qt + 1],
                            accum_out=zp[:, (h * 16 + qt) * 16 + jc:(h * 16 + qt) * 16 + jc + 1]),
                            wait=[("t", sch.mark("t"))])
                kcons[jc] = sch.mark("t")
            for col in range(0, 256, 64):
                for cc in range(col, col + 64):
                    sch.op("v", lambda v, cc=cc: v.reduce_sum(
                        ztot[:, cc:cc + 1], zp[:, cc * 16:(cc + 1) * 16],
                        axis=AX.X), wait=[("s", sch.mark("s"))] if cc == 0 else ())
            sch.op("s", lambda s: s.activation(p2b[:], ztot[:], AF.Ln),
                   wait=[("v", sch.mark("v"))])
            sch.op("v", lambda v: v.tensor_sub(p2b[:], negc[:], p2b[:]),
                   wait=[("s", sch.mark("s"))])
            vp2 = sch.mark("v")

            TT4 = TTb.ap().rearrange("p (jsub oct g q8) -> p jsub g oct q8",
                                     jsub=4, oct=16, g=16, q8=8)
            nfill = JC
            for qt in range(QT):
                sch.op("v", lambda v: v.memzero(o1p[:]),
                       wait=[("v", sch.mark("v")), ("t", sch.mark("t"))])
                for jc in range(JC):
                    n = nfill + qt * JC + jc
                    slot = n % 2
                    bslot = (qt * JC + jc) % 2
                    w = [("t", kcons[n - 2])] if (n - 2) in kcons else []
                    for c in range(DC):
                        sch.op("s", lambda s, slot=slot, c=c, jc=jc: s.dma_start(
                            out=kb[:, slot * 4096 + c * 512:slot * 4096 + (c + 1) * 512],
                            in_=kTf[c * 128:(c + 1) * 128, jc * 512:(jc + 1) * 512]),
                            wait=w if c == 0 else (), inc=("kj%d" % slot, 16))
                    for jt in range(4):
                        sch.op("s", lambda s, slot=slot, jt=jt, jc=jc: s.dma_start(
                            out=vb[:, slot * 4096 + jt * D:slot * 4096 + (jt + 1) * D],
                            in_=vtf[jc * 512 + jt * 128:jc * 512 + (jt + 1) * 128, :]),
                            wait=w if jt == 0 else (), inc=("vj%d" % slot, 16))
                    kw = ("kj%d" % slot, 16 * DC * (n // 2 + 1))
                    vw = ("vj%d" % slot, 16 * 4 * ((qt * JC + jc) // 2 + 1))
                    wex0 = [("ws%d" % ((qt * JC + jc - 1) % 2),
                             16 * ((qt * JC + jc - 1) // 2 + 1))] \
                        if (qt * JC + jc) >= 1 else []
                    for h in range(H):
                        sslot = h % 2
                        psl = ps_s[:, sslot * 512:(sslot + 1) * 512]
                        ch, base = h // 2, (h % 2) * 64
                        sch.op("t", lambda t, psl=psl, ch=ch, base=base, qt=qt, slot=slot: t.matmul(
                            psl,
                            kqb[base:base + 64, ch * TPC + qt * 128:ch * TPC + (qt + 1) * 128],
                            kb[base:base + 64, slot * 4096 + ch * 512:slot * 4096 + (ch + 1) * 512],
                            start=True, stop=True),
                            wait=[kw, ("s", max(0, sch.mark("s") - 1)), ("v", vp2)])
                        sch.op("s", lambda s, psl=psl, h=h, qt=qt: s.activation(
                            Abuf[:, h * 512:(h + 1) * 512], psl, AF.Exp,
                            bias=p2b[:, h * 16 + qt:h * 16 + qt + 1]),
                            wait=[("t", sch.mark("t"))] + (wex0 if h == 0 else []))
                    nb = qt * JC + jc
                    wb_ = [("s", sch.mark("s"))]
                    if nb >= 2:
                        wb_.append(("rs%d" % bslot, 16 * 16 * (nb // 2)))
                    sch.op("g", lambda g, bslot=bslot: g.dma_start(
                        out=scr[bslot], in_=Abuf[:]), wait=wb_,
                        inc=("ws%d" % bslot, 16))
                    scr4 = scr[bslot].rearrange(
                        "(oct q8) (h j) -> h q8 oct j", oct=16, q8=8, h=H, j=512)
                    wr_ = [("ws%d" % bslot, 16 * (nb // 2 + 1)),
                           ("t", sch.mark("t"))]
                    for h in range(H):
                        sch.op("y", lambda y, h=h, scr4=scr4: y.dma_start(
                            out=Ast[h * 8:(h + 1) * 8, :].rearrange(
                                "p (oct j) -> p oct j", oct=16, j=512),
                            in_=scr4[h]), wait=wr_ if h == 0 else (),
                            inc=("rs%d" % bslot, 16))
                    rw = ("rs%d" % bslot, 16 * 16 * (nb // 2 + 1))
                    for oct in range(16):
                        sch.op("t", lambda t, oct=oct: t.matmul(
                            ps_m[:], wmb[:],
                            Ast[:, oct * 512:(oct + 1) * 512],
                            start=True, stop=True),
                            wait=[rw, ("v", sch.mark("v"))] if oct == 0 else ())
                        sch.op("v", lambda v, oct=oct: v.tensor_copy(
                            Amx[:, oct * 512:(oct + 1) * 512], ps_m[:]),
                            wait=[("t", sch.mark("t"))])
                        sch.nop("t", [("v", sch.mark("v"))])
                    for jsub in range(4):
                        for og in range(2):
                            for o8 in range(8):
                                oct = og * 8 + o8
                                sch.op("t", lambda t, oct=oct, jsub=jsub, o8=o8: t.transpose(
                                    ps_tp[:, o8 * 128:(o8 + 1) * 128],
                                    Amx[:, oct * 512 + jsub * 128:oct * 512 + (jsub + 1) * 128],
                                    identb[:]),
                                    wait=[("v", sch.mark("v"))] if (o8 == 0) else ())
                            sch.op("v", lambda v, jsub=jsub, og=og: v.tensor_copy(
                                TTb[:, jsub * 2048 + og * 1024:jsub * 2048 + (og + 1) * 1024],
                                ps_tp[:]), wait=[("t", sch.mark("t"))])
                            sch.nop("t", [("v", sch.mark("v"))])
                    for g_ in range(16):
                        for jsub in range(4):
                            wav = [vw] if (g_ == 0 and jsub == 0) else []
                            if g_ == 0 and jsub == 0:
                                wav.append(("v", sch.mark("v")))
                            sch.op("t", lambda t, g_=g_, jsub=jsub, slot=slot: t.matmul(
                                ps_av[0:64, g_ * 128:(g_ + 1) * 128],
                                vb[:, slot * 4096 + jsub * D + g_ * 64:slot * 4096 + jsub * D + (g_ + 1) * 64],
                                TT4[:, jsub, g_], start=(jsub == 0),
                                stop=(jsub == 3)), wait=wav)
                    tav = sch.mark("t")
                    for g_ in range(16):
                        sch.op("v", lambda v, g_=g_: v.tensor_add(
                            o1p[(g_ % 2) * 64:(g_ % 2) * 64 + 64,
                                (g_ // 2) * 128:(g_ // 2 + 1) * 128],
                            ps_av[0:64, g_ * 128:(g_ + 1) * 128],
                            o1p[(g_ % 2) * 64:(g_ % 2) * 64 + 64,
                                (g_ // 2) * 128:(g_ // 2 + 1) * 128]),
                            wait=[("t", tav)])
                    kcons[n] = sch.mark("t")
                    sch.nop("t", [("v", sch.mark("v"))])
                sch.op("v", lambda v: v.tensor_copy(o1b[:], o1p[:]),
                       wait=[("v", sch.mark("v"))])
                for od in range(DC):
                    for pc in range(DC):
                        sch.op("t", lambda t, od=od, pc=pc: t.matmul(
                            ps_s[:, 0:128],
                            wob[:, pc * D + od * 128:pc * D + (od + 1) * 128],
                            o1b[:, pc * 128:(pc + 1) * 128],
                            start=(pc == 0), stop=(pc == DC - 1)),
                            wait=[("v", sch.mark("v"))] if (od == 0 and pc == 0) else ())
                    sch.op("v", lambda v, od=od, qt=qt: v.tensor_copy(
                        attb[:, od * TPC + qt * 128:od * TPC + (qt + 1) * 128],
                        ps_s[:, 0:128]), wait=[("t", sch.mark("t"))])
                    sch.nop("t", [("v", sch.mark("v"))])
        sqsc = sb("sqsc", [128, 512], BF16)
        strow = sb("strow", [1, 2 * TPC], F32)
        varsc = sb("varsc", [1, TPC], F32)
        mubc = sb("mubc", [128, TPC], F32)
        rsbc = sb("rsbc", [128, TPC], F32)
        tsc = sb("tsc", [128, TPC], F32)
        xo = sb("xo", [128, 2 * TPC], F32)
        xr = sb("xr", [128, 2 * TPC], F32)
        ps_sx = psum("ps_sx", [1, 512], F32)
        ps_sxx = psum("ps_sxx", [1, 512], F32)
        ps_bc = psum("ps_bc", [128, 512], F32)

        st_sx = strow[0:1, 0:TPC]
        st_sxx = strow[0:1, TPC:2 * TPC]
        phA = [("v", sch.mark("v")), ("t", sch.mark("t")),
               ("g", sch.mark("g")), ("s", sch.mark("s"))]
        _fm_stats(sch, attb, lambda c: c * TPC, TPC, ps_sx, ps_sxx, onesbt,
                  sqsc, st_sx, st_sxx, wait_in=phA)
        _fm_rows(sch, TPC, st_sx, st_sxx, st_sx, st_sxx, varsc,
                 cv[0:1, 0:1], INV_D)
        _fm_bcast(sch, TPC, st_sx, mubc, onesft, ps_bc)
        _fm_bcast(sch, TPC, st_sxx, rsbc, onesft, ps_bc)
        for c in range(2):
            sch.op("g", lambda g, c=c: g.dma_start(
                out=xr[:, (c % 2) * TPC:((c % 2) + 1) * TPC],
                in_=x0T[c * 128:(c + 1) * 128, :]), wait=phA,
                inc=("x%d" % (c % 2), 16))
        for c in range(DC):
            slot = c % 2
            w = [("v", sch.mark("v")), ("x%d" % slot, 16 * (c // 2 + 1))]
            if c >= 2:
                w.append(("o%d" % slot, 16 * (c // 2)))
            sch.op("v", lambda v, c=c: v.tensor_sub(
                tsc[:], attb[:, c * TPC:(c + 1) * TPC], mubc[:]), wait=w)
            sch.op("v", lambda v: v.tensor_mul(tsc[:], tsc[:], rsbc[:]),
                   wait=[("v", sch.mark("v"))])
            sch.op("v", lambda v, c=c, slot=slot: v.tensor_add(
                xo[:, slot * TPC:(slot + 1) * TPC], tsc[:],
                xr[:, slot * TPC:(slot + 1) * TPC]),
                wait=[("v", sch.mark("v"))])
            vdone = sch.mark("v")
            if c + 2 < DC:
                sch.op("g", lambda g, c=c, slot=slot: g.dma_start(
                    out=xr[:, slot * TPC:(slot + 1) * TPC],
                    in_=x0T[(c + 2) * 128:(c + 3) * 128, :]),
                    wait=[("v", vdone)], inc=("x%d" % slot, 16))
            sch.op("y", lambda y, c=c, slot=slot: y.dma_start(
                out=x1T_s[c * 128:(c + 1) * 128, :],
                in_=xo[:, slot * TPC:(slot + 1) * TPC]),
                wait=[("v", vdone)], inc=("o%d" % slot, 16))
        sch.nop("y", [("o0", sch.mark("o0")), ("o1", sch.mark("o1"))])

        block = ctx.enter_context(nc.Block())
        sch.attach(block, sems)
    return nc


_BUILT = {}


def _get(name, builder):
    if name not in _BUILT:
        _BUILT[name] = builder()
    return _BUILT[name]


def _fcols(vec):
    return np.ascontiguousarray(np.asarray(vec, np.float32).reshape(DC, 128).T)


def _rep(vec, rows=128):
    return np.ascontiguousarray(np.broadcast_to(
        np.asarray(vec, np.float32), (rows, len(vec))))


def _consts():
    n = np.arange(D)
    ang = 2.0 * np.pi * np.outer(n, n) / D
    Fr = np.cos(ang).astype(np.float32)
    Fi = -np.sin(ang).astype(np.float32)
    Gr = (np.cos(ang) / D).astype(np.float32)
    Gi = (np.sin(ang) / D).astype(np.float32)
    msk = np.zeros((128, 4 * 512), np.float32)
    jj = np.arange(128)[:, None]
    qq = np.arange(512)[None, :]
    for p in range(4):
        msk[:, p * 512:(p + 1) * 512] = np.where(p * 128 + jj > qq, -1e30, 0.0)
    selb = np.ascontiguousarray(
        np.eye(9, dtype=np.float32)[:, :, None].repeat(128, 2).reshape(9, 9 * 128)
    ).astype(BF)
    return {
        "ident": np.eye(128, dtype=BF),
        "ident_f": np.eye(128, dtype=np.float32),
        "ones_bf": np.ones((128, 1), BF),
        "ones_f": np.ones((1, 128), np.float32),
        "cvals": np.ascontiguousarray(np.broadcast_to(
            np.array([1e-5, 1e-6, 1.0 / 1024.0], np.float32), (128, NCONST))),
        "Fr": Fr, "Fin": np.ascontiguousarray(-Fi),
        "Gr": Gr, "Gi": Gi, "Gin": np.ascontiguousarray(-Gi),
        "msk": msk, "selb": selb,
    }


def kernel(**inputs):
    return kernel_fused(inputs)


def kernel_multilaunch(**inputs):
    C = _consts()
    x0 = np.ascontiguousarray(inputs["L0_attn_x"], dtype=np.float32)

    def run(name, builder, maps, outs):
        nc = _get(name, builder)
        res = run_bass_kernel_spmd(nc, maps, core_ids=list(range(NCORE)))
        return [{k: r[k] for k in outs} for r in res.results]

    lng = _rep(inputs["L0_attn_ln_g"])
    lnb = _rep(inputs["L0_attn_ln_b"])
    maps = []
    for c in range(NCORE):
        b, qq = c // 4, c % 4
        maps.append({"x_tok": np.ascontiguousarray(x0[b, qq * TPC:(qq + 1) * TPC]),
                     "lng": lng, "lnb": lnb,
                     "wkv": inputs["L0_attn_W_kv"],
                     "ident": C["ident"], "ones_bf": C["ones_bf"],
                     "cvals": C["cvals"]})
    r1 = run("u1", build_u1, maps, ["kvT_s", "vtok_s", "sqq_s"])
    kvT = [np.concatenate([r1[b * 4 + q]["kvT_s"] for q in range(4)], axis=1)
           for b in range(B)]
    vtok = [np.concatenate([r1[b * 4 + q]["vtok_s"] for q in range(4)], axis=0)
            for b in range(B)]
    sqq = [np.concatenate([r1[b * 4 + q]["sqq_s"] for q in range(4)], axis=1)
           for b in range(B)]

    Wx = np.asarray(inputs["L0_attn_W_x"], np.float32)
    wmix = np.zeros((128, 128), np.float32)
    for g in range(H):
        for h in range(H):
            wmix[h * 8:h * 8 + 8, g * 8:g * 8 + 8][np.arange(8), np.arange(8)] = Wx[g, h]
    maps = []
    for c in range(NCORE):
        b, qq = c // 4, c % 4
        sl = slice(qq * TPC, (qq + 1) * TPC)
        sqqc = np.ascontiguousarray(
            sqq[b][:, sl].reshape(H, NT, 128).transpose(2, 0, 1).reshape(128, 256))
        maps.append({"kqT_own": np.ascontiguousarray(kvT[b][:D, sl]),
                     "kTf": np.ascontiguousarray(kvT[b][:D]),
                     "vtf": vtok[b],
                     "sqqc": sqqc, "sqqf": sqq[b],
                     "wmix": wmix, "wo0": inputs["L0_attn_W_out"],
                     "x0T_s": np.ascontiguousarray(x0[b, sl].T),
                     "ident": C["ident"], "ident_f": C["ident_f"],
                     "ones_bf": C["ones_bf"], "ones_f": C["ones_f"],
                     "cvals": C["cvals"]})
    r2 = run("u2", build_u2, maps, ["x1T_s"])

    biasc = _fcols(inputs["L0_ffn_bias"])
    wrT = np.ascontiguousarray(np.asarray(inputs["L0_ffn_w_freq_real"]).T)
    wiT = np.ascontiguousarray(np.asarray(inputs["L0_ffn_w_freq_imag"]).T)
    maps = []
    for c in range(NCORE):
        maps.append({"x1T_s": r2[c]["x1T_s"],
                     "wrT": wrT, "wiT": wiT,
                     "Gr": C["Gr"], "Gi": C["Gi"], "Gin": C["Gin"],
                     "Fr": C["Fr"], "Fin": C["Fin"],
                     "biasc": biasc, "ones_bf": C["ones_bf"],
                     "ones_f": C["ones_f"], "cvals": C["cvals"]})
    r3 = run("u3", build_u3, maps, ["x2T_s"])

    maps = []
    for c in range(NCORE):
        maps.append({"x2T_s": r3[c]["x2T_s"],
                     "WqT": np.ascontiguousarray(np.asarray(inputs["L1_attn_Wq"]).T),
                     "WkT": np.ascontiguousarray(np.asarray(inputs["L1_attn_Wk"]).T),
                     "WvT": np.ascontiguousarray(np.asarray(inputs["L1_attn_Wv"]).T),
                     "nwc": _fcols(inputs["L1_attn_norm_w"]),
                     "bqc": _fcols(inputs["L1_attn_bq"]),
                     "bkc": _fcols(inputs["L1_attn_bk"]),
                     "bvr": _rep(inputs["L1_attn_bv"]),
                     "ones_bf": C["ones_bf"], "ones_f": C["ones_f"],
                     "cvals": C["cvals"]})
    r4 = run("u4", build_u4, maps, ["qT_s", "kT_s", "vtok1_s", "xnT_s"])
    qT1 = [np.concatenate([r4[b * 4 + q]["qT_s"] for q in range(4)], axis=1)
           for b in range(B)]
    kT1 = [np.concatenate([r4[b * 4 + q]["kT_s"] for q in range(4)], axis=1)
           for b in range(B)]
    vtok1 = [np.concatenate([r4[b * 4 + q]["vtok1_s"] for q in range(4)], axis=0)
             for b in range(B)]
    xnT1 = [np.concatenate([r4[b * 4 + q]["xnT_s"] for q in range(4)], axis=1)
            for b in range(B)]

    maps = []
    for c in range(NCORE):
        b, hg = c // 4, c % 4
        hsl = slice(hg * 256, (hg + 1) * 256)
        maps.append({"qTh": np.ascontiguousarray(qT1[b][hsl]),
                     "kTh": np.ascontiguousarray(kT1[b][hsl]),
                     "vth": np.ascontiguousarray(vtok1[b][:, hsl]),
                     "msk": C["msk"], "ones_bf": C["ones_bf"],
                     "ones_f": C["ones_f"]})
    r5 = run("u5", build_u5, maps, ["oT_s"])
    oT = [np.concatenate([r5[b * 4 + g]["oT_s"] for g in range(4)], axis=0)
          for b in range(B)]

    maps = []
    for c in range(NCORE):
        b, qq = c // 4, c % 4
        sl = slice(qq * TPC, (qq + 1) * TPC)
        maps.append({"oT_sl": np.ascontiguousarray(oT[b][:, sl]),
                     "xnT_sl": np.ascontiguousarray(xnT1[b][:, sl]),
                     "x2T_sl": r3[c]["x2T_s"],
                     "WoT": np.ascontiguousarray(np.asarray(inputs["L1_attn_Wo"]).T),
                     "boc": _fcols(inputs["L1_attn_bo"]),
                     "ones_bf": C["ones_bf"], "ones_f": C["ones_f"],
                     "cvals": C["cvals"]})
    r6a = run("u6a", build_u6a, maps, ["x3T_s"])
    x3T = [np.concatenate([r6a[b * 4 + q]["x3T_s"] for q in range(4)], axis=1)
           for b in range(B)]

    maps = []
    for c in range(NCORE):
        b, qq = c // 4, c % 4
        t0 = qq * TPC
        pad = np.zeros((D, W6), np.float32)
        lo, hi = t0 - PAD, t0 + TPC + PAD
        glo, ghi = max(lo, 0), min(hi, N)
        pad[:, glo - lo:glo - lo + (ghi - glo)] = x3T[b][:, glo:ghi]
        mk = np.asarray(inputs["L1_ffn_mix_kernel"], np.float32)
        maps.append({"x3T_pad": pad,
                     "mk0c": _fcols(mk[0]), "mk1c": _fcols(mk[1]),
                     "mk2c": _fcols(mk[2]),
                     "ones_bf": C["ones_bf"], "ones_f": C["ones_f"],
                     "cvals": C["cvals"]})
    r6b1 = run("u6b1", build_u6b1, maps, ["u3T_s", "xmT_s"])

    b1 = np.asarray(inputs["L1_ffn_b1"], np.float32)
    b1cols = np.zeros((128, 18), np.float32)
    for p in range(9):
        for mc in range(2):
            b1cols[:, p * 2 + mc] = b1[p, mc * 128:(mc + 1) * 128]
    bo2c = np.zeros((128, 1), np.float32)
    bo2c[0:9, 0] = np.asarray(inputs["L1_ffn_bo2"], np.float32)
    maps = []
    for c in range(NCORE):
        b, qq = c // 4, c % 4
        sl = slice(qq * TPC, (qq + 1) * TPC)
        maps.append({"xmT_sl": r6b1[c]["xmT_s"],
                     "u3T_sl": r6b1[c]["u3T_s"],
                     "x3T_sl": np.ascontiguousarray(x3T[b][:, sl]),
                     "W1T": np.ascontiguousarray(
                         np.asarray(inputs["L1_ffn_W1"], np.float32).reshape(9 * D, 256)),
                     "W2T": np.ascontiguousarray(
                         np.asarray(inputs["L1_ffn_W2"], np.float32).reshape(9 * 256, D)),
                     "b1c": b1cols,
                     "b2r": np.asarray(inputs["L1_ffn_b2"], np.float32),
                     "Wo1": np.asarray(inputs["L1_ffn_Wo1"], np.float32),
                     "bo1c": np.ascontiguousarray(
                         np.asarray(inputs["L1_ffn_bo1"], np.float32).reshape(2, 128).T),
                     "bo2c": bo2c,
                     "Wo2": np.asarray(inputs["L1_ffn_Wo2"], np.float32),
                     "gamc": _fcols(inputs["L1_ffn_gamma"]),
                     "betc": _fcols(inputs["L1_ffn_beta"]),
                     "ident_f": C["ident_f"], "selb": C["selb"],
                     "ones_bf": C["ones_bf"], "ones_f": C["ones_f"],
                     "cvals": C["cvals"]})
    r6b2 = run("u6b2", build_u6b2, maps, ["x4_s"])

    out = np.empty((B, N, D), np.float32)
    for c in range(NCORE):
        b, qq = c // 4, c % 4
        out[b, qq * TPC:(qq + 1) * TPC] = r6b2[c]["x4_s"]
    return out




def _fused_u1(nc, sch, sb_factory, phase, io, cv, onesbt, identb):
    x = io["x"]; lng = io["lng"]; lnb = io["lnb"]; wkv = io["wkv"]
    kvT_s = io["kvT_s"]; vtok_s = io["vtok_s"]; sqq_s = io["sqq_s"]
    with contextlib.ExitStack() as stk:
        sb, psum = sb_factory(stk)
        for _e in ("g", "v", "s", "t", "y"):
            sch.nop(_e, phase())

        xt = sb("a_xt", [128, 2 * D], F32)
        lngt = sb("a_a_lngt", [128, D], F32)
        lnbt = sb("a_lnbt", [128, D], F32)
        mu = sb("a_mu", [128, NT], F32)
        rstd = sb("a_rstd", [128, NT], F32)
        sq = sb("a_sq", [128, D], F32)
        usc = sb("a_usc", [128, D], F32)
        hbuf = sb("a_hbuf", [128, NT * D], BF16)
        hT = sb("a_hT", [128, DC * TPC], BF16)
        wkvb = sb("a_wkvb", [128, DC * 2 * D], BF16)
        kvo = sb("a_kvo", [128, 2 * TPC], BF16)
        sqo = sb("a_sqo", [128, TPC], BF16)
        sqqsb = sb("a_sqqsb", [1, 2 * TPC], F32)
        vtb = sb("a_vtb", [128, 2 * D], BF16)
        ps_t = psum("a_ps_t", [128, 128], BF16)
        ps_kv = psum("a_ps_kv", [128, 512], F32)
        ps_v = psum("a_ps_v", [128, 512], F32)
        ps_sq = psum("a_ps_sq", [1, 512], F32)
        
        sch.op("g", lambda g: g.dma_start(out=lngt[:], in_=lng[:]))
        sch.op("g", lambda g: g.dma_start(out=lnbt[:], in_=lnb[:]))
        for c in range(DC):
            sch.op("g", lambda g, c=c: g.dma_start(
                out=wkvb[:, c * 2 * D:(c + 1) * 2 * D],
                in_=wkv[c * 128:(c + 1) * 128, :]))
        lall = sch.mark("g")

        def ldx(t, wait=()):
            s_ = t % 2
            sch.op("g", lambda g, t=t, s_=s_: g.dma_start(
                out=xt[:, s_ * D:(s_ + 1) * D],
                in_=x[t * 128:(t + 1) * 128, :]), wait=wait,
                inc=("ax%d" % s_, 16))

        ldx(0)
        ldx(1)
        for t in range(NT):
            s_ = t % 2
            xin = xt[:, s_ * D:(s_ + 1) * D]
            _ln_stats(sch, xin, mu[:, t:t + 1], rstd[:, t:t + 1], sq[:],
                      cv[:, 0:1], cv[:, 2:3],
                      wait_in=[("ax%d" % s_, 16 * (t // 2 + 1)), ("g", lall)])
            sch.op("v", lambda v, xin=xin, t=t: v.tensor_scalar(
                usc[:], xin, mu[:, t:t + 1], rstd[:, t:t + 1],
                ALU.subtract, ALU.mult), wait=[("v", sch.mark("v"))])
            if t + 2 < NT:
                ldx(t + 2, wait=[("v", sch.mark("v"))])
            sch.op("v", lambda v: v.tensor_mul(sq[:], usc[:], lngt[:]),
                   wait=[("v", sch.mark("v"))])
            sch.op("v", lambda v, t=t: v.tensor_add(
                hbuf[:, t * D:(t + 1) * D], sq[:], lnbt[:]),
                wait=[("v", sch.mark("v"))])

        for t in range(NT):
            for c in range(DC):
                sch.op("t", lambda tt, t=t, c=c: tt.transpose(
                    ps_t[:], hbuf[:, t * D + c * 128: t * D + (c + 1) * 128],
                    identb[:]),
                    wait=[("v", sch.mark("v")), ("g", lall)])
                sch.op("v", lambda v, t=t, c=c: v.tensor_copy(
                    hT[:, c * TPC + t * 128: c * TPC + (t + 1) * 128], ps_t[:]),
                    wait=[("t", sch.mark("t"))])

        for o in range(2 * DC):
            slot = o % 2
            w = [("v", sch.mark("v")), ("g", lall)]
            if o >= 2:
                w.append(("ak%d" % slot, 16 * (o // 2)))
            if o >= 2 and (o - 2) < 8:
                w.append(("s", sch.mark("s")))
            for f in range(4):
                for c in range(DC):
                    sch.op("t", lambda tt, o=o, f=f, c=c: tt.matmul(
                        ps_kv[:],
                        wkvb[:, c * 2 * D + o * 128: c * 2 * D + (o + 1) * 128],
                        hT[:, c * TPC + f * 512: c * TPC + (f + 1) * 512],
                        start=(c == 0), stop=(c == DC - 1)),
                        wait=w if (f == 0 and c == 0) else ())
                sch.op("v", lambda v, slot=slot, f=f: v.tensor_copy(
                    kvo[:, slot * TPC + f * 512: slot * TPC + (f + 1) * 512],
                    ps_kv[:]), wait=[("t", sch.mark("t"))])
                if f < 3:
                    sch.nop("t", [("v", sch.mark("v"))])
            evm = sch.mark("v")
            sch.op("y", lambda y, o=o, slot=slot: y.dma_start(
                out=kvT_s[o * 128:(o + 1) * 128, :],
                in_=kvo[:, slot * TPC:(slot + 1) * TPC]), wait=[("v", evm)],
                inc=("ak%d" % slot, 16))
            if o < 8:
                sch.op("s", lambda s, slot=slot: s.activation(
                    sqo[:], kvo[:, slot * TPC:(slot + 1) * TPC], AF.Square),
                    wait=[("v", evm), ("t", sch.mark("t"))])
                sqm = sch.mark("s")
                for hh in (2 * o, 2 * o + 1):
                    base = (hh % 2) * 64
                    hs = hh % 2
                    for f in range(4):
                        wv = [("s", sqm)]
                        if f == 0 and hh >= 2:
                            wv.append(("asq%d" % hs, 16 * (hh // 2)))
                        sch.op("t", lambda tt, base=base, f=f: tt.matmul(
                            ps_sq[:], onesbt[base:base + 64, 0:1],
                            sqo[base:base + 64, f * 512:(f + 1) * 512],
                            start=True, stop=True), wait=wv)
                        sch.op("v", lambda v, hs=hs, f=f: v.tensor_copy(
                            sqqsb[0:1, hs * TPC + f * 512: hs * TPC + (f + 1) * 512],
                            ps_sq[:]), wait=[("t", sch.mark("t"))])
                        sch.nop("t", [("v", sch.mark("v"))])
                    sch.op("y", lambda y, hh=hh, hs=hs: y.dma_start(
                        out=sqq_s[hh:hh + 1, :],
                        in_=sqqsb[0:1, hs * TPC:(hs + 1) * TPC]),
                        wait=[("v", sch.mark("v"))], inc=("asq%d" % hs, 16))
        for t in range(NT):
            slot = t % 2
            w = [("v", sch.mark("v"))]
            if t >= 2:
                w.append(("aw%d" % slot, 16 * (t // 2)))
            for f in range(2):
                for c in range(DC):
                    sch.op("t", lambda tt, t=t, f=f, c=c: tt.matmul(
                        ps_v[:],
                        hT[:, c * TPC + t * 128: c * TPC + (t + 1) * 128],
                        wkvb[:, c * 2 * D + D + f * 512: c * 2 * D + D + (f + 1) * 512],
                        start=(c == 0), stop=(c == DC - 1)),
                        wait=w if (f == 0 and c == 0) else ())
                sch.op("v", lambda v, slot=slot, f=f: v.tensor_copy(
                    vtb[:, slot * D + f * 512: slot * D + (f + 1) * 512],
                    ps_v[:]), wait=[("t", sch.mark("t"))])
                if f == 0:
                    sch.nop("t", [("v", sch.mark("v"))])
            sch.op("y", lambda y, t=t, slot=slot: y.dma_start(
                out=vtok_s[t * 128:(t + 1) * 128, :],
                in_=vtb[:, slot * D:(slot + 1) * D]),
                wait=[("v", sch.mark("v"))], inc=("aw%d" % slot, 16))
        sch.nop("y", [(k, sch.mark(k)) for k in
                      ("ak0", "ak1", "aw0", "aw1", "asq0", "asq1")])




def _fused_u2(nc, sch, sb_factory, phase, ccm, io, cv, onesbt, onesft,
              identb, identft):
    kqT = io["kqT"]; kvg = io["kvg"]; vtg = io["vtg"]
    sqq_own = io["sqq_own"]; sqg = io["sqg"]
    wmix = io["wmix"]; wo0 = io["wo0"]; x0T = io["x0T"]
    x1T_s = io["x1T_s"]; scr = io["scr"]
    JC = N // 512
    QT = NT
    with contextlib.ExitStack() as stk:
        sb, psum = sb_factory(stk)
        for _e in ("g", "v", "s", "t", "y"):
            sch.nop(_e, phase() + [("cc", ccm)])

        attb = sb("b_attb", [128, DC * TPC], BF16)

        with contextlib.ExitStack() as mctx:
            def msb(name, shape, dtype):
                return mctx.enter_context(nc.sbuf_tensor(name, shape, dtype))

            def mpsum(name, shape, dtype):
                return mctx.enter_context(nc.psum_tensor(name, shape, dtype))

            kqb = msb("b_kqb", [128, DC * TPC], BF16)
            wmb = msb("b_wmb", [128, 128], BF16)
            wob = msb("b_wob", [128, DC * D], BF16)
            sqt = msb("b_sqt", [128, 256], F32)
            dcol = msb("b_dcol", [H, 1], F32)
            drow = msb("b_drow", [1, H], F32)
            dbc = msb("b_dbc", [128, H], F32)
            negc = msb("b_negc", [128, 256], F32)
            zp = msb("b_zp", [128, 4096], F32)
            ztot = msb("b_ztot", [128, 256], F32)
            p2b = msb("b_p2b", [128, 256], F32)
            o1p = msb("b_o1p", [128, DC * 128], F32)
            o1b = msb("b_o1b", [128, DC * 128], BF16)
            ps_s = mpsum("b_ps_s", [128, 2 * 512], F32)
            ps_m = mpsum("b_ps_m", [128, 512], F32)
            ps_tp = mpsum("b_ps_tp", [128, 1024], BF16)
            ps_av = mpsum("b_ps_av", [64, 16 * 128], F32)

            for c in range(DC):
                sch.op("g", lambda g, c=c: g.dma_start(
                    out=kqb[:, c * TPC:(c + 1) * TPC],
                    in_=kqT[c * 128:(c + 1) * 128, :]))
                sch.op("g", lambda g, c=c: g.dma_start(
                    out=wob[:, c * D:(c + 1) * D],
                    in_=wo0[c * 128:(c + 1) * 128, :]))
            sch.op("g", lambda g: g.dma_start(out=wmb[:], in_=wmix[:]))
            sqf_ctx = contextlib.ExitStack()
            sqown = sqf_ctx.enter_context(nc.sbuf_tensor("b_sqown", [H, TPC], F32))
            sch.op("g", lambda g: g.dma_start(out=sqown[:], in_=sqq_own[:]),
                   wait=[], inc=("bqsq", 16))
            sqt4 = sqt.ap().rearrange("p (h qt) -> p qt h", h=H, qt=NT)
            for qt_ in range(NT):
                sch.op("t", lambda t, qt_=qt_: t.transpose(
                    ps_m[:, 0:H], sqown[0:H, qt_ * 128:(qt_ + 1) * 128],
                    identft[0:H, 0:H]),
                    wait=[("bqsq", 16), ("v", sch.mark("v")), ("g", sch.mark("g"))])
                sch.op("v", lambda v, qt_=qt_: v.tensor_copy(
                    sqt4[:, qt_], ps_m[:, 0:H]), wait=[("t", sch.mark("t"))])
                sch.nop("t", [("v", sch.mark("v"))])
            sqfb = sqf_ctx.enter_context(nc.sbuf_tensor("sqfb", [H, N // 4], F32))
            dp4 = sqf_ctx.enter_context(nc.sbuf_tensor("dp4", [H, 4], F32))
            for part in range(4):
                sch.op("g", lambda g, part=part: g.dma_start(
                    out=sqfb[:], in_=sqg[part * H:(part + 1) * H, :]),
                    wait=[("v", sch.mark("v"))] if part > 0 else [],
                    inc=("bsq", 16))
                sch.op("v", lambda v, part=part: v.reduce_max(
                    dp4[0:H, part:part + 1], sqfb[0:H, :], axis=AX.X),
                    wait=[("bsq", 16 * (part + 1))])
            lall = sch.mark("g")

            sch.op("v", lambda v: v.reduce_max(dcol[0:H, 0:1], dp4[0:H, :],
                                               axis=AX.X), wait=[("v", sch.mark("v"))])
            sch.op("v", lambda v: v.tensor_scalar(
                dcol[0:H, 0:1], dcol[0:H, 0:1], 0.25, -80.0,
                ALU.mult, ALU.add), wait=[("v", sch.mark("v"))])
            sch.op("v", lambda v: v.tensor_scalar_max(
                dcol[0:H, 0:1], dcol[0:H, 0:1], 0.0),
                wait=[("v", sch.mark("v"))])
            sch.op("t", lambda t: t.transpose(
                ps_m[0:1, 0:H], dcol[0:H, 0:1], identft[0:H, 0:H]),
                wait=[("v", sch.mark("v")), ("g", lall)])
            sch.op("v", lambda v: v.tensor_copy(drow[0:1, :], ps_m[0:1, 0:H]),
                   wait=[("t", sch.mark("t"))])
            sch.op("t", lambda t: t.matmul(
                ps_m[:, 0:H], onesft[0:1, :], drow[0:1, :],
                start=True, stop=True), wait=[("v", sch.mark("v"))])
            sch.op("v", lambda v: v.tensor_copy(dbc[:], ps_m[:, 0:H]),
                   wait=[("t", sch.mark("t"))])
            for h in range(H):
                sch.op("v", lambda v, h=h: v.tensor_scalar(
                    negc[:, h * 16:(h + 1) * 16], sqt[:, h * 16:(h + 1) * 16],
                    dbc[:, h:h + 1], -1.0, ALU.add, ALU.mult),
                    wait=[("v", sch.mark("v"))])
            vnegc = sch.mark("v")
            sqf_ctx.close()
            kb = msb("b_kb", [128, 2 * DC * 512], BF16)
            vb = msb("b_vb", [128, 2 * 4 * D], BF16)
            Abuf = msb("b_Abuf", [128, H * 512], BF16)
            Ast = msb("b_Ast", [128, H * 512], BF16)
            Amx = msb("b_Amx", [128, H * 512], BF16)
            TTb = msb("b_TTb", [128, 4 * 2048], BF16)

            kcons = {}
            for jc in range(JC):
                slot = jc % 2
                w = [("g", lall), ("v", vnegc)]
                if jc >= 2:
                    w.append(("t", kcons[jc - 2]))
                for c in range(DC):
                    sch.op("s", lambda s, jc=jc, slot=slot, c=c: s.dma_start(
                        out=kb[:, slot * 4096 + c * 512:slot * 4096 + (c + 1) * 512],
                        in_=kvg[(jc // 4) * 2 * D + c * 128:(jc // 4) * 2 * D + (c + 1) * 128, (jc % 4) * 512:(jc % 4 + 1) * 512]),
                        wait=w if c == 0 else (), inc=("bkj%d" % slot, 16))
                kw = ("bkj%d" % slot, 16 * DC * (jc // 2 + 1))
                for qt in range(QT):
                    for h in range(H):
                        sslot = (qt * H + h) % 2
                        psl = ps_s[:, sslot * 512:(sslot + 1) * 512]
                        ch, base = h // 2, (h % 2) * 64
                        sch.op("t", lambda t, psl=psl, ch=ch, base=base, qt=qt, slot=slot: t.matmul(
                            psl,
                            kqb[base:base + 64, ch * TPC + qt * 128:ch * TPC + (qt + 1) * 128],
                            kb[base:base + 64, slot * 4096 + ch * 512:slot * 4096 + (ch + 1) * 512],
                            start=True, stop=True),
                            wait=[kw, ("s", max(0, sch.mark("s") - 1)), ("v", vnegc)])
                        sch.op("s", lambda s, psl=psl, h=h, qt=qt, jc=jc: s.activation(
                            Abuf[:, h * 512:(h + 1) * 512], psl, AF.Exp,
                            bias=negc[:, h * 16 + qt:h * 16 + qt + 1],
                            accum_out=zp[:, (h * 16 + qt) * 16 + jc:(h * 16 + qt) * 16 + jc + 1]),
                            wait=[("t", sch.mark("t"))])
                kcons[jc] = sch.mark("t")
            for col in range(0, 256, 64):
                for cc in range(col, col + 64):
                    sch.op("v", lambda v, cc=cc: v.reduce_sum(
                        ztot[:, cc:cc + 1], zp[:, cc * 16:(cc + 1) * 16],
                        axis=AX.X), wait=[("s", sch.mark("s"))] if cc == 0 else ())
            sch.op("s", lambda s: s.activation(p2b[:], ztot[:], AF.Ln),
                   wait=[("v", sch.mark("v"))])
            sch.op("v", lambda v: v.tensor_sub(p2b[:], negc[:], p2b[:]),
                   wait=[("s", sch.mark("s"))])
            vp2 = sch.mark("v")

            TT4 = TTb.ap().rearrange("p (jsub oct g q8) -> p jsub g oct q8",
                                     jsub=4, oct=16, g=16, q8=8)
            nfill = JC
            for qt in range(QT):
                sch.op("v", lambda v: v.memzero(o1p[:]),
                       wait=[("v", sch.mark("v")), ("t", sch.mark("t"))])
                for jc in range(JC):
                    n = nfill + qt * JC + jc
                    slot = n % 2
                    bslot = (qt * JC + jc) % 2
                    w = [("t", kcons[n - 2])] if (n - 2) in kcons else []
                    for c in range(DC):
                        sch.op("s", lambda s, slot=slot, c=c, jc=jc: s.dma_start(
                            out=kb[:, slot * 4096 + c * 512:slot * 4096 + (c + 1) * 512],
                            in_=kvg[(jc // 4) * 2 * D + c * 128:(jc // 4) * 2 * D + (c + 1) * 128, (jc % 4) * 512:(jc % 4 + 1) * 512]),
                            wait=w if c == 0 else (), inc=("bkj%d" % slot, 16))
                    for jt in range(4):
                        sch.op("s", lambda s, slot=slot, jt=jt, jc=jc: s.dma_start(
                            out=vb[:, slot * 4096 + jt * D:slot * 4096 + (jt + 1) * D],
                            in_=vtg[(jc // 4) * TPC + (jc % 4) * 512 + jt * 128:(jc // 4) * TPC + (jc % 4) * 512 + (jt + 1) * 128, :]),
                            wait=w if jt == 0 else (), inc=("bvj%d" % slot, 16))
                    kw = ("bkj%d" % slot, 16 * DC * (n // 2 + 1))
                    vw = ("bvj%d" % slot, 16 * 4 * ((qt * JC + jc) // 2 + 1))
                    wex0 = [("bws%d" % ((qt * JC + jc - 1) % 2),
                             16 * ((qt * JC + jc - 1) // 2 + 1))] \
                        if (qt * JC + jc) >= 1 else []
                    for h in range(H):
                        sslot = h % 2
                        psl = ps_s[:, sslot * 512:(sslot + 1) * 512]
                        ch, base = h // 2, (h % 2) * 64
                        sch.op("t", lambda t, psl=psl, ch=ch, base=base, qt=qt, slot=slot: t.matmul(
                            psl,
                            kqb[base:base + 64, ch * TPC + qt * 128:ch * TPC + (qt + 1) * 128],
                            kb[base:base + 64, slot * 4096 + ch * 512:slot * 4096 + (ch + 1) * 512],
                            start=True, stop=True),
                            wait=[kw, ("s", max(0, sch.mark("s") - 1)), ("v", vp2)])
                        sch.op("s", lambda s, psl=psl, h=h, qt=qt: s.activation(
                            Abuf[:, h * 512:(h + 1) * 512], psl, AF.Exp,
                            bias=p2b[:, h * 16 + qt:h * 16 + qt + 1]),
                            wait=[("t", sch.mark("t"))] + (wex0 if h == 0 else []))
                    nb = qt * JC + jc
                    wb_ = [("s", sch.mark("s"))]
                    if nb >= 2:
                        wb_.append(("brs%d" % bslot, 16 * 16 * (nb // 2)))
                    sch.op("g", lambda g, bslot=bslot: g.dma_start(
                        out=scr[bslot], in_=Abuf[:]), wait=wb_,
                        inc=("bws%d" % bslot, 16))
                    scr4 = scr[bslot].rearrange(
                        "(oct q8) (h j) -> h q8 oct j", oct=16, q8=8, h=H, j=512)
                    wr_ = [("bws%d" % bslot, 16 * (nb // 2 + 1)),
                           ("t", sch.mark("t"))]
                    for h in range(H):
                        sch.op("y", lambda y, h=h, scr4=scr4: y.dma_start(
                            out=Ast[h * 8:(h + 1) * 8, :].rearrange(
                                "p (oct j) -> p oct j", oct=16, j=512),
                            in_=scr4[h]), wait=wr_ if h == 0 else (),
                            inc=("brs%d" % bslot, 16))
                    rw = ("brs%d" % bslot, 16 * 16 * (nb // 2 + 1))
                    for oct in range(16):
                        sch.op("t", lambda t, oct=oct: t.matmul(
                            ps_m[:], wmb[:],
                            Ast[:, oct * 512:(oct + 1) * 512],
                            start=True, stop=True),
                            wait=[rw, ("v", sch.mark("v"))] if oct == 0 else ())
                        sch.op("v", lambda v, oct=oct: v.tensor_copy(
                            Amx[:, oct * 512:(oct + 1) * 512], ps_m[:]),
                            wait=[("t", sch.mark("t"))])
                        sch.nop("t", [("v", sch.mark("v"))])
                    for jsub in range(4):
                        for og in range(2):
                            for o8 in range(8):
                                oct = og * 8 + o8
                                sch.op("t", lambda t, oct=oct, jsub=jsub, o8=o8: t.transpose(
                                    ps_tp[:, o8 * 128:(o8 + 1) * 128],
                                    Amx[:, oct * 512 + jsub * 128:oct * 512 + (jsub + 1) * 128],
                                    identb[:]),
                                    wait=[("v", sch.mark("v"))] if (o8 == 0) else ())
                            sch.op("v", lambda v, jsub=jsub, og=og: v.tensor_copy(
                                TTb[:, jsub * 2048 + og * 1024:jsub * 2048 + (og + 1) * 1024],
                                ps_tp[:]), wait=[("t", sch.mark("t"))])
                            sch.nop("t", [("v", sch.mark("v"))])
                    for g_ in range(16):
                        for jsub in range(4):
                            wav = [vw] if (g_ == 0 and jsub == 0) else []
                            if g_ == 0 and jsub == 0:
                                wav.append(("v", sch.mark("v")))
                            sch.op("t", lambda t, g_=g_, jsub=jsub, slot=slot: t.matmul(
                                ps_av[0:64, g_ * 128:(g_ + 1) * 128],
                                vb[:, slot * 4096 + jsub * D + g_ * 64:slot * 4096 + jsub * D + (g_ + 1) * 64],
                                TT4[:, jsub, g_], start=(jsub == 0),
                                stop=(jsub == 3)), wait=wav)
                    tav = sch.mark("t")
                    for g_ in range(16):
                        sch.op("v", lambda v, g_=g_: v.tensor_add(
                            o1p[(g_ % 2) * 64:(g_ % 2) * 64 + 64,
                                (g_ // 2) * 128:(g_ // 2 + 1) * 128],
                            ps_av[0:64, g_ * 128:(g_ + 1) * 128],
                            o1p[(g_ % 2) * 64:(g_ % 2) * 64 + 64,
                                (g_ // 2) * 128:(g_ // 2 + 1) * 128]),
                            wait=[("t", tav)])
                    kcons[n] = sch.mark("t")
                    sch.nop("t", [("v", sch.mark("v"))])
                sch.op("v", lambda v: v.tensor_copy(o1b[:], o1p[:]),
                       wait=[("v", sch.mark("v"))])
                for od in range(DC):
                    for pc in range(DC):
                        sch.op("t", lambda t, od=od, pc=pc: t.matmul(
                            ps_s[:, 0:128],
                            wob[:, pc * D + od * 128:pc * D + (od + 1) * 128],
                            o1b[:, pc * 128:(pc + 1) * 128],
                            start=(pc == 0), stop=(pc == DC - 1)),
                            wait=[("v", sch.mark("v"))] if (od == 0 and pc == 0) else ())
                    sch.op("v", lambda v, od=od, qt=qt: v.tensor_copy(
                        attb[:, od * TPC + qt * 128:od * TPC + (qt + 1) * 128],
                        ps_s[:, 0:128]), wait=[("t", sch.mark("t"))])
                    sch.nop("t", [("v", sch.mark("v"))])
        sqsc = sb("b_sqsc", [128, 512], BF16)
        strow = sb("b_strow", [1, 2 * TPC], F32)
        varsc = sb("b_varsc", [1, TPC], F32)
        mubc = sb("b_mubc", [128, TPC], F32)
        rsbc = sb("b_rsbc", [128, TPC], F32)
        tsc = sb("b_tsc", [128, TPC], F32)
        xo = sb("b_xo", [128, 2 * TPC], F32)
        xr = sb("b_xr", [128, 2 * TPC], F32)
        ps_sx = psum("b_ps_sx", [1, 512], F32)
        ps_sxx = psum("b_ps_sxx", [1, 512], F32)
        ps_bc = psum("b_ps_bc", [128, 512], F32)

        st_sx = strow[0:1, 0:TPC]
        st_sxx = strow[0:1, TPC:2 * TPC]
        phA = [("v", sch.mark("v")), ("t", sch.mark("t")),
               ("g", sch.mark("g")), ("s", sch.mark("s"))]
        _fm_stats(sch, attb, lambda c: c * TPC, TPC, ps_sx, ps_sxx, onesbt,
                  sqsc, st_sx, st_sxx, wait_in=phA)
        _fm_rows(sch, TPC, st_sx, st_sxx, st_sx, st_sxx, varsc,
                 cv[0:1, 0:1], INV_D)
        _fm_bcast(sch, TPC, st_sx, mubc, onesft, ps_bc)
        _fm_bcast(sch, TPC, st_sxx, rsbc, onesft, ps_bc)
        for c in range(2):
            sch.op("g", lambda g, c=c: g.dma_start(
                out=xr[:, (c % 2) * TPC:((c % 2) + 1) * TPC],
                in_=x0T[c * 128:(c + 1) * 128, :]), wait=phA,
                inc=("bx%d" % (c % 2), 16))
        for c in range(DC):
            slot = c % 2
            w = [("v", sch.mark("v")), ("bx%d" % slot, 16 * (c // 2 + 1))]
            if c >= 2:
                w.append(("bo%d" % slot, 16 * (c // 2)))
            sch.op("v", lambda v, c=c: v.tensor_sub(
                tsc[:], attb[:, c * TPC:(c + 1) * TPC], mubc[:]), wait=w)
            sch.op("v", lambda v: v.tensor_mul(tsc[:], tsc[:], rsbc[:]),
                   wait=[("v", sch.mark("v"))])
            sch.op("v", lambda v, c=c, slot=slot: v.tensor_add(
                xo[:, slot * TPC:(slot + 1) * TPC], tsc[:],
                xr[:, slot * TPC:(slot + 1) * TPC]),
                wait=[("v", sch.mark("v"))])
            vdone = sch.mark("v")
            if c + 2 < DC:
                sch.op("g", lambda g, c=c, slot=slot: g.dma_start(
                    out=xr[:, slot * TPC:(slot + 1) * TPC],
                    in_=x0T[(c + 2) * 128:(c + 3) * 128, :]),
                    wait=[("v", vdone)], inc=("bx%d" % slot, 16))
            sch.op("y", lambda y, c=c, slot=slot: y.dma_start(
                out=x1T_s[c * 128:(c + 1) * 128, :],
                in_=xo[:, slot * TPC:(slot + 1) * TPC]),
                wait=[("v", vdone)], inc=("bo%d" % slot, 16))
        sch.nop("y", [("bo0", sch.mark("bo0")), ("bo1", sch.mark("bo1"))])



def _fused_u3(nc, sch, sb_factory, phase, io, cv, onesbt, onesft):
    x1T = io["x1T"]; wrT = io["wrT"]; wiT = io["wiT"]
    Gr = io["Gr"]; Gi = io["Gi"]; Gin = io["Gin"]
    Fr = io["Fr"]; Fin = io["Fin"]; biasc = io["biasc"]
    x2T_s = io["x2T_s"]
    with contextlib.ExitStack() as stk:
        sb, psum = sb_factory(stk)
        for _e in ("g", "v", "s", "t", "y"):
            sch.nop(_e, phase())

        Msb = sb("c_Msb", [128, DC * D], BF16)
        bic = sb("c_bic", [128, DC], F32)

        with contextlib.ExitStack() as mctx:
            def msb(name, shape, dtype):
                return mctx.enter_context(nc.sbuf_tensor(name, shape, dtype))

            wrb = msb("c_wrb", [128, DC * D], BF16)
            wib = msb("c_wib", [128, DC * D], BF16)
            Grb = msb("c_Grb", [128, DC * D], BF16)
            Gib = msb("c_Gib", [128, DC * D], BF16)
            Ginb = msb("c_Ginb", [128, DC * D], BF16)
            Trb = msb("c_Trb", [128, DC * D], BF16)
            Tib = msb("c_Tib", [128, DC * D], BF16)
            ps_m = psum("c_ps_m", [128, 512], F32)

            for c in range(DC):
                for dst, srcap in ((wrb, wrT), (wib, wiT), (Grb, Gr),
                                   (Gib, Gi), (Ginb, Gin)):
                    sch.op("g", lambda g, dst=dst, srcap=srcap, c=c: g.dma_start(
                        out=dst[:, c * D:(c + 1) * D],
                        in_=srcap[c * 128:(c + 1) * 128, :]))
            sch.op("g", lambda g: g.dma_start(out=bic[:], in_=biasc[:]))
            lA = sch.mark("g")

            def mm_pair(dst, lhs1, rhs1, lhs2, rhs2, wait0):
                for k in range(DC):
                    for f in range(D // 512):
                        for c in range(DC):
                            sch.op("t", lambda t, k=k, f=f, c=c, lhs1=lhs1, rhs1=rhs1: t.matmul(
                                ps_m[:],
                                lhs1[:, c * D + k * 128:c * D + (k + 1) * 128],
                                rhs1[:, c * D + f * 512:c * D + (f + 1) * 512],
                                start=(c == 0), stop=False),
                                wait=wait0 if (k == 0 and f == 0 and c == 0) else ())
                        for c in range(DC):
                            sch.op("t", lambda t, k=k, f=f, c=c, lhs2=lhs2, rhs2=rhs2: t.matmul(
                                ps_m[:],
                                lhs2[:, c * D + k * 128:c * D + (k + 1) * 128],
                                rhs2[:, c * D + f * 512:c * D + (f + 1) * 512],
                                start=False, stop=(c == DC - 1)))
                        sch.op("v", lambda v, dst=dst, k=k, f=f: v.tensor_copy(
                            dst[:, k * D + f * 512:k * D + (f + 1) * 512],
                            ps_m[:]), wait=[("t", sch.mark("t"))])
                        sch.nop("t", [("v", sch.mark("v"))])

            mm_pair(Trb, wrb, Grb, wib, Ginb, [("g", lA)])
            mm_pair(Tib, wrb, Gib, wib, Grb, [])
            wrF = wrb
            wiF = wib
            for c in range(DC):
                sch.op("g", lambda g, c=c: g.dma_start(
                    out=wrF[:, c * D:(c + 1) * D],
                    in_=Fr[c * 128:(c + 1) * 128, :]),
                    wait=[("v", sch.mark("v"))])
                sch.op("g", lambda g, c=c: g.dma_start(
                    out=wiF[:, c * D:(c + 1) * D],
                    in_=Fin[c * 128:(c + 1) * 128, :]))
            lF = sch.mark("g")
            mm_pair(Msb, wrF, Trb, wiF, Tib, [("g", lF)])
        xb = sb("c_xb", [128, DC * TPC], BF16)
        ub = sb("c_ub", [128, DC * TPC], BF16)
        yb = sb("c_yb", [128, DC * TPC], BF16)
        sqsc = sb("c_sqsc", [128, 512], BF16)
        strow = sb("c_strow", [1, 2 * TPC], F32)
        varsc = sb("c_varsc", [1, TPC], F32)
        mubc = sb("c_mubc", [128, TPC], F32)
        rsbc = sb("c_rsbc", [128, TPC], F32)
        tsc = sb("c_tsc", [128, TPC], F32)
        xo = sb("c_xo", [128, 2 * TPC], F32)
        xr = sb("c_xr", [128, 2 * TPC], F32)
        ps_sx = psum("c_ps_sx", [1, 512], F32)
        ps_sxx = psum("c_ps_sxx", [1, 512], F32)
        ps_bc = psum("c_ps_bc", [128, 512], F32)
        ps_y = psum("c_ps_y", [128, 512], F32)

        st_sx = strow[0:1, 0:TPC]
        st_sxx = strow[0:1, TPC:2 * TPC]
        murow = st_sx
        rsrow = st_sxx

        phA = [("v", sch.mark("v")), ("t", sch.mark("t"))]
        for c in range(DC):
            sch.op("g", lambda g, c=c: g.dma_start(
                out=xb[:, c * TPC:(c + 1) * TPC],
                in_=x1T[c * 128:(c + 1) * 128, :]), wait=phA)
        lB = sch.mark("g")
        _fm_stats(sch, xb, lambda c: c * TPC, TPC, ps_sx, ps_sxx, onesbt,
                  sqsc, st_sx, st_sxx, wait_in=[("g", lB)])
        _fm_rows(sch, TPC, st_sx, st_sxx, murow, rsrow, varsc,
                 cv[0:1, 0:1], INV_D)
        _fm_bcast(sch, TPC, murow, mubc, onesft, ps_bc)
        _fm_bcast(sch, TPC, rsrow, rsbc, onesft, ps_bc)
        for c in range(DC):
            sch.op("v", lambda v, c=c: v.tensor_sub(
                tsc[:], xb[:, c * TPC:(c + 1) * TPC], mubc[:]),
                wait=[("v", sch.mark("v"))])
            sch.op("v", lambda v, c=c: v.tensor_mul(
                ub[:, c * TPC:(c + 1) * TPC], tsc[:], rsbc[:]),
                wait=[("v", sch.mark("v"))])
        vu = sch.mark("v")
        for o in range(DC):
            for f in range(4):
                for c in range(DC):
                    sch.op("t", lambda t, o=o, f=f, c=c: t.matmul(
                        ps_y[:],
                        Msb[:, c * D + o * 128:c * D + (o + 1) * 128],
                        ub[:, c * TPC + f * 512:c * TPC + (f + 1) * 512],
                        start=(c == 0), stop=(c == DC - 1)),
                        wait=[("v", vu)] if (f == 0 and c == 0) else ())
                sch.op("v", lambda v, o=o, f=f: v.tensor_scalar(
                    yb[:, o * TPC + f * 512:o * TPC + (f + 1) * 512],
                    ps_y[:], bic[:, o:o + 1], None, ALU.add),
                    wait=[("t", sch.mark("t"))])
                sch.nop("t", [("v", sch.mark("v"))])
        _fm_stats(sch, yb, lambda c: c * TPC, TPC, ps_sx, ps_sxx, onesbt,
                  sqsc, st_sx, st_sxx, wait_in=[("v", sch.mark("v"))])
        _fm_rows(sch, TPC, st_sx, st_sxx, murow, rsrow, varsc,
                 cv[0:1, 0:1], INV_D)
        _fm_bcast(sch, TPC, murow, mubc, onesft, ps_bc)
        _fm_bcast(sch, TPC, rsrow, rsbc, onesft, ps_bc)
        xrm = [0] * DC
        for c in range(2):
            sch.op("g", lambda g, c=c: g.dma_start(
                out=xr[:, (c % 2) * TPC:((c % 2) + 1) * TPC],
                in_=x1T[c * 128:(c + 1) * 128, :]), inc=("cx%d" % (c % 2), 16))
        for c in range(DC):
            slot = c % 2
            w = [("v", sch.mark("v")), ("cx%d" % slot, 16 * (c // 2 + 1))]
            if c >= 2:
                w.append(("co%d" % slot, 16 * (c // 2)))
            sch.op("v", lambda v, c=c: v.tensor_sub(
                tsc[:], yb[:, c * TPC:(c + 1) * TPC], mubc[:]), wait=w)
            sch.op("v", lambda v: v.tensor_mul(tsc[:], tsc[:], rsbc[:]),
                   wait=[("v", sch.mark("v"))])
            sch.op("v", lambda v, c=c, slot=slot: v.tensor_add(
                xo[:, slot * TPC:(slot + 1) * TPC], tsc[:],
                xr[:, slot * TPC:(slot + 1) * TPC]),
                wait=[("v", sch.mark("v"))])
            vdone = sch.mark("v")
            if c + 2 < DC:
                sch.op("g", lambda g, c=c, slot=slot: g.dma_start(
                    out=xr[:, slot * TPC:(slot + 1) * TPC],
                    in_=x1T[(c + 2) * 128:(c + 3) * 128, :]),
                    wait=[("v", vdone)], inc=("cx%d" % slot, 16))
            sch.op("y", lambda y, c=c, slot=slot: y.dma_start(
                out=x2T_s[c * 128:(c + 1) * 128, :],
                in_=xo[:, slot * TPC:(slot + 1) * TPC]),
                wait=[("v", vdone)], inc=("co%d" % slot, 16))
        sch.nop("y", [("co0", sch.mark("co0")), ("co1", sch.mark("co1"))])



def _fused_u4(nc, sch, sb_factory, phase, io, cv, onesbt, onesft):
    x2T = io["x2T"]; WqT = io["WqT"]; WkT = io["WkT"]; WvT = io["WvT"]
    nwc = io["nwc"]; bqc = io["bqc"]; bkc = io["bkc"]; bvr = io["bvr"]
    qT_s = io["qT_s"]; kT_s = io["kT_s"]
    vtok1_s = io["vtok1_s"]; xnT_s = io["xnT_s"]
    with contextlib.ExitStack() as stk:
        sb, psum = sb_factory(stk)
        for _e in ("g", "v", "s", "t", "y"):
            sch.nop(_e, phase())

        xb = sb("d_xb", [128, DC * TPC], BF16)
        xnb = sb("d_xnb", [128, DC * TPC], BF16)
        wqb = sb("d_wqb", [128, DC * D], BF16)
        wkb = sb("d_wkb", [128, DC * D], BF16)
        wvb = sb("d_wvb", [128, DC * D], BF16)
        nwt = sb("d_nwt", [128, DC], F32)
        bqt = sb("d_bqt", [128, DC], F32)
        bkt = sb("d_bkt", [128, DC], F32)
        bvt = sb("d_bvt", [128, D], F32)
        sqsc = sb("d_sqsc", [128, 512], BF16)
        strow = sb("d_strow", [1, 2 * TPC], F32)
        varsc = sb("d_varsc", [1, TPC], F32)
        mubc = sb("d_mubc", [128, TPC], F32)
        rsbc = sb("d_rsbc", [128, TPC], F32)
        tsc = sb("d_tsc", [128, TPC], F32)
        qo = sb("d_qo", [128, 2 * TPC], BF16)
        ko = sb("d_ko", [128, 2 * TPC], BF16)
        vo = sb("d_vo", [128, 2 * D], BF16)
        ps_sx = psum("d_ps_sx", [1, 512], F32)
        ps_sxx = psum("d_ps_sxx", [1, 512], F32)
        ps_bc = psum("d_ps_bc", [128, 512], F32)
        ps_q = psum("d_ps_q", [128, 512], F32)
        ps_k = psum("d_ps_k", [128, 512], F32)
        ps_v = psum("d_ps_v", [128, 512], F32)

        for c in range(DC):
            sch.op("g", lambda g, c=c: g.dma_start(
                out=xb[:, c * TPC:(c + 1) * TPC],
                in_=x2T[c * 128:(c + 1) * 128, :]))
            for dst, srcap in ((wqb, WqT), (wkb, WkT), (wvb, WvT)):
                sch.op("g", lambda g, dst=dst, srcap=srcap, c=c: g.dma_start(
                    out=dst[:, c * D:(c + 1) * D],
                    in_=srcap[c * 128:(c + 1) * 128, :]))
        for dst, srcap in ((nwt, nwc), (bqt, bqc), (bkt, bkc), (bvt, bvr)):
            sch.op("g", lambda g, dst=dst, srcap=srcap: g.dma_start(
                out=dst[:], in_=srcap[:]))
        lall = sch.mark("g")

        st_sx = strow[0:1, 0:TPC]
        st_sxx = strow[0:1, TPC:2 * TPC]

        _fm_stats(sch, xb, lambda c: c * TPC, TPC, ps_sx, ps_sxx, onesbt,
                  sqsc, st_sx, st_sxx, wait_in=[("g", lall)])
        sch.op("v", lambda v: v.tensor_scalar_mul(
            st_sx[0:1, :], st_sx[0:1, :], INV_D), wait=[("v", sch.mark("v"))])
        sch.op("s", lambda s: s.activation(
            varsc[0:1, :], st_sx[0:1, :], AF.Square),
            wait=[("v", sch.mark("v"))])
        sch.op("v", lambda v: v.tensor_scalar(
            st_sxx[0:1, :], st_sxx[0:1, :], INV_D, None, ALU.mult),
            wait=[("s", sch.mark("s"))])
        sch.op("v", lambda v: v.tensor_sub(
            varsc[0:1, :], st_sxx[0:1, :], varsc[0:1, :]),
            wait=[("v", sch.mark("v"))])
        sch.op("s", lambda s: s.activation(
            st_sxx[0:1, :], varsc[0:1, :], AF.Sqrt, bias=cv[0:1, 0:1]),
            wait=[("v", sch.mark("v"))])
        sch.op("v", lambda v: v.reciprocal(st_sxx[0:1, :], st_sxx[0:1, :]),
               wait=[("s", sch.mark("s"))])
        sch.op("s", lambda s: s.activation(
            varsc[0:1, :], st_sxx[0:1, :], AF.Square),
            wait=[("v", sch.mark("v"))])
        sch.op("v", lambda v: v.tensor_scalar(
            varsc[0:1, :], varsc[0:1, :], -1e-5, 1.0, ALU.mult, ALU.add),
            wait=[("s", sch.mark("s"))])
        sch.op("s", lambda s: s.activation(
            varsc[0:1, :], varsc[0:1, :], AF.Sqrt, bias=cv[0:1, 1:2]),
            wait=[("v", sch.mark("v"))])
        sch.op("v", lambda v: v.reciprocal(varsc[0:1, :], varsc[0:1, :]),
               wait=[("s", sch.mark("s"))])
        sch.op("v", lambda v: v.tensor_mul(
            st_sxx[0:1, :], st_sxx[0:1, :], varsc[0:1, :]),
            wait=[("v", sch.mark("v"))])
        _fm_bcast(sch, TPC, st_sx, mubc, onesft, ps_bc)
        _fm_bcast(sch, TPC, st_sxx, rsbc, onesft, ps_bc)
        for c in range(DC):
            slot = c % 2
            sch.op("v", lambda v, c=c: v.tensor_sub(
                tsc[:], xb[:, c * TPC:(c + 1) * TPC], mubc[:]),
                wait=[("v", sch.mark("v"))])
            sch.op("v", lambda v, c=c: v.scalar_tensor_tensor(
                xnb[:, c * TPC:(c + 1) * TPC], tsc[:],
                nwt[:, c:c + 1], rsbc[:], ALU.mult, ALU.mult),
                wait=[("v", sch.mark("v"))])
            sch.op("y", lambda y, c=c: y.dma_start(
                out=xnT_s[c * 128:(c + 1) * 128, :],
                in_=xnb[:, c * TPC:(c + 1) * TPC]),
                wait=[("v", sch.mark("v"))], inc=("dn%d" % slot, 16))
        vxn = sch.mark("v")

        for o in range(DC):
            slot = o % 2
            for kind, wb, ps, bt, outap, semk in (
                    ("q", wqb, ps_q, bqt, qT_s, "q"),
                    ("k", wkb, ps_k, bkt, kT_s, "k")):
                ob = qo if kind == "q" else ko
                w = [("v", sch.mark("v"))]
                if o >= 2:
                    w.append(("%s%d" % (semk, slot), 16 * (o // 2)))
                for f in range(4):
                    for c in range(DC):
                        sch.op("t", lambda t, wb=wb, ps=ps, o=o, f=f, c=c: t.matmul(
                            ps[:],
                            wb[:, c * D + o * 128:c * D + (o + 1) * 128],
                            xnb[:, c * TPC + f * 512:c * TPC + (f + 1) * 512],
                            start=(c == 0), stop=(c == DC - 1)),
                            wait=w if (f == 0 and c == 0) else ())
                    if kind == "q":
                        sch.op("v", lambda v, ob=ob, ps=ps, bt=bt, slot=slot, f=f, o=o: v.tensor_scalar(
                            ob[:, slot * TPC + f * 512:slot * TPC + (f + 1) * 512],
                            ps[:], bt[:, o:o + 1], 0.125, ALU.add, ALU.mult),
                            wait=[("t", sch.mark("t"))])
                    else:
                        sch.op("v", lambda v, ob=ob, ps=ps, bt=bt, slot=slot, f=f, o=o: v.tensor_scalar(
                            ob[:, slot * TPC + f * 512:slot * TPC + (f + 1) * 512],
                            ps[:], bt[:, o:o + 1], None, ALU.add),
                            wait=[("t", sch.mark("t"))])
                    if f < 3:
                        sch.nop("t", [("v", sch.mark("v"))])
                sch.op("y", lambda y, ob=ob, outap=outap, o=o, slot=slot: y.dma_start(
                    out=outap[o * 128:(o + 1) * 128, :],
                    in_=ob[:, slot * TPC:(slot + 1) * TPC]),
                    wait=[("v", sch.mark("v"))], inc=("%s%d" % (semk, slot), 16))
        for t in range(NT):
            slot = t % 2
            w = [("v", sch.mark("v"))]
            if t >= 2:
                w.append(("dw%d" % slot, 16 * (t // 2)))
            for f in range(2):
                for c in range(DC):
                    sch.op("t", lambda tt, t=t, f=f, c=c: tt.matmul(
                        ps_v[:],
                        xnb[:, c * TPC + t * 128:c * TPC + (t + 1) * 128],
                        wvb[:, c * D + f * 512:c * D + (f + 1) * 512],
                        start=(c == 0), stop=(c == DC - 1)),
                        wait=w if (f == 0 and c == 0) else ())
                sch.op("v", lambda v, slot=slot, f=f: v.tensor_add(
                    vo[:, slot * D + f * 512:slot * D + (f + 1) * 512],
                    ps_v[:], bvt[:, f * 512:(f + 1) * 512]),
                    wait=[("t", sch.mark("t"))])
                if f == 0:
                    sch.nop("t", [("v", sch.mark("v"))])
            sch.op("y", lambda y, t=t, slot=slot: y.dma_start(
                out=vtok1_s[t * 128:(t + 1) * 128, :],
                in_=vo[:, slot * D:(slot + 1) * D]),
                wait=[("v", sch.mark("v"))], inc=("dw%d" % slot, 16))
        sch.nop("y", [(k, sch.mark(k)) for k in
                      ("dq0", "dq1", "dk0", "dk1", "dw0", "dw1", "dn0", "dn1")])



def _fused_u5(nc, sch, sb_factory, phase, ccm, io, onesbt, onesft):
    qTg = io["qTg"]; kTg = io["kTg"]; vt1g = io["vt1g"]
    msk = io["msk"]; selh = io["selh"]; oT_s = io["oT_s"]
    NQT = N // 512
    with contextlib.ExitStack() as stk:
        sb, psum = sb_factory(stk)
        for _e in ("g", "v", "s", "t", "y"):
            sch.nop(_e, phase() + [("cc", ccm)])

        qb = sb("e_qb", [128, 2 * N], BF16)
        kb = sb("e_kb", [128, 2 * N], BF16)
        vb = sb("e_vb", [128, (N // 128) * 256], BF16)
        ob = sb("e_ob", [128, 2 * N], BF16)
        mskb = sb("e_mskb", [128, 4 * 512], F32)
        esc = sb("e_esc", [128, 2 * 512], BF16)
        zrow = sb("e_zrow", [1, 512], F32)
        zbsb = sb("e_zbsb", [64, 512], F32)
        ps_s = psum("e_ps_s", [128, 2 * 512], F32)
        ps_o = psum("e_ps_o", [64, 512], F32)
        ps_z = psum("e_ps_z", [1, 512], F32)
        ps_zb = psum("e_ps_zb", [64, 512], F32)

        selht = sb("e_selht", [128, 4], F32)
        tmq = sb("e_tmq", [128, 2 * TPC], BF16)
        tmv = sb("e_tmv", [128, 2 * 256], BF16)
        sch.op("g", lambda g: g.dma_start(out=selht[:], in_=selh[:]))
        selm = sch.mark("g")
        nldq = 0
        consm = [0, 0]
        for dstb, srcg in ((qb, qTg), (kb, kTg)):
            for ch in range(2):
                for r in range(4):
                    dsl = slice(ch * N + r * TPC, ch * N + (r + 1) * TPC)
                    for hgo in range(4):
                        slot = nldq % 2
                        nldq += 1
                        sch.op("g", lambda g, srcg=srcg, r=r, hgo=hgo, ch=ch, slot=slot: g.dma_start(
                            out=tmq[:, slot * TPC:(slot + 1) * TPC],
                            in_=srcg[r * D + hgo * 256 + ch * 128:
                                     r * D + hgo * 256 + (ch + 1) * 128, :]),
                            wait=[("v", consm[slot]), ("g", selm)],
                            inc=("ee%d" % slot, 16))
                        if hgo == 0:
                            sch.op("v", lambda v, dstb=dstb, dsl=dsl, slot=slot: v.tensor_scalar(
                                dstb[:, dsl], tmq[:, slot * TPC:(slot + 1) * TPC],
                                selht[:, 0:1], None, ALU.mult),
                                wait=[("ee%d" % slot, 16 * ((nldq + 1) // 2))])
                        else:
                            sch.op("v", lambda v, dstb=dstb, dsl=dsl, slot=slot, hgo=hgo: v.scalar_tensor_tensor(
                                dstb[:, dsl], tmq[:, slot * TPC:(slot + 1) * TPC],
                                selht[:, hgo:hgo + 1], dstb[:, dsl],
                                ALU.mult, ALU.add),
                                wait=[("ee%d" % slot, 16 * ((nldq + 1) // 2)),
                                      ("v", sch.mark("v"))])
                        consm[slot] = sch.mark("v")
        nldv = 0
        consv = [0, 0]
        for jt in range(N // 128):
            r, loc = jt // 16, jt % 16
            for hgo in range(4):
                slot = nldv % 2
                nldv += 1
                sch.op("g", lambda g, r=r, loc=loc, hgo=hgo, slot=slot: g.dma_start(
                    out=tmv[:, slot * 256:(slot + 1) * 256],
                    in_=vt1g[r * TPC + loc * 128:r * TPC + (loc + 1) * 128,
                             hgo * 256:(hgo + 1) * 256]),
                    wait=[("v", consv[slot]), ("g", selm)],
                    inc=("ef%d" % slot, 16))
                if hgo == 0:
                    sch.op("v", lambda v, jt=jt, slot=slot: v.tensor_scalar(
                        vb[:, jt * 256:(jt + 1) * 256],
                        tmv[:, slot * 256:(slot + 1) * 256],
                        selht[:, 0:1], None, ALU.mult),
                        wait=[("ef%d" % slot, 16 * ((nldv + 1) // 2))])
                else:
                    sch.op("v", lambda v, jt=jt, slot=slot, hgo=hgo: v.scalar_tensor_tensor(
                        vb[:, jt * 256:(jt + 1) * 256],
                        tmv[:, slot * 256:(slot + 1) * 256],
                        selht[:, hgo:hgo + 1], vb[:, jt * 256:(jt + 1) * 256],
                        ALU.mult, ALU.add),
                        wait=[("ef%d" % slot, 16 * ((nldv + 1) // 2)),
                              ("v", sch.mark("v"))])
                consv[slot] = sch.mark("v")
        sch.op("g", lambda g: g.dma_start(out=mskb[:], in_=msk[:]))
        lall = sch.mark("g")

        for hl in range(4):
            ch = hl // 2
            base = (hl % 2) * 64
            for qt in range(NQT):
                jbl = (qt + 1) * 4
                for jb in range(jbl):
                    sslot = jb % 2
                    pss = ps_s[:, sslot * 512:(sslot + 1) * 512]
                    esl = esc[:, sslot * 512:(sslot + 1) * 512]
                    w = [("g", lall), ("s", sch.mark("s"))]
                    sch.op("t", lambda t, ch=ch, base=base, qt=qt, jb=jb, pss=pss: t.matmul(
                        pss,
                        kb[base:base + 64, ch * N + jb * 128:ch * N + (jb + 1) * 128],
                        qb[base:base + 64, ch * N + qt * 512:ch * N + (qt + 1) * 512],
                        start=True, stop=True), wait=w)
                    p = jb - qt * 4
                    if p >= 0:
                        sch.op("v", lambda v, pss=pss, p=p: v.tensor_add(
                            pss, pss, mskb[:, p * 512:(p + 1) * 512]),
                            wait=[("t", sch.mark("t"))])
                        wexp = [("v", sch.mark("v")), ("t", sch.mark("t"))]
                    else:
                        wexp = [("t", sch.mark("t"))]
                    sch.op("s", lambda s, esl=esl, pss=pss: s.activation(
                        esl, pss, AF.Exp), wait=wexp)
                    sm = sch.mark("s")
                    sch.op("t", lambda t, jb=jb, hl=hl, esl=esl, jbl=jbl: t.matmul(
                        ps_o[:], vb[:, jb * 256 + hl * 64:jb * 256 + (hl + 1) * 64],
                        esl, start=(jb == 0), stop=(jb == jbl - 1)),
                        wait=[("s", sm)])
                    sch.op("t", lambda t, jb=jb, esl=esl, jbl=jbl: t.matmul(
                        ps_z[:], onesbt[:, 0:1], esl,
                        start=(jb == 0), stop=(jb == jbl - 1)))
                sch.op("v", lambda v: v.tensor_copy(zrow[0:1, :], ps_z[:]),
                       wait=[("t", sch.mark("t"))])
                sch.op("v", lambda v: v.reciprocal(zrow[0:1, :], zrow[0:1, :]),
                       wait=[("v", sch.mark("v"))])
                sch.op("t", lambda t: t.matmul(
                    ps_zb[:], onesft[0:1, 0:64], zrow[0:1, :],
                    start=True, stop=True), wait=[("v", sch.mark("v"))])
                sch.op("v", lambda v: v.tensor_copy(zbsb[:], ps_zb[:]),
                       wait=[("t", sch.mark("t"))])
                sch.op("v", lambda v, ch=ch, base=base, qt=qt: v.tensor_mul(
                    ob[base:base + 64, ch * N + qt * 512:ch * N + (qt + 1) * 512],
                    ps_o[:], zbsb[:]), wait=[("v", sch.mark("v"))])
                sch.nop("t", [("v", sch.mark("v"))])
        for ch in range(2):
            sch.op("y", lambda y, ch=ch: y.dma_start(
                out=oT_s[ch * 128:(ch + 1) * 128, :],
                in_=ob[:, ch * N:(ch + 1) * N]),
                wait=[("v", sch.mark("v"))])
        sch.nop("y", [("y", sch.mark("y"))])



def _fused_u6a(nc, sch, sb_factory, phase, ccm, io, cv, onesbt, onesft):
    oTg = io["oTg"]; xnT = io["xnT"]; x2T = io["x2T"]
    WoT = io["WoT"]; boc = io["boc"]; selq = io["selq"]
    x3T_s = io["x3T_s"]; edge = io["edge"]
    with contextlib.ExitStack() as stk:
        sb, psum = sb_factory(stk)
        for _e in ("g", "v", "s", "t", "y"):
            sch.nop(_e, phase() + [("cc", ccm)])

        ob = sb("f_ob", [128, DC * TPC], BF16)
        wob = sb("f_wob", [128, DC * D], BF16)
        blk = sb("f_blk", [128, DC * TPC], BF16)
        xns = sb("f_xns", [128, 2 * TPC], BF16)
        xrs = sb("f_xrs", [128, 2 * TPC], F32)
        xo = sb("f_xo", [128, 2 * TPC], F32)
        boct = sb("f_boct", [128, DC], F32)
        sqsc = sb("f_sqsc", [128, 512], BF16)
        strow = sb("f_strow", [1, 2 * TPC], F32)
        varsc = sb("f_varsc", [1, TPC], F32)
        mubc = sb("f_mubc", [128, TPC], F32)
        rsbc = sb("f_rsbc", [128, TPC], F32)
        tsc = sb("f_tsc", [128, TPC], F32)
        ps_a = psum("f_ps_a", [128, 512], F32)
        ps_sx = psum("f_ps_sx", [1, 512], F32)
        ps_sxx = psum("f_ps_sxx", [1, 512], F32)
        ps_bc = psum("f_ps_bc", [128, 512], F32)

        selqt = sb("f_selqt", [128, 4], F32)
        tmo = sb("f_tmo", [128, 2 * TPC], BF16)
        sch.op("g", lambda g: g.dma_start(out=selqt[:], in_=selq[:]))
        selm = sch.mark("g")
        nld = 0
        consm = [0, 0]
        for c in range(DC):
            sch.op("g", lambda g, c=c: g.dma_start(
                out=wob[:, c * D:(c + 1) * D],
                in_=WoT[c * 128:(c + 1) * 128, :]))
            for qqo in range(4):
                slot = nld % 2
                nld += 1
                sch.op("g", lambda g, c=c, qqo=qqo, slot=slot: g.dma_start(
                    out=tmo[:, slot * TPC:(slot + 1) * TPC],
                    in_=oTg[c * 128:(c + 1) * 128, qqo * TPC:(qqo + 1) * TPC]),
                    wait=[("v", consm[slot]), ("g", selm)],
                    inc=("fg%d" % slot, 16))
                if qqo == 0:
                    sch.op("v", lambda v, c=c, slot=slot: v.tensor_scalar(
                        ob[:, c * TPC:(c + 1) * TPC],
                        tmo[:, slot * TPC:(slot + 1) * TPC],
                        selqt[:, 0:1], None, ALU.mult),
                        wait=[("fg%d" % slot, 16 * ((nld + 1) // 2))])
                else:
                    sch.op("v", lambda v, c=c, slot=slot, qqo=qqo: v.scalar_tensor_tensor(
                        ob[:, c * TPC:(c + 1) * TPC],
                        tmo[:, slot * TPC:(slot + 1) * TPC],
                        selqt[:, qqo:qqo + 1], ob[:, c * TPC:(c + 1) * TPC],
                        ALU.mult, ALU.add),
                        wait=[("fg%d" % slot, 16 * ((nld + 1) // 2)),
                              ("v", sch.mark("v"))])
                consm[slot] = sch.mark("v")
        sch.op("g", lambda g: g.dma_start(out=boct[:], in_=boc[:]))
        lall = sch.mark("g")
        for c in range(2):
            sch.op("g", lambda g, c=c: g.dma_start(
                out=xns[:, (c % 2) * TPC:((c % 2) + 1) * TPC],
                in_=xnT[c * 128:(c + 1) * 128, :]), inc=("fn%d" % (c % 2), 16))
        for o in range(DC):
            slot = o % 2
            w = [("g", lall), ("fn%d" % slot, 16 * (o // 2 + 1)),
                 ("v", sch.mark("v"))]
            for f in range(4):
                for c in range(DC):
                    sch.op("t", lambda t, o=o, f=f, c=c: t.matmul(
                        ps_a[:],
                        wob[:, c * D + o * 128:c * D + (o + 1) * 128],
                        ob[:, c * TPC + f * 512:c * TPC + (f + 1) * 512],
                        start=(c == 0), stop=(c == DC - 1)),
                        wait=w if (f == 0 and c == 0) else ())
                sch.op("v", lambda v, o=o, f=f, slot=slot: v.tensor_scalar(
                    tsc[:, f * 512:(f + 1) * 512], ps_a[:], boct[:, o:o + 1],
                    None, ALU.add), wait=[("t", sch.mark("t"))])
                sch.op("v", lambda v, o=o, f=f, slot=slot: v.tensor_add(
                    blk[:, o * TPC + f * 512:o * TPC + (f + 1) * 512],
                    tsc[:, f * 512:(f + 1) * 512],
                    xns[:, slot * TPC + f * 512:slot * TPC + (f + 1) * 512]),
                    wait=[("v", sch.mark("v"))])
                if f < 3:
                    sch.nop("t", [("v", sch.mark("v"))])
            vdone = sch.mark("v")
            if o + 2 < DC:
                sch.op("g", lambda g, o=o, slot=slot: g.dma_start(
                    out=xns[:, slot * TPC:(slot + 1) * TPC],
                    in_=xnT[(o + 2) * 128:(o + 3) * 128, :]),
                    wait=[("v", vdone)], inc=("fn%d" % slot, 16))
        st_sx = strow[0:1, 0:TPC]
        st_sxx = strow[0:1, TPC:2 * TPC]
        _fm_stats(sch, blk, lambda c: c * TPC, TPC, ps_sx, ps_sxx, onesbt,
                  sqsc, st_sx, st_sxx, wait_in=[("v", sch.mark("v"))])
        _fm_rows(sch, TPC, st_sx, st_sxx, st_sx, st_sxx, varsc,
                 cv[0:1, 0:1], INV_D)
        _fm_bcast(sch, TPC, st_sx, mubc, onesft, ps_bc)
        _fm_bcast(sch, TPC, st_sxx, rsbc, onesft, ps_bc)
        for c in range(2):
            sch.op("g", lambda g, c=c: g.dma_start(
                out=xrs[:, (c % 2) * TPC:((c % 2) + 1) * TPC],
                in_=x2T[c * 128:(c + 1) * 128, :]), inc=("fx%d" % (c % 2), 16))
        for c in range(DC):
            slot = c % 2
            w = [("v", sch.mark("v")), ("fx%d" % slot, 16 * (c // 2 + 1))]
            if c >= 2:
                w.append(("fo%d" % slot, 16 * (c // 2)))
            sch.op("v", lambda v, c=c: v.tensor_sub(
                tsc[:], blk[:, c * TPC:(c + 1) * TPC], mubc[:]), wait=w)
            sch.op("v", lambda v: v.tensor_mul(tsc[:], tsc[:], rsbc[:]),
                   wait=[("v", sch.mark("v"))])
            sch.op("v", lambda v, c=c, slot=slot: v.tensor_add(
                xo[:, slot * TPC:(slot + 1) * TPC], tsc[:],
                xrs[:, slot * TPC:(slot + 1) * TPC]),
                wait=[("v", sch.mark("v"))])
            vdone = sch.mark("v")
            if c + 2 < DC:
                sch.op("g", lambda g, c=c, slot=slot: g.dma_start(
                    out=xrs[:, slot * TPC:(slot + 1) * TPC],
                    in_=x2T[(c + 2) * 128:(c + 3) * 128, :]),
                    wait=[("v", vdone)], inc=("fx%d" % slot, 16))
            sch.op("y", lambda y, c=c, slot=slot: y.dma_start(
                out=x3T_s[c * 128:(c + 1) * 128, :],
                in_=xo[:, slot * TPC:(slot + 1) * TPC]),
                wait=[("v", vdone)], inc=("fo%d" % slot, 16))
        sch.op("g", lambda g: g.dma_start(
            out=edge[:, 0:256], in_=x3T_s[:, 0:256]),
            wait=[("fo0", sch.mark("fo0")), ("fo1", sch.mark("fo1"))],
            inc=("fe", 16))
        sch.op("g", lambda g: g.dma_start(
            out=edge[:, 256:512], in_=x3T_s[:, TPC - 256:TPC]),
            inc=("fe", 16))
        sch.nop("y", [("fo0", sch.mark("fo0")), ("fo1", sch.mark("fo1")),
                      ("fe", 32)])



def _fused_u6b1(nc, sch, sb_factory, phase, ccm, io, cv, onesbt, onesft):
    x3T = io["x3T"]; edgeg = io["edgeg"]
    selL = io["selL"]; selR = io["selR"]
    mk0 = io["mk0"]; mk1 = io["mk1"]; mk2 = io["mk2"]
    u3T_s = io["u3T_s"]; xmT_s = io["xmT_s"]
    with contextlib.ExitStack() as stk:
        sb, psum = sb_factory(stk)
        for _e in ("g", "v", "s", "t", "y"):
            sch.nop(_e, phase() + [("cc", ccm)])

        xb = sb("h_xb", [128, DC * W6], BF16)
        mks = [sb("mk%d" % i, [128, DC], F32) for i in range(3)]
        sqsc = sb("h_sqsc", [128, 512], BF16)
        strow = sb("h_strow", [1, 2 * W6], F32)
        varsc = sb("h_varsc", [1, W6], F32)
        mubc = sb("h_mubc", [128, W6], F32)
        rsbc = sb("h_rsbc", [128, W6], F32)
        tsc = sb("h_tsc", [128, W6], F32)
        t2 = sb("h_t2", [128, TPC], F32)
        xmo = sb("h_xmo", [128, 2 * TPC], BF16)
        ps_sx = psum("h_ps_sx", [1, 512], F32)
        ps_sxx = psum("h_ps_sxx", [1, 512], F32)
        ps_bc = psum("h_ps_bc", [128, 512], F32)

        selLt = sb("h_selLt", [128, 4], F32)
        selRt = sb("h_selRt", [128, 4], F32)
        tme = sb("h_tme", [128, 256], F32)
        pacc = sb("h_pacc", [128, 2 * 256], F32)
        sch.op("g", lambda g: g.dma_start(out=selLt[:], in_=selL[:]))
        sch.op("g", lambda g: g.dma_start(out=selRt[:], in_=selR[:]))
        selm = sch.mark("g")
        for c in range(DC):
            sch.op("g", lambda g, c=c: g.dma_start(
                out=xb[:, c * W6 + PAD:c * W6 + PAD + TPC],
                in_=x3T[c * 128:(c + 1) * 128, :]))
            for pi, (selt, eoff, xoff) in enumerate(
                    ((selLt, 256, 0), (selRt, 0, PAD + TPC))):
                for r in range(4):
                    sch.op("g", lambda g, c=c, r=r, eoff=eoff: g.dma_start(
                        out=tme[:],
                        in_=edgeg[r * D + c * 128:r * D + (c + 1) * 128,
                                  eoff:eoff + 256]),
                        wait=[("v", sch.mark("v")), ("g", selm)],
                        inc=("he", 16))
                    if r == 0:
                        sch.op("v", lambda v, selt=selt, pi=pi: v.tensor_scalar(
                            pacc[:, pi * 256:(pi + 1) * 256], tme[:],
                            selt[:, 0:1], None, ALU.mult),
                            wait=[("he", sch.mark("he"))])
                    else:
                        sch.op("v", lambda v, selt=selt, pi=pi, r=r: v.scalar_tensor_tensor(
                            pacc[:, pi * 256:(pi + 1) * 256], tme[:],
                            selt[:, r:r + 1], pacc[:, pi * 256:(pi + 1) * 256],
                            ALU.mult, ALU.add),
                            wait=[("he", sch.mark("he")), ("v", sch.mark("v"))])
                sch.op("v", lambda v, c=c, pi=pi, xoff=xoff: v.tensor_copy(
                    xb[:, c * W6 + xoff:c * W6 + xoff + 256],
                    pacc[:, pi * 256:(pi + 1) * 256]),
                    wait=[("v", sch.mark("v"))])
        for dst, srcap in ((mks[0], mk0), (mks[1], mk1), (mks[2], mk2)):
            sch.op("g", lambda g, dst=dst, srcap=srcap: g.dma_start(
                out=dst[:], in_=srcap[:]))
        lall = sch.mark("g")

        st_sx = strow[0:1, 0:W6]
        st_sxx = strow[0:1, W6:2 * W6]
        _fm_stats(sch, xb, lambda c: c * W6, W6, ps_sx, ps_sxx, onesbt,
                  sqsc, st_sx, st_sxx, wait_in=[("g", lall)])
        _fm_rows(sch, W6, st_sx, st_sxx, st_sx, st_sxx, varsc,
                 cv[0:1, 0:1], INV_D)
        _fm_bcast(sch, W6, st_sx, mubc, onesft, ps_bc)
        _fm_bcast(sch, W6, st_sxx, rsbc, onesft, ps_bc)
        for c in range(DC):
            slot = c % 2
            sch.op("v", lambda v, c=c: v.tensor_sub(
                tsc[:], xb[:, c * W6:(c + 1) * W6], mubc[:]),
                wait=[("v", sch.mark("v"))])
            sch.op("v", lambda v, c=c: v.tensor_mul(
                xb[:, c * W6:(c + 1) * W6], tsc[:], rsbc[:]),
                wait=[("v", sch.mark("v"))])
            vu = sch.mark("v")
            sch.op("y", lambda y, c=c: y.dma_start(
                out=u3T_s[c * 128:(c + 1) * 128, :],
                in_=xb[:, c * W6 + PAD:c * W6 + PAD + TPC]),
                wait=[("v", vu)], inc=("hu%d" % slot, 16))
            own = lambda c, off: xb[:, c * W6 + PAD + off:c * W6 + PAD + TPC + off]
            sch.op("v", lambda v, c=c, own=own: v.tensor_scalar(
                t2[:], own(c, -1), mks[0][:, c:c + 1], None, ALU.mult),
                wait=[("v", vu)])
            sch.op("v", lambda v, c=c, own=own: v.scalar_tensor_tensor(
                t2[:], own(c, 0), mks[1][:, c:c + 1], t2[:],
                ALU.mult, ALU.add), wait=[("v", sch.mark("v"))])
            w = [("v", sch.mark("v"))]
            if c >= 2:
                w.append(("ho%d" % slot, 16 * (c // 2)))
            sch.op("v", lambda v, c=c, slot=slot, own=own: v.scalar_tensor_tensor(
                xmo[:, slot * TPC:(slot + 1) * TPC], own(c, 1),
                mks[2][:, c:c + 1], t2[:], ALU.mult, ALU.add), wait=w)
            sch.op("y", lambda y, c=c, slot=slot: y.dma_start(
                out=xmT_s[c * 128:(c + 1) * 128, :],
                in_=xmo[:, slot * TPC:(slot + 1) * TPC]),
                wait=[("v", sch.mark("v"))], inc=("ho%d" % slot, 16))
        sch.nop("y", [(k, sch.mark(k)) for k in ("ho0", "ho1", "hu0", "hu1")])



def _fused_u6b2(nc, sch, sb_factory, phase, io, cv, onesbt, onesft, identft):
    xmT = io["xmT"]; u3T = io["u3T"]; x3T = io["x3T"]
    W1T = io["W1T"]; W2T = io["W2T"]; b1c = io["b1c"]; b2r = io["b2r"]
    Wo1 = io["Wo1"]; bo1c = io["bo1c"]; Wo2 = io["Wo2"]; bo2c = io["bo2c"]
    gamc = io["gamc"]; betc = io["betc"]; selbc = io["selb"]
    x4_s = io["x4_s"]
    HF = TPC // 4
    NF = HF // 512
    with contextlib.ExitStack() as stk:
        sb, psum = sb_factory(stk)
        for _e in ("g", "v", "s", "t", "y"):
            sch.nop(_e, phase())

        zb = sb("i_zb", [128, DC * TPC], BF16)
        gamt = sb("i_gamt", [128, DC], F32)
        bett = sb("i_bett", [128, DC], F32)
        for dst, srcap in ((gamt, gamc), (bett, betc)):
            sch.op("g", lambda g, dst=dst, srcap=srcap: g.dma_start(
                out=dst[:], in_=srcap[:]))

        with contextlib.ExitStack() as mctx:
            def msb(name, shape, dtype):
                return mctx.enter_context(nc.sbuf_tensor(name, shape, dtype))

            def mpsum(name, shape, dtype):
                return mctx.enter_context(nc.psum_tensor(name, shape, dtype))

            xmb = msb("i_xmb", [128, DC * TPC], BF16)
            w1b = msb("i_w1b", [128, 9 * DC * 256], BF16)
            w2b = msb("i_w2b", [128, 18 * D], BF16)
            wo1b = msb("i_wo1b", [128, DC * 256], BF16)
            wo2b = msb("i_wo2b", [128, 2 * 9], BF16)
            b1t = msb("i_b1t", [128, 18], F32)
            b2b = msb("i_b2b", [9, D], BF16)
            bo1t = msb("i_bo1t", [128, 2], F32)
            bo2t = msb("i_bo2t", [128, 1], F32)
            tb = msb("i_tb", [128, 2 * TPC], BF16)
            s9 = msb("i_s9", [9, TPC], F32)
            stok = msb("i_stok", [128, NT * 9], F32)
            mxc = msb("i_mxc", [128, NT], F32)
            ohtok = msb("i_ohtok", [128, NT * 9], F32)
            ohT = msb("i_ohT", [9, TPC], F32)
            ohTb = msb("i_ohTb", [9, TPC], BF16)
            ohbc = msb("i_ohbc", [128, HF], BF16)
            selbt = msb("i_selbt", [9, 9 * 128], BF16)
            hb = msb("i_hb", [128, 2 * HF], BF16)
            hmall = msb("i_hmall", [128, 18 * HF], BF16)
            u3s = msb("i_u3s", [128, 2 * HF], BF16)
            ps_h = mpsum("i_ps_h", [128, 512], F32)
            ps_c = mpsum("i_ps_c", [128, 512], F32)
            ps_t9 = mpsum("i_ps_t9", [9, 512], F32)
            ps_tp = mpsum("i_ps_tp", [128, NT * 9], F32)
            ps_oh = mpsum("i_ps_oh", [9, 512], F32)
            ps_b = mpsum("i_ps_b", [128, 512], F32)

            for c in range(DC):
                sch.op("g", lambda g, c=c: g.dma_start(
                    out=xmb[:, c * TPC:(c + 1) * TPC],
                    in_=xmT[c * 128:(c + 1) * 128, :]))
                sch.op("g", lambda g, c=c: g.dma_start(
                    out=wo1b[:, c * 256:(c + 1) * 256],
                    in_=Wo1[c * 128:(c + 1) * 128, :]))
                for p in range(9):
                    sch.op("g", lambda g, p=p, c=c: g.dma_start(
                        out=w1b[:, (p * DC + c) * 256:(p * DC + c + 1) * 256],
                        in_=W1T[p * D + c * 128:p * D + (c + 1) * 128, :]))
            for p in range(9):
                for mc in range(2):
                    sch.op("g", lambda g, p=p, mc=mc: g.dma_start(
                        out=w2b[:, (p * 2 + mc) * D:(p * 2 + mc + 1) * D],
                        in_=W2T[p * 256 + mc * 128:p * 256 + (mc + 1) * 128, :]))
            for mc in range(2):
                sch.op("g", lambda g, mc=mc: g.dma_start(
                    out=wo2b[:, mc * 9:(mc + 1) * 9],
                    in_=Wo2[mc * 128:(mc + 1) * 128, :]))
            for dst, srcap in ((b1t, b1c), (b2b, b2r), (bo1t, bo1c),
                               (bo2t, bo2c), (selbt, selbc)):
                sch.op("g", lambda g, dst=dst, srcap=srcap: g.dma_start(
                    out=dst[:], in_=srcap[:]))
            lall = sch.mark("g")

            for mc in range(2):
                for f in range(4):
                    for c in range(DC):
                        sch.op("t", lambda t, mc=mc, f=f, c=c: t.matmul(
                            ps_h[:],
                            wo1b[:, c * 256 + mc * 128:c * 256 + (mc + 1) * 128],
                            xmb[:, c * TPC + f * 512:c * TPC + (f + 1) * 512],
                            start=(c == 0), stop=(c == DC - 1)),
                            wait=[("g", lall), ("s", sch.mark("s"))] if (f == 0 and c == 0) else ())
                    sch.op("s", lambda s, mc=mc, f=f: s.activation(
                        tb[:, mc * TPC + f * 512:mc * TPC + (f + 1) * 512],
                        ps_h[:], AF.Tanh, bias=bo1t[:, mc:mc + 1]),
                        wait=[("t", sch.mark("t"))])
                    sch.nop("t", [("s", sch.mark("s"))])
            for f in range(4):
                for mc in range(2):
                    sch.op("t", lambda t, mc=mc, f=f: t.matmul(
                        ps_t9[:],
                        wo2b[:, mc * 9:(mc + 1) * 9],
                        tb[:, mc * TPC + f * 512:mc * TPC + (f + 1) * 512],
                        start=(mc == 0), stop=(mc == 1)),
                        wait=[("s", sch.mark("s"))] if mc == 0 else ())
                sch.op("v", lambda v, f=f: v.tensor_scalar(
                    s9[0:9, f * 512:(f + 1) * 512], ps_t9[:],
                    bo2t[0:9, 0:1], None, ALU.add),
                    wait=[("t", sch.mark("t"))])
                sch.nop("t", [("v", sch.mark("v"))])
            for t_ in range(NT):
                sch.op("t", lambda tt, t_=t_: tt.transpose(
                    ps_tp[:, t_ * 9:(t_ + 1) * 9],
                    s9[0:9, t_ * 128:(t_ + 1) * 128], identft[0:9, 0:9]),
                    wait=[("v", sch.mark("v"))])
            sch.op("v", lambda v: v.tensor_copy(stok[:], ps_tp[:]),
                   wait=[("t", sch.mark("t"))])
            for t_ in range(NT):
                sch.op("v", lambda v, t_=t_: v.reduce_max(
                    mxc[:, t_:t_ + 1], stok[:, t_ * 9:(t_ + 1) * 9],
                    axis=AX.X), wait=[("v", sch.mark("v"))])
                sch.op("v", lambda v, t_=t_: v.tensor_scalar(
                    ohtok[:, t_ * 9:(t_ + 1) * 9], stok[:, t_ * 9:(t_ + 1) * 9],
                    mxc[:, t_:t_ + 1], None, ALU.is_ge),
                    wait=[("v", sch.mark("v"))])
            for t_ in range(NT):
                sch.op("t", lambda tt, t_=t_: tt.transpose(
                    ps_oh[0:9, (t_ % 4) * 128:((t_ % 4) + 1) * 128],
                    ohtok[:, t_ * 9:(t_ + 1) * 9], identft[:]),
                    wait=[("v", sch.mark("v"))])
                if t_ % 4 == 3:
                    sch.op("v", lambda v, t_=t_: v.tensor_copy(
                        ohT[0:9, (t_ - 3) * 128:(t_ + 1) * 128], ps_oh[0:9, :]),
                        wait=[("t", sch.mark("t"))])
                    sch.nop("t", [("v", sch.mark("v"))])
            sch.op("v", lambda v: v.tensor_copy(ohTb[0:9, :], ohT[0:9, :]),
                   wait=[("v", sch.mark("v"))])

            for hf in range(4):
                for p in range(9):
                    for f in range(NF):
                        sch.op("t", lambda t, p=p, hf=hf, f=f: t.matmul(
                            ps_b[:], selbt[0:9, p * 128:(p + 1) * 128],
                            ohTb[0:9, hf * HF + f * 512:hf * HF + (f + 1) * 512],
                            start=True, stop=True),
                            wait=[("v", sch.mark("v"))])
                        sch.op("v", lambda v, f=f: v.tensor_copy(
                            ohbc[:, f * 512:(f + 1) * 512], ps_b[:]),
                            wait=[("t", sch.mark("t"))])
                        sch.nop("t", [("v", sch.mark("v"))])
                    for mc in range(2):
                        for f in range(NF):
                            for c in range(DC):
                                sch.op("t", lambda t, p=p, mc=mc, hf=hf, f=f, c=c: t.matmul(
                                    ps_h[:],
                                    w1b[:, (p * DC + c) * 256 + mc * 128:(p * DC + c) * 256 + (mc + 1) * 128],
                                    xmb[:, c * TPC + hf * HF + f * 512:c * TPC + hf * HF + (f + 1) * 512],
                                    start=(c == 0), stop=(c == DC - 1)),
                                    wait=[("v", sch.mark("v"))] if (f == 0 and c == 0) else ())
                            sch.op("v", lambda v, p=p, mc=mc, f=f: v.tensor_scalar(
                                hb[:, mc * HF + f * 512:mc * HF + (f + 1) * 512],
                                ps_h[:], b1t[:, p * 2 + mc:p * 2 + mc + 1],
                                None, ALU.add), wait=[("t", sch.mark("t"))])
                            sch.op("v", lambda v, p=p, mc=mc, f=f: v.tensor_mul(
                                hmall[:, (p * 2 + mc) * HF + f * 512:(p * 2 + mc) * HF + (f + 1) * 512],
                                hb[:, mc * HF + f * 512:mc * HF + (f + 1) * 512],
                                ohbc[:, f * 512:(f + 1) * 512]),
                                wait=[("v", sch.mark("v"))])
                            sch.nop("t", [("v", sch.mark("v"))])
                for o in range(2):
                    sch.op("g", lambda g, o=o, hf=hf: g.dma_start(
                        out=u3s[:, (o % 2) * HF:((o % 2) + 1) * HF],
                        in_=u3T[o * 128:(o + 1) * 128, hf * HF:(hf + 1) * HF]),
                        wait=[("v", sch.mark("v"))], inc=("iu%d" % (o % 2), 16))
                for o in range(DC):
                    slot = o % 2
                    w = [("v", sch.mark("v")),
                         ("iu%d" % slot, 16 * (hf * 4 + o // 2 + 1))]
                    for f in range(NF):
                        sch.op("t", lambda t, o=o, f=f, hf=hf: t.matmul(
                            ps_c[:], b2b[0:9, o * 128:(o + 1) * 128],
                            ohTb[0:9, hf * HF + f * 512:hf * HF + (f + 1) * 512],
                            start=True, stop=False),
                            wait=w if f == 0 else ())
                        for pm in range(18):
                            sch.op("t", lambda t, pm=pm, o=o, f=f: t.matmul(
                                ps_c[:],
                                w2b[:, pm * D + o * 128:pm * D + (o + 1) * 128],
                                hmall[:, pm * HF + f * 512:pm * HF + (f + 1) * 512],
                                start=False, stop=(pm == 17)))
                        sch.op("v", lambda v, o=o, f=f, hf=hf, slot=slot: v.tensor_add(
                            zb[:, o * TPC + hf * HF + f * 512:o * TPC + hf * HF + (f + 1) * 512],
                            ps_c[:],
                            u3s[:, slot * HF + f * 512:slot * HF + (f + 1) * 512]),
                            wait=[("t", sch.mark("t"))])
                        sch.nop("t", [("v", sch.mark("v"))])
                    vdone = sch.mark("v")
                    if o + 2 < DC:
                        sch.op("g", lambda g, o=o, hf=hf, slot=slot: g.dma_start(
                            out=u3s[:, slot * HF:(slot + 1) * HF],
                            in_=u3T[(o + 2) * 128:(o + 3) * 128, hf * HF:(hf + 1) * HF]),
                            wait=[("v", vdone)], inc=("iu%d" % slot, 16))
        fb = sb("i_fb", [128, DC * TPC], BF16)
        sqsc = sb("i_sqsc", [128, 512], BF16)
        strow = sb("i_strow", [1, 2 * TPC], F32)
        varsc = sb("i_varsc", [1, TPC], F32)
        mubc = sb("i_mubc", [128, TPC], F32)
        rsbc = sb("i_rsbc", [128, TPC], F32)
        tsc = sb("i_tsc", [128, TPC], F32)
        x3s = sb("i_x3s", [128, 2 * TPC], F32)
        x4c = sb("i_x4c", [128, TPC], F32)
        xto = sb("i_xto", [128, 2 * 128], F32)
        ps_sx = psum("i_ps_sx", [1, 512], F32)
        ps_sxx = psum("i_ps_sxx", [1, 512], F32)
        ps_bc = psum("i_ps_bc", [128, 512], F32)
        ps_f = psum("i_ps_f", [128, 128], F32)

        st_sx = strow[0:1, 0:TPC]
        st_sxx = strow[0:1, TPC:2 * TPC]
        phA = [("v", sch.mark("v")), ("t", sch.mark("t")), ("g", sch.mark("g"))]
        _fm_stats(sch, zb, lambda c: c * TPC, TPC, ps_sx, ps_sxx, onesbt,
                  sqsc, st_sx, st_sxx, wait_in=phA)
        _fm_rows(sch, TPC, st_sx, st_sxx, st_sx, st_sxx, varsc,
                 cv[0:1, 0:1], INV_D)
        _fm_bcast(sch, TPC, st_sx, mubc, onesft, ps_bc)
        _fm_bcast(sch, TPC, st_sxx, rsbc, onesft, ps_bc)
        for c in range(DC):
            sch.op("v", lambda v, c=c: v.tensor_sub(
                tsc[:], zb[:, c * TPC:(c + 1) * TPC], mubc[:]),
                wait=[("v", sch.mark("v"))])
            sch.op("v", lambda v, c=c: v.scalar_tensor_tensor(
                tsc[:], tsc[:], gamt[:, c:c + 1], rsbc[:],
                ALU.mult, ALU.mult), wait=[("v", sch.mark("v"))])
            sch.op("v", lambda v, c=c: v.tensor_scalar(
                fb[:, c * TPC:(c + 1) * TPC], tsc[:], bett[:, c:c + 1],
                None, ALU.add), wait=[("v", sch.mark("v"))])
        _fm_stats(sch, fb, lambda c: c * TPC, TPC, ps_sx, ps_sxx, onesbt,
                  sqsc, st_sx, st_sxx, wait_in=[("v", sch.mark("v"))])
        _fm_rows(sch, TPC, st_sx, st_sxx, st_sx, st_sxx, varsc,
                 cv[0:1, 0:1], INV_D)
        _fm_bcast(sch, TPC, st_sx, mubc, onesft, ps_bc)
        _fm_bcast(sch, TPC, st_sxx, rsbc, onesft, ps_bc)
        for c in range(2):
            sch.op("g", lambda g, c=c: g.dma_start(
                out=x3s[:, (c % 2) * TPC:((c % 2) + 1) * TPC],
                in_=x3T[c * 128:(c + 1) * 128, :]),
                wait=[("v", sch.mark("v")), ("t", sch.mark("t"))],
                inc=("ix%d" % (c % 2), 16))
        nyo = 0
        for c in range(DC):
            slot = c % 2
            w = [("v", sch.mark("v")), ("ix%d" % slot, 16 * (c // 2 + 1))]
            sch.op("v", lambda v, c=c: v.tensor_sub(
                tsc[:], fb[:, c * TPC:(c + 1) * TPC], mubc[:]), wait=w)
            sch.op("v", lambda v: v.tensor_mul(tsc[:], tsc[:], rsbc[:]),
                   wait=[("v", sch.mark("v"))])
            sch.op("v", lambda v, c=c, slot=slot: v.tensor_add(
                x4c[:], tsc[:], x3s[:, slot * TPC:(slot + 1) * TPC]),
                wait=[("v", sch.mark("v")), ("t", sch.mark("t"))])
            vdone = sch.mark("v")
            if c + 2 < DC:
                sch.op("g", lambda g, c=c, slot=slot: g.dma_start(
                    out=x3s[:, slot * TPC:(slot + 1) * TPC],
                    in_=x3T[(c + 2) * 128:(c + 3) * 128, :]),
                    wait=[("v", vdone)], inc=("ix%d" % slot, 16))
            for t_ in range(NT):
                ts_ = t_ % 2
                wtp = [("v", vdone)]
                if nyo >= 2:
                    wtp.append(("io%d" % ts_, 16 * (nyo // 2)))
                sch.op("t", lambda tt, t_=t_: tt.transpose(
                    ps_f[:], x4c[:, t_ * 128:(t_ + 1) * 128], identft[:]),
                    wait=wtp)
                sch.op("v", lambda v, ts_=ts_: v.tensor_copy(
                    xto[:, ts_ * 128:(ts_ + 1) * 128], ps_f[:]),
                    wait=[("t", sch.mark("t"))])
                sch.op("y", lambda y, c=c, t_=t_, ts_=ts_: y.dma_start(
                    out=x4_s[t_ * 128:(t_ + 1) * 128, c * 128:(c + 1) * 128],
                    in_=xto[:, ts_ * 128:(ts_ + 1) * 128]),
                    wait=[("v", sch.mark("v"))], inc=("io%d" % ts_, 16))
                nyo += 1
                sch.nop("t", [("v", sch.mark("v"))])
        sch.nop("y", [("io0", sch.mark("io0")), ("io1", sch.mark("io1"))])



def build_fused():
    nc = bass.Bass("TRN2", target_bir_lowering=False)
    E = lambda n, s, d: nc.dram_tensor(n, s, d, kind="ExternalInput").ap()
    O = lambda n, s, d: nc.dram_tensor(n, s, d, kind="ExternalOutput").ap()
    I = lambda n, s, d: nc.dram_tensor(n, s, d, kind="Internal").ap()

    x = E("x_tok", [TPC, D], F32)
    x0T = E("x0T_s", [D, TPC], F32)
    lng, lnb = E("lng", [128, D], F32), E("lnb", [128, D], F32)
    wkv = E("wkv", [D, 2 * D], F32)
    wmix = E("wmix", [128, 128], F32)
    wo0 = E("wo0", [D, D], F32)
    wrT, wiT = E("wrT", [D, D], F32), E("wiT", [D, D], F32)
    Gr, Gi, Gin = E("Gr", [D, D], F32), E("Gi", [D, D], F32), E("Gin", [D, D], F32)
    Fr, Fin = E("Fr", [D, D], F32), E("Fin", [D, D], F32)
    biasc = E("biasc", [128, DC], F32)
    WqT, WkT, WvT = E("WqT", [D, D], F32), E("WkT", [D, D], F32), E("WvT", [D, D], F32)
    nwc, bqc, bkc = E("nwc", [128, DC], F32), E("bqc", [128, DC], F32), E("bkc", [128, DC], F32)
    bvr = E("bvr", [128, D], F32)
    msk = E("msk", [128, 4 * 512], F32)
    WoT, boc = E("WoT", [D, D], F32), E("boc", [128, DC], F32)
    mk0, mk1, mk2 = E("mk0c", [128, DC], F32), E("mk1c", [128, DC], F32), E("mk2c", [128, DC], F32)
    W1T, W2T = E("W1T", [9 * D, 256], F32), E("W2T", [9 * 256, D], F32)
    b1c, b2r = E("b1c", [128, 18], F32), E("b2r", [9, D], F32)
    Wo1, bo1c = E("Wo1", [D, 256], F32), E("bo1c", [128, 2], F32)
    Wo2, bo2c = E("Wo2", [256, 9], F32), E("bo2c", [128, 1], F32)
    gamc, betc = E("gamc", [128, DC], F32), E("betc", [128, DC], F32)
    ident, identf = E("ident", [128, 128], BF16), E("ident_f", [128, 128], F32)
    selbc = E("selb", [9, 9 * 128], BF16)
    onesb, onesf = E("ones_bf", [128, 1], BF16), E("ones_f", [1, 128], F32)
    cvals = E("cvals", [128, NCONST], F32)
    selh = E("selh", [128, 4], F32)
    selL = E("selL", [128, 4], F32)
    selR = E("selR", [128, 4], F32)
    selq = E("selq", [128, 4], F32)
    x4_s = O("x4_s", [TPC, D], F32)

    kvT_d = I("d_kvT", [2 * D, TPC], BF16)
    vtok_d = I("d_vtok", [TPC, D], BF16)
    sqq_d = I("d_sqq", [H, TPC], F32)
    kvg = I("d_kvg", [4 * 2 * D, TPC], BF16)
    vtg = I("d_vtg", [4 * TPC, D], BF16)
    sqg = I("d_sqg", [4 * H, TPC], F32)
    x1T_d = I("d_x1T", [D, TPC], F32)
    x2T_d = I("d_x2T", [D, TPC], F32)
    qT_d = I("d_qT", [D, TPC], BF16)
    kT_d = I("d_kT", [D, TPC], BF16)
    vt1_d = I("d_vt1", [TPC, D], BF16)
    xnT_d = I("d_xnT", [D, TPC], BF16)
    qTg = I("d_qTg", [4 * D, TPC], BF16)
    kTg = I("d_kTg", [4 * D, TPC], BF16)
    vt1g = I("d_vt1g", [4 * TPC, D], BF16)
    oT_d = I("d_oT", [4 * HD, N], BF16)
    oTg = I("d_oTg", [4 * 4 * HD, N], BF16)
    x3T_d = I("d_x3T", [D, TPC], F32)
    edge_d = I("d_edge", [D, 512], F32)
    edgeg = I("d_edgeg", [4 * D, 512], F32)
    u3T_d = I("d_u3T", [D, TPC], BF16)
    xmT_d = I("d_xmT", [D, TPC], BF16)
    scr = I("d_scr", [2, 128, H * 512], BF16)

    with contextlib.ExitStack() as ctx:
        def sem(name):
            return ctx.enter_context(nc.semaphore(name))

        sch = Sched()
        semnames = ["g", "v", "s", "t", "y", "cc"]
        for p in ("a", "b", "c", "d", "e", "f", "h", "i"):
            semnames += [p + k for k in
                         ("x0", "x1", "o0", "o1", "k0", "k1", "w0", "w1",
                          "q0", "q1", "n0", "n1", "u0", "u1", "sq0", "sq1",
                          "kj0", "kj1", "vj0", "vj1", "ws0", "ws1",
                          "rs0", "rs1", "sq")]
        sems = {}

        def SEM(k):
            if k not in sems:
                sems[k] = sem("q" + k)
            return sems[k]

        class LazySems(dict):
            def __getitem__(self, k):
                return SEM(k)
        lazysems = LazySems()

        def phase():
            return [(k, v) for k, v in sch.cnt.items() if v > 0]

        def sb_factory(stack):
            def sb(name, shape, dtype):
                return stack.enter_context(nc.sbuf_tensor(name, shape, dtype))

            def ps(name, shape, dtype):
                return stack.enter_context(nc.psum_tensor(name, shape, dtype))
            return sb, ps

        cv = ctx.enter_context(nc.sbuf_tensor("cv", [128, NCONST], F32))
        onesbt = ctx.enter_context(nc.sbuf_tensor("onesbt", [128, 1], BF16))
        onesft = ctx.enter_context(nc.sbuf_tensor("onesft", [1, 128], F32))
        identb = ctx.enter_context(nc.sbuf_tensor("identb", [128, 128], BF16))
        identft = ctx.enter_context(nc.sbuf_tensor("identft", [128, 128], F32))
        for dst, srcap in ((cv, cvals), (onesbt, onesb), (onesft, onesf),
                           (identb, ident), (identft, identf)):
            sch.op("g", lambda g, dst=dst, srcap=srcap: g.dma_start(
                out=dst[:], in_=srcap[:]))

        _fused_u1(nc, sch, sb_factory, phase,
                  dict(x=x, lng=lng, lnb=lnb, wkv=wkv,
                       kvT_s=kvT_d, vtok_s=vtok_d, sqq_s=sqq_d),
                  cv, onesbt, identb)
        wcoll = phase()
        sch.op("g", lambda g: g.collective_compute(
            "AllGather", ALU.bypass, replica_groups=[[0, 1, 2, 3], [4, 5, 6, 7]],
            ins=[kvT_d.opt()], outs=[kvg.opt()]), wait=wcoll, inc=("cc", 1))
        sch.op("g", lambda g: g.collective_compute(
            "AllGather", ALU.bypass, replica_groups=[[0, 1, 2, 3], [4, 5, 6, 7]],
            ins=[vtok_d.opt()], outs=[vtg.opt()]), inc=("cc", 1))
        sch.op("g", lambda g: g.collective_compute(
            "AllGather", ALU.bypass, replica_groups=[[0, 1, 2, 3], [4, 5, 6, 7]],
            ins=[sqq_d.opt()], outs=[sqg.opt()]), inc=("cc", 1))
        ccm = sch.mark("cc")

        _fused_u2(nc, sch, sb_factory, phase, ccm,
                  dict(kqT=kvT_d, kvg=kvg, vtg=vtg, sqq_own=sqq_d, sqg=sqg,
                       wmix=wmix, wo0=wo0, x0T=x0T, x1T_s=x1T_d, scr=scr),
                  cv, onesbt, onesft, identb, identft)
        _fused_u3(nc, sch, sb_factory, phase,
                  dict(x1T=x1T_d, wrT=wrT, wiT=wiT, Gr=Gr, Gi=Gi, Gin=Gin,
                       Fr=Fr, Fin=Fin, biasc=biasc, x2T_s=x2T_d),
                  cv, onesbt, onesft)
        _fused_u4(nc, sch, sb_factory, phase,
                  dict(x2T=x2T_d, WqT=WqT, WkT=WkT, WvT=WvT, nwc=nwc,
                       bqc=bqc, bkc=bkc, bvr=bvr, qT_s=qT_d, kT_s=kT_d,
                       vtok1_s=vt1_d, xnT_s=xnT_d),
                  cv, onesbt, onesft)
        wcoll = phase()
        sch.op("g", lambda g: g.collective_compute(
            "AllGather", ALU.bypass, replica_groups=[[0, 1, 2, 3], [4, 5, 6, 7]],
            ins=[qT_d.opt()], outs=[qTg.opt()]), wait=wcoll, inc=("cc", 1))
        sch.op("g", lambda g: g.collective_compute(
            "AllGather", ALU.bypass, replica_groups=[[0, 1, 2, 3], [4, 5, 6, 7]],
            ins=[kT_d.opt()], outs=[kTg.opt()]), inc=("cc", 1))
        sch.op("g", lambda g: g.collective_compute(
            "AllGather", ALU.bypass, replica_groups=[[0, 1, 2, 3], [4, 5, 6, 7]],
            ins=[vt1_d.opt()], outs=[vt1g.opt()]), inc=("cc", 1))
        ccm2 = sch.mark("cc")

        _fused_u5(nc, sch, sb_factory, phase, ccm2,
                  dict(qTg=qTg, kTg=kTg, vt1g=vt1g, msk=msk, selh=selh,
                       oT_s=oT_d),
                  onesbt, onesft)
        wcoll = phase()
        sch.op("g", lambda g: g.collective_compute(
            "AllGather", ALU.bypass, replica_groups=[[0, 1, 2, 3], [4, 5, 6, 7]],
            ins=[oT_d.opt()], outs=[oTg.opt()]), wait=wcoll, inc=("cc", 1))
        ccm3 = sch.mark("cc")

        _fused_u6a(nc, sch, sb_factory, phase, ccm3,
                   dict(oTg=oTg, xnT=xnT_d, x2T=x2T_d, WoT=WoT, boc=boc,
                        selq=selq, x3T_s=x3T_d, edge=edge_d),
                   cv, onesbt, onesft)
        wcoll = phase()
        sch.op("g", lambda g: g.collective_compute(
            "AllGather", ALU.bypass, replica_groups=[[0, 1, 2, 3], [4, 5, 6, 7]],
            ins=[edge_d.opt()], outs=[edgeg.opt()]), wait=wcoll, inc=("cc", 1))
        ccm4 = sch.mark("cc")

        _fused_u6b1(nc, sch, sb_factory, phase, ccm4,
                    dict(x3T=x3T_d, edgeg=edgeg, selL=selL, selR=selR,
                         mk0=mk0, mk1=mk1, mk2=mk2, u3T_s=u3T_d, xmT_s=xmT_d),
                    cv, onesbt, onesft)
        _fused_u6b2(nc, sch, sb_factory, phase,
                    dict(xmT=xmT_d, u3T=u3T_d, x3T=x3T_d, W1T=W1T, W2T=W2T,
                         b1c=b1c, b2r=b2r, Wo1=Wo1, bo1c=bo1c, Wo2=Wo2,
                         bo2c=bo2c, gamc=gamc, betc=betc, selb=selbc,
                         x4_s=x4_s),
                    cv, onesbt, onesft, identft)

        sch.nop("y", phase())
        block = ctx.enter_context(nc.Block())
        sch.attach(block, lazysems)
    return nc


def kernel_fused(inputs):
    C = _consts()
    x0 = np.ascontiguousarray(inputs["L0_attn_x"], dtype=np.float32)
    Wx = np.asarray(inputs["L0_attn_W_x"], np.float32)
    wmix = np.zeros((128, 128), np.float32)
    for g in range(H):
        for h in range(H):
            wmix[h * 8:h * 8 + 8, g * 8:g * 8 + 8][np.arange(8), np.arange(8)] = Wx[g, h]
    b1 = np.asarray(inputs["L1_ffn_b1"], np.float32)
    b1cols = np.zeros((128, 18), np.float32)
    for p in range(9):
        for mc in range(2):
            b1cols[:, p * 2 + mc] = b1[p, mc * 128:(mc + 1) * 128]
    bo2c = np.zeros((128, 1), np.float32)
    bo2c[0:9, 0] = np.asarray(inputs["L1_ffn_bo2"], np.float32)
    mk = np.asarray(inputs["L1_ffn_mix_kernel"], np.float32)

    shared = {
        "lng": _rep(inputs["L0_attn_ln_g"]), "lnb": _rep(inputs["L0_attn_ln_b"]),
        "wkv": np.asarray(inputs["L0_attn_W_kv"], np.float32),
        "wmix": wmix, "wo0": np.asarray(inputs["L0_attn_W_out"], np.float32),
        "wrT": np.ascontiguousarray(np.asarray(inputs["L0_ffn_w_freq_real"]).T),
        "wiT": np.ascontiguousarray(np.asarray(inputs["L0_ffn_w_freq_imag"]).T),
        "Gr": C["Gr"], "Gi": C["Gi"], "Gin": C["Gin"],
        "Fr": C["Fr"], "Fin": C["Fin"],
        "biasc": _fcols(inputs["L0_ffn_bias"]),
        "WqT": np.ascontiguousarray(np.asarray(inputs["L1_attn_Wq"]).T),
        "WkT": np.ascontiguousarray(np.asarray(inputs["L1_attn_Wk"]).T),
        "WvT": np.ascontiguousarray(np.asarray(inputs["L1_attn_Wv"]).T),
        "nwc": _fcols(inputs["L1_attn_norm_w"]),
        "bqc": _fcols(inputs["L1_attn_bq"]), "bkc": _fcols(inputs["L1_attn_bk"]),
        "bvr": _rep(inputs["L1_attn_bv"]),
        "msk": C["msk"],
        "WoT": np.ascontiguousarray(np.asarray(inputs["L1_attn_Wo"]).T),
        "boc": _fcols(inputs["L1_attn_bo"]),
        "mk0c": _fcols(mk[0]), "mk1c": _fcols(mk[1]), "mk2c": _fcols(mk[2]),
        "W1T": np.ascontiguousarray(
            np.asarray(inputs["L1_ffn_W1"], np.float32).reshape(9 * D, 256)),
        "W2T": np.ascontiguousarray(
            np.asarray(inputs["L1_ffn_W2"], np.float32).reshape(9 * 256, D)),
        "b1c": b1cols, "b2r": np.asarray(inputs["L1_ffn_b2"], np.float32),
        "Wo1": np.asarray(inputs["L1_ffn_Wo1"], np.float32),
        "bo1c": np.ascontiguousarray(
            np.asarray(inputs["L1_ffn_bo1"], np.float32).reshape(2, 128).T),
        "Wo2": np.asarray(inputs["L1_ffn_Wo2"], np.float32),
        "bo2c": bo2c,
        "gamc": _fcols(inputs["L1_ffn_gamma"]), "betc": _fcols(inputs["L1_ffn_beta"]),
        "ident": C["ident"], "ident_f": C["ident_f"], "selb": C["selb"],
        "ones_bf": C["ones_bf"], "ones_f": C["ones_f"], "cvals": C["cvals"],
    }

    def onehot(i):
        v = np.zeros((128, 4), np.float32)
        if 0 <= i < 4:
            v[:, i] = 1.0
        return v

    maps = []
    for c in range(NCORE):
        b, qq = c // 4, c % 4
        sl = slice(qq * TPC, (qq + 1) * TPC)
        m = dict(shared)
        m["x_tok"] = np.ascontiguousarray(x0[b, sl])
        m["x0T_s"] = np.ascontiguousarray(x0[b, sl].T)
        m["selh"] = onehot(qq)
        m["selq"] = onehot(qq)
        m["selL"] = onehot(qq - 1)
        m["selR"] = onehot(qq + 1) if qq < 3 else onehot(-1)
        maps.append(m)

    nc = _get("fused", build_fused)
    res = run_bass_kernel_spmd(nc, maps, core_ids=list(range(NCORE)))
    out = np.empty((B, N, D), np.float32)
    for c in range(NCORE):
        b, qq = c // 4, c % 4
        out[b, qq * TPC:(qq + 1) * TPC] = res.results[c]["x4_s"]
    return out


# version 03
# speedup vs baseline: 1.0656x; 1.0656x over previous
import contextlib
import numpy as np
import ml_dtypes
import concourse.bass as bass
import concourse.mybir as mybir
from concourse.bass_utils import run_bass_kernel_spmd

F32 = mybir.dt.float32
BF16 = mybir.dt.bfloat16
AF = mybir.ActivationFunctionType
ALU = mybir.AluOpType
AX = mybir.AxisListType
BF = ml_dtypes.bfloat16

NCORE = 8
CONSTS = {"eps5": 0, "eps6": 1, "invd": 2}
NCONST = 3
B = 2
N = 8192
D = 1024
TPC = 2048
NT = TPC // 128
DC = D // 128
H = 16
HD = 64
INV_D = 1.0 / 1024.0


class Sched:
    ENGINES = ("g", "v", "s", "t", "y")

    def __init__(self):
        self.ops = []
        self.cnt = {}

    def op(self, eng, emit, wait=(), inc=None):
        if inc is None:
            by = 16 if eng in ("g", "y") else 1
            inc = (eng, by)
        sem_key, by = inc
        self.cnt[sem_key] = self.cnt.get(sem_key, 0) + by
        self.ops.append((eng, emit, tuple(wait), sem_key, by))
        return self.cnt[sem_key]

    def mark(self, key):
        return self.cnt.get(key, 0)

    def marks(self, *keys):
        return tuple((k, self.mark(k)) for k in keys if self.mark(k) > 0)

    def nop(self, eng, wait):
        self.ops.append((eng, lambda e: None, tuple(wait), eng, 0))

    def replay(self, eng_key, eng_obj, sems):
        waited = {}
        for eng, emit, waits, sem_key, by in self.ops:
            if eng != eng_key:
                continue
            for wk, wv in waits:
                if wv <= 0:
                    continue
                if waited.get(wk, -1) >= wv:
                    continue
                eng_obj.wait_ge(sems[wk], wv)
                waited[wk] = wv
            ins = emit(eng_obj)
            if ins is not None:
                ins.then_inc(sems[sem_key], by)

    def attach(self, block, sems):
        block.gpsimd(lambda e: self.replay("g", e, sems))
        block.vector(lambda e: self.replay("v", e, sems))
        block.scalar(lambda e: self.replay("s", e, sems))
        block.tensor(lambda e: self.replay("t", e, sems))
        block.sync(lambda e: self.replay("y", e, sems))


def _ln_stats(sch, xin, mu_c, rstd_c, sq, eps_col, invd_col, wait_in=()):
    F = 1.0 / xin.shape[-1]
    sch.op("v", lambda v: v.reduce_sum(mu_c, xin, axis=AX.X), wait=wait_in)
    sch.op("v", lambda v: v.tensor_scalar_mul(mu_c, mu_c, F),
           wait=[("v", sch.mark("v"))])
    sch.op("v", lambda v: v.tensor_scalar(sq, xin, mu_c, None, ALU.subtract),
           wait=[("v", sch.mark("v"))])
    sch.op("s", lambda s: s.activation(sq, sq, AF.Square),
           wait=[("v", sch.mark("v"))])
    sch.op("v", lambda v: v.reduce_sum(rstd_c, sq, axis=AX.X),
           wait=[("s", sch.mark("s"))])
    sch.op("s", lambda s: s.activation(rstd_c, rstd_c, AF.Sqrt,
                                       bias=eps_col, scale=invd_col),
           wait=[("v", sch.mark("v"))])
    sch.op("v", lambda v: v.reciprocal(rstd_c, rstd_c),
           wait=[("s", sch.mark("s"))])


def build_u1():
    nc = bass.Bass("TRN2", target_bir_lowering=False)
    x = nc.dram_tensor("x_tok", [TPC, D], F32, kind="ExternalInput").ap()
    lng = nc.dram_tensor("lng", [128, D], F32, kind="ExternalInput").ap()
    lnb = nc.dram_tensor("lnb", [128, D], F32, kind="ExternalInput").ap()
    wkv = nc.dram_tensor("wkv", [D, 2 * D], F32, kind="ExternalInput").ap()
    ident = nc.dram_tensor("ident", [128, 128], BF16, kind="ExternalInput").ap()
    onesb = nc.dram_tensor("ones_bf", [128, 1], BF16, kind="ExternalInput").ap()
    cvals = nc.dram_tensor("cvals", [128, NCONST], F32, kind="ExternalInput").ap()
    kvT_s = nc.dram_tensor("kvT_s", [2 * D, TPC], BF16, kind="ExternalOutput").ap()
    vtok_s = nc.dram_tensor("vtok_s", [TPC, D], BF16, kind="ExternalOutput").ap()
    sqq_s = nc.dram_tensor("sqq_s", [H, TPC], F32, kind="ExternalOutput").ap()

    with contextlib.ExitStack() as ctx:
        def sb(name, shape, dtype):
            return ctx.enter_context(nc.sbuf_tensor(name, shape, dtype))

        def psum(name, shape, dtype):
            return ctx.enter_context(nc.psum_tensor(name, shape, dtype))

        def sem(name):
            return ctx.enter_context(nc.semaphore(name))

        xt = sb("xt", [128, 2 * D], F32)
        lngt = sb("lngt", [128, D], F32)
        lnbt = sb("lnbt", [128, D], F32)
        identb = sb("identb", [128, 128], BF16)
        onesbt = sb("onesbt", [128, 1], BF16)
        cv = sb("cv", [128, NCONST], F32)
        mu = sb("mu", [128, NT], F32)
        rstd = sb("rstd", [128, NT], F32)
        sq = sb("sq", [128, D], F32)
        usc = sb("usc", [128, D], F32)
        hbuf = sb("hbuf", [128, NT * D], BF16)
        hT = sb("hT", [128, DC * TPC], BF16)
        wkvb = sb("wkvb", [128, DC * 2 * D], BF16)
        kvo = sb("kvo", [128, 2 * TPC], BF16)
        sqo = sb("sqo", [128, TPC], BF16)
        sqqsb = sb("sqqsb", [1, 2 * TPC], F32)
        vtb = sb("vtb", [128, 2 * D], BF16)
        ps_t = psum("ps_t", [128, 128], BF16)
        ps_kv = psum("ps_kv", [128, 512], F32)
        ps_v = psum("ps_v", [128, 512], F32)
        ps_sq = psum("ps_sq", [1, 512], F32)
        qG, qV, qS, qT, qY = sem("qG"), sem("qV"), sem("qS"), sem("qT"), sem("qY")
        block = ctx.enter_context(nc.Block())
        qX0, qX1 = sem("qX0"), sem("qX1")
        qK0, qK1 = sem("qK0"), sem("qK1")
        qW0, qW1 = sem("qW0"), sem("qW1")
        qQ0, qQ1 = sem("qQ0"), sem("qQ1")
        sems = {"g": qG, "v": qV, "s": qS, "t": qT, "y": qY,
                "x0": qX0, "x1": qX1, "k0": qK0, "k1": qK1,
                "w0": qW0, "w1": qW1, "sq0": qQ0, "sq1": qQ1}
        sch = Sched()

        sch.op("g", lambda g: g.dma_start(out=lngt[:], in_=lng[:]))
        sch.op("g", lambda g: g.dma_start(out=lnbt[:], in_=lnb[:]))
        sch.op("g", lambda g: g.dma_start(out=identb[:], in_=ident[:]))
        sch.op("g", lambda g: g.dma_start(out=onesbt[:], in_=onesb[:]))
        sch.op("g", lambda g: g.dma_start(out=cv[:], in_=cvals[:]))
        for c in range(DC):
            sch.op("g", lambda g, c=c: g.dma_start(
                out=wkvb[:, c * 2 * D:(c + 1) * 2 * D],
                in_=wkv[c * 128:(c + 1) * 128, :]))
        lall = sch.mark("g")

        def ldx(t, wait=()):
            s_ = t % 2
            sch.op("g", lambda g, t=t, s_=s_: g.dma_start(
                out=xt[:, s_ * D:(s_ + 1) * D],
                in_=x[t * 128:(t + 1) * 128, :]), wait=wait,
                inc=("x%d" % s_, 16))

        ldx(0)
        ldx(1)
        for t in range(NT):
            s_ = t % 2
            xin = xt[:, s_ * D:(s_ + 1) * D]
            _ln_stats(sch, xin, mu[:, t:t + 1], rstd[:, t:t + 1], sq[:],
                      cv[:, 0:1], cv[:, 2:3],
                      wait_in=[("x%d" % s_, 16 * (t // 2 + 1)), ("g", lall)])
            sch.op("v", lambda v, xin=xin, t=t: v.tensor_scalar(
                usc[:], xin, mu[:, t:t + 1], rstd[:, t:t + 1],
                ALU.subtract, ALU.mult), wait=[("v", sch.mark("v"))])
            if t + 2 < NT:
                ldx(t + 2, wait=[("v", sch.mark("v"))])
            sch.op("v", lambda v: v.tensor_mul(sq[:], usc[:], lngt[:]),
                   wait=[("v", sch.mark("v"))])
            sch.op("v", lambda v, t=t: v.tensor_add(
                hbuf[:, t * D:(t + 1) * D], sq[:], lnbt[:]),
                wait=[("v", sch.mark("v"))])

        for t in range(NT):
            for c in range(DC):
                sch.op("t", lambda tt, t=t, c=c: tt.transpose(
                    ps_t[:], hbuf[:, t * D + c * 128: t * D + (c + 1) * 128],
                    identb[:]),
                    wait=[("v", sch.mark("v")), ("g", lall)])
                sch.op("v", lambda v, t=t, c=c: v.tensor_copy(
                    hT[:, c * TPC + t * 128: c * TPC + (t + 1) * 128], ps_t[:]),
                    wait=[("t", sch.mark("t"))])

        for o in range(2 * DC):
            slot = o % 2
            w = [("v", sch.mark("v")), ("g", lall)]
            if o >= 2:
                w.append(("k%d" % slot, 16 * (o // 2)))
            if o >= 2 and (o - 2) < 8:
                w.append(("s", sch.mark("s")))
            for f in range(4):
                for c in range(DC):
                    sch.op("t", lambda tt, o=o, f=f, c=c: tt.matmul(
                        ps_kv[:],
                        wkvb[:, c * 2 * D + o * 128: c * 2 * D + (o + 1) * 128],
                        hT[:, c * TPC + f * 512: c * TPC + (f + 1) * 512],
                        start=(c == 0), stop=(c == DC - 1)),
                        wait=w if (f == 0 and c == 0) else ())
                sch.op("v", lambda v, slot=slot, f=f: v.tensor_copy(
                    kvo[:, slot * TPC + f * 512: slot * TPC + (f + 1) * 512],
                    ps_kv[:]), wait=[("t", sch.mark("t"))])
                if f < 3:
                    sch.nop("t", [("v", sch.mark("v"))])
            evm = sch.mark("v")
            sch.op("y", lambda y, o=o, slot=slot: y.dma_start(
                out=kvT_s[o * 128:(o + 1) * 128, :],
                in_=kvo[:, slot * TPC:(slot + 1) * TPC]), wait=[("v", evm)],
                inc=("k%d" % slot, 16))
            if o < 8:
                sch.op("s", lambda s, slot=slot: s.activation(
                    sqo[:], kvo[:, slot * TPC:(slot + 1) * TPC], AF.Square),
                    wait=[("v", evm), ("t", sch.mark("t"))])
                sqm = sch.mark("s")
                for hh in (2 * o, 2 * o + 1):
                    base = (hh % 2) * 64
                    hs = hh % 2
                    for f in range(4):
                        wv = [("s", sqm)]
                        if f == 0 and hh >= 2:
                            wv.append(("sq%d" % hs, 16 * (hh // 2)))
                        sch.op("t", lambda tt, base=base, f=f: tt.matmul(
                            ps_sq[:], onesbt[base:base + 64, 0:1],
                            sqo[base:base + 64, f * 512:(f + 1) * 512],
                            start=True, stop=True), wait=wv)
                        sch.op("v", lambda v, hs=hs, f=f: v.tensor_copy(
                            sqqsb[0:1, hs * TPC + f * 512: hs * TPC + (f + 1) * 512],
                            ps_sq[:]), wait=[("t", sch.mark("t"))])
                        sch.nop("t", [("v", sch.mark("v"))])
                    sch.op("y", lambda y, hh=hh, hs=hs: y.dma_start(
                        out=sqq_s[hh:hh + 1, :],
                        in_=sqqsb[0:1, hs * TPC:(hs + 1) * TPC]),
                        wait=[("v", sch.mark("v"))], inc=("sq%d" % hs, 16))
        for t in range(NT):
            slot = t % 2
            w = [("v", sch.mark("v"))]
            if t >= 2:
                w.append(("w%d" % slot, 16 * (t // 2)))
            for f in range(2):
                for c in range(DC):
                    sch.op("t", lambda tt, t=t, f=f, c=c: tt.matmul(
                        ps_v[:],
                        hT[:, c * TPC + t * 128: c * TPC + (t + 1) * 128],
                        wkvb[:, c * 2 * D + D + f * 512: c * 2 * D + D + (f + 1) * 512],
                        start=(c == 0), stop=(c == DC - 1)),
                        wait=w if (f == 0 and c == 0) else ())
                sch.op("v", lambda v, slot=slot, f=f: v.tensor_copy(
                    vtb[:, slot * D + f * 512: slot * D + (f + 1) * 512],
                    ps_v[:]), wait=[("t", sch.mark("t"))])
                if f == 0:
                    sch.nop("t", [("v", sch.mark("v"))])
            sch.op("y", lambda y, t=t, slot=slot: y.dma_start(
                out=vtok_s[t * 128:(t + 1) * 128, :],
                in_=vtb[:, slot * D:(slot + 1) * D]),
                wait=[("v", sch.mark("v"))], inc=("w%d" % slot, 16))
        sch.nop("y", [(k, sch.mark(k)) for k in
                      ("k0", "k1", "w0", "w1", "sq0", "sq1")])

        sch.attach(block, sems)
    return nc


def _fm_stats(sch, xb, c_off, Ftok, ps_sx, ps_sxx, ones1, sqsc,
              st_sx, st_sxx, wait_in=()):
    for f in range(Ftok // 512):
        for c in range(DC):
            sl = slice(c_off(c) + f * 512, c_off(c) + (f + 1) * 512)
            w = list(wait_in) + [("t", sch.mark("t"))]
            sch.op("s", lambda s, sl=sl: s.activation(
                sqsc[:, 0:512], xb[:, sl], AF.Square), wait=w)
            sch.op("t", lambda t, sl=sl, c=c: t.matmul(
                ps_sx[:], ones1[:, 0:1], xb[:, sl],
                start=(c == 0), stop=(c == DC - 1)),
                wait=[("s", sch.mark("s"))])
            sch.op("t", lambda t, c=c: t.matmul(
                ps_sxx[:], ones1[:, 0:1], sqsc[:, 0:512],
                start=(c == 0), stop=(c == DC - 1)))
        sch.op("v", lambda v, f=f: v.tensor_copy(
            st_sx[0:1, f * 512:(f + 1) * 512], ps_sx[:]),
            wait=[("t", sch.mark("t"))])
        sch.op("v", lambda v, f=f: v.tensor_copy(
            st_sxx[0:1, f * 512:(f + 1) * 512], ps_sxx[:]))
        sch.nop("t", [("v", sch.mark("v"))])


def _fm_rows(sch, Ftok, st_sx, st_sxx, murow, rsrow, varsc, eps_ap, invd):
    sch.op("v", lambda v: v.tensor_scalar_mul(
        murow[0:1, :], st_sx[0:1, :], invd), wait=[("v", sch.mark("v"))])
    sch.op("s", lambda s: s.activation(
        varsc[0:1, :], murow[0:1, :], AF.Square), wait=[("v", sch.mark("v"))])
    sch.op("v", lambda v: v.tensor_scalar(
        rsrow[0:1, :], st_sxx[0:1, :], invd, None, ALU.mult),
        wait=[("s", sch.mark("s"))])
    sch.op("v", lambda v: v.tensor_sub(
        rsrow[0:1, :], rsrow[0:1, :], varsc[0:1, :]),
        wait=[("v", sch.mark("v"))])
    sch.op("s", lambda s: s.activation(
        rsrow[0:1, :], rsrow[0:1, :], AF.Sqrt, bias=eps_ap),
        wait=[("v", sch.mark("v"))])
    sch.op("v", lambda v: v.reciprocal(rsrow[0:1, :], rsrow[0:1, :]),
        wait=[("s", sch.mark("s"))])


def _fm_bcast(sch, Ftok, row, out_bc, onesf1, ps_bc):
    for f in range(Ftok // 512):
        sch.op("t", lambda t, f=f: t.matmul(
            ps_bc[:], onesf1[0:1, :], row[0:1, f * 512:(f + 1) * 512],
            start=True, stop=True), wait=[("v", sch.mark("v"))])
        sch.op("v", lambda v, f=f: v.tensor_copy(
            out_bc[:, f * 512:(f + 1) * 512], ps_bc[:]),
            wait=[("t", sch.mark("t"))])
        sch.nop("t", [("v", sch.mark("v"))])


def build_u3():
    nc = bass.Bass("TRN2", target_bir_lowering=False)
    x1T = nc.dram_tensor("x1T_s", [D, TPC], F32, kind="ExternalInput").ap()
    wrT = nc.dram_tensor("wrT", [D, D], F32, kind="ExternalInput").ap()
    wiT = nc.dram_tensor("wiT", [D, D], F32, kind="ExternalInput").ap()
    Gr = nc.dram_tensor("Gr", [D, D], F32, kind="ExternalInput").ap()
    Gi = nc.dram_tensor("Gi", [D, D], F32, kind="ExternalInput").ap()
    Gin = nc.dram_tensor("Gin", [D, D], F32, kind="ExternalInput").ap()
    Fr = nc.dram_tensor("Fr", [D, D], F32, kind="ExternalInput").ap()
    Fin = nc.dram_tensor("Fin", [D, D], F32, kind="ExternalInput").ap()
    biasc = nc.dram_tensor("biasc", [128, DC], F32, kind="ExternalInput").ap()
    onesb = nc.dram_tensor("ones_bf", [128, 1], BF16, kind="ExternalInput").ap()
    onesf = nc.dram_tensor("ones_f", [1, 128], F32, kind="ExternalInput").ap()
    cvals = nc.dram_tensor("cvals", [128, NCONST], F32, kind="ExternalInput").ap()
    x2T_s = nc.dram_tensor("x2T_s", [D, TPC], F32, kind="ExternalOutput").ap()

    with contextlib.ExitStack() as ctx:
        def sb(name, shape, dtype):
            return ctx.enter_context(nc.sbuf_tensor(name, shape, dtype))

        def psum(name, shape, dtype):
            return ctx.enter_context(nc.psum_tensor(name, shape, dtype))

        def sem(name):
            return ctx.enter_context(nc.semaphore(name))

        Msb = sb("Msb", [128, DC * D], BF16)
        cv = sb("cv", [128, NCONST], F32)
        onesbt = sb("onesbt", [128, 1], BF16)
        onesft = sb("onesft", [1, 128], F32)
        bic = sb("bic", [128, DC], F32)
        sems = {k: sem("q" + k) for k in
                ("g", "v", "s", "t", "y", "o0", "o1", "x0", "x1")}
        block = None
        sch = Sched()

        with contextlib.ExitStack() as mctx:
            def msb(name, shape, dtype):
                return mctx.enter_context(nc.sbuf_tensor(name, shape, dtype))

            wrb = msb("wrb", [128, DC * D], BF16)
            wib = msb("wib", [128, DC * D], BF16)
            Grb = msb("Grb", [128, DC * D], BF16)
            Gib = msb("Gib", [128, DC * D], BF16)
            Ginb = msb("Ginb", [128, DC * D], BF16)
            Trb = msb("Trb", [128, DC * D], BF16)
            Tib = msb("Tib", [128, DC * D], BF16)
            ps_m = psum("ps_m", [128, 512], F32)

            for c in range(DC):
                for dst, srcap in ((wrb, wrT), (wib, wiT), (Grb, Gr),
                                   (Gib, Gi), (Ginb, Gin)):
                    sch.op("g", lambda g, dst=dst, srcap=srcap, c=c: g.dma_start(
                        out=dst[:, c * D:(c + 1) * D],
                        in_=srcap[c * 128:(c + 1) * 128, :]))
            sch.op("g", lambda g: g.dma_start(out=cv[:], in_=cvals[:]))
            sch.op("g", lambda g: g.dma_start(out=onesbt[:], in_=onesb[:]))
            sch.op("g", lambda g: g.dma_start(out=onesft[:], in_=onesf[:]))
            sch.op("g", lambda g: g.dma_start(out=bic[:], in_=biasc[:]))
            lA = sch.mark("g")

            def mm_pair(dst, lhs1, rhs1, lhs2, rhs2, wait0):
                for k in range(DC):
                    for f in range(D // 512):
                        for c in range(DC):
                            sch.op("t", lambda t, k=k, f=f, c=c, lhs1=lhs1, rhs1=rhs1: t.matmul(
                                ps_m[:],
                                lhs1[:, c * D + k * 128:c * D + (k + 1) * 128],
                                rhs1[:, c * D + f * 512:c * D + (f + 1) * 512],
                                start=(c == 0), stop=False),
                                wait=wait0 if (k == 0 and f == 0 and c == 0) else ())
                        for c in range(DC):
                            sch.op("t", lambda t, k=k, f=f, c=c, lhs2=lhs2, rhs2=rhs2: t.matmul(
                                ps_m[:],
                                lhs2[:, c * D + k * 128:c * D + (k + 1) * 128],
                                rhs2[:, c * D + f * 512:c * D + (f + 1) * 512],
                                start=False, stop=(c == DC - 1)))
                        sch.op("v", lambda v, dst=dst, k=k, f=f: v.tensor_copy(
                            dst[:, k * D + f * 512:k * D + (f + 1) * 512],
                            ps_m[:]), wait=[("t", sch.mark("t"))])
                        sch.nop("t", [("v", sch.mark("v"))])

            mm_pair(Trb, wrb, Grb, wib, Ginb, [("g", lA)])
            mm_pair(Tib, wrb, Gib, wib, Grb, [])
            wrF = wrb
            wiF = wib
            for c in range(DC):
                sch.op("g", lambda g, c=c: g.dma_start(
                    out=wrF[:, c * D:(c + 1) * D],
                    in_=Fr[c * 128:(c + 1) * 128, :]),
                    wait=[("v", sch.mark("v"))])
                sch.op("g", lambda g, c=c: g.dma_start(
                    out=wiF[:, c * D:(c + 1) * D],
                    in_=Fin[c * 128:(c + 1) * 128, :]))
            lF = sch.mark("g")
            mm_pair(Msb, wrF, Trb, wiF, Tib, [("g", lF)])
        xb = sb("xb", [128, DC * TPC], BF16)
        ub = sb("ub", [128, DC * TPC], BF16)
        yb = sb("yb", [128, DC * TPC], BF16)
        sqsc = sb("sqsc", [128, 512], BF16)
        strow = sb("strow", [1, 2 * TPC], F32)
        varsc = sb("varsc", [1, TPC], F32)
        mubc = sb("mubc", [128, TPC], F32)
        rsbc = sb("rsbc", [128, TPC], F32)
        tsc = sb("tsc", [128, TPC], F32)
        xo = sb("xo", [128, 2 * TPC], F32)
        xr = sb("xr", [128, 2 * TPC], F32)
        ps_sx = psum("ps_sx", [1, 512], F32)
        ps_sxx = psum("ps_sxx", [1, 512], F32)
        ps_bc = psum("ps_bc", [128, 512], F32)
        ps_y = psum("ps_y", [128, 512], F32)

        st_sx = strow[0:1, 0:TPC]
        st_sxx = strow[0:1, TPC:2 * TPC]
        murow = st_sx
        rsrow = st_sxx

        phA = [("v", sch.mark("v")), ("t", sch.mark("t"))]
        for c in range(DC):
            sch.op("g", lambda g, c=c: g.dma_start(
                out=xb[:, c * TPC:(c + 1) * TPC],
                in_=x1T[c * 128:(c + 1) * 128, :]), wait=phA)
        lB = sch.mark("g")
        _fm_stats(sch, xb, lambda c: c * TPC, TPC, ps_sx, ps_sxx, onesbt,
                  sqsc, st_sx, st_sxx, wait_in=[("g", lB)])
        _fm_rows(sch, TPC, st_sx, st_sxx, murow, rsrow, varsc,
                 cv[0:1, 0:1], INV_D)
        _fm_bcast(sch, TPC, murow, mubc, onesft, ps_bc)
        _fm_bcast(sch, TPC, rsrow, rsbc, onesft, ps_bc)
        for c in range(DC):
            sch.op("v", lambda v, c=c: v.tensor_sub(
                tsc[:], xb[:, c * TPC:(c + 1) * TPC], mubc[:]),
                wait=[("v", sch.mark("v"))])
            sch.op("v", lambda v, c=c: v.tensor_mul(
                ub[:, c * TPC:(c + 1) * TPC], tsc[:], rsbc[:]),
                wait=[("v", sch.mark("v"))])
        vu = sch.mark("v")
        for o in range(DC):
            for f in range(4):
                for c in range(DC):
                    sch.op("t", lambda t, o=o, f=f, c=c: t.matmul(
                        ps_y[:],
                        Msb[:, c * D + o * 128:c * D + (o + 1) * 128],
                        ub[:, c * TPC + f * 512:c * TPC + (f + 1) * 512],
                        start=(c == 0), stop=(c == DC - 1)),
                        wait=[("v", vu)] if (f == 0 and c == 0) else ())
                sch.op("v", lambda v, o=o, f=f: v.tensor_scalar(
                    yb[:, o * TPC + f * 512:o * TPC + (f + 1) * 512],
                    ps_y[:], bic[:, o:o + 1], None, ALU.add),
                    wait=[("t", sch.mark("t"))])
                sch.nop("t", [("v", sch.mark("v"))])
        _fm_stats(sch, yb, lambda c: c * TPC, TPC, ps_sx, ps_sxx, onesbt,
                  sqsc, st_sx, st_sxx, wait_in=[("v", sch.mark("v"))])
        _fm_rows(sch, TPC, st_sx, st_sxx, murow, rsrow, varsc,
                 cv[0:1, 0:1], INV_D)
        _fm_bcast(sch, TPC, murow, mubc, onesft, ps_bc)
        _fm_bcast(sch, TPC, rsrow, rsbc, onesft, ps_bc)
        xrm = [0] * DC
        for c in range(2):
            sch.op("g", lambda g, c=c: g.dma_start(
                out=xr[:, (c % 2) * TPC:((c % 2) + 1) * TPC],
                in_=x1T[c * 128:(c + 1) * 128, :]), inc=("x%d" % (c % 2), 16))
        for c in range(DC):
            slot = c % 2
            w = [("v", sch.mark("v")), ("x%d" % slot, 16 * (c // 2 + 1))]
            if c >= 2:
                w.append(("o%d" % slot, 16 * (c // 2)))
            sch.op("v", lambda v, c=c: v.tensor_sub(
                tsc[:], yb[:, c * TPC:(c + 1) * TPC], mubc[:]), wait=w)
            sch.op("v", lambda v: v.tensor_mul(tsc[:], tsc[:], rsbc[:]),
                   wait=[("v", sch.mark("v"))])
            sch.op("v", lambda v, c=c, slot=slot: v.tensor_add(
                xo[:, slot * TPC:(slot + 1) * TPC], tsc[:],
                xr[:, slot * TPC:(slot + 1) * TPC]),
                wait=[("v", sch.mark("v"))])
            vdone = sch.mark("v")
            if c + 2 < DC:
                sch.op("g", lambda g, c=c, slot=slot: g.dma_start(
                    out=xr[:, slot * TPC:(slot + 1) * TPC],
                    in_=x1T[(c + 2) * 128:(c + 3) * 128, :]),
                    wait=[("v", vdone)], inc=("x%d" % slot, 16))
            sch.op("y", lambda y, c=c, slot=slot: y.dma_start(
                out=x2T_s[c * 128:(c + 1) * 128, :],
                in_=xo[:, slot * TPC:(slot + 1) * TPC]),
                wait=[("v", vdone)], inc=("o%d" % slot, 16))
        sch.nop("y", [("o0", sch.mark("o0")), ("o1", sch.mark("o1"))])

        block = ctx.enter_context(nc.Block())
        sch.attach(block, sems)
    return nc






def build_u4():
    nc = bass.Bass("TRN2", target_bir_lowering=False)
    x2T = nc.dram_tensor("x2T_s", [D, TPC], F32, kind="ExternalInput").ap()
    WqT = nc.dram_tensor("WqT", [D, D], F32, kind="ExternalInput").ap()
    WkT = nc.dram_tensor("WkT", [D, D], F32, kind="ExternalInput").ap()
    WvT = nc.dram_tensor("WvT", [D, D], F32, kind="ExternalInput").ap()
    nwc = nc.dram_tensor("nwc", [128, DC], F32, kind="ExternalInput").ap()
    bqc = nc.dram_tensor("bqc", [128, DC], F32, kind="ExternalInput").ap()
    bkc = nc.dram_tensor("bkc", [128, DC], F32, kind="ExternalInput").ap()
    bvr = nc.dram_tensor("bvr", [128, D], F32, kind="ExternalInput").ap()
    onesb = nc.dram_tensor("ones_bf", [128, 1], BF16, kind="ExternalInput").ap()
    onesf = nc.dram_tensor("ones_f", [1, 128], F32, kind="ExternalInput").ap()
    cvals = nc.dram_tensor("cvals", [128, NCONST], F32, kind="ExternalInput").ap()
    qT_s = nc.dram_tensor("qT_s", [D, TPC], BF16, kind="ExternalOutput").ap()
    kT_s = nc.dram_tensor("kT_s", [D, TPC], BF16, kind="ExternalOutput").ap()
    vtok1_s = nc.dram_tensor("vtok1_s", [TPC, D], BF16, kind="ExternalOutput").ap()
    xnT_s = nc.dram_tensor("xnT_s", [D, TPC], BF16, kind="ExternalOutput").ap()

    with contextlib.ExitStack() as ctx:
        def sb(name, shape, dtype):
            return ctx.enter_context(nc.sbuf_tensor(name, shape, dtype))

        def psum(name, shape, dtype):
            return ctx.enter_context(nc.psum_tensor(name, shape, dtype))

        def sem(name):
            return ctx.enter_context(nc.semaphore(name))

        xb = sb("xb", [128, DC * TPC], BF16)
        xnb = sb("xnb", [128, DC * TPC], BF16)
        wqb = sb("wqb", [128, DC * D], BF16)
        wkb = sb("wkb", [128, DC * D], BF16)
        wvb = sb("wvb", [128, DC * D], BF16)
        nwt = sb("nwt", [128, DC], F32)
        bqt = sb("bqt", [128, DC], F32)
        bkt = sb("bkt", [128, DC], F32)
        bvt = sb("bvt", [128, D], F32)
        cv = sb("cv", [128, NCONST], F32)
        onesbt = sb("onesbt", [128, 1], BF16)
        onesft = sb("onesft", [1, 128], F32)
        sqsc = sb("sqsc", [128, 512], BF16)
        strow = sb("strow", [1, 2 * TPC], F32)
        varsc = sb("varsc", [1, TPC], F32)
        mubc = sb("mubc", [128, TPC], F32)
        rsbc = sb("rsbc", [128, TPC], F32)
        tsc = sb("tsc", [128, TPC], F32)
        qo = sb("qo", [128, 2 * TPC], BF16)
        ko = sb("ko", [128, 2 * TPC], BF16)
        vo = sb("vo", [128, 2 * D], BF16)
        ps_sx = psum("ps_sx", [1, 512], F32)
        ps_sxx = psum("ps_sxx", [1, 512], F32)
        ps_bc = psum("ps_bc", [128, 512], F32)
        ps_q = psum("ps_q", [128, 512], F32)
        ps_k = psum("ps_k", [128, 512], F32)
        ps_v = psum("ps_v", [128, 512], F32)
        sems = {k: sem("q" + k) for k in
                ("g", "v", "s", "t", "y", "q0", "q1", "k0", "k1",
                 "w0", "w1", "n0", "n1")}
        sch = Sched()

        for c in range(DC):
            sch.op("g", lambda g, c=c: g.dma_start(
                out=xb[:, c * TPC:(c + 1) * TPC],
                in_=x2T[c * 128:(c + 1) * 128, :]))
            for dst, srcap in ((wqb, WqT), (wkb, WkT), (wvb, WvT)):
                sch.op("g", lambda g, dst=dst, srcap=srcap, c=c: g.dma_start(
                    out=dst[:, c * D:(c + 1) * D],
                    in_=srcap[c * 128:(c + 1) * 128, :]))
        for dst, srcap in ((nwt, nwc), (bqt, bqc), (bkt, bkc), (bvt, bvr),
                           (cv, cvals), (onesbt, onesb), (onesft, onesf)):
            sch.op("g", lambda g, dst=dst, srcap=srcap: g.dma_start(
                out=dst[:], in_=srcap[:]))
        lall = sch.mark("g")

        st_sx = strow[0:1, 0:TPC]
        st_sxx = strow[0:1, TPC:2 * TPC]

        _fm_stats(sch, xb, lambda c: c * TPC, TPC, ps_sx, ps_sxx, onesbt,
                  sqsc, st_sx, st_sxx, wait_in=[("g", lall)])
        sch.op("v", lambda v: v.tensor_scalar_mul(
            st_sx[0:1, :], st_sx[0:1, :], INV_D), wait=[("v", sch.mark("v"))])
        sch.op("s", lambda s: s.activation(
            varsc[0:1, :], st_sx[0:1, :], AF.Square),
            wait=[("v", sch.mark("v"))])
        sch.op("v", lambda v: v.tensor_scalar(
            st_sxx[0:1, :], st_sxx[0:1, :], INV_D, None, ALU.mult),
            wait=[("s", sch.mark("s"))])
        sch.op("v", lambda v: v.tensor_sub(
            varsc[0:1, :], st_sxx[0:1, :], varsc[0:1, :]),
            wait=[("v", sch.mark("v"))])
        sch.op("s", lambda s: s.activation(
            st_sxx[0:1, :], varsc[0:1, :], AF.Sqrt, bias=cv[0:1, 0:1]),
            wait=[("v", sch.mark("v"))])
        sch.op("v", lambda v: v.reciprocal(st_sxx[0:1, :], st_sxx[0:1, :]),
               wait=[("s", sch.mark("s"))])
        sch.op("s", lambda s: s.activation(
            varsc[0:1, :], st_sxx[0:1, :], AF.Square),
            wait=[("v", sch.mark("v"))])
        sch.op("v", lambda v: v.tensor_scalar(
            varsc[0:1, :], varsc[0:1, :], -1e-5, 1.0, ALU.mult, ALU.add),
            wait=[("s", sch.mark("s"))])
        sch.op("s", lambda s: s.activation(
            varsc[0:1, :], varsc[0:1, :], AF.Sqrt, bias=cv[0:1, 1:2]),
            wait=[("v", sch.mark("v"))])
        sch.op("v", lambda v: v.reciprocal(varsc[0:1, :], varsc[0:1, :]),
               wait=[("s", sch.mark("s"))])
        sch.op("v", lambda v: v.tensor_mul(
            st_sxx[0:1, :], st_sxx[0:1, :], varsc[0:1, :]),
            wait=[("v", sch.mark("v"))])
        _fm_bcast(sch, TPC, st_sx, mubc, onesft, ps_bc)
        _fm_bcast(sch, TPC, st_sxx, rsbc, onesft, ps_bc)
        for c in range(DC):
            slot = c % 2
            sch.op("v", lambda v, c=c: v.tensor_sub(
                tsc[:], xb[:, c * TPC:(c + 1) * TPC], mubc[:]),
                wait=[("v", sch.mark("v"))])
            sch.op("v", lambda v, c=c: v.scalar_tensor_tensor(
                xnb[:, c * TPC:(c + 1) * TPC], tsc[:],
                nwt[:, c:c + 1], rsbc[:], ALU.mult, ALU.mult),
                wait=[("v", sch.mark("v"))])
            sch.op("y", lambda y, c=c: y.dma_start(
                out=xnT_s[c * 128:(c + 1) * 128, :],
                in_=xnb[:, c * TPC:(c + 1) * TPC]),
                wait=[("v", sch.mark("v"))], inc=("n%d" % slot, 16))
        vxn = sch.mark("v")

        for o in range(DC):
            slot = o % 2
            for kind, wb, ps, bt, outap, semk in (
                    ("q", wqb, ps_q, bqt, qT_s, "q"),
                    ("k", wkb, ps_k, bkt, kT_s, "k")):
                ob = qo if kind == "q" else ko
                w = [("v", sch.mark("v"))]
                if o >= 2:
                    w.append(("%s%d" % (semk, slot), 16 * (o // 2)))
                for f in range(4):
                    for c in range(DC):
                        sch.op("t", lambda t, wb=wb, ps=ps, o=o, f=f, c=c: t.matmul(
                            ps[:],
                            wb[:, c * D + o * 128:c * D + (o + 1) * 128],
                            xnb[:, c * TPC + f * 512:c * TPC + (f + 1) * 512],
                            start=(c == 0), stop=(c == DC - 1)),
                            wait=w if (f == 0 and c == 0) else ())
                    if kind == "q":
                        sch.op("v", lambda v, ob=ob, ps=ps, bt=bt, slot=slot, f=f, o=o: v.tensor_scalar(
                            ob[:, slot * TPC + f * 512:slot * TPC + (f + 1) * 512],
                            ps[:], bt[:, o:o + 1], 0.125, ALU.add, ALU.mult),
                            wait=[("t", sch.mark("t"))])
                    else:
                        sch.op("v", lambda v, ob=ob, ps=ps, bt=bt, slot=slot, f=f, o=o: v.tensor_scalar(
                            ob[:, slot * TPC + f * 512:slot * TPC + (f + 1) * 512],
                            ps[:], bt[:, o:o + 1], None, ALU.add),
                            wait=[("t", sch.mark("t"))])
                    if f < 3:
                        sch.nop("t", [("v", sch.mark("v"))])
                sch.op("y", lambda y, ob=ob, outap=outap, o=o, slot=slot: y.dma_start(
                    out=outap[o * 128:(o + 1) * 128, :],
                    in_=ob[:, slot * TPC:(slot + 1) * TPC]),
                    wait=[("v", sch.mark("v"))], inc=("%s%d" % (semk, slot), 16))
        for t in range(NT):
            slot = t % 2
            w = [("v", sch.mark("v"))]
            if t >= 2:
                w.append(("w%d" % slot, 16 * (t // 2)))
            for f in range(2):
                for c in range(DC):
                    sch.op("t", lambda tt, t=t, f=f, c=c: tt.matmul(
                        ps_v[:],
                        xnb[:, c * TPC + t * 128:c * TPC + (t + 1) * 128],
                        wvb[:, c * D + f * 512:c * D + (f + 1) * 512],
                        start=(c == 0), stop=(c == DC - 1)),
                        wait=w if (f == 0 and c == 0) else ())
                sch.op("v", lambda v, slot=slot, f=f: v.tensor_add(
                    vo[:, slot * D + f * 512:slot * D + (f + 1) * 512],
                    ps_v[:], bvt[:, f * 512:(f + 1) * 512]),
                    wait=[("t", sch.mark("t"))])
                if f == 0:
                    sch.nop("t", [("v", sch.mark("v"))])
            sch.op("y", lambda y, t=t, slot=slot: y.dma_start(
                out=vtok1_s[t * 128:(t + 1) * 128, :],
                in_=vo[:, slot * D:(slot + 1) * D]),
                wait=[("v", sch.mark("v"))], inc=("w%d" % slot, 16))
        sch.nop("y", [(k, sch.mark(k)) for k in
                      ("q0", "q1", "k0", "k1", "w0", "w1", "n0", "n1")])

        block = ctx.enter_context(nc.Block())
        sch.attach(block, sems)
    return nc


def build_u5():
    nc = bass.Bass("TRN2", target_bir_lowering=False)
    qTh = nc.dram_tensor("qTh", [4 * HD, N], BF16, kind="ExternalInput").ap()
    kTh = nc.dram_tensor("kTh", [4 * HD, N], BF16, kind="ExternalInput").ap()
    vth = nc.dram_tensor("vth", [N, 4 * HD], BF16, kind="ExternalInput").ap()
    msk = nc.dram_tensor("msk", [128, 4 * 512], F32, kind="ExternalInput").ap()
    onesb = nc.dram_tensor("ones_bf", [128, 1], BF16, kind="ExternalInput").ap()
    onesf = nc.dram_tensor("ones_f", [1, 128], F32, kind="ExternalInput").ap()
    oT_s = nc.dram_tensor("oT_s", [4 * HD, N], BF16, kind="ExternalOutput").ap()

    NQT = N // 512
    with contextlib.ExitStack() as ctx:
        def sb(name, shape, dtype):
            return ctx.enter_context(nc.sbuf_tensor(name, shape, dtype))

        def psum(name, shape, dtype):
            return ctx.enter_context(nc.psum_tensor(name, shape, dtype))

        def sem(name):
            return ctx.enter_context(nc.semaphore(name))

        qb = sb("qb", [128, 2 * N], BF16)
        kb = sb("kb", [128, 2 * N], BF16)
        vb = sb("vb", [128, (N // 128) * 256], BF16)
        ob = sb("ob", [128, 2 * N], BF16)
        mskb = sb("mskb", [128, 4 * 512], F32)
        onesbt = sb("onesbt", [128, 1], BF16)
        onesft = sb("onesft", [1, 128], F32)
        esc = sb("esc", [128, 2 * 512], BF16)
        zrow = sb("zrow", [1, 512], F32)
        zbsb = sb("zbsb", [64, 512], F32)
        ps_s = psum("ps_s", [128, 2 * 512], F32)
        ps_o = psum("ps_o", [64, 512], F32)
        ps_z = psum("ps_z", [1, 512], F32)
        ps_zb = psum("ps_zb", [64, 512], F32)
        sems = {k: sem("q" + k) for k in ("g", "v", "s", "t", "y")}
        sch = Sched()

        for ch in range(2):
            sch.op("g", lambda g, ch=ch: g.dma_start(
                out=qb[:, ch * N:(ch + 1) * N], in_=qTh[ch * 128:(ch + 1) * 128, :]))
            sch.op("g", lambda g, ch=ch: g.dma_start(
                out=kb[:, ch * N:(ch + 1) * N], in_=kTh[ch * 128:(ch + 1) * 128, :]))
        for jt in range(N // 128):
            sch.op("g", lambda g, jt=jt: g.dma_start(
                out=vb[:, jt * 256:(jt + 1) * 256],
                in_=vth[jt * 128:(jt + 1) * 128, :]))
        sch.op("g", lambda g: g.dma_start(out=mskb[:], in_=msk[:]))
        sch.op("g", lambda g: g.dma_start(out=onesbt[:], in_=onesb[:]))
        sch.op("g", lambda g: g.dma_start(out=onesft[:], in_=onesf[:]))
        lall = sch.mark("g")

        for hl in range(4):
            ch = hl // 2
            base = (hl % 2) * 64
            for qt in range(NQT):
                jbl = (qt + 1) * 4
                for jb in range(jbl):
                    sslot = jb % 2
                    pss = ps_s[:, sslot * 512:(sslot + 1) * 512]
                    esl = esc[:, sslot * 512:(sslot + 1) * 512]
                    w = [("g", lall), ("s", sch.mark("s"))]
                    sch.op("t", lambda t, ch=ch, base=base, qt=qt, jb=jb, pss=pss: t.matmul(
                        pss,
                        kb[base:base + 64, ch * N + jb * 128:ch * N + (jb + 1) * 128],
                        qb[base:base + 64, ch * N + qt * 512:ch * N + (qt + 1) * 512],
                        start=True, stop=True), wait=w)
                    p = jb - qt * 4
                    if p >= 0:
                        sch.op("v", lambda v, pss=pss, p=p: v.tensor_add(
                            pss, pss, mskb[:, p * 512:(p + 1) * 512]),
                            wait=[("t", sch.mark("t"))])
                        wexp = [("v", sch.mark("v")), ("t", sch.mark("t"))]
                    else:
                        wexp = [("t", sch.mark("t"))]
                    sch.op("s", lambda s, esl=esl, pss=pss: s.activation(
                        esl, pss, AF.Exp), wait=wexp)
                    sm = sch.mark("s")
                    sch.op("t", lambda t, jb=jb, hl=hl, esl=esl, jbl=jbl: t.matmul(
                        ps_o[:], vb[:, jb * 256 + hl * 64:jb * 256 + (hl + 1) * 64],
                        esl, start=(jb == 0), stop=(jb == jbl - 1)),
                        wait=[("s", sm)])
                    sch.op("t", lambda t, jb=jb, esl=esl, jbl=jbl: t.matmul(
                        ps_z[:], onesbt[:, 0:1], esl,
                        start=(jb == 0), stop=(jb == jbl - 1)))
                sch.op("v", lambda v: v.tensor_copy(zrow[0:1, :], ps_z[:]),
                       wait=[("t", sch.mark("t"))])
                sch.op("v", lambda v: v.reciprocal(zrow[0:1, :], zrow[0:1, :]),
                       wait=[("v", sch.mark("v"))])
                sch.op("t", lambda t: t.matmul(
                    ps_zb[:], onesft[0:1, 0:64], zrow[0:1, :],
                    start=True, stop=True), wait=[("v", sch.mark("v"))])
                sch.op("v", lambda v: v.tensor_copy(zbsb[:], ps_zb[:]),
                       wait=[("t", sch.mark("t"))])
                sch.op("v", lambda v, ch=ch, base=base, qt=qt: v.tensor_mul(
                    ob[base:base + 64, ch * N + qt * 512:ch * N + (qt + 1) * 512],
                    ps_o[:], zbsb[:]), wait=[("v", sch.mark("v"))])
                sch.nop("t", [("v", sch.mark("v"))])
        for ch in range(2):
            sch.op("y", lambda y, ch=ch: y.dma_start(
                out=oT_s[ch * 128:(ch + 1) * 128, :],
                in_=ob[:, ch * N:(ch + 1) * N]),
                wait=[("v", sch.mark("v"))])
        sch.nop("y", [("y", sch.mark("y"))])

        block = ctx.enter_context(nc.Block())
        sch.attach(block, sems)
    return nc


def build_u6a():
    nc = bass.Bass("TRN2", target_bir_lowering=False)
    oT = nc.dram_tensor("oT_sl", [D, TPC], BF16, kind="ExternalInput").ap()
    xnT = nc.dram_tensor("xnT_sl", [D, TPC], BF16, kind="ExternalInput").ap()
    x2T = nc.dram_tensor("x2T_sl", [D, TPC], F32, kind="ExternalInput").ap()
    WoT = nc.dram_tensor("WoT", [D, D], F32, kind="ExternalInput").ap()
    boc = nc.dram_tensor("boc", [128, DC], F32, kind="ExternalInput").ap()
    onesb = nc.dram_tensor("ones_bf", [128, 1], BF16, kind="ExternalInput").ap()
    onesf = nc.dram_tensor("ones_f", [1, 128], F32, kind="ExternalInput").ap()
    cvals = nc.dram_tensor("cvals", [128, NCONST], F32, kind="ExternalInput").ap()
    x3T_s = nc.dram_tensor("x3T_s", [D, TPC], F32, kind="ExternalOutput").ap()

    with contextlib.ExitStack() as ctx:
        def sb(name, shape, dtype):
            return ctx.enter_context(nc.sbuf_tensor(name, shape, dtype))

        def psum(name, shape, dtype):
            return ctx.enter_context(nc.psum_tensor(name, shape, dtype))

        def sem(name):
            return ctx.enter_context(nc.semaphore(name))

        ob = sb("ob", [128, DC * TPC], BF16)
        wob = sb("wob", [128, DC * D], BF16)
        blk = sb("blk", [128, DC * TPC], BF16)
        xns = sb("xns", [128, 2 * TPC], BF16)
        xrs = sb("xrs", [128, 2 * TPC], F32)
        xo = sb("xo", [128, 2 * TPC], F32)
        boct = sb("boct", [128, DC], F32)
        cv = sb("cv", [128, NCONST], F32)
        onesbt = sb("onesbt", [128, 1], BF16)
        onesft = sb("onesft", [1, 128], F32)
        sqsc = sb("sqsc", [128, 512], BF16)
        strow = sb("strow", [1, 2 * TPC], F32)
        varsc = sb("varsc", [1, TPC], F32)
        mubc = sb("mubc", [128, TPC], F32)
        rsbc = sb("rsbc", [128, TPC], F32)
        tsc = sb("tsc", [128, TPC], F32)
        ps_a = psum("ps_a", [128, 512], F32)
        ps_sx = psum("ps_sx", [1, 512], F32)
        ps_sxx = psum("ps_sxx", [1, 512], F32)
        ps_bc = psum("ps_bc", [128, 512], F32)
        sems = {k: sem("q" + k) for k in
                ("g", "v", "s", "t", "y", "n0", "n1", "x0", "x1", "o0", "o1")}
        sch = Sched()

        for c in range(DC):
            sch.op("g", lambda g, c=c: g.dma_start(
                out=ob[:, c * TPC:(c + 1) * TPC],
                in_=oT[c * 128:(c + 1) * 128, :]))
            sch.op("g", lambda g, c=c: g.dma_start(
                out=wob[:, c * D:(c + 1) * D],
                in_=WoT[c * 128:(c + 1) * 128, :]))
        for dst, srcap in ((boct, boc), (cv, cvals), (onesbt, onesb),
                           (onesft, onesf)):
            sch.op("g", lambda g, dst=dst, srcap=srcap: g.dma_start(
                out=dst[:], in_=srcap[:]))
        lall = sch.mark("g")
        for c in range(2):
            sch.op("g", lambda g, c=c: g.dma_start(
                out=xns[:, (c % 2) * TPC:((c % 2) + 1) * TPC],
                in_=xnT[c * 128:(c + 1) * 128, :]), inc=("n%d" % (c % 2), 16))
        for o in range(DC):
            slot = o % 2
            w = [("g", lall), ("n%d" % slot, 16 * (o // 2 + 1)),
                 ("v", sch.mark("v"))]
            for f in range(4):
                for c in range(DC):
                    sch.op("t", lambda t, o=o, f=f, c=c: t.matmul(
                        ps_a[:],
                        wob[:, c * D + o * 128:c * D + (o + 1) * 128],
                        ob[:, c * TPC + f * 512:c * TPC + (f + 1) * 512],
                        start=(c == 0), stop=(c == DC - 1)),
                        wait=w if (f == 0 and c == 0) else ())
                sch.op("v", lambda v, o=o, f=f, slot=slot: v.tensor_scalar(
                    tsc[:, f * 512:(f + 1) * 512], ps_a[:], boct[:, o:o + 1],
                    None, ALU.add), wait=[("t", sch.mark("t"))])
                sch.op("v", lambda v, o=o, f=f, slot=slot: v.tensor_add(
                    blk[:, o * TPC + f * 512:o * TPC + (f + 1) * 512],
                    tsc[:, f * 512:(f + 1) * 512],
                    xns[:, slot * TPC + f * 512:slot * TPC + (f + 1) * 512]),
                    wait=[("v", sch.mark("v"))])
                if f < 3:
                    sch.nop("t", [("v", sch.mark("v"))])
            vdone = sch.mark("v")
            if o + 2 < DC:
                sch.op("g", lambda g, o=o, slot=slot: g.dma_start(
                    out=xns[:, slot * TPC:(slot + 1) * TPC],
                    in_=xnT[(o + 2) * 128:(o + 3) * 128, :]),
                    wait=[("v", vdone)], inc=("n%d" % slot, 16))
        st_sx = strow[0:1, 0:TPC]
        st_sxx = strow[0:1, TPC:2 * TPC]
        _fm_stats(sch, blk, lambda c: c * TPC, TPC, ps_sx, ps_sxx, onesbt,
                  sqsc, st_sx, st_sxx, wait_in=[("v", sch.mark("v"))])
        _fm_rows(sch, TPC, st_sx, st_sxx, st_sx, st_sxx, varsc,
                 cv[0:1, 0:1], INV_D)
        _fm_bcast(sch, TPC, st_sx, mubc, onesft, ps_bc)
        _fm_bcast(sch, TPC, st_sxx, rsbc, onesft, ps_bc)
        for c in range(2):
            sch.op("g", lambda g, c=c: g.dma_start(
                out=xrs[:, (c % 2) * TPC:((c % 2) + 1) * TPC],
                in_=x2T[c * 128:(c + 1) * 128, :]), inc=("x%d" % (c % 2), 16))
        for c in range(DC):
            slot = c % 2
            w = [("v", sch.mark("v")), ("x%d" % slot, 16 * (c // 2 + 1))]
            if c >= 2:
                w.append(("o%d" % slot, 16 * (c // 2)))
            sch.op("v", lambda v, c=c: v.tensor_sub(
                tsc[:], blk[:, c * TPC:(c + 1) * TPC], mubc[:]), wait=w)
            sch.op("v", lambda v: v.tensor_mul(tsc[:], tsc[:], rsbc[:]),
                   wait=[("v", sch.mark("v"))])
            sch.op("v", lambda v, c=c, slot=slot: v.tensor_add(
                xo[:, slot * TPC:(slot + 1) * TPC], tsc[:],
                xrs[:, slot * TPC:(slot + 1) * TPC]),
                wait=[("v", sch.mark("v"))])
            vdone = sch.mark("v")
            if c + 2 < DC:
                sch.op("g", lambda g, c=c, slot=slot: g.dma_start(
                    out=xrs[:, slot * TPC:(slot + 1) * TPC],
                    in_=x2T[(c + 2) * 128:(c + 3) * 128, :]),
                    wait=[("v", vdone)], inc=("x%d" % slot, 16))
            sch.op("y", lambda y, c=c, slot=slot: y.dma_start(
                out=x3T_s[c * 128:(c + 1) * 128, :],
                in_=xo[:, slot * TPC:(slot + 1) * TPC]),
                wait=[("v", vdone)], inc=("o%d" % slot, 16))
        sch.nop("y", [("o0", sch.mark("o0")), ("o1", sch.mark("o1"))])

        block = ctx.enter_context(nc.Block())
        sch.attach(block, sems)
    return nc


W6 = 2560
PAD = 256


def build_u6b1():
    nc = bass.Bass("TRN2", target_bir_lowering=False)
    x3p = nc.dram_tensor("x3T_pad", [D, W6], F32, kind="ExternalInput").ap()
    mk0 = nc.dram_tensor("mk0c", [128, DC], F32, kind="ExternalInput").ap()
    mk1 = nc.dram_tensor("mk1c", [128, DC], F32, kind="ExternalInput").ap()
    mk2 = nc.dram_tensor("mk2c", [128, DC], F32, kind="ExternalInput").ap()
    onesb = nc.dram_tensor("ones_bf", [128, 1], BF16, kind="ExternalInput").ap()
    onesf = nc.dram_tensor("ones_f", [1, 128], F32, kind="ExternalInput").ap()
    cvals = nc.dram_tensor("cvals", [128, NCONST], F32, kind="ExternalInput").ap()
    u3T_s = nc.dram_tensor("u3T_s", [D, TPC], BF16, kind="ExternalOutput").ap()
    xmT_s = nc.dram_tensor("xmT_s", [D, TPC], BF16, kind="ExternalOutput").ap()

    with contextlib.ExitStack() as ctx:
        def sb(name, shape, dtype):
            return ctx.enter_context(nc.sbuf_tensor(name, shape, dtype))

        def psum(name, shape, dtype):
            return ctx.enter_context(nc.psum_tensor(name, shape, dtype))

        def sem(name):
            return ctx.enter_context(nc.semaphore(name))

        xb = sb("xb", [128, DC * W6], BF16)
        mks = [sb("mk%d" % i, [128, DC], F32) for i in range(3)]
        cv = sb("cv", [128, NCONST], F32)
        onesbt = sb("onesbt", [128, 1], BF16)
        onesft = sb("onesft", [1, 128], F32)
        sqsc = sb("sqsc", [128, 512], BF16)
        strow = sb("strow", [1, 2 * W6], F32)
        varsc = sb("varsc", [1, W6], F32)
        mubc = sb("mubc", [128, W6], F32)
        rsbc = sb("rsbc", [128, W6], F32)
        tsc = sb("tsc", [128, W6], F32)
        t2 = sb("t2", [128, TPC], F32)
        xmo = sb("xmo", [128, 2 * TPC], BF16)
        ps_sx = psum("ps_sx", [1, 512], F32)
        ps_sxx = psum("ps_sxx", [1, 512], F32)
        ps_bc = psum("ps_bc", [128, 512], F32)
        sems = {k: sem("q" + k) for k in
                ("g", "v", "s", "t", "y", "o0", "o1", "u0", "u1")}
        sch = Sched()

        for c in range(DC):
            sch.op("g", lambda g, c=c: g.dma_start(
                out=xb[:, c * W6:(c + 1) * W6],
                in_=x3p[c * 128:(c + 1) * 128, :]))
        for dst, srcap in ((mks[0], mk0), (mks[1], mk1), (mks[2], mk2),
                           (cv, cvals), (onesbt, onesb), (onesft, onesf)):
            sch.op("g", lambda g, dst=dst, srcap=srcap: g.dma_start(
                out=dst[:], in_=srcap[:]))
        lall = sch.mark("g")

        st_sx = strow[0:1, 0:W6]
        st_sxx = strow[0:1, W6:2 * W6]
        _fm_stats(sch, xb, lambda c: c * W6, W6, ps_sx, ps_sxx, onesbt,
                  sqsc, st_sx, st_sxx, wait_in=[("g", lall)])
        _fm_rows(sch, W6, st_sx, st_sxx, st_sx, st_sxx, varsc,
                 cv[0:1, 0:1], INV_D)
        _fm_bcast(sch, W6, st_sx, mubc, onesft, ps_bc)
        _fm_bcast(sch, W6, st_sxx, rsbc, onesft, ps_bc)
        for c in range(DC):
            slot = c % 2
            sch.op("v", lambda v, c=c: v.tensor_sub(
                tsc[:], xb[:, c * W6:(c + 1) * W6], mubc[:]),
                wait=[("v", sch.mark("v"))])
            sch.op("v", lambda v, c=c: v.tensor_mul(
                xb[:, c * W6:(c + 1) * W6], tsc[:], rsbc[:]),
                wait=[("v", sch.mark("v"))])
            vu = sch.mark("v")
            sch.op("y", lambda y, c=c: y.dma_start(
                out=u3T_s[c * 128:(c + 1) * 128, :],
                in_=xb[:, c * W6 + PAD:c * W6 + PAD + TPC]),
                wait=[("v", vu)], inc=("u%d" % slot, 16))
            own = lambda c, off: xb[:, c * W6 + PAD + off:c * W6 + PAD + TPC + off]
            sch.op("v", lambda v, c=c, own=own: v.tensor_scalar(
                t2[:], own(c, -1), mks[0][:, c:c + 1], None, ALU.mult),
                wait=[("v", vu)])
            sch.op("v", lambda v, c=c, own=own: v.scalar_tensor_tensor(
                t2[:], own(c, 0), mks[1][:, c:c + 1], t2[:],
                ALU.mult, ALU.add), wait=[("v", sch.mark("v"))])
            w = [("v", sch.mark("v"))]
            if c >= 2:
                w.append(("o%d" % slot, 16 * (c // 2)))
            sch.op("v", lambda v, c=c, slot=slot, own=own: v.scalar_tensor_tensor(
                xmo[:, slot * TPC:(slot + 1) * TPC], own(c, 1),
                mks[2][:, c:c + 1], t2[:], ALU.mult, ALU.add), wait=w)
            sch.op("y", lambda y, c=c, slot=slot: y.dma_start(
                out=xmT_s[c * 128:(c + 1) * 128, :],
                in_=xmo[:, slot * TPC:(slot + 1) * TPC]),
                wait=[("v", sch.mark("v"))], inc=("o%d" % slot, 16))
        sch.nop("y", [(k, sch.mark(k)) for k in ("o0", "o1", "u0", "u1")])

        block = ctx.enter_context(nc.Block())
        sch.attach(block, sems)
    return nc


def build_u6b2():
    nc = bass.Bass("TRN2", target_bir_lowering=False)
    xmT = nc.dram_tensor("xmT_sl", [D, TPC], BF16, kind="ExternalInput").ap()
    u3T = nc.dram_tensor("u3T_sl", [D, TPC], BF16, kind="ExternalInput").ap()
    x3T = nc.dram_tensor("x3T_sl", [D, TPC], F32, kind="ExternalInput").ap()
    W1T = nc.dram_tensor("W1T", [9 * D, 256], F32, kind="ExternalInput").ap()
    W2T = nc.dram_tensor("W2T", [9 * 256, D], F32, kind="ExternalInput").ap()
    b1c = nc.dram_tensor("b1c", [128, 18], F32, kind="ExternalInput").ap()
    b2r = nc.dram_tensor("b2r", [9, D], F32, kind="ExternalInput").ap()
    Wo1 = nc.dram_tensor("Wo1", [D, 256], F32, kind="ExternalInput").ap()
    bo1c = nc.dram_tensor("bo1c", [128, 2], F32, kind="ExternalInput").ap()
    Wo2 = nc.dram_tensor("Wo2", [256, 9], F32, kind="ExternalInput").ap()
    bo2c = nc.dram_tensor("bo2c", [128, 1], F32, kind="ExternalInput").ap()
    gamc = nc.dram_tensor("gamc", [128, DC], F32, kind="ExternalInput").ap()
    betc = nc.dram_tensor("betc", [128, DC], F32, kind="ExternalInput").ap()
    identf = nc.dram_tensor("ident_f", [128, 128], F32, kind="ExternalInput").ap()
    selbc = nc.dram_tensor("selb", [9, 9 * 128], BF16, kind="ExternalInput").ap()
    onesb = nc.dram_tensor("ones_bf", [128, 1], BF16, kind="ExternalInput").ap()
    onesf = nc.dram_tensor("ones_f", [1, 128], F32, kind="ExternalInput").ap()
    cvals = nc.dram_tensor("cvals", [128, NCONST], F32, kind="ExternalInput").ap()
    x4_s = nc.dram_tensor("x4_s", [TPC, D], F32, kind="ExternalOutput").ap()

    HF = TPC // 4
    NF = HF // 512
    with contextlib.ExitStack() as ctx:
        def sb(name, shape, dtype):
            return ctx.enter_context(nc.sbuf_tensor(name, shape, dtype))

        def psum(name, shape, dtype):
            return ctx.enter_context(nc.psum_tensor(name, shape, dtype))

        def sem(name):
            return ctx.enter_context(nc.semaphore(name))

        zb = sb("zb", [128, DC * TPC], BF16)
        cv = sb("cv", [128, NCONST], F32)
        onesbt = sb("onesbt", [128, 1], BF16)
        onesft = sb("onesft", [1, 128], F32)
        identft = sb("identft", [128, 128], F32)
        gamt = sb("gamt", [128, DC], F32)
        bett = sb("bett", [128, DC], F32)
        sems = {k: sem("q" + k) for k in
                ("g", "v", "s", "t", "y", "o0", "o1", "x0", "x1", "u0", "u1")}
        sch = Sched()
        for dst, srcap in ((cv, cvals), (onesbt, onesb), (onesft, onesf),
                           (identft, identf), (gamt, gamc), (bett, betc)):
            sch.op("g", lambda g, dst=dst, srcap=srcap: g.dma_start(
                out=dst[:], in_=srcap[:]))

        with contextlib.ExitStack() as mctx:
            def msb(name, shape, dtype):
                return mctx.enter_context(nc.sbuf_tensor(name, shape, dtype))

            def mpsum(name, shape, dtype):
                return mctx.enter_context(nc.psum_tensor(name, shape, dtype))

            xmb = msb("xmb", [128, DC * TPC], BF16)
            w1b = msb("w1b", [128, 9 * DC * 256], BF16)
            w2b = msb("w2b", [128, 18 * D], BF16)
            wo1b = msb("wo1b", [128, DC * 256], BF16)
            wo2b = msb("wo2b", [128, 2 * 9], BF16)
            b1t = msb("b1t", [128, 18], F32)
            b2b = msb("b2b", [9, D], BF16)
            bo1t = msb("bo1t", [128, 2], F32)
            bo2t = msb("bo2t", [128, 1], F32)
            tb = msb("tb", [128, 2 * TPC], BF16)
            s9 = msb("s9", [9, TPC], F32)
            stok = msb("stok", [128, NT * 9], F32)
            mxc = msb("mxc", [128, NT], F32)
            ohtok = msb("ohtok", [128, NT * 9], F32)
            ohT = msb("ohT", [9, TPC], F32)
            ohTb = msb("ohTb", [9, TPC], BF16)
            ohbc = msb("ohbc", [128, HF], BF16)
            selbt = msb("selbt", [9, 9 * 128], BF16)
            hb = msb("hb", [128, 2 * HF], BF16)
            hmall = msb("hmall", [128, 18 * HF], BF16)
            u3s = msb("u3s", [128, 2 * HF], BF16)
            ps_h = mpsum("ps_h", [128, 512], F32)
            ps_c = mpsum("ps_c", [128, 512], F32)
            ps_t9 = mpsum("ps_t9", [9, 512], F32)
            ps_tp = mpsum("ps_tp", [128, NT * 9], F32)
            ps_oh = mpsum("ps_oh", [9, 512], F32)
            ps_b = mpsum("ps_b", [128, 512], F32)

            for c in range(DC):
                sch.op("g", lambda g, c=c: g.dma_start(
                    out=xmb[:, c * TPC:(c + 1) * TPC],
                    in_=xmT[c * 128:(c + 1) * 128, :]))
                sch.op("g", lambda g, c=c: g.dma_start(
                    out=wo1b[:, c * 256:(c + 1) * 256],
                    in_=Wo1[c * 128:(c + 1) * 128, :]))
                for p in range(9):
                    sch.op("g", lambda g, p=p, c=c: g.dma_start(
                        out=w1b[:, (p * DC + c) * 256:(p * DC + c + 1) * 256],
                        in_=W1T[p * D + c * 128:p * D + (c + 1) * 128, :]))
            for p in range(9):
                for mc in range(2):
                    sch.op("g", lambda g, p=p, mc=mc: g.dma_start(
                        out=w2b[:, (p * 2 + mc) * D:(p * 2 + mc + 1) * D],
                        in_=W2T[p * 256 + mc * 128:p * 256 + (mc + 1) * 128, :]))
            for mc in range(2):
                sch.op("g", lambda g, mc=mc: g.dma_start(
                    out=wo2b[:, mc * 9:(mc + 1) * 9],
                    in_=Wo2[mc * 128:(mc + 1) * 128, :]))
            for dst, srcap in ((b1t, b1c), (b2b, b2r), (bo1t, bo1c),
                               (bo2t, bo2c), (selbt, selbc)):
                sch.op("g", lambda g, dst=dst, srcap=srcap: g.dma_start(
                    out=dst[:], in_=srcap[:]))
            lall = sch.mark("g")

            for mc in range(2):
                for f in range(4):
                    for c in range(DC):
                        sch.op("t", lambda t, mc=mc, f=f, c=c: t.matmul(
                            ps_h[:],
                            wo1b[:, c * 256 + mc * 128:c * 256 + (mc + 1) * 128],
                            xmb[:, c * TPC + f * 512:c * TPC + (f + 1) * 512],
                            start=(c == 0), stop=(c == DC - 1)),
                            wait=[("g", lall), ("s", sch.mark("s"))] if (f == 0 and c == 0) else ())
                    sch.op("s", lambda s, mc=mc, f=f: s.activation(
                        tb[:, mc * TPC + f * 512:mc * TPC + (f + 1) * 512],
                        ps_h[:], AF.Tanh, bias=bo1t[:, mc:mc + 1]),
                        wait=[("t", sch.mark("t"))])
                    sch.nop("t", [("s", sch.mark("s"))])
            for f in range(4):
                for mc in range(2):
                    sch.op("t", lambda t, mc=mc, f=f: t.matmul(
                        ps_t9[:],
                        wo2b[:, mc * 9:(mc + 1) * 9],
                        tb[:, mc * TPC + f * 512:mc * TPC + (f + 1) * 512],
                        start=(mc == 0), stop=(mc == 1)),
                        wait=[("s", sch.mark("s"))] if mc == 0 else ())
                sch.op("v", lambda v, f=f: v.tensor_scalar(
                    s9[0:9, f * 512:(f + 1) * 512], ps_t9[:],
                    bo2t[0:9, 0:1], None, ALU.add),
                    wait=[("t", sch.mark("t"))])
                sch.nop("t", [("v", sch.mark("v"))])
            for t_ in range(NT):
                sch.op("t", lambda tt, t_=t_: tt.transpose(
                    ps_tp[:, t_ * 9:(t_ + 1) * 9],
                    s9[0:9, t_ * 128:(t_ + 1) * 128], identft[0:9, 0:9]),
                    wait=[("v", sch.mark("v"))])
            sch.op("v", lambda v: v.tensor_copy(stok[:], ps_tp[:]),
                   wait=[("t", sch.mark("t"))])
            for t_ in range(NT):
                sch.op("v", lambda v, t_=t_: v.reduce_max(
                    mxc[:, t_:t_ + 1], stok[:, t_ * 9:(t_ + 1) * 9],
                    axis=AX.X), wait=[("v", sch.mark("v"))])
                sch.op("v", lambda v, t_=t_: v.tensor_scalar(
                    ohtok[:, t_ * 9:(t_ + 1) * 9], stok[:, t_ * 9:(t_ + 1) * 9],
                    mxc[:, t_:t_ + 1], None, ALU.is_ge),
                    wait=[("v", sch.mark("v"))])
            for t_ in range(NT):
                sch.op("t", lambda tt, t_=t_: tt.transpose(
                    ps_oh[0:9, (t_ % 4) * 128:((t_ % 4) + 1) * 128],
                    ohtok[:, t_ * 9:(t_ + 1) * 9], identft[:]),
                    wait=[("v", sch.mark("v"))])
                if t_ % 4 == 3:
                    sch.op("v", lambda v, t_=t_: v.tensor_copy(
                        ohT[0:9, (t_ - 3) * 128:(t_ + 1) * 128], ps_oh[0:9, :]),
                        wait=[("t", sch.mark("t"))])
                    sch.nop("t", [("v", sch.mark("v"))])
            sch.op("v", lambda v: v.tensor_copy(ohTb[0:9, :], ohT[0:9, :]),
                   wait=[("v", sch.mark("v"))])

            for hf in range(4):
                for p in range(9):
                    for f in range(NF):
                        sch.op("t", lambda t, p=p, hf=hf, f=f: t.matmul(
                            ps_b[:], selbt[0:9, p * 128:(p + 1) * 128],
                            ohTb[0:9, hf * HF + f * 512:hf * HF + (f + 1) * 512],
                            start=True, stop=True),
                            wait=[("v", sch.mark("v"))])
                        sch.op("v", lambda v, f=f: v.tensor_copy(
                            ohbc[:, f * 512:(f + 1) * 512], ps_b[:]),
                            wait=[("t", sch.mark("t"))])
                        sch.nop("t", [("v", sch.mark("v"))])
                    for mc in range(2):
                        for f in range(NF):
                            for c in range(DC):
                                sch.op("t", lambda t, p=p, mc=mc, hf=hf, f=f, c=c: t.matmul(
                                    ps_h[:],
                                    w1b[:, (p * DC + c) * 256 + mc * 128:(p * DC + c) * 256 + (mc + 1) * 128],
                                    xmb[:, c * TPC + hf * HF + f * 512:c * TPC + hf * HF + (f + 1) * 512],
                                    start=(c == 0), stop=(c == DC - 1)),
                                    wait=[("v", sch.mark("v"))] if (f == 0 and c == 0) else ())
                            sch.op("v", lambda v, p=p, mc=mc, f=f: v.tensor_scalar(
                                hb[:, mc * HF + f * 512:mc * HF + (f + 1) * 512],
                                ps_h[:], b1t[:, p * 2 + mc:p * 2 + mc + 1],
                                None, ALU.add), wait=[("t", sch.mark("t"))])
                            sch.op("v", lambda v, p=p, mc=mc, f=f: v.tensor_mul(
                                hmall[:, (p * 2 + mc) * HF + f * 512:(p * 2 + mc) * HF + (f + 1) * 512],
                                hb[:, mc * HF + f * 512:mc * HF + (f + 1) * 512],
                                ohbc[:, f * 512:(f + 1) * 512]),
                                wait=[("v", sch.mark("v"))])
                            sch.nop("t", [("v", sch.mark("v"))])
                for o in range(2):
                    sch.op("g", lambda g, o=o, hf=hf: g.dma_start(
                        out=u3s[:, (o % 2) * HF:((o % 2) + 1) * HF],
                        in_=u3T[o * 128:(o + 1) * 128, hf * HF:(hf + 1) * HF]),
                        wait=[("v", sch.mark("v"))], inc=("u%d" % (o % 2), 16))
                for o in range(DC):
                    slot = o % 2
                    w = [("v", sch.mark("v")),
                         ("u%d" % slot, 16 * (hf * 4 + o // 2 + 1))]
                    for f in range(NF):
                        sch.op("t", lambda t, o=o, f=f, hf=hf: t.matmul(
                            ps_c[:], b2b[0:9, o * 128:(o + 1) * 128],
                            ohTb[0:9, hf * HF + f * 512:hf * HF + (f + 1) * 512],
                            start=True, stop=False),
                            wait=w if f == 0 else ())
                        for pm in range(18):
                            sch.op("t", lambda t, pm=pm, o=o, f=f: t.matmul(
                                ps_c[:],
                                w2b[:, pm * D + o * 128:pm * D + (o + 1) * 128],
                                hmall[:, pm * HF + f * 512:pm * HF + (f + 1) * 512],
                                start=False, stop=(pm == 17)))
                        sch.op("v", lambda v, o=o, f=f, hf=hf, slot=slot: v.tensor_add(
                            zb[:, o * TPC + hf * HF + f * 512:o * TPC + hf * HF + (f + 1) * 512],
                            ps_c[:],
                            u3s[:, slot * HF + f * 512:slot * HF + (f + 1) * 512]),
                            wait=[("t", sch.mark("t"))])
                        sch.nop("t", [("v", sch.mark("v"))])
                    vdone = sch.mark("v")
                    if o + 2 < DC:
                        sch.op("g", lambda g, o=o, hf=hf, slot=slot: g.dma_start(
                            out=u3s[:, slot * HF:(slot + 1) * HF],
                            in_=u3T[(o + 2) * 128:(o + 3) * 128, hf * HF:(hf + 1) * HF]),
                            wait=[("v", vdone)], inc=("u%d" % slot, 16))
        fb = sb("fb", [128, DC * TPC], BF16)
        sqsc = sb("sqsc", [128, 512], BF16)
        strow = sb("strow", [1, 2 * TPC], F32)
        varsc = sb("varsc", [1, TPC], F32)
        mubc = sb("mubc", [128, TPC], F32)
        rsbc = sb("rsbc", [128, TPC], F32)
        tsc = sb("tsc", [128, TPC], F32)
        x3s = sb("x3s", [128, 2 * TPC], F32)
        x4c = sb("x4c", [128, TPC], F32)
        xto = sb("xto", [128, 2 * 128], F32)
        ps_sx = psum("ps_sx", [1, 512], F32)
        ps_sxx = psum("ps_sxx", [1, 512], F32)
        ps_bc = psum("ps_bc", [128, 512], F32)
        ps_f = psum("ps_f", [128, 128], F32)

        st_sx = strow[0:1, 0:TPC]
        st_sxx = strow[0:1, TPC:2 * TPC]
        phA = [("v", sch.mark("v")), ("t", sch.mark("t")), ("g", sch.mark("g"))]
        _fm_stats(sch, zb, lambda c: c * TPC, TPC, ps_sx, ps_sxx, onesbt,
                  sqsc, st_sx, st_sxx, wait_in=phA)
        _fm_rows(sch, TPC, st_sx, st_sxx, st_sx, st_sxx, varsc,
                 cv[0:1, 0:1], INV_D)
        _fm_bcast(sch, TPC, st_sx, mubc, onesft, ps_bc)
        _fm_bcast(sch, TPC, st_sxx, rsbc, onesft, ps_bc)
        for c in range(DC):
            sch.op("v", lambda v, c=c: v.tensor_sub(
                tsc[:], zb[:, c * TPC:(c + 1) * TPC], mubc[:]),
                wait=[("v", sch.mark("v"))])
            sch.op("v", lambda v, c=c: v.scalar_tensor_tensor(
                tsc[:], tsc[:], gamt[:, c:c + 1], rsbc[:],
                ALU.mult, ALU.mult), wait=[("v", sch.mark("v"))])
            sch.op("v", lambda v, c=c: v.tensor_scalar(
                fb[:, c * TPC:(c + 1) * TPC], tsc[:], bett[:, c:c + 1],
                None, ALU.add), wait=[("v", sch.mark("v"))])
        _fm_stats(sch, fb, lambda c: c * TPC, TPC, ps_sx, ps_sxx, onesbt,
                  sqsc, st_sx, st_sxx, wait_in=[("v", sch.mark("v"))])
        _fm_rows(sch, TPC, st_sx, st_sxx, st_sx, st_sxx, varsc,
                 cv[0:1, 0:1], INV_D)
        _fm_bcast(sch, TPC, st_sx, mubc, onesft, ps_bc)
        _fm_bcast(sch, TPC, st_sxx, rsbc, onesft, ps_bc)
        for c in range(2):
            sch.op("g", lambda g, c=c: g.dma_start(
                out=x3s[:, (c % 2) * TPC:((c % 2) + 1) * TPC],
                in_=x3T[c * 128:(c + 1) * 128, :]),
                wait=[("v", sch.mark("v")), ("t", sch.mark("t"))],
                inc=("x%d" % (c % 2), 16))
        nyo = 0
        for c in range(DC):
            slot = c % 2
            w = [("v", sch.mark("v")), ("x%d" % slot, 16 * (c // 2 + 1))]
            sch.op("v", lambda v, c=c: v.tensor_sub(
                tsc[:], fb[:, c * TPC:(c + 1) * TPC], mubc[:]), wait=w)
            sch.op("v", lambda v: v.tensor_mul(tsc[:], tsc[:], rsbc[:]),
                   wait=[("v", sch.mark("v"))])
            sch.op("v", lambda v, c=c, slot=slot: v.tensor_add(
                x4c[:], tsc[:], x3s[:, slot * TPC:(slot + 1) * TPC]),
                wait=[("v", sch.mark("v")), ("t", sch.mark("t"))])
            vdone = sch.mark("v")
            if c + 2 < DC:
                sch.op("g", lambda g, c=c, slot=slot: g.dma_start(
                    out=x3s[:, slot * TPC:(slot + 1) * TPC],
                    in_=x3T[(c + 2) * 128:(c + 3) * 128, :]),
                    wait=[("v", vdone)], inc=("x%d" % slot, 16))
            for t_ in range(NT):
                ts_ = t_ % 2
                wtp = [("v", vdone)]
                if nyo >= 2:
                    wtp.append(("o%d" % ts_, 16 * (nyo // 2)))
                sch.op("t", lambda tt, t_=t_: tt.transpose(
                    ps_f[:], x4c[:, t_ * 128:(t_ + 1) * 128], identft[:]),
                    wait=wtp)
                sch.op("v", lambda v, ts_=ts_: v.tensor_copy(
                    xto[:, ts_ * 128:(ts_ + 1) * 128], ps_f[:]),
                    wait=[("t", sch.mark("t"))])
                sch.op("y", lambda y, c=c, t_=t_, ts_=ts_: y.dma_start(
                    out=x4_s[t_ * 128:(t_ + 1) * 128, c * 128:(c + 1) * 128],
                    in_=xto[:, ts_ * 128:(ts_ + 1) * 128]),
                    wait=[("v", sch.mark("v"))], inc=("o%d" % ts_, 16))
                nyo += 1
                sch.nop("t", [("v", sch.mark("v"))])
        sch.nop("y", [("o0", sch.mark("o0")), ("o1", sch.mark("o1"))])

        block = ctx.enter_context(nc.Block())
        sch.attach(block, sems)
    return nc


def build_u2():
    nc = bass.Bass("TRN2", target_bir_lowering=False)
    kqT = nc.dram_tensor("kqT_own", [D, TPC], BF16, kind="ExternalInput").ap()
    kTf = nc.dram_tensor("kTf", [D, N], BF16, kind="ExternalInput").ap()
    vtf = nc.dram_tensor("vtf", [N, D], BF16, kind="ExternalInput").ap()
    sqqc = nc.dram_tensor("sqqc", [128, 256], F32, kind="ExternalInput").ap()
    sqqf = nc.dram_tensor("sqqf", [H, N], F32, kind="ExternalInput").ap()
    wmix = nc.dram_tensor("wmix", [128, 128], F32, kind="ExternalInput").ap()
    wo0 = nc.dram_tensor("wo0", [D, D], F32, kind="ExternalInput").ap()
    x0T = nc.dram_tensor("x0T_s", [D, TPC], F32, kind="ExternalInput").ap()
    ident = nc.dram_tensor("ident", [128, 128], BF16, kind="ExternalInput").ap()
    identf = nc.dram_tensor("ident_f", [128, 128], F32, kind="ExternalInput").ap()
    onesb = nc.dram_tensor("ones_bf", [128, 1], BF16, kind="ExternalInput").ap()
    onesf = nc.dram_tensor("ones_f", [1, 128], F32, kind="ExternalInput").ap()
    cvals = nc.dram_tensor("cvals", [128, NCONST], F32, kind="ExternalInput").ap()
    x1T_s = nc.dram_tensor("x1T_s", [D, TPC], F32, kind="ExternalOutput").ap()
    scr = nc.dram_tensor("scr", [2, 128, H * 512], BF16, kind="Internal").ap()

    JC = N // 512
    QT = NT
    with contextlib.ExitStack() as ctx:
        def sb(name, shape, dtype):
            return ctx.enter_context(nc.sbuf_tensor(name, shape, dtype))

        def psum(name, shape, dtype):
            return ctx.enter_context(nc.psum_tensor(name, shape, dtype))

        def sem(name):
            return ctx.enter_context(nc.semaphore(name))

        attb = sb("attb", [128, DC * TPC], BF16)
        cv = sb("cv", [128, NCONST], F32)
        onesbt = sb("onesbt", [128, 1], BF16)
        onesft = sb("onesft", [1, 128], F32)
        identb = sb("identb", [128, 128], BF16)
        identft = sb("identft", [128, 128], F32)
        sems = {k: sem("q" + k) for k in
                ("g", "v", "s", "t", "y",
                 "kj0", "kj1", "vj0", "vj1", "ws0", "ws1", "rs0", "rs1",
                 "x0", "x1", "o0", "o1", "sq")}
        sch = Sched()
        for dst, srcap in ((cv, cvals), (onesbt, onesb), (onesft, onesf),
                           (identb, ident), (identft, identf)):
            sch.op("g", lambda g, dst=dst, srcap=srcap: g.dma_start(
                out=dst[:], in_=srcap[:]))

        with contextlib.ExitStack() as mctx:
            def msb(name, shape, dtype):
                return mctx.enter_context(nc.sbuf_tensor(name, shape, dtype))

            def mpsum(name, shape, dtype):
                return mctx.enter_context(nc.psum_tensor(name, shape, dtype))

            kqb = msb("kqb", [128, DC * TPC], BF16)
            wmb = msb("wmb", [128, 128], BF16)
            wob = msb("wob", [128, DC * D], BF16)
            sqt = msb("sqt", [128, 256], F32)
            dcol = msb("dcol", [H, 1], F32)
            drow = msb("drow", [1, H], F32)
            dbc = msb("dbc", [128, H], F32)
            negc = msb("negc", [128, 256], F32)
            zp = msb("zp", [128, 4096], F32)
            ztot = msb("ztot", [128, 256], F32)
            p2b = msb("p2b", [128, 256], F32)
            o1p = msb("o1p", [128, DC * 128], F32)
            o1b = msb("o1b", [128, DC * 128], BF16)
            ps_s = mpsum("ps_s", [128, 2 * 512], F32)
            ps_m = mpsum("ps_m", [128, 512], F32)
            ps_tp = mpsum("ps_tp", [128, 1024], BF16)
            ps_av = mpsum("ps_av", [64, 16 * 128], F32)

            for c in range(DC):
                sch.op("g", lambda g, c=c: g.dma_start(
                    out=kqb[:, c * TPC:(c + 1) * TPC],
                    in_=kqT[c * 128:(c + 1) * 128, :]))
                sch.op("g", lambda g, c=c: g.dma_start(
                    out=wob[:, c * D:(c + 1) * D],
                    in_=wo0[c * 128:(c + 1) * 128, :]))
            sch.op("g", lambda g: g.dma_start(out=wmb[:], in_=wmix[:]))
            sch.op("g", lambda g: g.dma_start(out=sqt[:], in_=sqqc[:]))
            sqf_ctx = contextlib.ExitStack()
            sqfb = sqf_ctx.enter_context(nc.sbuf_tensor("sqfb", [H, N // 4], F32))
            dp4 = sqf_ctx.enter_context(nc.sbuf_tensor("dp4", [H, 4], F32))
            for part in range(4):
                sch.op("g", lambda g, part=part: g.dma_start(
                    out=sqfb[:], in_=sqqf[:, part * (N // 4):(part + 1) * (N // 4)]),
                    wait=[("v", sch.mark("v"))] if part > 0 else [],
                    inc=("sq", 16))
                sch.op("v", lambda v, part=part: v.reduce_max(
                    dp4[0:H, part:part + 1], sqfb[0:H, :], axis=AX.X),
                    wait=[("sq", 16 * (part + 1))])
            lall = sch.mark("g")

            sch.op("v", lambda v: v.reduce_max(dcol[0:H, 0:1], dp4[0:H, :],
                                               axis=AX.X), wait=[("v", sch.mark("v"))])
            sch.op("v", lambda v: v.tensor_scalar(
                dcol[0:H, 0:1], dcol[0:H, 0:1], 0.25, -80.0,
                ALU.mult, ALU.add), wait=[("v", sch.mark("v"))])
            sch.op("v", lambda v: v.tensor_scalar_max(
                dcol[0:H, 0:1], dcol[0:H, 0:1], 0.0),
                wait=[("v", sch.mark("v"))])
            sch.op("t", lambda t: t.transpose(
                ps_m[0:1, 0:H], dcol[0:H, 0:1], identft[0:H, 0:H]),
                wait=[("v", sch.mark("v")), ("g", lall)])
            sch.op("v", lambda v: v.tensor_copy(drow[0:1, :], ps_m[0:1, 0:H]),
                   wait=[("t", sch.mark("t"))])
            sch.op("t", lambda t: t.matmul(
                ps_m[:, 0:H], onesft[0:1, :], drow[0:1, :],
                start=True, stop=True), wait=[("v", sch.mark("v"))])
            sch.op("v", lambda v: v.tensor_copy(dbc[:], ps_m[:, 0:H]),
                   wait=[("t", sch.mark("t"))])
            for h in range(H):
                sch.op("v", lambda v, h=h: v.tensor_scalar(
                    negc[:, h * 16:(h + 1) * 16], sqt[:, h * 16:(h + 1) * 16],
                    dbc[:, h:h + 1], -1.0, ALU.add, ALU.mult),
                    wait=[("v", sch.mark("v"))])
            vnegc = sch.mark("v")
            sqf_ctx.close()
            kb = msb("kb", [128, 2 * DC * 512], BF16)
            vb = msb("vb", [128, 2 * 4 * D], BF16)
            Abuf = msb("Abuf", [128, H * 512], BF16)
            Ast = msb("Ast", [128, H * 512], BF16)
            Amx = msb("Amx", [128, H * 512], BF16)
            TTb = msb("TTb", [128, 4 * 2048], BF16)

            kcons = {}
            for jc in range(JC):
                slot = jc % 2
                w = [("g", lall), ("v", vnegc)]
                if jc >= 2:
                    w.append(("t", kcons[jc - 2]))
                for c in range(DC):
                    sch.op("s", lambda s, jc=jc, slot=slot, c=c: s.dma_start(
                        out=kb[:, slot * 4096 + c * 512:slot * 4096 + (c + 1) * 512],
                        in_=kTf[c * 128:(c + 1) * 128, jc * 512:(jc + 1) * 512]),
                        wait=w if c == 0 else (), inc=("kj%d" % slot, 16))
                kw = ("kj%d" % slot, 16 * DC * (jc // 2 + 1))
                for qt in range(QT):
                    for h in range(H):
                        sslot = (qt * H + h) % 2
                        psl = ps_s[:, sslot * 512:(sslot + 1) * 512]
                        ch, base = h // 2, (h % 2) * 64
                        sch.op("t", lambda t, psl=psl, ch=ch, base=base, qt=qt, slot=slot: t.matmul(
                            psl,
                            kqb[base:base + 64, ch * TPC + qt * 128:ch * TPC + (qt + 1) * 128],
                            kb[base:base + 64, slot * 4096 + ch * 512:slot * 4096 + (ch + 1) * 512],
                            start=True, stop=True),
                            wait=[kw, ("s", max(0, sch.mark("s") - 1)), ("v", vnegc)])
                        sch.op("s", lambda s, psl=psl, h=h, qt=qt, jc=jc: s.activation(
                            Abuf[:, h * 512:(h + 1) * 512], psl, AF.Exp,
                            bias=negc[:, h * 16 + qt:h * 16 + qt + 1],
                            accum_out=zp[:, (h * 16 + qt) * 16 + jc:(h * 16 + qt) * 16 + jc + 1]),
                            wait=[("t", sch.mark("t"))])
                kcons[jc] = sch.mark("t")
            for col in range(0, 256, 64):
                for cc in range(col, col + 64):
                    sch.op("v", lambda v, cc=cc: v.reduce_sum(
                        ztot[:, cc:cc + 1], zp[:, cc * 16:(cc + 1) * 16],
                        axis=AX.X), wait=[("s", sch.mark("s"))] if cc == 0 else ())
            sch.op("s", lambda s: s.activation(p2b[:], ztot[:], AF.Ln),
                   wait=[("v", sch.mark("v"))])
            sch.op("v", lambda v: v.tensor_sub(p2b[:], negc[:], p2b[:]),
                   wait=[("s", sch.mark("s"))])
            vp2 = sch.mark("v")

            TT4 = TTb.ap().rearrange("p (jsub oct g q8) -> p jsub g oct q8",
                                     jsub=4, oct=16, g=16, q8=8)
            nfill = JC
            for qt in range(QT):
                sch.op("v", lambda v: v.memzero(o1p[:]),
                       wait=[("v", sch.mark("v")), ("t", sch.mark("t"))])
                for jc in range(JC):
                    n = nfill + qt * JC + jc
                    slot = n % 2
                    bslot = (qt * JC + jc) % 2
                    w = [("t", kcons[n - 2])] if (n - 2) in kcons else []
                    for c in range(DC):
                        sch.op("s", lambda s, slot=slot, c=c, jc=jc: s.dma_start(
                            out=kb[:, slot * 4096 + c * 512:slot * 4096 + (c + 1) * 512],
                            in_=kTf[c * 128:(c + 1) * 128, jc * 512:(jc + 1) * 512]),
                            wait=w if c == 0 else (), inc=("kj%d" % slot, 16))
                    for jt in range(4):
                        sch.op("s", lambda s, slot=slot, jt=jt, jc=jc: s.dma_start(
                            out=vb[:, slot * 4096 + jt * D:slot * 4096 + (jt + 1) * D],
                            in_=vtf[jc * 512 + jt * 128:jc * 512 + (jt + 1) * 128, :]),
                            wait=w if jt == 0 else (), inc=("vj%d" % slot, 16))
                    kw = ("kj%d" % slot, 16 * DC * (n // 2 + 1))
                    vw = ("vj%d" % slot, 16 * 4 * ((qt * JC + jc) // 2 + 1))
                    wex0 = [("ws%d" % ((qt * JC + jc - 1) % 2),
                             16 * ((qt * JC + jc - 1) // 2 + 1))] \
                        if (qt * JC + jc) >= 1 else []
                    for h in range(H):
                        sslot = h % 2
                        psl = ps_s[:, sslot * 512:(sslot + 1) * 512]
                        ch, base = h // 2, (h % 2) * 64
                        sch.op("t", lambda t, psl=psl, ch=ch, base=base, qt=qt, slot=slot: t.matmul(
                            psl,
                            kqb[base:base + 64, ch * TPC + qt * 128:ch * TPC + (qt + 1) * 128],
                            kb[base:base + 64, slot * 4096 + ch * 512:slot * 4096 + (ch + 1) * 512],
                            start=True, stop=True),
                            wait=[kw, ("s", max(0, sch.mark("s") - 1)), ("v", vp2)])
                        sch.op("s", lambda s, psl=psl, h=h, qt=qt: s.activation(
                            Abuf[:, h * 512:(h + 1) * 512], psl, AF.Exp,
                            bias=p2b[:, h * 16 + qt:h * 16 + qt + 1]),
                            wait=[("t", sch.mark("t"))] + (wex0 if h == 0 else []))
                    nb = qt * JC + jc
                    wb_ = [("s", sch.mark("s"))]
                    if nb >= 2:
                        wb_.append(("rs%d" % bslot, 16 * 16 * (nb // 2)))
                    sch.op("g", lambda g, bslot=bslot: g.dma_start(
                        out=scr[bslot], in_=Abuf[:]), wait=wb_,
                        inc=("ws%d" % bslot, 16))
                    scr4 = scr[bslot].rearrange(
                        "(oct q8) (h j) -> h q8 oct j", oct=16, q8=8, h=H, j=512)
                    wr_ = [("ws%d" % bslot, 16 * (nb // 2 + 1)),
                           ("t", sch.mark("t"))]
                    for h in range(H):
                        sch.op("y", lambda y, h=h, scr4=scr4: y.dma_start(
                            out=Ast[h * 8:(h + 1) * 8, :].rearrange(
                                "p (oct j) -> p oct j", oct=16, j=512),
                            in_=scr4[h]), wait=wr_ if h == 0 else (),
                            inc=("rs%d" % bslot, 16))
                    rw = ("rs%d" % bslot, 16 * 16 * (nb // 2 + 1))
                    for oct in range(16):
                        sch.op("t", lambda t, oct=oct: t.matmul(
                            ps_m[:], wmb[:],
                            Ast[:, oct * 512:(oct + 1) * 512],
                            start=True, stop=True),
                            wait=[rw, ("v", sch.mark("v"))] if oct == 0 else ())
                        sch.op("v", lambda v, oct=oct: v.tensor_copy(
                            Amx[:, oct * 512:(oct + 1) * 512], ps_m[:]),
                            wait=[("t", sch.mark("t"))])
                        sch.nop("t", [("v", sch.mark("v"))])
                    for jsub in range(4):
                        for og in range(2):
                            for o8 in range(8):
                                oct = og * 8 + o8
                                sch.op("t", lambda t, oct=oct, jsub=jsub, o8=o8: t.transpose(
                                    ps_tp[:, o8 * 128:(o8 + 1) * 128],
                                    Amx[:, oct * 512 + jsub * 128:oct * 512 + (jsub + 1) * 128],
                                    identb[:]),
                                    wait=[("v", sch.mark("v"))] if (o8 == 0) else ())
                            sch.op("v", lambda v, jsub=jsub, og=og: v.tensor_copy(
                                TTb[:, jsub * 2048 + og * 1024:jsub * 2048 + (og + 1) * 1024],
                                ps_tp[:]), wait=[("t", sch.mark("t"))])
                            sch.nop("t", [("v", sch.mark("v"))])
                    for g_ in range(16):
                        for jsub in range(4):
                            wav = [vw] if (g_ == 0 and jsub == 0) else []
                            if g_ == 0 and jsub == 0:
                                wav.append(("v", sch.mark("v")))
                            sch.op("t", lambda t, g_=g_, jsub=jsub, slot=slot: t.matmul(
                                ps_av[0:64, g_ * 128:(g_ + 1) * 128],
                                vb[:, slot * 4096 + jsub * D + g_ * 64:slot * 4096 + jsub * D + (g_ + 1) * 64],
                                TT4[:, jsub, g_], start=(jsub == 0),
                                stop=(jsub == 3)), wait=wav)
                    tav = sch.mark("t")
                    for g_ in range(16):
                        sch.op("v", lambda v, g_=g_: v.tensor_add(
                            o1p[(g_ % 2) * 64:(g_ % 2) * 64 + 64,
                                (g_ // 2) * 128:(g_ // 2 + 1) * 128],
                            ps_av[0:64, g_ * 128:(g_ + 1) * 128],
                            o1p[(g_ % 2) * 64:(g_ % 2) * 64 + 64,
                                (g_ // 2) * 128:(g_ // 2 + 1) * 128]),
                            wait=[("t", tav)])
                    kcons[n] = sch.mark("t")
                    sch.nop("t", [("v", sch.mark("v"))])
                sch.op("v", lambda v: v.tensor_copy(o1b[:], o1p[:]),
                       wait=[("v", sch.mark("v"))])
                for od in range(DC):
                    for pc in range(DC):
                        sch.op("t", lambda t, od=od, pc=pc: t.matmul(
                            ps_s[:, 0:128],
                            wob[:, pc * D + od * 128:pc * D + (od + 1) * 128],
                            o1b[:, pc * 128:(pc + 1) * 128],
                            start=(pc == 0), stop=(pc == DC - 1)),
                            wait=[("v", sch.mark("v"))] if (od == 0 and pc == 0) else ())
                    sch.op("v", lambda v, od=od, qt=qt: v.tensor_copy(
                        attb[:, od * TPC + qt * 128:od * TPC + (qt + 1) * 128],
                        ps_s[:, 0:128]), wait=[("t", sch.mark("t"))])
                    sch.nop("t", [("v", sch.mark("v"))])
        sqsc = sb("sqsc", [128, 512], BF16)
        strow = sb("strow", [1, 2 * TPC], F32)
        varsc = sb("varsc", [1, TPC], F32)
        mubc = sb("mubc", [128, TPC], F32)
        rsbc = sb("rsbc", [128, TPC], F32)
        tsc = sb("tsc", [128, TPC], F32)
        xo = sb("xo", [128, 2 * TPC], F32)
        xr = sb("xr", [128, 2 * TPC], F32)
        ps_sx = psum("ps_sx", [1, 512], F32)
        ps_sxx = psum("ps_sxx", [1, 512], F32)
        ps_bc = psum("ps_bc", [128, 512], F32)

        st_sx = strow[0:1, 0:TPC]
        st_sxx = strow[0:1, TPC:2 * TPC]
        phA = [("v", sch.mark("v")), ("t", sch.mark("t")),
               ("g", sch.mark("g")), ("s", sch.mark("s"))]
        _fm_stats(sch, attb, lambda c: c * TPC, TPC, ps_sx, ps_sxx, onesbt,
                  sqsc, st_sx, st_sxx, wait_in=phA)
        _fm_rows(sch, TPC, st_sx, st_sxx, st_sx, st_sxx, varsc,
                 cv[0:1, 0:1], INV_D)
        _fm_bcast(sch, TPC, st_sx, mubc, onesft, ps_bc)
        _fm_bcast(sch, TPC, st_sxx, rsbc, onesft, ps_bc)
        for c in range(2):
            sch.op("g", lambda g, c=c: g.dma_start(
                out=xr[:, (c % 2) * TPC:((c % 2) + 1) * TPC],
                in_=x0T[c * 128:(c + 1) * 128, :]), wait=phA,
                inc=("x%d" % (c % 2), 16))
        for c in range(DC):
            slot = c % 2
            w = [("v", sch.mark("v")), ("x%d" % slot, 16 * (c // 2 + 1))]
            if c >= 2:
                w.append(("o%d" % slot, 16 * (c // 2)))
            sch.op("v", lambda v, c=c: v.tensor_sub(
                tsc[:], attb[:, c * TPC:(c + 1) * TPC], mubc[:]), wait=w)
            sch.op("v", lambda v: v.tensor_mul(tsc[:], tsc[:], rsbc[:]),
                   wait=[("v", sch.mark("v"))])
            sch.op("v", lambda v, c=c, slot=slot: v.tensor_add(
                xo[:, slot * TPC:(slot + 1) * TPC], tsc[:],
                xr[:, slot * TPC:(slot + 1) * TPC]),
                wait=[("v", sch.mark("v"))])
            vdone = sch.mark("v")
            if c + 2 < DC:
                sch.op("g", lambda g, c=c, slot=slot: g.dma_start(
                    out=xr[:, slot * TPC:(slot + 1) * TPC],
                    in_=x0T[(c + 2) * 128:(c + 3) * 128, :]),
                    wait=[("v", vdone)], inc=("x%d" % slot, 16))
            sch.op("y", lambda y, c=c, slot=slot: y.dma_start(
                out=x1T_s[c * 128:(c + 1) * 128, :],
                in_=xo[:, slot * TPC:(slot + 1) * TPC]),
                wait=[("v", vdone)], inc=("o%d" % slot, 16))
        sch.nop("y", [("o0", sch.mark("o0")), ("o1", sch.mark("o1"))])

        block = ctx.enter_context(nc.Block())
        sch.attach(block, sems)
    return nc


_BUILT = {}


def _get(name, builder):
    if name not in _BUILT:
        _BUILT[name] = builder()
    return _BUILT[name]


def _fcols(vec):
    return np.ascontiguousarray(np.asarray(vec, np.float32).reshape(DC, 128).T)


def _rep(vec, rows=128):
    return np.ascontiguousarray(np.broadcast_to(
        np.asarray(vec, np.float32), (rows, len(vec))))


def _consts():
    n = np.arange(D)
    ang = 2.0 * np.pi * np.outer(n, n) / D
    Fr = np.cos(ang).astype(np.float32)
    Fi = -np.sin(ang).astype(np.float32)
    Gr = (np.cos(ang) / D).astype(np.float32)
    Gi = (np.sin(ang) / D).astype(np.float32)
    msk = np.zeros((128, 4 * 512), np.float32)
    jj = np.arange(128)[:, None]
    qq = np.arange(512)[None, :]
    for p in range(4):
        msk[:, p * 512:(p + 1) * 512] = np.where(p * 128 + jj > qq, -1e30, 0.0)
    selb = np.ascontiguousarray(
        np.eye(9, dtype=np.float32)[:, :, None].repeat(128, 2).reshape(9, 9 * 128)
    ).astype(BF)
    return {
        "ident": np.eye(128, dtype=BF),
        "ident_f": np.eye(128, dtype=np.float32),
        "ones_bf": np.ones((128, 1), BF),
        "ones_f": np.ones((1, 128), np.float32),
        "cvals": np.ascontiguousarray(np.broadcast_to(
            np.array([1e-5, 1e-6, 1.0 / 1024.0], np.float32), (128, NCONST))),
        "Fr": Fr, "Fin": np.ascontiguousarray(-Fi),
        "Gr": Gr, "Gi": Gi, "Gin": np.ascontiguousarray(-Gi),
        "msk": msk, "selb": selb,
    }


def kernel(**inputs):
    return kernel_fused(inputs)


def kernel_multilaunch(**inputs):
    C = _consts()
    x0 = np.ascontiguousarray(inputs["L0_attn_x"], dtype=np.float32)

    def run(name, builder, maps, outs):
        nc = _get(name, builder)
        res = run_bass_kernel_spmd(nc, maps, core_ids=list(range(NCORE)))
        return [{k: r[k] for k in outs} for r in res.results]

    lng = _rep(inputs["L0_attn_ln_g"])
    lnb = _rep(inputs["L0_attn_ln_b"])
    maps = []
    for c in range(NCORE):
        b, qq = c // 4, c % 4
        maps.append({"x_tok": np.ascontiguousarray(x0[b, qq * TPC:(qq + 1) * TPC]),
                     "lng": lng, "lnb": lnb,
                     "wkv": inputs["L0_attn_W_kv"],
                     "ident": C["ident"], "ones_bf": C["ones_bf"],
                     "cvals": C["cvals"]})
    r1 = run("u1", build_u1, maps, ["kvT_s", "vtok_s", "sqq_s"])
    kvT = [np.concatenate([r1[b * 4 + q]["kvT_s"] for q in range(4)], axis=1)
           for b in range(B)]
    vtok = [np.concatenate([r1[b * 4 + q]["vtok_s"] for q in range(4)], axis=0)
            for b in range(B)]
    sqq = [np.concatenate([r1[b * 4 + q]["sqq_s"] for q in range(4)], axis=1)
           for b in range(B)]

    Wx = np.asarray(inputs["L0_attn_W_x"], np.float32)
    wmix = np.zeros((128, 128), np.float32)
    for g in range(H):
        for h in range(H):
            wmix[h * 8:h * 8 + 8, g * 8:g * 8 + 8][np.arange(8), np.arange(8)] = Wx[g, h]
    maps = []
    for c in range(NCORE):
        b, qq = c // 4, c % 4
        sl = slice(qq * TPC, (qq + 1) * TPC)
        sqqc = np.ascontiguousarray(
            sqq[b][:, sl].reshape(H, NT, 128).transpose(2, 0, 1).reshape(128, 256))
        maps.append({"kqT_own": np.ascontiguousarray(kvT[b][:D, sl]),
                     "kTf": np.ascontiguousarray(kvT[b][:D]),
                     "vtf": vtok[b],
                     "sqqc": sqqc, "sqqf": sqq[b],
                     "wmix": wmix, "wo0": inputs["L0_attn_W_out"],
                     "x0T_s": np.ascontiguousarray(x0[b, sl].T),
                     "ident": C["ident"], "ident_f": C["ident_f"],
                     "ones_bf": C["ones_bf"], "ones_f": C["ones_f"],
                     "cvals": C["cvals"]})
    r2 = run("u2", build_u2, maps, ["x1T_s"])

    biasc = _fcols(inputs["L0_ffn_bias"])
    wrT = np.ascontiguousarray(np.asarray(inputs["L0_ffn_w_freq_real"]).T)
    wiT = np.ascontiguousarray(np.asarray(inputs["L0_ffn_w_freq_imag"]).T)
    maps = []
    for c in range(NCORE):
        maps.append({"x1T_s": r2[c]["x1T_s"],
                     "wrT": wrT, "wiT": wiT,
                     "Gr": C["Gr"], "Gi": C["Gi"], "Gin": C["Gin"],
                     "Fr": C["Fr"], "Fin": C["Fin"],
                     "biasc": biasc, "ones_bf": C["ones_bf"],
                     "ones_f": C["ones_f"], "cvals": C["cvals"]})
    r3 = run("u3", build_u3, maps, ["x2T_s"])

    maps = []
    for c in range(NCORE):
        maps.append({"x2T_s": r3[c]["x2T_s"],
                     "WqT": np.ascontiguousarray(np.asarray(inputs["L1_attn_Wq"]).T),
                     "WkT": np.ascontiguousarray(np.asarray(inputs["L1_attn_Wk"]).T),
                     "WvT": np.ascontiguousarray(np.asarray(inputs["L1_attn_Wv"]).T),
                     "nwc": _fcols(inputs["L1_attn_norm_w"]),
                     "bqc": _fcols(inputs["L1_attn_bq"]),
                     "bkc": _fcols(inputs["L1_attn_bk"]),
                     "bvr": _rep(inputs["L1_attn_bv"]),
                     "ones_bf": C["ones_bf"], "ones_f": C["ones_f"],
                     "cvals": C["cvals"]})
    r4 = run("u4", build_u4, maps, ["qT_s", "kT_s", "vtok1_s", "xnT_s"])
    qT1 = [np.concatenate([r4[b * 4 + q]["qT_s"] for q in range(4)], axis=1)
           for b in range(B)]
    kT1 = [np.concatenate([r4[b * 4 + q]["kT_s"] for q in range(4)], axis=1)
           for b in range(B)]
    vtok1 = [np.concatenate([r4[b * 4 + q]["vtok1_s"] for q in range(4)], axis=0)
             for b in range(B)]
    xnT1 = [np.concatenate([r4[b * 4 + q]["xnT_s"] for q in range(4)], axis=1)
            for b in range(B)]

    maps = []
    for c in range(NCORE):
        b, hg = c // 4, c % 4
        hsl = slice(hg * 256, (hg + 1) * 256)
        maps.append({"qTh": np.ascontiguousarray(qT1[b][hsl]),
                     "kTh": np.ascontiguousarray(kT1[b][hsl]),
                     "vth": np.ascontiguousarray(vtok1[b][:, hsl]),
                     "msk": C["msk"], "ones_bf": C["ones_bf"],
                     "ones_f": C["ones_f"]})
    r5 = run("u5", build_u5, maps, ["oT_s"])
    oT = [np.concatenate([r5[b * 4 + g]["oT_s"] for g in range(4)], axis=0)
          for b in range(B)]

    maps = []
    for c in range(NCORE):
        b, qq = c // 4, c % 4
        sl = slice(qq * TPC, (qq + 1) * TPC)
        maps.append({"oT_sl": np.ascontiguousarray(oT[b][:, sl]),
                     "xnT_sl": np.ascontiguousarray(xnT1[b][:, sl]),
                     "x2T_sl": r3[c]["x2T_s"],
                     "WoT": np.ascontiguousarray(np.asarray(inputs["L1_attn_Wo"]).T),
                     "boc": _fcols(inputs["L1_attn_bo"]),
                     "ones_bf": C["ones_bf"], "ones_f": C["ones_f"],
                     "cvals": C["cvals"]})
    r6a = run("u6a", build_u6a, maps, ["x3T_s"])
    x3T = [np.concatenate([r6a[b * 4 + q]["x3T_s"] for q in range(4)], axis=1)
           for b in range(B)]

    maps = []
    for c in range(NCORE):
        b, qq = c // 4, c % 4
        t0 = qq * TPC
        pad = np.zeros((D, W6), np.float32)
        lo, hi = t0 - PAD, t0 + TPC + PAD
        glo, ghi = max(lo, 0), min(hi, N)
        pad[:, glo - lo:glo - lo + (ghi - glo)] = x3T[b][:, glo:ghi]
        mk = np.asarray(inputs["L1_ffn_mix_kernel"], np.float32)
        maps.append({"x3T_pad": pad,
                     "mk0c": _fcols(mk[0]), "mk1c": _fcols(mk[1]),
                     "mk2c": _fcols(mk[2]),
                     "ones_bf": C["ones_bf"], "ones_f": C["ones_f"],
                     "cvals": C["cvals"]})
    r6b1 = run("u6b1", build_u6b1, maps, ["u3T_s", "xmT_s"])

    b1 = np.asarray(inputs["L1_ffn_b1"], np.float32)
    b1cols = np.zeros((128, 18), np.float32)
    for p in range(9):
        for mc in range(2):
            b1cols[:, p * 2 + mc] = b1[p, mc * 128:(mc + 1) * 128]
    bo2c = np.zeros((128, 1), np.float32)
    bo2c[0:9, 0] = np.asarray(inputs["L1_ffn_bo2"], np.float32)
    maps = []
    for c in range(NCORE):
        b, qq = c // 4, c % 4
        sl = slice(qq * TPC, (qq + 1) * TPC)
        maps.append({"xmT_sl": r6b1[c]["xmT_s"],
                     "u3T_sl": r6b1[c]["u3T_s"],
                     "x3T_sl": np.ascontiguousarray(x3T[b][:, sl]),
                     "W1T": np.ascontiguousarray(
                         np.asarray(inputs["L1_ffn_W1"], np.float32).reshape(9 * D, 256)),
                     "W2T": np.ascontiguousarray(
                         np.asarray(inputs["L1_ffn_W2"], np.float32).reshape(9 * 256, D)),
                     "b1c": b1cols,
                     "b2r": np.asarray(inputs["L1_ffn_b2"], np.float32),
                     "Wo1": np.asarray(inputs["L1_ffn_Wo1"], np.float32),
                     "bo1c": np.ascontiguousarray(
                         np.asarray(inputs["L1_ffn_bo1"], np.float32).reshape(2, 128).T),
                     "bo2c": bo2c,
                     "Wo2": np.asarray(inputs["L1_ffn_Wo2"], np.float32),
                     "gamc": _fcols(inputs["L1_ffn_gamma"]),
                     "betc": _fcols(inputs["L1_ffn_beta"]),
                     "ident_f": C["ident_f"], "selb": C["selb"],
                     "ones_bf": C["ones_bf"], "ones_f": C["ones_f"],
                     "cvals": C["cvals"]})
    r6b2 = run("u6b2", build_u6b2, maps, ["x4_s"])

    out = np.empty((B, N, D), np.float32)
    for c in range(NCORE):
        b, qq = c // 4, c % 4
        out[b, qq * TPC:(qq + 1) * TPC] = r6b2[c]["x4_s"]
    return out




def _fused_u1(nc, sch, sb_factory, phase, io, cv, onesbt, identb):
    x = io["x"]; lng = io["lng"]; lnb = io["lnb"]; wkv = io["wkv"]
    kvT_s = io["kvT_s"]; vtok_s = io["vtok_s"]; sqq_s = io["sqq_s"]
    with contextlib.ExitStack() as stk:
        sb, psum = sb_factory(stk)
        for _e in ("g", "v", "s", "t", "y"):
            sch.nop(_e, phase())

        xt = sb("a_xt", [128, 2 * D], F32)
        lngt = sb("a_a_lngt", [128, D], F32)
        lnbt = sb("a_lnbt", [128, D], F32)
        mu = sb("a_mu", [128, NT], F32)
        rstd = sb("a_rstd", [128, NT], F32)
        sq = sb("a_sq", [128, D], F32)
        usc = sb("a_usc", [128, D], F32)
        hbuf = sb("a_hbuf", [128, NT * D], BF16)
        hT = sb("a_hT", [128, DC * TPC], BF16)
        wkvb = sb("a_wkvb", [128, DC * 2 * D], BF16)
        kvo = sb("a_kvo", [128, 2 * TPC], BF16)
        sqo = sb("a_sqo", [128, TPC], BF16)
        sqqsb = sb("a_sqqsb", [1, 2 * TPC], F32)
        vtb = sb("a_vtb", [128, 2 * D], BF16)
        ps_t = psum("a_ps_t", [128, 128], BF16)
        ps_kv = psum("a_ps_kv", [128, 512], F32)
        ps_v = psum("a_ps_v", [128, 512], F32)
        ps_sq = psum("a_ps_sq", [1, 512], F32)
        
        sch.op("g", lambda g: g.dma_start(out=lngt[:], in_=lng[:]))
        sch.op("g", lambda g: g.dma_start(out=lnbt[:], in_=lnb[:]))
        for c in range(DC):
            sch.op("g", lambda g, c=c: g.dma_start(
                out=wkvb[:, c * 2 * D:(c + 1) * 2 * D],
                in_=wkv[c * 128:(c + 1) * 128, :]))
        lall = sch.mark("g")

        def ldx(t, wait=()):
            s_ = t % 2
            sch.op("g", lambda g, t=t, s_=s_: g.dma_start(
                out=xt[:, s_ * D:(s_ + 1) * D],
                in_=x[t * 128:(t + 1) * 128, :]), wait=wait,
                inc=("ax%d" % s_, 16))

        ldx(0)
        ldx(1)
        for t in range(NT):
            s_ = t % 2
            xin = xt[:, s_ * D:(s_ + 1) * D]
            _ln_stats(sch, xin, mu[:, t:t + 1], rstd[:, t:t + 1], sq[:],
                      cv[:, 0:1], cv[:, 2:3],
                      wait_in=[("ax%d" % s_, 16 * (t // 2 + 1)), ("g", lall)])
            sch.op("v", lambda v, xin=xin, t=t: v.tensor_scalar(
                usc[:], xin, mu[:, t:t + 1], rstd[:, t:t + 1],
                ALU.subtract, ALU.mult), wait=[("v", sch.mark("v"))])
            if t + 2 < NT:
                ldx(t + 2, wait=[("v", sch.mark("v"))])
            sch.op("v", lambda v: v.tensor_mul(sq[:], usc[:], lngt[:]),
                   wait=[("v", sch.mark("v"))])
            sch.op("v", lambda v, t=t: v.tensor_add(
                hbuf[:, t * D:(t + 1) * D], sq[:], lnbt[:]),
                wait=[("v", sch.mark("v"))])

        for t in range(NT):
            for c in range(DC):
                sch.op("t", lambda tt, t=t, c=c: tt.transpose(
                    ps_t[:], hbuf[:, t * D + c * 128: t * D + (c + 1) * 128],
                    identb[:]),
                    wait=[("v", sch.mark("v")), ("g", lall)])
                sch.op("v", lambda v, t=t, c=c: v.tensor_copy(
                    hT[:, c * TPC + t * 128: c * TPC + (t + 1) * 128], ps_t[:]),
                    wait=[("t", sch.mark("t"))])

        for o in range(2 * DC):
            slot = o % 2
            w = [("v", sch.mark("v")), ("g", lall)]
            if o >= 2:
                w.append(("ak%d" % slot, 16 * (o // 2)))
            if o >= 2 and (o - 2) < 8:
                w.append(("s", sch.mark("s")))
            for f in range(4):
                for c in range(DC):
                    sch.op("t", lambda tt, o=o, f=f, c=c: tt.matmul(
                        ps_kv[:],
                        wkvb[:, c * 2 * D + o * 128: c * 2 * D + (o + 1) * 128],
                        hT[:, c * TPC + f * 512: c * TPC + (f + 1) * 512],
                        start=(c == 0), stop=(c == DC - 1)),
                        wait=w if (f == 0 and c == 0) else ())
                sch.op("v", lambda v, slot=slot, f=f: v.tensor_copy(
                    kvo[:, slot * TPC + f * 512: slot * TPC + (f + 1) * 512],
                    ps_kv[:]), wait=[("t", sch.mark("t"))])
                if f < 3:
                    sch.nop("t", [("v", sch.mark("v"))])
            evm = sch.mark("v")
            sch.op("y", lambda y, o=o, slot=slot: y.dma_start(
                out=kvT_s[o * 128:(o + 1) * 128, :],
                in_=kvo[:, slot * TPC:(slot + 1) * TPC]), wait=[("v", evm)],
                inc=("ak%d" % slot, 16))
            if o < 8:
                sch.op("s", lambda s, slot=slot: s.activation(
                    sqo[:], kvo[:, slot * TPC:(slot + 1) * TPC], AF.Square),
                    wait=[("v", evm), ("t", sch.mark("t"))])
                sqm = sch.mark("s")
                for hh in (2 * o, 2 * o + 1):
                    base = (hh % 2) * 64
                    hs = hh % 2
                    for f in range(4):
                        wv = [("s", sqm)]
                        if f == 0 and hh >= 2:
                            wv.append(("asq%d" % hs, 16 * (hh // 2)))
                        sch.op("t", lambda tt, base=base, f=f: tt.matmul(
                            ps_sq[:], onesbt[base:base + 64, 0:1],
                            sqo[base:base + 64, f * 512:(f + 1) * 512],
                            start=True, stop=True), wait=wv)
                        sch.op("v", lambda v, hs=hs, f=f: v.tensor_copy(
                            sqqsb[0:1, hs * TPC + f * 512: hs * TPC + (f + 1) * 512],
                            ps_sq[:]), wait=[("t", sch.mark("t"))])
                        sch.nop("t", [("v", sch.mark("v"))])
                    sch.op("y", lambda y, hh=hh, hs=hs: y.dma_start(
                        out=sqq_s[hh:hh + 1, :],
                        in_=sqqsb[0:1, hs * TPC:(hs + 1) * TPC]),
                        wait=[("v", sch.mark("v"))], inc=("asq%d" % hs, 16))
        for t in range(NT):
            slot = t % 2
            w = [("v", sch.mark("v"))]
            if t >= 2:
                w.append(("aw%d" % slot, 16 * (t // 2)))
            for f in range(2):
                for c in range(DC):
                    sch.op("t", lambda tt, t=t, f=f, c=c: tt.matmul(
                        ps_v[:],
                        hT[:, c * TPC + t * 128: c * TPC + (t + 1) * 128],
                        wkvb[:, c * 2 * D + D + f * 512: c * 2 * D + D + (f + 1) * 512],
                        start=(c == 0), stop=(c == DC - 1)),
                        wait=w if (f == 0 and c == 0) else ())
                sch.op("v", lambda v, slot=slot, f=f: v.tensor_copy(
                    vtb[:, slot * D + f * 512: slot * D + (f + 1) * 512],
                    ps_v[:]), wait=[("t", sch.mark("t"))])
                if f == 0:
                    sch.nop("t", [("v", sch.mark("v"))])
            sch.op("y", lambda y, t=t, slot=slot: y.dma_start(
                out=vtok_s[t * 128:(t + 1) * 128, :],
                in_=vtb[:, slot * D:(slot + 1) * D]),
                wait=[("v", sch.mark("v"))], inc=("aw%d" % slot, 16))
        sch.nop("y", [(k, sch.mark(k)) for k in
                      ("ak0", "ak1", "aw0", "aw1", "asq0", "asq1")])




def _fused_u2(nc, sch, sb_factory, phase, ccm, io, cv, onesbt, onesft,
              identb, identft):
    kqT = io["kqT"]; kvg = io["kvg"]; vtg = io["vtg"]
    sqq_own = io["sqq_own"]; sqg = io["sqg"]
    wmix = io["wmix"]; wo0 = io["wo0"]; x0T = io["x0T"]
    x1T_s = io["x1T_s"]; scr = io["scr"]
    JC = N // 512
    QT = NT
    attd = nc.dram_tensor("d_attd", [128, NT * DC * 128], BF16, kind="Internal").ap()
    with contextlib.ExitStack() as stk:
        sb, psum = sb_factory(stk)
        for _e in ("g", "v", "s", "t", "y"):
            sch.nop(_e, phase() + [("cc", ccm)])

        atts = sb("b_atts", [128, 2 * DC * 128], BF16)

        with contextlib.ExitStack() as mctx:
            def msb(name, shape, dtype):
                return mctx.enter_context(nc.sbuf_tensor(name, shape, dtype))

            def mpsum(name, shape, dtype):
                return mctx.enter_context(nc.psum_tensor(name, shape, dtype))

            kqb = msb("b_kqb", [128, DC * TPC], BF16)
            wmb = msb("b_wmb", [128, 128], BF16)
            wob = msb("b_wob", [128, DC * D], BF16)
            sqt = msb("b_sqt", [128, 256], F32)
            dcol = msb("b_dcol", [H, 1], F32)
            drow = msb("b_drow", [1, H], F32)
            dbc = msb("b_dbc", [128, H], F32)
            negc = msb("b_negc", [128, 256], F32)
            zp = msb("b_zp", [128, 4096], F32)
            ztot = msb("b_ztot", [128, 256], F32)
            p2b = msb("b_p2b", [128, 256], F32)
            o1p = msb("b_o1p", [128, DC * 128], F32)
            o1b = msb("b_o1b", [128, DC * 128], BF16)
            ps_s = mpsum("b_ps_s", [128, 2 * 512], F32)
            ps_m = mpsum("b_ps_m", [128, 512], F32)
            ps_tp = mpsum("b_ps_tp", [128, 1024], BF16)
            ps_av = mpsum("b_ps_av", [64, 16 * 128], F32)

            for c in range(DC):
                sch.op("g", lambda g, c=c: g.dma_start(
                    out=kqb[:, c * TPC:(c + 1) * TPC],
                    in_=kqT[c * 128:(c + 1) * 128, :]))
                sch.op("g", lambda g, c=c: g.dma_start(
                    out=wob[:, c * D:(c + 1) * D],
                    in_=wo0[c * 128:(c + 1) * 128, :]))
            sch.op("g", lambda g: g.dma_start(out=wmb[:], in_=wmix[:]))
            sqf_ctx = contextlib.ExitStack()
            sqown = sqf_ctx.enter_context(nc.sbuf_tensor("b_sqown", [H, TPC], F32))
            sch.op("g", lambda g: g.dma_start(out=sqown[:], in_=sqq_own[:]),
                   wait=[], inc=("bqsq", 16))
            sqt4 = sqt.ap().rearrange("p (h qt) -> p qt h", h=H, qt=NT)
            for qt_ in range(NT):
                sch.op("t", lambda t, qt_=qt_: t.transpose(
                    ps_m[:, 0:H], sqown[0:H, qt_ * 128:(qt_ + 1) * 128],
                    identft[0:H, 0:H]),
                    wait=[("bqsq", 16), ("v", sch.mark("v")), ("g", sch.mark("g"))])
                sch.op("v", lambda v, qt_=qt_: v.tensor_copy(
                    sqt4[:, qt_], ps_m[:, 0:H]), wait=[("t", sch.mark("t"))])
                sch.nop("t", [("v", sch.mark("v"))])
            sqfb = sqf_ctx.enter_context(nc.sbuf_tensor("sqfb", [H, N // 4], F32))
            dp4 = sqf_ctx.enter_context(nc.sbuf_tensor("dp4", [H, 4], F32))
            for part in range(4):
                sch.op("g", lambda g, part=part: g.dma_start(
                    out=sqfb[:], in_=sqg[part * H:(part + 1) * H, :]),
                    wait=[("v", sch.mark("v"))] if part > 0 else [],
                    inc=("bsq", 16))
                sch.op("v", lambda v, part=part: v.reduce_max(
                    dp4[0:H, part:part + 1], sqfb[0:H, :], axis=AX.X),
                    wait=[("bsq", 16 * (part + 1))])
            lall = sch.mark("g")

            sch.op("v", lambda v: v.reduce_max(dcol[0:H, 0:1], dp4[0:H, :],
                                               axis=AX.X), wait=[("v", sch.mark("v"))])
            sch.op("v", lambda v: v.tensor_scalar(
                dcol[0:H, 0:1], dcol[0:H, 0:1], 0.25, -80.0,
                ALU.mult, ALU.add), wait=[("v", sch.mark("v"))])
            sch.op("v", lambda v: v.tensor_scalar_max(
                dcol[0:H, 0:1], dcol[0:H, 0:1], 0.0),
                wait=[("v", sch.mark("v"))])
            sch.op("t", lambda t: t.transpose(
                ps_m[0:1, 0:H], dcol[0:H, 0:1], identft[0:H, 0:H]),
                wait=[("v", sch.mark("v")), ("g", lall)])
            sch.op("v", lambda v: v.tensor_copy(drow[0:1, :], ps_m[0:1, 0:H]),
                   wait=[("t", sch.mark("t"))])
            sch.op("t", lambda t: t.matmul(
                ps_m[:, 0:H], onesft[0:1, :], drow[0:1, :],
                start=True, stop=True), wait=[("v", sch.mark("v"))])
            sch.op("v", lambda v: v.tensor_copy(dbc[:], ps_m[:, 0:H]),
                   wait=[("t", sch.mark("t"))])
            for h in range(H):
                sch.op("v", lambda v, h=h: v.tensor_scalar(
                    negc[:, h * 16:(h + 1) * 16], sqt[:, h * 16:(h + 1) * 16],
                    dbc[:, h:h + 1], -1.0, ALU.add, ALU.mult),
                    wait=[("v", sch.mark("v"))])
            vnegc = sch.mark("v")
            sqf_ctx.close()
            kb = msb("b_kb", [128, 2 * DC * 512], BF16)
            vb = msb("b_vb", [128, 2 * 4 * D], BF16)
            Abuf = msb("b_Abuf", [128, 2 * H * 512], BF16)
            Ast = msb("b_Ast", [128, H * 512], BF16)
            Amx = msb("b_Amx", [128, H * 512], BF16)
            TTb = msb("b_TTb", [128, 4 * 2048], BF16)

            kcons = {}
            for jc in range(JC):
                slot = jc % 2
                w = [("g", lall), ("v", vnegc)]
                if jc >= 2:
                    w.append(("t", kcons[jc - 2]))
                for c in range(DC):
                    sch.op("s", lambda s, jc=jc, slot=slot, c=c: s.dma_start(
                        out=kb[:, slot * 4096 + c * 512:slot * 4096 + (c + 1) * 512],
                        in_=kvg[(jc // 4) * 2 * D + c * 128:(jc // 4) * 2 * D + (c + 1) * 128, (jc % 4) * 512:(jc % 4 + 1) * 512]),
                        wait=w if c == 0 else (), inc=("bkj%d" % slot, 16))
                kw = ("bkj%d" % slot, 16 * DC * (jc // 2 + 1))
                for qt in range(QT):
                    for h in range(H):
                        sslot = (qt * H + h) % 2
                        psl = ps_s[:, sslot * 512:(sslot + 1) * 512]
                        ch, base = h // 2, (h % 2) * 64
                        sch.op("t", lambda t, psl=psl, ch=ch, base=base, qt=qt, slot=slot: t.matmul(
                            psl,
                            kqb[base:base + 64, ch * TPC + qt * 128:ch * TPC + (qt + 1) * 128],
                            kb[base:base + 64, slot * 4096 + ch * 512:slot * 4096 + (ch + 1) * 512],
                            start=True, stop=True),
                            wait=[kw, ("s", max(0, sch.mark("s") - 1)), ("v", vnegc)])
                        sch.op("s", lambda s, psl=psl, h=h, qt=qt, jc=jc: s.activation(
                            Abuf[:, h * 512:(h + 1) * 512], psl, AF.Exp,
                            bias=negc[:, h * 16 + qt:h * 16 + qt + 1],
                            accum_out=zp[:, (h * 16 + qt) * 16 + jc:(h * 16 + qt) * 16 + jc + 1]),
                            wait=[("t", sch.mark("t"))])
                kcons[jc] = sch.mark("t")
            for col in range(0, 256, 64):
                for cc in range(col, col + 64):
                    sch.op("v", lambda v, cc=cc: v.reduce_sum(
                        ztot[:, cc:cc + 1], zp[:, cc * 16:(cc + 1) * 16],
                        axis=AX.X), wait=[("s", sch.mark("s"))] if cc == 0 else ())
            sch.op("s", lambda s: s.activation(p2b[:], ztot[:], AF.Ln),
                   wait=[("v", sch.mark("v"))])
            sch.op("v", lambda v: v.tensor_sub(p2b[:], negc[:], p2b[:]),
                   wait=[("s", sch.mark("s"))])
            vp2 = sch.mark("v")

            TT4 = TTb.ap().rearrange("p (jsub oct g q8) -> p jsub g oct q8",
                                     jsub=4, oct=16, g=16, q8=8)
            nfill = JC
            NCH = QT * JC
            vws = {}

            def rec_fill(k):
                qt, jc = divmod(k, JC)
                n = nfill + k
                slot = n % 2
                w = [("t", kcons[n - 2])] if (n - 2) in kcons else []
                for c in range(DC):
                    sch.op("s", lambda s, slot=slot, c=c, jc=jc: s.dma_start(
                        out=kb[:, slot * 4096 + c * 512:slot * 4096 + (c + 1) * 512],
                        in_=kvg[(jc // 4) * 2 * D + c * 128:(jc // 4) * 2 * D + (c + 1) * 128, (jc % 4) * 512:(jc % 4 + 1) * 512]),
                        wait=w if c == 0 else (), inc=("bkj%d" % slot, 16))
                for jt in range(4):
                    sch.op("s", lambda s, slot=slot, jt=jt, jc=jc: s.dma_start(
                        out=vb[:, slot * 4096 + jt * D:slot * 4096 + (jt + 1) * D],
                        in_=vtg[(jc // 4) * TPC + (jc % 4) * 512 + jt * 128:(jc // 4) * TPC + (jc % 4) * 512 + (jt + 1) * 128, :]),
                        wait=w if jt == 0 else (), inc=("bvj%d" % slot, 16))
                vws[k] = ("bvj%d" % slot, 16 * 4 * (k // 2 + 1))
                return ("bkj%d" % slot, 16 * DC * (n // 2 + 1))

            def rec_sexp(k, kw):
                qt, jc = divmod(k, JC)
                slot = (nfill + k) % 2
                ab = (k % 2) * H * 512
                wex0 = [("bws%d" % (k % 2), 16 * (k // 2))] if k >= 2 else []
                for h in range(H):
                    sslot = h % 2
                    psl = ps_s[:, sslot * 512:(sslot + 1) * 512]
                    ch, base = h // 2, (h % 2) * 64
                    sch.op("t", lambda t, psl=psl, ch=ch, base=base, qt=qt, slot=slot: t.matmul(
                        psl,
                        kqb[base:base + 64, ch * TPC + qt * 128:ch * TPC + (qt + 1) * 128],
                        kb[base:base + 64, slot * 4096 + ch * 512:slot * 4096 + (ch + 1) * 512],
                        start=True, stop=True),
                        wait=[kw, ("s", max(0, sch.mark("s") - 1)), ("v", vp2)])
                    sch.op("s", lambda s, psl=psl, h=h, qt=qt, ab=ab: s.activation(
                        Abuf[:, ab + h * 512:ab + (h + 1) * 512], psl, AF.Exp,
                        bias=p2b[:, h * 16 + qt:h * 16 + qt + 1]),
                        wait=[("t", sch.mark("t"))] + (wex0 if h == 0 else []))

            def rec_write(k):
                bslot = k % 2
                ab = (k % 2) * H * 512
                wb_ = [("s", sch.mark("s"))]
                if k >= 2:
                    wb_.append(("brs%d" % bslot, 16 * 16 * (k // 2)))
                sch.op("g", lambda g, bslot=bslot, ab=ab: g.dma_start(
                    out=scr[bslot], in_=Abuf[:, ab:ab + H * 512]), wait=wb_,
                    inc=("bws%d" % bslot, 16))

            def rec_consume(k):
                qt, jc = divmod(k, JC)
                n = nfill + k
                slot = n % 2
                bslot = k % 2
                vw = vws[k]
                scr4 = scr[bslot].rearrange(
                    "(oct q8) (h j) -> h q8 oct j", oct=16, q8=8, h=H, j=512)
                wr_ = [("bws%d" % bslot, 16 * (k // 2 + 1)),
                       ("t", sch.mark("t"))]
                for h in range(H):
                    sch.op("y", lambda y, h=h, scr4=scr4: y.dma_start(
                        out=Ast[h * 8:(h + 1) * 8, :].rearrange(
                            "p (oct j) -> p oct j", oct=16, j=512),
                        in_=scr4[h]), wait=wr_ if h == 0 else (),
                        inc=("brs%d" % bslot, 16))
                rw = ("brs%d" % bslot, 16 * 16 * (k // 2 + 1))
                for oct in range(16):
                    sch.op("t", lambda t, oct=oct: t.matmul(
                        ps_m[:], wmb[:],
                        Ast[:, oct * 512:(oct + 1) * 512],
                        start=True, stop=True),
                        wait=[rw, ("v", sch.mark("v"))] if oct == 0 else ())
                    sch.op("v", lambda v, oct=oct: v.tensor_copy(
                        Amx[:, oct * 512:(oct + 1) * 512], ps_m[:]),
                        wait=[("t", sch.mark("t"))])
                    sch.nop("t", [("v", sch.mark("v"))])
                for jsub in range(4):
                    for og in range(2):
                        for o8 in range(8):
                            oct = og * 8 + o8
                            sch.op("t", lambda t, oct=oct, jsub=jsub, o8=o8: t.transpose(
                                ps_tp[:, o8 * 128:(o8 + 1) * 128],
                                Amx[:, oct * 512 + jsub * 128:oct * 512 + (jsub + 1) * 128],
                                identb[:]),
                                wait=[("v", sch.mark("v"))] if (o8 == 0) else ())
                        sch.op("v", lambda v, jsub=jsub, og=og: v.tensor_copy(
                            TTb[:, jsub * 2048 + og * 1024:jsub * 2048 + (og + 1) * 1024],
                            ps_tp[:]), wait=[("t", sch.mark("t"))])
                        sch.nop("t", [("v", sch.mark("v"))])
                for g_ in range(16):
                    for jsub in range(4):
                        wav = []
                        if g_ == 0 and jsub == 0:
                            wav = [vw, ("v", sch.mark("v"))]
                        sch.op("t", lambda t, g_=g_, jsub=jsub, slot=slot: t.matmul(
                            ps_av[0:64, g_ * 128:(g_ + 1) * 128],
                            vb[:, slot * 4096 + jsub * D + g_ * 64:slot * 4096 + jsub * D + (g_ + 1) * 64],
                            TT4[:, jsub, g_], start=(jsub == 0),
                            stop=(jsub == 3)), wait=wav)
                tav = sch.mark("t")
                for g_ in range(16):
                    sch.op("v", lambda v, g_=g_: v.tensor_add(
                        o1p[(g_ % 2) * 64:(g_ % 2) * 64 + 64,
                            (g_ // 2) * 128:(g_ // 2 + 1) * 128],
                        ps_av[0:64, g_ * 128:(g_ + 1) * 128],
                        o1p[(g_ % 2) * 64:(g_ % 2) * 64 + 64,
                            (g_ // 2) * 128:(g_ // 2 + 1) * 128]),
                        wait=[("t", tav)])
                kcons[n] = sch.mark("t")
                sch.nop("t", [("v", sch.mark("v"))])

            def rec_wout(qt):
                aslot = qt % 2
                sch.op("v", lambda v: v.tensor_copy(o1b[:], o1p[:]),
                       wait=[("v", sch.mark("v"))])
                for od in range(DC):
                    w0 = [("v", sch.mark("v"))]
                    if od == 0 and qt >= 2:
                        w0.append(("bao%d" % aslot, 16 * (qt // 2)))
                    for pc in range(DC):
                        sch.op("t", lambda t, od=od, pc=pc: t.matmul(
                            ps_m[:, 0:128],
                            wob[:, pc * D + od * 128:pc * D + (od + 1) * 128],
                            o1b[:, pc * 128:(pc + 1) * 128],
                            start=(pc == 0), stop=(pc == DC - 1)),
                            wait=w0 if pc == 0 else ())
                    sch.op("v", lambda v, od=od, aslot=aslot: v.tensor_copy(
                        atts[:, aslot * DC * 128 + od * 128:aslot * DC * 128 + (od + 1) * 128],
                        ps_m[:, 0:128]), wait=[("t", sch.mark("t"))])
                    sch.nop("t", [("v", sch.mark("v"))])
                sch.op("y", lambda y, qt=qt, aslot=aslot: y.dma_start(
                    out=attd[:, qt * DC * 128:(qt + 1) * DC * 128],
                    in_=atts[:, aslot * DC * 128:(aslot + 1) * DC * 128]),
                    wait=[("v", sch.mark("v"))], inc=("bao%d" % aslot, 16))

            kw = rec_fill(0)
            rec_sexp(0, kw)
            rec_write(0)
            for k in range(NCH):
                qt, jc = divmod(k, JC)
                if jc == 0:
                    sch.op("v", lambda v: v.memzero(o1p[:]),
                           wait=[("v", sch.mark("v")), ("t", sch.mark("t"))])
                if k + 1 < NCH:
                    kw = rec_fill(k + 1)
                    rec_sexp(k + 1, kw)
                    rec_write(k + 1)
                rec_consume(k)
                if jc == JC - 1:
                    rec_wout(qt)
        attb = sb("b_attb", [128, DC * TPC], BF16)
        phS = [(kk, vv) for kk, vv in sch.cnt.items() if vv > 0]
        attd4 = attd.rearrange("p (qt c q) -> p c qt q", qt=NT, c=DC, q=128)
        for c in range(DC):
            sch.op("g", lambda g, c=c: g.dma_start(
                out=attb[:, c * TPC:(c + 1) * TPC].rearrange(
                    "p (qt q) -> p qt q", qt=NT, q=128),
                in_=attd4[:, c]), wait=phS if c == 0 else ())
        sqsc = sb("b_sqsc", [128, 512], BF16)
        strow = sb("b_strow", [1, 2 * TPC], F32)
        varsc = sb("b_varsc", [1, TPC], F32)
        mubc = sb("b_mubc", [128, TPC], F32)
        rsbc = sb("b_rsbc", [128, TPC], F32)
        tsc = sb("b_tsc", [128, TPC], F32)
        xo = sb("b_xo", [128, 2 * TPC], F32)
        xr = sb("b_xr", [128, 2 * TPC], F32)
        ps_sx = psum("b_ps_sx", [1, 512], F32)
        ps_sxx = psum("b_ps_sxx", [1, 512], F32)
        ps_bc = psum("b_ps_bc", [128, 512], F32)

        st_sx = strow[0:1, 0:TPC]
        st_sxx = strow[0:1, TPC:2 * TPC]
        phA = [("v", sch.mark("v")), ("t", sch.mark("t")),
               ("g", sch.mark("g")), ("s", sch.mark("s"))]
        _fm_stats(sch, attb, lambda c: c * TPC, TPC, ps_sx, ps_sxx, onesbt,
                  sqsc, st_sx, st_sxx, wait_in=phA)
        _fm_rows(sch, TPC, st_sx, st_sxx, st_sx, st_sxx, varsc,
                 cv[0:1, 0:1], INV_D)
        _fm_bcast(sch, TPC, st_sx, mubc, onesft, ps_bc)
        _fm_bcast(sch, TPC, st_sxx, rsbc, onesft, ps_bc)
        for c in range(2):
            sch.op("g", lambda g, c=c: g.dma_start(
                out=xr[:, (c % 2) * TPC:((c % 2) + 1) * TPC],
                in_=x0T[c * 128:(c + 1) * 128, :]), wait=phA,
                inc=("bx%d" % (c % 2), 16))
        for c in range(DC):
            slot = c % 2
            w = [("v", sch.mark("v")), ("bx%d" % slot, 16 * (c // 2 + 1))]
            if c >= 2:
                w.append(("bo%d" % slot, 16 * (c // 2)))
            sch.op("v", lambda v, c=c: v.tensor_sub(
                tsc[:], attb[:, c * TPC:(c + 1) * TPC], mubc[:]), wait=w)
            sch.op("v", lambda v: v.tensor_mul(tsc[:], tsc[:], rsbc[:]),
                   wait=[("v", sch.mark("v"))])
            sch.op("v", lambda v, c=c, slot=slot: v.tensor_add(
                xo[:, slot * TPC:(slot + 1) * TPC], tsc[:],
                xr[:, slot * TPC:(slot + 1) * TPC]),
                wait=[("v", sch.mark("v"))])
            vdone = sch.mark("v")
            if c + 2 < DC:
                sch.op("g", lambda g, c=c, slot=slot: g.dma_start(
                    out=xr[:, slot * TPC:(slot + 1) * TPC],
                    in_=x0T[(c + 2) * 128:(c + 3) * 128, :]),
                    wait=[("v", vdone)], inc=("bx%d" % slot, 16))
            sch.op("y", lambda y, c=c, slot=slot: y.dma_start(
                out=x1T_s[c * 128:(c + 1) * 128, :],
                in_=xo[:, slot * TPC:(slot + 1) * TPC]),
                wait=[("v", vdone)], inc=("bo%d" % slot, 16))
        sch.nop("y", [("bo0", sch.mark("bo0")), ("bo1", sch.mark("bo1"))])



def _fused_u3(nc, sch, sb_factory, phase, io, cv, onesbt, onesft):
    x1T = io["x1T"]; wrT = io["wrT"]; wiT = io["wiT"]
    Gr = io["Gr"]; Gi = io["Gi"]; Gin = io["Gin"]
    Fr = io["Fr"]; Fin = io["Fin"]; biasc = io["biasc"]
    x2T_s = io["x2T_s"]
    with contextlib.ExitStack() as stk:
        sb, psum = sb_factory(stk)
        for _e in ("g", "v", "s", "t", "y"):
            sch.nop(_e, phase())

        Msb = sb("c_Msb", [128, DC * D], BF16)
        bic = sb("c_bic", [128, DC], F32)

        with contextlib.ExitStack() as mctx:
            def msb(name, shape, dtype):
                return mctx.enter_context(nc.sbuf_tensor(name, shape, dtype))

            wrb = msb("c_wrb", [128, DC * D], BF16)
            wib = msb("c_wib", [128, DC * D], BF16)
            Grb = msb("c_Grb", [128, DC * D], BF16)
            Gib = msb("c_Gib", [128, DC * D], BF16)
            Ginb = msb("c_Ginb", [128, DC * D], BF16)
            Trb = msb("c_Trb", [128, DC * D], BF16)
            Tib = msb("c_Tib", [128, DC * D], BF16)
            ps_m = psum("c_ps_m", [128, 512], F32)

            for c in range(DC):
                for dst, srcap in ((wrb, wrT), (wib, wiT), (Grb, Gr),
                                   (Gib, Gi), (Ginb, Gin)):
                    sch.op("g", lambda g, dst=dst, srcap=srcap, c=c: g.dma_start(
                        out=dst[:, c * D:(c + 1) * D],
                        in_=srcap[c * 128:(c + 1) * 128, :]))
            sch.op("g", lambda g: g.dma_start(out=bic[:], in_=biasc[:]))
            lA = sch.mark("g")

            def mm_pair(dst, lhs1, rhs1, lhs2, rhs2, wait0):
                for k in range(DC):
                    for f in range(D // 512):
                        for c in range(DC):
                            sch.op("t", lambda t, k=k, f=f, c=c, lhs1=lhs1, rhs1=rhs1: t.matmul(
                                ps_m[:],
                                lhs1[:, c * D + k * 128:c * D + (k + 1) * 128],
                                rhs1[:, c * D + f * 512:c * D + (f + 1) * 512],
                                start=(c == 0), stop=False),
                                wait=wait0 if (k == 0 and f == 0 and c == 0) else ())
                        for c in range(DC):
                            sch.op("t", lambda t, k=k, f=f, c=c, lhs2=lhs2, rhs2=rhs2: t.matmul(
                                ps_m[:],
                                lhs2[:, c * D + k * 128:c * D + (k + 1) * 128],
                                rhs2[:, c * D + f * 512:c * D + (f + 1) * 512],
                                start=False, stop=(c == DC - 1)))
                        sch.op("v", lambda v, dst=dst, k=k, f=f: v.tensor_copy(
                            dst[:, k * D + f * 512:k * D + (f + 1) * 512],
                            ps_m[:]), wait=[("t", sch.mark("t"))])
                        sch.nop("t", [("v", sch.mark("v"))])

            mm_pair(Trb, wrb, Grb, wib, Ginb, [("g", lA)])
            mm_pair(Tib, wrb, Gib, wib, Grb, [])
            wrF = wrb
            wiF = wib
            for c in range(DC):
                sch.op("g", lambda g, c=c: g.dma_start(
                    out=wrF[:, c * D:(c + 1) * D],
                    in_=Fr[c * 128:(c + 1) * 128, :]),
                    wait=[("v", sch.mark("v"))])
                sch.op("g", lambda g, c=c: g.dma_start(
                    out=wiF[:, c * D:(c + 1) * D],
                    in_=Fin[c * 128:(c + 1) * 128, :]))
            lF = sch.mark("g")
            mm_pair(Msb, wrF, Trb, wiF, Tib, [("g", lF)])
        xb = sb("c_xb", [128, DC * TPC], BF16)
        ub = sb("c_ub", [128, DC * TPC], BF16)
        yb = sb("c_yb", [128, DC * TPC], BF16)
        sqsc = sb("c_sqsc", [128, 512], BF16)
        strow = sb("c_strow", [1, 2 * TPC], F32)
        varsc = sb("c_varsc", [1, TPC], F32)
        mubc = sb("c_mubc", [128, TPC], F32)
        rsbc = sb("c_rsbc", [128, TPC], F32)
        tsc = sb("c_tsc", [128, TPC], F32)
        xo = sb("c_xo", [128, 2 * TPC], F32)
        xr = sb("c_xr", [128, 2 * TPC], F32)
        ps_sx = psum("c_ps_sx", [1, 512], F32)
        ps_sxx = psum("c_ps_sxx", [1, 512], F32)
        ps_bc = psum("c_ps_bc", [128, 512], F32)
        ps_y = psum("c_ps_y", [128, 512], F32)

        st_sx = strow[0:1, 0:TPC]
        st_sxx = strow[0:1, TPC:2 * TPC]
        murow = st_sx
        rsrow = st_sxx

        phA = [("v", sch.mark("v")), ("t", sch.mark("t"))]
        for c in range(DC):
            sch.op("g", lambda g, c=c: g.dma_start(
                out=xb[:, c * TPC:(c + 1) * TPC],
                in_=x1T[c * 128:(c + 1) * 128, :]), wait=phA)
        lB = sch.mark("g")
        _fm_stats(sch, xb, lambda c: c * TPC, TPC, ps_sx, ps_sxx, onesbt,
                  sqsc, st_sx, st_sxx, wait_in=[("g", lB)])
        _fm_rows(sch, TPC, st_sx, st_sxx, murow, rsrow, varsc,
                 cv[0:1, 0:1], INV_D)
        _fm_bcast(sch, TPC, murow, mubc, onesft, ps_bc)
        _fm_bcast(sch, TPC, rsrow, rsbc, onesft, ps_bc)
        for c in range(DC):
            sch.op("v", lambda v, c=c: v.tensor_sub(
                tsc[:], xb[:, c * TPC:(c + 1) * TPC], mubc[:]),
                wait=[("v", sch.mark("v"))])
            sch.op("v", lambda v, c=c: v.tensor_mul(
                ub[:, c * TPC:(c + 1) * TPC], tsc[:], rsbc[:]),
                wait=[("v", sch.mark("v"))])
        vu = sch.mark("v")
        for o in range(DC):
            for f in range(4):
                for c in range(DC):
                    sch.op("t", lambda t, o=o, f=f, c=c: t.matmul(
                        ps_y[:],
                        Msb[:, c * D + o * 128:c * D + (o + 1) * 128],
                        ub[:, c * TPC + f * 512:c * TPC + (f + 1) * 512],
                        start=(c == 0), stop=(c == DC - 1)),
                        wait=[("v", vu)] if (f == 0 and c == 0) else ())
                sch.op("v", lambda v, o=o, f=f: v.tensor_scalar(
                    yb[:, o * TPC + f * 512:o * TPC + (f + 1) * 512],
                    ps_y[:], bic[:, o:o + 1], None, ALU.add),
                    wait=[("t", sch.mark("t"))])
                sch.nop("t", [("v", sch.mark("v"))])
        _fm_stats(sch, yb, lambda c: c * TPC, TPC, ps_sx, ps_sxx, onesbt,
                  sqsc, st_sx, st_sxx, wait_in=[("v", sch.mark("v"))])
        _fm_rows(sch, TPC, st_sx, st_sxx, murow, rsrow, varsc,
                 cv[0:1, 0:1], INV_D)
        _fm_bcast(sch, TPC, murow, mubc, onesft, ps_bc)
        _fm_bcast(sch, TPC, rsrow, rsbc, onesft, ps_bc)
        xrm = [0] * DC
        for c in range(2):
            sch.op("g", lambda g, c=c: g.dma_start(
                out=xr[:, (c % 2) * TPC:((c % 2) + 1) * TPC],
                in_=x1T[c * 128:(c + 1) * 128, :]), inc=("cx%d" % (c % 2), 16))
        for c in range(DC):
            slot = c % 2
            w = [("v", sch.mark("v")), ("cx%d" % slot, 16 * (c // 2 + 1))]
            if c >= 2:
                w.append(("co%d" % slot, 16 * (c // 2)))
            sch.op("v", lambda v, c=c: v.tensor_sub(
                tsc[:], yb[:, c * TPC:(c + 1) * TPC], mubc[:]), wait=w)
            sch.op("v", lambda v: v.tensor_mul(tsc[:], tsc[:], rsbc[:]),
                   wait=[("v", sch.mark("v"))])
            sch.op("v", lambda v, c=c, slot=slot: v.tensor_add(
                xo[:, slot * TPC:(slot + 1) * TPC], tsc[:],
                xr[:, slot * TPC:(slot + 1) * TPC]),
                wait=[("v", sch.mark("v"))])
            vdone = sch.mark("v")
            if c + 2 < DC:
                sch.op("g", lambda g, c=c, slot=slot: g.dma_start(
                    out=xr[:, slot * TPC:(slot + 1) * TPC],
                    in_=x1T[(c + 2) * 128:(c + 3) * 128, :]),
                    wait=[("v", vdone)], inc=("cx%d" % slot, 16))
            sch.op("y", lambda y, c=c, slot=slot: y.dma_start(
                out=x2T_s[c * 128:(c + 1) * 128, :],
                in_=xo[:, slot * TPC:(slot + 1) * TPC]),
                wait=[("v", vdone)], inc=("co%d" % slot, 16))
        sch.nop("y", [("co0", sch.mark("co0")), ("co1", sch.mark("co1"))])



def _fused_u4(nc, sch, sb_factory, phase, io, cv, onesbt, onesft):
    x2T = io["x2T"]; WqT = io["WqT"]; WkT = io["WkT"]; WvT = io["WvT"]
    nwc = io["nwc"]; bqc = io["bqc"]; bkc = io["bkc"]; bvr = io["bvr"]
    qT_s = io["qT_s"]; kT_s = io["kT_s"]
    vtok1_s = io["vtok1_s"]; xnT_s = io["xnT_s"]
    with contextlib.ExitStack() as stk:
        sb, psum = sb_factory(stk)
        for _e in ("g", "v", "s", "t", "y"):
            sch.nop(_e, phase())

        xb = sb("d_xb", [128, DC * TPC], BF16)
        xnb = sb("d_xnb", [128, DC * TPC], BF16)
        wqb = sb("d_wqb", [128, DC * D], BF16)
        wkb = sb("d_wkb", [128, DC * D], BF16)
        wvb = sb("d_wvb", [128, DC * D], BF16)
        nwt = sb("d_nwt", [128, DC], F32)
        bqt = sb("d_bqt", [128, DC], F32)
        bkt = sb("d_bkt", [128, DC], F32)
        bvt = sb("d_bvt", [128, D], F32)
        sqsc = sb("d_sqsc", [128, 512], BF16)
        strow = sb("d_strow", [1, 2 * TPC], F32)
        varsc = sb("d_varsc", [1, TPC], F32)
        mubc = sb("d_mubc", [128, TPC], F32)
        rsbc = sb("d_rsbc", [128, TPC], F32)
        tsc = sb("d_tsc", [128, TPC], F32)
        qo = sb("d_qo", [128, 2 * TPC], BF16)
        ko = sb("d_ko", [128, 2 * TPC], BF16)
        vo = sb("d_vo", [128, 2 * D], BF16)
        ps_sx = psum("d_ps_sx", [1, 512], F32)
        ps_sxx = psum("d_ps_sxx", [1, 512], F32)
        ps_bc = psum("d_ps_bc", [128, 512], F32)
        ps_q = psum("d_ps_q", [128, 512], F32)
        ps_k = psum("d_ps_k", [128, 512], F32)
        ps_v = psum("d_ps_v", [128, 512], F32)

        for c in range(DC):
            sch.op("g", lambda g, c=c: g.dma_start(
                out=xb[:, c * TPC:(c + 1) * TPC],
                in_=x2T[c * 128:(c + 1) * 128, :]))
            for dst, srcap in ((wqb, WqT), (wkb, WkT), (wvb, WvT)):
                sch.op("g", lambda g, dst=dst, srcap=srcap, c=c: g.dma_start(
                    out=dst[:, c * D:(c + 1) * D],
                    in_=srcap[c * 128:(c + 1) * 128, :]))
        for dst, srcap in ((nwt, nwc), (bqt, bqc), (bkt, bkc), (bvt, bvr)):
            sch.op("g", lambda g, dst=dst, srcap=srcap: g.dma_start(
                out=dst[:], in_=srcap[:]))
        lall = sch.mark("g")

        st_sx = strow[0:1, 0:TPC]
        st_sxx = strow[0:1, TPC:2 * TPC]

        _fm_stats(sch, xb, lambda c: c * TPC, TPC, ps_sx, ps_sxx, onesbt,
                  sqsc, st_sx, st_sxx, wait_in=[("g", lall)])
        sch.op("v", lambda v: v.tensor_scalar_mul(
            st_sx[0:1, :], st_sx[0:1, :], INV_D), wait=[("v", sch.mark("v"))])
        sch.op("s", lambda s: s.activation(
            varsc[0:1, :], st_sx[0:1, :], AF.Square),
            wait=[("v", sch.mark("v"))])
        sch.op("v", lambda v: v.tensor_scalar(
            st_sxx[0:1, :], st_sxx[0:1, :], INV_D, None, ALU.mult),
            wait=[("s", sch.mark("s"))])
        sch.op("v", lambda v: v.tensor_sub(
            varsc[0:1, :], st_sxx[0:1, :], varsc[0:1, :]),
            wait=[("v", sch.mark("v"))])
        sch.op("s", lambda s: s.activation(
            st_sxx[0:1, :], varsc[0:1, :], AF.Sqrt, bias=cv[0:1, 0:1]),
            wait=[("v", sch.mark("v"))])
        sch.op("v", lambda v: v.reciprocal(st_sxx[0:1, :], st_sxx[0:1, :]),
               wait=[("s", sch.mark("s"))])
        sch.op("s", lambda s: s.activation(
            varsc[0:1, :], st_sxx[0:1, :], AF.Square),
            wait=[("v", sch.mark("v"))])
        sch.op("v", lambda v: v.tensor_scalar(
            varsc[0:1, :], varsc[0:1, :], -1e-5, 1.0, ALU.mult, ALU.add),
            wait=[("s", sch.mark("s"))])
        sch.op("s", lambda s: s.activation(
            varsc[0:1, :], varsc[0:1, :], AF.Sqrt, bias=cv[0:1, 1:2]),
            wait=[("v", sch.mark("v"))])
        sch.op("v", lambda v: v.reciprocal(varsc[0:1, :], varsc[0:1, :]),
               wait=[("s", sch.mark("s"))])
        sch.op("v", lambda v: v.tensor_mul(
            st_sxx[0:1, :], st_sxx[0:1, :], varsc[0:1, :]),
            wait=[("v", sch.mark("v"))])
        _fm_bcast(sch, TPC, st_sx, mubc, onesft, ps_bc)
        _fm_bcast(sch, TPC, st_sxx, rsbc, onesft, ps_bc)
        for c in range(DC):
            slot = c % 2
            sch.op("v", lambda v, c=c: v.tensor_sub(
                tsc[:], xb[:, c * TPC:(c + 1) * TPC], mubc[:]),
                wait=[("v", sch.mark("v"))])
            sch.op("v", lambda v, c=c: v.scalar_tensor_tensor(
                xnb[:, c * TPC:(c + 1) * TPC], tsc[:],
                nwt[:, c:c + 1], rsbc[:], ALU.mult, ALU.mult),
                wait=[("v", sch.mark("v"))])
            sch.op("y", lambda y, c=c: y.dma_start(
                out=xnT_s[c * 128:(c + 1) * 128, :],
                in_=xnb[:, c * TPC:(c + 1) * TPC]),
                wait=[("v", sch.mark("v"))], inc=("dn%d" % slot, 16))
        vxn = sch.mark("v")

        for o in range(DC):
            slot = o % 2
            for kind, wb, ps, bt, outap, semk in (
                    ("q", wqb, ps_q, bqt, qT_s, "q"),
                    ("k", wkb, ps_k, bkt, kT_s, "k")):
                ob = qo if kind == "q" else ko
                w = [("v", sch.mark("v"))]
                if o >= 2:
                    w.append(("%s%d" % (semk, slot), 16 * (o // 2)))
                for f in range(4):
                    for c in range(DC):
                        sch.op("t", lambda t, wb=wb, ps=ps, o=o, f=f, c=c: t.matmul(
                            ps[:],
                            wb[:, c * D + o * 128:c * D + (o + 1) * 128],
                            xnb[:, c * TPC + f * 512:c * TPC + (f + 1) * 512],
                            start=(c == 0), stop=(c == DC - 1)),
                            wait=w if (f == 0 and c == 0) else ())
                    if kind == "q":
                        sch.op("v", lambda v, ob=ob, ps=ps, bt=bt, slot=slot, f=f, o=o: v.tensor_scalar(
                            ob[:, slot * TPC + f * 512:slot * TPC + (f + 1) * 512],
                            ps[:], bt[:, o:o + 1], 0.125, ALU.add, ALU.mult),
                            wait=[("t", sch.mark("t"))])
                    else:
                        sch.op("v", lambda v, ob=ob, ps=ps, bt=bt, slot=slot, f=f, o=o: v.tensor_scalar(
                            ob[:, slot * TPC + f * 512:slot * TPC + (f + 1) * 512],
                            ps[:], bt[:, o:o + 1], None, ALU.add),
                            wait=[("t", sch.mark("t"))])
                    if f < 3:
                        sch.nop("t", [("v", sch.mark("v"))])
                sch.op("y", lambda y, ob=ob, outap=outap, o=o, slot=slot: y.dma_start(
                    out=outap[o * 128:(o + 1) * 128, :],
                    in_=ob[:, slot * TPC:(slot + 1) * TPC]),
                    wait=[("v", sch.mark("v"))], inc=("%s%d" % (semk, slot), 16))
        for t in range(NT):
            slot = t % 2
            w = [("v", sch.mark("v"))]
            if t >= 2:
                w.append(("dw%d" % slot, 16 * (t // 2)))
            for f in range(2):
                for c in range(DC):
                    sch.op("t", lambda tt, t=t, f=f, c=c: tt.matmul(
                        ps_v[:],
                        xnb[:, c * TPC + t * 128:c * TPC + (t + 1) * 128],
                        wvb[:, c * D + f * 512:c * D + (f + 1) * 512],
                        start=(c == 0), stop=(c == DC - 1)),
                        wait=w if (f == 0 and c == 0) else ())
                sch.op("v", lambda v, slot=slot, f=f: v.tensor_add(
                    vo[:, slot * D + f * 512:slot * D + (f + 1) * 512],
                    ps_v[:], bvt[:, f * 512:(f + 1) * 512]),
                    wait=[("t", sch.mark("t"))])
                if f == 0:
                    sch.nop("t", [("v", sch.mark("v"))])
            sch.op("y", lambda y, t=t, slot=slot: y.dma_start(
                out=vtok1_s[t * 128:(t + 1) * 128, :],
                in_=vo[:, slot * D:(slot + 1) * D]),
                wait=[("v", sch.mark("v"))], inc=("dw%d" % slot, 16))
        sch.nop("y", [(k, sch.mark(k)) for k in
                      ("dq0", "dq1", "dk0", "dk1", "dw0", "dw1", "dn0", "dn1")])



def _fused_u5(nc, sch, sb_factory, phase, ccm, io, onesbt, onesft):
    qTg = io["qTg"]; kTg = io["kTg"]; vt1g = io["vt1g"]
    msk = io["msk"]; selh = io["selh"]; oT_s = io["oT_s"]
    NQT = N // 512
    with contextlib.ExitStack() as stk:
        sb, psum = sb_factory(stk)
        for _e in ("g", "v", "s", "t", "y"):
            sch.nop(_e, phase() + [("cc", ccm)])

        qb = sb("e_qb", [128, 2 * N], BF16)
        kb = sb("e_kb", [128, 2 * N], BF16)
        vb = sb("e_vb", [128, (N // 128) * 256], BF16)
        ob = sb("e_ob", [128, 2 * N], BF16)
        mskb = sb("e_mskb", [128, 4 * 512], F32)
        esc = sb("e_esc", [128, 2 * 512], BF16)
        zrow = sb("e_zrow", [1, 512], F32)
        zbsb = sb("e_zbsb", [64, 512], F32)
        ps_s = psum("e_ps_s", [128, 2 * 512], F32)
        ps_o = psum("e_ps_o", [64, 512], F32)
        ps_z = psum("e_ps_z", [1, 512], F32)
        ps_zb = psum("e_ps_zb", [64, 512], F32)

        selht = sb("e_selht", [128, 4], F32)
        tmq = sb("e_tmq", [128, 2 * TPC], BF16)
        tmv = sb("e_tmv", [128, 2 * 256], BF16)
        sch.op("g", lambda g: g.dma_start(out=selht[:], in_=selh[:]))
        selm = sch.mark("g")
        nldq = 0
        consm = [0, 0]
        for dstb, srcg in ((qb, qTg), (kb, kTg)):
            for ch in range(2):
                for r in range(4):
                    dsl = slice(ch * N + r * TPC, ch * N + (r + 1) * TPC)
                    for hgo in range(4):
                        slot = nldq % 2
                        nldq += 1
                        sch.op("g", lambda g, srcg=srcg, r=r, hgo=hgo, ch=ch, slot=slot: g.dma_start(
                            out=tmq[:, slot * TPC:(slot + 1) * TPC],
                            in_=srcg[r * D + hgo * 256 + ch * 128:
                                     r * D + hgo * 256 + (ch + 1) * 128, :]),
                            wait=[("v", consm[slot]), ("g", selm)],
                            inc=("ee%d" % slot, 16))
                        if hgo == 0:
                            sch.op("v", lambda v, dstb=dstb, dsl=dsl, slot=slot: v.tensor_scalar(
                                dstb[:, dsl], tmq[:, slot * TPC:(slot + 1) * TPC],
                                selht[:, 0:1], None, ALU.mult),
                                wait=[("ee%d" % slot, 16 * ((nldq + 1) // 2))])
                        else:
                            sch.op("v", lambda v, dstb=dstb, dsl=dsl, slot=slot, hgo=hgo: v.scalar_tensor_tensor(
                                dstb[:, dsl], tmq[:, slot * TPC:(slot + 1) * TPC],
                                selht[:, hgo:hgo + 1], dstb[:, dsl],
                                ALU.mult, ALU.add),
                                wait=[("ee%d" % slot, 16 * ((nldq + 1) // 2)),
                                      ("v", sch.mark("v"))])
                        consm[slot] = sch.mark("v")
        nldv = 0
        consv = [0, 0]
        for jt in range(N // 128):
            r, loc = jt // 16, jt % 16
            for hgo in range(4):
                slot = nldv % 2
                nldv += 1
                sch.op("g", lambda g, r=r, loc=loc, hgo=hgo, slot=slot: g.dma_start(
                    out=tmv[:, slot * 256:(slot + 1) * 256],
                    in_=vt1g[r * TPC + loc * 128:r * TPC + (loc + 1) * 128,
                             hgo * 256:(hgo + 1) * 256]),
                    wait=[("v", consv[slot]), ("g", selm)],
                    inc=("ef%d" % slot, 16))
                if hgo == 0:
                    sch.op("v", lambda v, jt=jt, slot=slot: v.tensor_scalar(
                        vb[:, jt * 256:(jt + 1) * 256],
                        tmv[:, slot * 256:(slot + 1) * 256],
                        selht[:, 0:1], None, ALU.mult),
                        wait=[("ef%d" % slot, 16 * ((nldv + 1) // 2))])
                else:
                    sch.op("v", lambda v, jt=jt, slot=slot, hgo=hgo: v.scalar_tensor_tensor(
                        vb[:, jt * 256:(jt + 1) * 256],
                        tmv[:, slot * 256:(slot + 1) * 256],
                        selht[:, hgo:hgo + 1], vb[:, jt * 256:(jt + 1) * 256],
                        ALU.mult, ALU.add),
                        wait=[("ef%d" % slot, 16 * ((nldv + 1) // 2)),
                              ("v", sch.mark("v"))])
                consv[slot] = sch.mark("v")
        sch.op("g", lambda g: g.dma_start(out=mskb[:], in_=msk[:]))
        lall = sch.mark("g")

        for hl in range(4):
            ch = hl // 2
            base = (hl % 2) * 64
            for qt in range(NQT):
                jbl = (qt + 1) * 4

                def rec_mm(jb):
                    sslot = jb % 2
                    pss = ps_s[:, sslot * 512:(sslot + 1) * 512]
                    w = [("g", lall), ("s", max(0, sch.mark("s") - 1))]
                    sch.op("t", lambda t, ch=ch, base=base, qt=qt, jb=jb, pss=pss: t.matmul(
                        pss,
                        kb[base:base + 64, ch * N + jb * 128:ch * N + (jb + 1) * 128],
                        qb[base:base + 64, ch * N + qt * 512:ch * N + (qt + 1) * 512],
                        start=True, stop=True), wait=w)
                    p = jb - qt * 4
                    if p >= 0:
                        sch.op("v", lambda v, pss=pss, p=p: v.tensor_add(
                            pss, pss, mskb[:, p * 512:(p + 1) * 512]),
                            wait=[("t", sch.mark("t"))])
                        return [("v", sch.mark("v")), ("t", sch.mark("t"))]
                    return [("t", sch.mark("t"))]

                wexp = rec_mm(0)
                for jb in range(jbl):
                    sslot = jb % 2
                    pss = ps_s[:, sslot * 512:(sslot + 1) * 512]
                    esl = esc[:, sslot * 512:(sslot + 1) * 512]
                    sch.op("s", lambda s, esl=esl, pss=pss: s.activation(
                        esl, pss, AF.Exp), wait=wexp)
                    sm = sch.mark("s")
                    if jb + 1 < jbl:
                        wexp = rec_mm(jb + 1)
                    sch.op("t", lambda t, jb=jb, hl=hl, esl=esl, jbl=jbl: t.matmul(
                        ps_o[:], vb[:, jb * 256 + hl * 64:jb * 256 + (hl + 1) * 64],
                        esl, start=(jb == 0), stop=(jb == jbl - 1)),
                        wait=[("s", sm)])
                    sch.op("t", lambda t, jb=jb, esl=esl, jbl=jbl: t.matmul(
                        ps_z[:], onesbt[:, 0:1], esl,
                        start=(jb == 0), stop=(jb == jbl - 1)))
                sch.op("v", lambda v: v.tensor_copy(zrow[0:1, :], ps_z[:]),
                       wait=[("t", sch.mark("t"))])
                sch.op("v", lambda v: v.reciprocal(zrow[0:1, :], zrow[0:1, :]),
                       wait=[("v", sch.mark("v"))])
                sch.op("t", lambda t: t.matmul(
                    ps_zb[:], onesft[0:1, 0:64], zrow[0:1, :],
                    start=True, stop=True), wait=[("v", sch.mark("v"))])
                sch.op("v", lambda v: v.tensor_copy(zbsb[:], ps_zb[:]),
                       wait=[("t", sch.mark("t"))])
                sch.op("v", lambda v, ch=ch, base=base, qt=qt: v.tensor_mul(
                    ob[base:base + 64, ch * N + qt * 512:ch * N + (qt + 1) * 512],
                    ps_o[:], zbsb[:]), wait=[("v", sch.mark("v"))])
                sch.nop("t", [("v", sch.mark("v"))])
        for ch in range(2):
            sch.op("y", lambda y, ch=ch: y.dma_start(
                out=oT_s[ch * 128:(ch + 1) * 128, :],
                in_=ob[:, ch * N:(ch + 1) * N]),
                wait=[("v", sch.mark("v"))])
        sch.nop("y", [("y", sch.mark("y"))])



def _fused_u6a(nc, sch, sb_factory, phase, ccm, io, cv, onesbt, onesft):
    oTg = io["oTg"]; xnT = io["xnT"]; x2T = io["x2T"]
    WoT = io["WoT"]; boc = io["boc"]; selq = io["selq"]
    x3T_s = io["x3T_s"]; edge = io["edge"]
    with contextlib.ExitStack() as stk:
        sb, psum = sb_factory(stk)
        for _e in ("g", "v", "s", "t", "y"):
            sch.nop(_e, phase() + [("cc", ccm)])

        ob = sb("f_ob", [128, DC * TPC], BF16)
        wob = sb("f_wob", [128, DC * D], BF16)
        blk = sb("f_blk", [128, DC * TPC], BF16)
        xns = sb("f_xns", [128, 2 * TPC], BF16)
        xrs = sb("f_xrs", [128, 2 * TPC], F32)
        xo = sb("f_xo", [128, 2 * TPC], F32)
        boct = sb("f_boct", [128, DC], F32)
        sqsc = sb("f_sqsc", [128, 512], BF16)
        strow = sb("f_strow", [1, 2 * TPC], F32)
        varsc = sb("f_varsc", [1, TPC], F32)
        mubc = sb("f_mubc", [128, TPC], F32)
        rsbc = sb("f_rsbc", [128, TPC], F32)
        tsc = sb("f_tsc", [128, TPC], F32)
        ps_a = psum("f_ps_a", [128, 512], F32)
        ps_sx = psum("f_ps_sx", [1, 512], F32)
        ps_sxx = psum("f_ps_sxx", [1, 512], F32)
        ps_bc = psum("f_ps_bc", [128, 512], F32)

        selqt = sb("f_selqt", [128, 4], F32)
        tmo = sb("f_tmo", [128, 2 * TPC], BF16)
        sch.op("g", lambda g: g.dma_start(out=selqt[:], in_=selq[:]))
        selm = sch.mark("g")
        nld = 0
        consm = [0, 0]
        for c in range(DC):
            sch.op("g", lambda g, c=c: g.dma_start(
                out=wob[:, c * D:(c + 1) * D],
                in_=WoT[c * 128:(c + 1) * 128, :]))
            for qqo in range(4):
                slot = nld % 2
                nld += 1
                sch.op("g", lambda g, c=c, qqo=qqo, slot=slot: g.dma_start(
                    out=tmo[:, slot * TPC:(slot + 1) * TPC],
                    in_=oTg[c * 128:(c + 1) * 128, qqo * TPC:(qqo + 1) * TPC]),
                    wait=[("v", consm[slot]), ("g", selm)],
                    inc=("fg%d" % slot, 16))
                if qqo == 0:
                    sch.op("v", lambda v, c=c, slot=slot: v.tensor_scalar(
                        ob[:, c * TPC:(c + 1) * TPC],
                        tmo[:, slot * TPC:(slot + 1) * TPC],
                        selqt[:, 0:1], None, ALU.mult),
                        wait=[("fg%d" % slot, 16 * ((nld + 1) // 2))])
                else:
                    sch.op("v", lambda v, c=c, slot=slot, qqo=qqo: v.scalar_tensor_tensor(
                        ob[:, c * TPC:(c + 1) * TPC],
                        tmo[:, slot * TPC:(slot + 1) * TPC],
                        selqt[:, qqo:qqo + 1], ob[:, c * TPC:(c + 1) * TPC],
                        ALU.mult, ALU.add),
                        wait=[("fg%d" % slot, 16 * ((nld + 1) // 2)),
                              ("v", sch.mark("v"))])
                consm[slot] = sch.mark("v")
        sch.op("g", lambda g: g.dma_start(out=boct[:], in_=boc[:]))
        lall = sch.mark("g")
        for c in range(2):
            sch.op("g", lambda g, c=c: g.dma_start(
                out=xns[:, (c % 2) * TPC:((c % 2) + 1) * TPC],
                in_=xnT[c * 128:(c + 1) * 128, :]), inc=("fn%d" % (c % 2), 16))
        for o in range(DC):
            slot = o % 2
            w = [("g", lall), ("fn%d" % slot, 16 * (o // 2 + 1)),
                 ("v", sch.mark("v"))]
            for f in range(4):
                for c in range(DC):
                    sch.op("t", lambda t, o=o, f=f, c=c: t.matmul(
                        ps_a[:],
                        wob[:, c * D + o * 128:c * D + (o + 1) * 128],
                        ob[:, c * TPC + f * 512:c * TPC + (f + 1) * 512],
                        start=(c == 0), stop=(c == DC - 1)),
                        wait=w if (f == 0 and c == 0) else ())
                sch.op("v", lambda v, o=o, f=f, slot=slot: v.tensor_scalar(
                    tsc[:, f * 512:(f + 1) * 512], ps_a[:], boct[:, o:o + 1],
                    None, ALU.add), wait=[("t", sch.mark("t"))])
                sch.op("v", lambda v, o=o, f=f, slot=slot: v.tensor_add(
                    blk[:, o * TPC + f * 512:o * TPC + (f + 1) * 512],
                    tsc[:, f * 512:(f + 1) * 512],
                    xns[:, slot * TPC + f * 512:slot * TPC + (f + 1) * 512]),
                    wait=[("v", sch.mark("v"))])
                if f < 3:
                    sch.nop("t", [("v", sch.mark("v"))])
            vdone = sch.mark("v")
            if o + 2 < DC:
                sch.op("g", lambda g, o=o, slot=slot: g.dma_start(
                    out=xns[:, slot * TPC:(slot + 1) * TPC],
                    in_=xnT[(o + 2) * 128:(o + 3) * 128, :]),
                    wait=[("v", vdone)], inc=("fn%d" % slot, 16))
        st_sx = strow[0:1, 0:TPC]
        st_sxx = strow[0:1, TPC:2 * TPC]
        _fm_stats(sch, blk, lambda c: c * TPC, TPC, ps_sx, ps_sxx, onesbt,
                  sqsc, st_sx, st_sxx, wait_in=[("v", sch.mark("v"))])
        _fm_rows(sch, TPC, st_sx, st_sxx, st_sx, st_sxx, varsc,
                 cv[0:1, 0:1], INV_D)
        _fm_bcast(sch, TPC, st_sx, mubc, onesft, ps_bc)
        _fm_bcast(sch, TPC, st_sxx, rsbc, onesft, ps_bc)
        for c in range(2):
            sch.op("g", lambda g, c=c: g.dma_start(
                out=xrs[:, (c % 2) * TPC:((c % 2) + 1) * TPC],
                in_=x2T[c * 128:(c + 1) * 128, :]), inc=("fx%d" % (c % 2), 16))
        for c in range(DC):
            slot = c % 2
            w = [("v", sch.mark("v")), ("fx%d" % slot, 16 * (c // 2 + 1))]
            if c >= 2:
                w.append(("fo%d" % slot, 16 * (c // 2)))
            sch.op("v", lambda v, c=c: v.tensor_sub(
                tsc[:], blk[:, c * TPC:(c + 1) * TPC], mubc[:]), wait=w)
            sch.op("v", lambda v: v.tensor_mul(tsc[:], tsc[:], rsbc[:]),
                   wait=[("v", sch.mark("v"))])
            sch.op("v", lambda v, c=c, slot=slot: v.tensor_add(
                xo[:, slot * TPC:(slot + 1) * TPC], tsc[:],
                xrs[:, slot * TPC:(slot + 1) * TPC]),
                wait=[("v", sch.mark("v"))])
            vdone = sch.mark("v")
            if c + 2 < DC:
                sch.op("g", lambda g, c=c, slot=slot: g.dma_start(
                    out=xrs[:, slot * TPC:(slot + 1) * TPC],
                    in_=x2T[(c + 2) * 128:(c + 3) * 128, :]),
                    wait=[("v", vdone)], inc=("fx%d" % slot, 16))
            sch.op("y", lambda y, c=c, slot=slot: y.dma_start(
                out=x3T_s[c * 128:(c + 1) * 128, :],
                in_=xo[:, slot * TPC:(slot + 1) * TPC]),
                wait=[("v", vdone)], inc=("fo%d" % slot, 16))
        sch.op("g", lambda g: g.dma_start(
            out=edge[:, 0:256], in_=x3T_s[:, 0:256]),
            wait=[("fo0", sch.mark("fo0")), ("fo1", sch.mark("fo1"))],
            inc=("fe", 16))
        sch.op("g", lambda g: g.dma_start(
            out=edge[:, 256:512], in_=x3T_s[:, TPC - 256:TPC]),
            inc=("fe", 16))
        sch.nop("y", [("fo0", sch.mark("fo0")), ("fo1", sch.mark("fo1")),
                      ("fe", 32)])



def _fused_u6b1(nc, sch, sb_factory, phase, ccm, io, cv, onesbt, onesft):
    x3T = io["x3T"]; edgeg = io["edgeg"]
    selL = io["selL"]; selR = io["selR"]
    mk0 = io["mk0"]; mk1 = io["mk1"]; mk2 = io["mk2"]
    u3T_s = io["u3T_s"]; xmT_s = io["xmT_s"]
    with contextlib.ExitStack() as stk:
        sb, psum = sb_factory(stk)
        for _e in ("g", "v", "s", "t", "y"):
            sch.nop(_e, phase() + [("cc", ccm)])

        xb = sb("h_xb", [128, DC * W6], BF16)
        mks = [sb("mk%d" % i, [128, DC], F32) for i in range(3)]
        sqsc = sb("h_sqsc", [128, 512], BF16)
        strow = sb("h_strow", [1, 2 * W6], F32)
        varsc = sb("h_varsc", [1, W6], F32)
        mubc = sb("h_mubc", [128, W6], F32)
        rsbc = sb("h_rsbc", [128, W6], F32)
        tsc = sb("h_tsc", [128, W6], F32)
        t2 = sb("h_t2", [128, TPC], F32)
        xmo = sb("h_xmo", [128, 2 * TPC], BF16)
        ps_sx = psum("h_ps_sx", [1, 512], F32)
        ps_sxx = psum("h_ps_sxx", [1, 512], F32)
        ps_bc = psum("h_ps_bc", [128, 512], F32)

        selLt = sb("h_selLt", [128, 4], F32)
        selRt = sb("h_selRt", [128, 4], F32)
        tme = sb("h_tme", [128, 256], F32)
        pacc = sb("h_pacc", [128, 2 * 256], F32)
        sch.op("g", lambda g: g.dma_start(out=selLt[:], in_=selL[:]))
        sch.op("g", lambda g: g.dma_start(out=selRt[:], in_=selR[:]))
        selm = sch.mark("g")
        for c in range(DC):
            sch.op("g", lambda g, c=c: g.dma_start(
                out=xb[:, c * W6 + PAD:c * W6 + PAD + TPC],
                in_=x3T[c * 128:(c + 1) * 128, :]))
            for pi, (selt, eoff, xoff) in enumerate(
                    ((selLt, 256, 0), (selRt, 0, PAD + TPC))):
                for r in range(4):
                    sch.op("g", lambda g, c=c, r=r, eoff=eoff: g.dma_start(
                        out=tme[:],
                        in_=edgeg[r * D + c * 128:r * D + (c + 1) * 128,
                                  eoff:eoff + 256]),
                        wait=[("v", sch.mark("v")), ("g", selm)],
                        inc=("he", 16))
                    if r == 0:
                        sch.op("v", lambda v, selt=selt, pi=pi: v.tensor_scalar(
                            pacc[:, pi * 256:(pi + 1) * 256], tme[:],
                            selt[:, 0:1], None, ALU.mult),
                            wait=[("he", sch.mark("he"))])
                    else:
                        sch.op("v", lambda v, selt=selt, pi=pi, r=r: v.scalar_tensor_tensor(
                            pacc[:, pi * 256:(pi + 1) * 256], tme[:],
                            selt[:, r:r + 1], pacc[:, pi * 256:(pi + 1) * 256],
                            ALU.mult, ALU.add),
                            wait=[("he", sch.mark("he")), ("v", sch.mark("v"))])
                sch.op("v", lambda v, c=c, pi=pi, xoff=xoff: v.tensor_copy(
                    xb[:, c * W6 + xoff:c * W6 + xoff + 256],
                    pacc[:, pi * 256:(pi + 1) * 256]),
                    wait=[("v", sch.mark("v"))])
        for dst, srcap in ((mks[0], mk0), (mks[1], mk1), (mks[2], mk2)):
            sch.op("g", lambda g, dst=dst, srcap=srcap: g.dma_start(
                out=dst[:], in_=srcap[:]))
        lall = sch.mark("g")

        st_sx = strow[0:1, 0:W6]
        st_sxx = strow[0:1, W6:2 * W6]
        _fm_stats(sch, xb, lambda c: c * W6, W6, ps_sx, ps_sxx, onesbt,
                  sqsc, st_sx, st_sxx, wait_in=[("g", lall)])
        _fm_rows(sch, W6, st_sx, st_sxx, st_sx, st_sxx, varsc,
                 cv[0:1, 0:1], INV_D)
        _fm_bcast(sch, W6, st_sx, mubc, onesft, ps_bc)
        _fm_bcast(sch, W6, st_sxx, rsbc, onesft, ps_bc)
        for c in range(DC):
            slot = c % 2
            sch.op("v", lambda v, c=c: v.tensor_sub(
                tsc[:], xb[:, c * W6:(c + 1) * W6], mubc[:]),
                wait=[("v", sch.mark("v"))])
            sch.op("v", lambda v, c=c: v.tensor_mul(
                xb[:, c * W6:(c + 1) * W6], tsc[:], rsbc[:]),
                wait=[("v", sch.mark("v"))])
            vu = sch.mark("v")
            sch.op("y", lambda y, c=c: y.dma_start(
                out=u3T_s[c * 128:(c + 1) * 128, :],
                in_=xb[:, c * W6 + PAD:c * W6 + PAD + TPC]),
                wait=[("v", vu)], inc=("hu%d" % slot, 16))
            own = lambda c, off: xb[:, c * W6 + PAD + off:c * W6 + PAD + TPC + off]
            sch.op("v", lambda v, c=c, own=own: v.tensor_scalar(
                t2[:], own(c, -1), mks[0][:, c:c + 1], None, ALU.mult),
                wait=[("v", vu)])
            sch.op("v", lambda v, c=c, own=own: v.scalar_tensor_tensor(
                t2[:], own(c, 0), mks[1][:, c:c + 1], t2[:],
                ALU.mult, ALU.add), wait=[("v", sch.mark("v"))])
            w = [("v", sch.mark("v"))]
            if c >= 2:
                w.append(("ho%d" % slot, 16 * (c // 2)))
            sch.op("v", lambda v, c=c, slot=slot, own=own: v.scalar_tensor_tensor(
                xmo[:, slot * TPC:(slot + 1) * TPC], own(c, 1),
                mks[2][:, c:c + 1], t2[:], ALU.mult, ALU.add), wait=w)
            sch.op("y", lambda y, c=c, slot=slot: y.dma_start(
                out=xmT_s[c * 128:(c + 1) * 128, :],
                in_=xmo[:, slot * TPC:(slot + 1) * TPC]),
                wait=[("v", sch.mark("v"))], inc=("ho%d" % slot, 16))
        sch.nop("y", [(k, sch.mark(k)) for k in ("ho0", "ho1", "hu0", "hu1")])



def _fused_u6b2(nc, sch, sb_factory, phase, io, cv, onesbt, onesft, identft):
    xmT = io["xmT"]; u3T = io["u3T"]; x3T = io["x3T"]
    W1T = io["W1T"]; W2T = io["W2T"]; b1c = io["b1c"]; b2r = io["b2r"]
    Wo1 = io["Wo1"]; bo1c = io["bo1c"]; Wo2 = io["Wo2"]; bo2c = io["bo2c"]
    gamc = io["gamc"]; betc = io["betc"]; selbc = io["selb"]
    x4_s = io["x4_s"]
    HF = TPC // 4
    NF = HF // 512
    with contextlib.ExitStack() as stk:
        sb, psum = sb_factory(stk)
        for _e in ("g", "v", "s", "t", "y"):
            sch.nop(_e, phase())

        zb = sb("i_zb", [128, DC * TPC], BF16)
        gamt = sb("i_gamt", [128, DC], F32)
        bett = sb("i_bett", [128, DC], F32)
        for dst, srcap in ((gamt, gamc), (bett, betc)):
            sch.op("g", lambda g, dst=dst, srcap=srcap: g.dma_start(
                out=dst[:], in_=srcap[:]))

        with contextlib.ExitStack() as mctx:
            def msb(name, shape, dtype):
                return mctx.enter_context(nc.sbuf_tensor(name, shape, dtype))

            def mpsum(name, shape, dtype):
                return mctx.enter_context(nc.psum_tensor(name, shape, dtype))

            xmb = msb("i_xmb", [128, DC * TPC], BF16)
            w1b = msb("i_w1b", [128, 9 * DC * 256], BF16)
            w2b = msb("i_w2b", [128, 18 * D], BF16)
            wo1b = msb("i_wo1b", [128, DC * 256], BF16)
            wo2b = msb("i_wo2b", [128, 2 * 9], BF16)
            b1t = msb("i_b1t", [128, 18], F32)
            b2b = msb("i_b2b", [9, D], BF16)
            bo1t = msb("i_bo1t", [128, 2], F32)
            bo2t = msb("i_bo2t", [128, 1], F32)
            tb = msb("i_tb", [128, 2 * TPC], BF16)
            s9 = msb("i_s9", [9, TPC], F32)
            stok = msb("i_stok", [128, NT * 9], F32)
            mxc = msb("i_mxc", [128, NT], F32)
            ohtok = msb("i_ohtok", [128, NT * 9], F32)
            ohT = msb("i_ohT", [9, TPC], F32)
            ohTb = msb("i_ohTb", [9, TPC], BF16)
            ohbc = msb("i_ohbc", [128, HF], BF16)
            selbt = msb("i_selbt", [9, 9 * 128], BF16)
            hb = msb("i_hb", [128, 2 * HF], BF16)
            hmall = msb("i_hmall", [128, 18 * HF], BF16)
            u3s = msb("i_u3s", [128, 2 * HF], BF16)
            ps_h = mpsum("i_ps_h", [128, 512], F32)
            ps_c = mpsum("i_ps_c", [128, 512], F32)
            ps_t9 = mpsum("i_ps_t9", [9, 512], F32)
            ps_tp = mpsum("i_ps_tp", [128, NT * 9], F32)
            ps_oh = mpsum("i_ps_oh", [9, 512], F32)
            ps_b = mpsum("i_ps_b", [128, 512], F32)

            for c in range(DC):
                sch.op("g", lambda g, c=c: g.dma_start(
                    out=xmb[:, c * TPC:(c + 1) * TPC],
                    in_=xmT[c * 128:(c + 1) * 128, :]))
                sch.op("g", lambda g, c=c: g.dma_start(
                    out=wo1b[:, c * 256:(c + 1) * 256],
                    in_=Wo1[c * 128:(c + 1) * 128, :]))
                for p in range(9):
                    sch.op("g", lambda g, p=p, c=c: g.dma_start(
                        out=w1b[:, (p * DC + c) * 256:(p * DC + c + 1) * 256],
                        in_=W1T[p * D + c * 128:p * D + (c + 1) * 128, :]))
            for p in range(9):
                for mc in range(2):
                    sch.op("g", lambda g, p=p, mc=mc: g.dma_start(
                        out=w2b[:, (p * 2 + mc) * D:(p * 2 + mc + 1) * D],
                        in_=W2T[p * 256 + mc * 128:p * 256 + (mc + 1) * 128, :]))
            for mc in range(2):
                sch.op("g", lambda g, mc=mc: g.dma_start(
                    out=wo2b[:, mc * 9:(mc + 1) * 9],
                    in_=Wo2[mc * 128:(mc + 1) * 128, :]))
            for dst, srcap in ((b1t, b1c), (b2b, b2r), (bo1t, bo1c),
                               (bo2t, bo2c), (selbt, selbc)):
                sch.op("g", lambda g, dst=dst, srcap=srcap: g.dma_start(
                    out=dst[:], in_=srcap[:]))
            lall = sch.mark("g")

            for mc in range(2):
                for f in range(4):
                    for c in range(DC):
                        sch.op("t", lambda t, mc=mc, f=f, c=c: t.matmul(
                            ps_h[:],
                            wo1b[:, c * 256 + mc * 128:c * 256 + (mc + 1) * 128],
                            xmb[:, c * TPC + f * 512:c * TPC + (f + 1) * 512],
                            start=(c == 0), stop=(c == DC - 1)),
                            wait=[("g", lall), ("s", sch.mark("s"))] if (f == 0 and c == 0) else ())
                    sch.op("s", lambda s, mc=mc, f=f: s.activation(
                        tb[:, mc * TPC + f * 512:mc * TPC + (f + 1) * 512],
                        ps_h[:], AF.Tanh, bias=bo1t[:, mc:mc + 1]),
                        wait=[("t", sch.mark("t"))])
                    sch.nop("t", [("s", sch.mark("s"))])
            for f in range(4):
                for mc in range(2):
                    sch.op("t", lambda t, mc=mc, f=f: t.matmul(
                        ps_t9[:],
                        wo2b[:, mc * 9:(mc + 1) * 9],
                        tb[:, mc * TPC + f * 512:mc * TPC + (f + 1) * 512],
                        start=(mc == 0), stop=(mc == 1)),
                        wait=[("s", sch.mark("s"))] if mc == 0 else ())
                sch.op("v", lambda v, f=f: v.tensor_scalar(
                    s9[0:9, f * 512:(f + 1) * 512], ps_t9[:],
                    bo2t[0:9, 0:1], None, ALU.add),
                    wait=[("t", sch.mark("t"))])
                sch.nop("t", [("v", sch.mark("v"))])
            for t_ in range(NT):
                sch.op("t", lambda tt, t_=t_: tt.transpose(
                    ps_tp[:, t_ * 9:(t_ + 1) * 9],
                    s9[0:9, t_ * 128:(t_ + 1) * 128], identft[0:9, 0:9]),
                    wait=[("v", sch.mark("v"))])
            sch.op("v", lambda v: v.tensor_copy(stok[:], ps_tp[:]),
                   wait=[("t", sch.mark("t"))])
            for t_ in range(NT):
                sch.op("v", lambda v, t_=t_: v.reduce_max(
                    mxc[:, t_:t_ + 1], stok[:, t_ * 9:(t_ + 1) * 9],
                    axis=AX.X), wait=[("v", sch.mark("v"))])
                sch.op("v", lambda v, t_=t_: v.tensor_scalar(
                    ohtok[:, t_ * 9:(t_ + 1) * 9], stok[:, t_ * 9:(t_ + 1) * 9],
                    mxc[:, t_:t_ + 1], None, ALU.is_ge),
                    wait=[("v", sch.mark("v"))])
            for t_ in range(NT):
                sch.op("t", lambda tt, t_=t_: tt.transpose(
                    ps_oh[0:9, (t_ % 4) * 128:((t_ % 4) + 1) * 128],
                    ohtok[:, t_ * 9:(t_ + 1) * 9], identft[:]),
                    wait=[("v", sch.mark("v"))])
                if t_ % 4 == 3:
                    sch.op("v", lambda v, t_=t_: v.tensor_copy(
                        ohT[0:9, (t_ - 3) * 128:(t_ + 1) * 128], ps_oh[0:9, :]),
                        wait=[("t", sch.mark("t"))])
                    sch.nop("t", [("v", sch.mark("v"))])
            sch.op("v", lambda v: v.tensor_copy(ohTb[0:9, :], ohT[0:9, :]),
                   wait=[("v", sch.mark("v"))])

            for hf in range(4):
                for p in range(9):
                    for f in range(NF):
                        sch.op("t", lambda t, p=p, hf=hf, f=f: t.matmul(
                            ps_b[:], selbt[0:9, p * 128:(p + 1) * 128],
                            ohTb[0:9, hf * HF + f * 512:hf * HF + (f + 1) * 512],
                            start=True, stop=True),
                            wait=[("v", sch.mark("v"))])
                        sch.op("v", lambda v, f=f: v.tensor_copy(
                            ohbc[:, f * 512:(f + 1) * 512], ps_b[:]),
                            wait=[("t", sch.mark("t"))])
                        sch.nop("t", [("v", sch.mark("v"))])
                    for mc in range(2):
                        for f in range(NF):
                            for c in range(DC):
                                sch.op("t", lambda t, p=p, mc=mc, hf=hf, f=f, c=c: t.matmul(
                                    ps_h[:],
                                    w1b[:, (p * DC + c) * 256 + mc * 128:(p * DC + c) * 256 + (mc + 1) * 128],
                                    xmb[:, c * TPC + hf * HF + f * 512:c * TPC + hf * HF + (f + 1) * 512],
                                    start=(c == 0), stop=(c == DC - 1)),
                                    wait=[("v", sch.mark("v"))] if (f == 0 and c == 0) else ())
                            sch.op("v", lambda v, p=p, mc=mc, f=f: v.tensor_scalar(
                                hb[:, mc * HF + f * 512:mc * HF + (f + 1) * 512],
                                ps_h[:], b1t[:, p * 2 + mc:p * 2 + mc + 1],
                                None, ALU.add), wait=[("t", sch.mark("t"))])
                            sch.op("v", lambda v, p=p, mc=mc, f=f: v.tensor_mul(
                                hmall[:, (p * 2 + mc) * HF + f * 512:(p * 2 + mc) * HF + (f + 1) * 512],
                                hb[:, mc * HF + f * 512:mc * HF + (f + 1) * 512],
                                ohbc[:, f * 512:(f + 1) * 512]),
                                wait=[("v", sch.mark("v"))])
                            sch.nop("t", [("v", sch.mark("v"))])
                for o in range(2):
                    sch.op("g", lambda g, o=o, hf=hf: g.dma_start(
                        out=u3s[:, (o % 2) * HF:((o % 2) + 1) * HF],
                        in_=u3T[o * 128:(o + 1) * 128, hf * HF:(hf + 1) * HF]),
                        wait=[("v", sch.mark("v"))], inc=("iu%d" % (o % 2), 16))
                for o in range(DC):
                    slot = o % 2
                    w = [("v", sch.mark("v")),
                         ("iu%d" % slot, 16 * (hf * 4 + o // 2 + 1))]
                    for f in range(NF):
                        sch.op("t", lambda t, o=o, f=f, hf=hf: t.matmul(
                            ps_c[:], b2b[0:9, o * 128:(o + 1) * 128],
                            ohTb[0:9, hf * HF + f * 512:hf * HF + (f + 1) * 512],
                            start=True, stop=False),
                            wait=w if f == 0 else ())
                        for pm in range(18):
                            sch.op("t", lambda t, pm=pm, o=o, f=f: t.matmul(
                                ps_c[:],
                                w2b[:, pm * D + o * 128:pm * D + (o + 1) * 128],
                                hmall[:, pm * HF + f * 512:pm * HF + (f + 1) * 512],
                                start=False, stop=(pm == 17)))
                        sch.op("v", lambda v, o=o, f=f, hf=hf, slot=slot: v.tensor_add(
                            zb[:, o * TPC + hf * HF + f * 512:o * TPC + hf * HF + (f + 1) * 512],
                            ps_c[:],
                            u3s[:, slot * HF + f * 512:slot * HF + (f + 1) * 512]),
                            wait=[("t", sch.mark("t"))])
                        sch.nop("t", [("v", sch.mark("v"))])
                    vdone = sch.mark("v")
                    if o + 2 < DC:
                        sch.op("g", lambda g, o=o, hf=hf, slot=slot: g.dma_start(
                            out=u3s[:, slot * HF:(slot + 1) * HF],
                            in_=u3T[(o + 2) * 128:(o + 3) * 128, hf * HF:(hf + 1) * HF]),
                            wait=[("v", vdone)], inc=("iu%d" % slot, 16))
        fb = sb("i_fb", [128, DC * TPC], BF16)
        sqsc = sb("i_sqsc", [128, 512], BF16)
        strow = sb("i_strow", [1, 2 * TPC], F32)
        varsc = sb("i_varsc", [1, TPC], F32)
        mubc = sb("i_mubc", [128, TPC], F32)
        rsbc = sb("i_rsbc", [128, TPC], F32)
        tsc = sb("i_tsc", [128, TPC], F32)
        x3s = sb("i_x3s", [128, 2 * TPC], F32)
        x4c = sb("i_x4c", [128, TPC], F32)
        xto = sb("i_xto", [128, 2 * 128], F32)
        ps_sx = psum("i_ps_sx", [1, 512], F32)
        ps_sxx = psum("i_ps_sxx", [1, 512], F32)
        ps_bc = psum("i_ps_bc", [128, 512], F32)
        ps_f = psum("i_ps_f", [128, 128], F32)

        st_sx = strow[0:1, 0:TPC]
        st_sxx = strow[0:1, TPC:2 * TPC]
        phA = [("v", sch.mark("v")), ("t", sch.mark("t")), ("g", sch.mark("g"))]
        _fm_stats(sch, zb, lambda c: c * TPC, TPC, ps_sx, ps_sxx, onesbt,
                  sqsc, st_sx, st_sxx, wait_in=phA)
        _fm_rows(sch, TPC, st_sx, st_sxx, st_sx, st_sxx, varsc,
                 cv[0:1, 0:1], INV_D)
        _fm_bcast(sch, TPC, st_sx, mubc, onesft, ps_bc)
        _fm_bcast(sch, TPC, st_sxx, rsbc, onesft, ps_bc)
        for c in range(DC):
            sch.op("v", lambda v, c=c: v.tensor_sub(
                tsc[:], zb[:, c * TPC:(c + 1) * TPC], mubc[:]),
                wait=[("v", sch.mark("v"))])
            sch.op("v", lambda v, c=c: v.scalar_tensor_tensor(
                tsc[:], tsc[:], gamt[:, c:c + 1], rsbc[:],
                ALU.mult, ALU.mult), wait=[("v", sch.mark("v"))])
            sch.op("v", lambda v, c=c: v.tensor_scalar(
                fb[:, c * TPC:(c + 1) * TPC], tsc[:], bett[:, c:c + 1],
                None, ALU.add), wait=[("v", sch.mark("v"))])
        _fm_stats(sch, fb, lambda c: c * TPC, TPC, ps_sx, ps_sxx, onesbt,
                  sqsc, st_sx, st_sxx, wait_in=[("v", sch.mark("v"))])
        _fm_rows(sch, TPC, st_sx, st_sxx, st_sx, st_sxx, varsc,
                 cv[0:1, 0:1], INV_D)
        _fm_bcast(sch, TPC, st_sx, mubc, onesft, ps_bc)
        _fm_bcast(sch, TPC, st_sxx, rsbc, onesft, ps_bc)
        for c in range(2):
            sch.op("g", lambda g, c=c: g.dma_start(
                out=x3s[:, (c % 2) * TPC:((c % 2) + 1) * TPC],
                in_=x3T[c * 128:(c + 1) * 128, :]),
                wait=[("v", sch.mark("v")), ("t", sch.mark("t"))],
                inc=("ix%d" % (c % 2), 16))
        nyo = 0
        for c in range(DC):
            slot = c % 2
            w = [("v", sch.mark("v")), ("ix%d" % slot, 16 * (c // 2 + 1))]
            sch.op("v", lambda v, c=c: v.tensor_sub(
                tsc[:], fb[:, c * TPC:(c + 1) * TPC], mubc[:]), wait=w)
            sch.op("v", lambda v: v.tensor_mul(tsc[:], tsc[:], rsbc[:]),
                   wait=[("v", sch.mark("v"))])
            sch.op("v", lambda v, c=c, slot=slot: v.tensor_add(
                x4c[:], tsc[:], x3s[:, slot * TPC:(slot + 1) * TPC]),
                wait=[("v", sch.mark("v")), ("t", sch.mark("t"))])
            vdone = sch.mark("v")
            if c + 2 < DC:
                sch.op("g", lambda g, c=c, slot=slot: g.dma_start(
                    out=x3s[:, slot * TPC:(slot + 1) * TPC],
                    in_=x3T[(c + 2) * 128:(c + 3) * 128, :]),
                    wait=[("v", vdone)], inc=("ix%d" % slot, 16))
            for t_ in range(NT):
                ts_ = t_ % 2
                wtp = [("v", vdone)]
                if nyo >= 2:
                    wtp.append(("io%d" % ts_, 16 * (nyo // 2)))
                sch.op("t", lambda tt, t_=t_: tt.transpose(
                    ps_f[:], x4c[:, t_ * 128:(t_ + 1) * 128], identft[:]),
                    wait=wtp)
                sch.op("v", lambda v, ts_=ts_: v.tensor_copy(
                    xto[:, ts_ * 128:(ts_ + 1) * 128], ps_f[:]),
                    wait=[("t", sch.mark("t"))])
                sch.op("y", lambda y, c=c, t_=t_, ts_=ts_: y.dma_start(
                    out=x4_s[t_ * 128:(t_ + 1) * 128, c * 128:(c + 1) * 128],
                    in_=xto[:, ts_ * 128:(ts_ + 1) * 128]),
                    wait=[("v", sch.mark("v"))], inc=("io%d" % ts_, 16))
                nyo += 1
                sch.nop("t", [("v", sch.mark("v"))])
        sch.nop("y", [("io0", sch.mark("io0")), ("io1", sch.mark("io1"))])



def build_fused():
    nc = bass.Bass("TRN2", target_bir_lowering=False)
    E = lambda n, s, d: nc.dram_tensor(n, s, d, kind="ExternalInput").ap()
    O = lambda n, s, d: nc.dram_tensor(n, s, d, kind="ExternalOutput").ap()
    I = lambda n, s, d: nc.dram_tensor(n, s, d, kind="Internal").ap()

    x = E("x_tok", [TPC, D], F32)
    x0T = E("x0T_s", [D, TPC], F32)
    lng, lnb = E("lng", [128, D], F32), E("lnb", [128, D], F32)
    wkv = E("wkv", [D, 2 * D], F32)
    wmix = E("wmix", [128, 128], F32)
    wo0 = E("wo0", [D, D], F32)
    wrT, wiT = E("wrT", [D, D], F32), E("wiT", [D, D], F32)
    Gr, Gi, Gin = E("Gr", [D, D], F32), E("Gi", [D, D], F32), E("Gin", [D, D], F32)
    Fr, Fin = E("Fr", [D, D], F32), E("Fin", [D, D], F32)
    biasc = E("biasc", [128, DC], F32)
    WqT, WkT, WvT = E("WqT", [D, D], F32), E("WkT", [D, D], F32), E("WvT", [D, D], F32)
    nwc, bqc, bkc = E("nwc", [128, DC], F32), E("bqc", [128, DC], F32), E("bkc", [128, DC], F32)
    bvr = E("bvr", [128, D], F32)
    msk = E("msk", [128, 4 * 512], F32)
    WoT, boc = E("WoT", [D, D], F32), E("boc", [128, DC], F32)
    mk0, mk1, mk2 = E("mk0c", [128, DC], F32), E("mk1c", [128, DC], F32), E("mk2c", [128, DC], F32)
    W1T, W2T = E("W1T", [9 * D, 256], F32), E("W2T", [9 * 256, D], F32)
    b1c, b2r = E("b1c", [128, 18], F32), E("b2r", [9, D], F32)
    Wo1, bo1c = E("Wo1", [D, 256], F32), E("bo1c", [128, 2], F32)
    Wo2, bo2c = E("Wo2", [256, 9], F32), E("bo2c", [128, 1], F32)
    gamc, betc = E("gamc", [128, DC], F32), E("betc", [128, DC], F32)
    ident, identf = E("ident", [128, 128], BF16), E("ident_f", [128, 128], F32)
    selbc = E("selb", [9, 9 * 128], BF16)
    onesb, onesf = E("ones_bf", [128, 1], BF16), E("ones_f", [1, 128], F32)
    cvals = E("cvals", [128, NCONST], F32)
    selh = E("selh", [128, 4], F32)
    selL = E("selL", [128, 4], F32)
    selR = E("selR", [128, 4], F32)
    selq = E("selq", [128, 4], F32)
    x4_s = O("x4_s", [TPC, D], F32)

    kvT_d = I("d_kvT", [2 * D, TPC], BF16)
    vtok_d = I("d_vtok", [TPC, D], BF16)
    sqq_d = I("d_sqq", [H, TPC], F32)
    kvg = I("d_kvg", [4 * 2 * D, TPC], BF16)
    vtg = I("d_vtg", [4 * TPC, D], BF16)
    sqg = I("d_sqg", [4 * H, TPC], F32)
    x1T_d = I("d_x1T", [D, TPC], F32)
    x2T_d = I("d_x2T", [D, TPC], F32)
    qT_d = I("d_qT", [D, TPC], BF16)
    kT_d = I("d_kT", [D, TPC], BF16)
    vt1_d = I("d_vt1", [TPC, D], BF16)
    xnT_d = I("d_xnT", [D, TPC], BF16)
    qTg = I("d_qTg", [4 * D, TPC], BF16)
    kTg = I("d_kTg", [4 * D, TPC], BF16)
    vt1g = I("d_vt1g", [4 * TPC, D], BF16)
    oT_d = I("d_oT", [4 * HD, N], BF16)
    oTg = I("d_oTg", [4 * 4 * HD, N], BF16)
    x3T_d = I("d_x3T", [D, TPC], F32)
    edge_d = I("d_edge", [D, 512], F32)
    edgeg = I("d_edgeg", [4 * D, 512], F32)
    u3T_d = I("d_u3T", [D, TPC], BF16)
    xmT_d = I("d_xmT", [D, TPC], BF16)
    scr = I("d_scr", [2, 128, H * 512], BF16)

    with contextlib.ExitStack() as ctx:
        def sem(name):
            return ctx.enter_context(nc.semaphore(name))

        sch = Sched()
        semnames = ["g", "v", "s", "t", "y", "cc"]
        for p in ("a", "b", "c", "d", "e", "f", "h", "i"):
            semnames += [p + k for k in
                         ("x0", "x1", "o0", "o1", "k0", "k1", "w0", "w1",
                          "q0", "q1", "n0", "n1", "u0", "u1", "sq0", "sq1",
                          "kj0", "kj1", "vj0", "vj1", "ws0", "ws1",
                          "rs0", "rs1", "sq")]
        sems = {}

        def SEM(k):
            if k not in sems:
                sems[k] = sem("q" + k)
            return sems[k]

        class LazySems(dict):
            def __getitem__(self, k):
                return SEM(k)
        lazysems = LazySems()

        def phase():
            return [(k, v) for k, v in sch.cnt.items() if v > 0]

        def sb_factory(stack):
            def sb(name, shape, dtype):
                return stack.enter_context(nc.sbuf_tensor(name, shape, dtype))

            def ps(name, shape, dtype):
                return stack.enter_context(nc.psum_tensor(name, shape, dtype))
            return sb, ps

        cv = ctx.enter_context(nc.sbuf_tensor("cv", [128, NCONST], F32))
        onesbt = ctx.enter_context(nc.sbuf_tensor("onesbt", [128, 1], BF16))
        onesft = ctx.enter_context(nc.sbuf_tensor("onesft", [1, 128], F32))
        identb = ctx.enter_context(nc.sbuf_tensor("identb", [128, 128], BF16))
        identft = ctx.enter_context(nc.sbuf_tensor("identft", [128, 128], F32))
        for dst, srcap in ((cv, cvals), (onesbt, onesb), (onesft, onesf),
                           (identb, ident), (identft, identf)):
            sch.op("g", lambda g, dst=dst, srcap=srcap: g.dma_start(
                out=dst[:], in_=srcap[:]))

        _fused_u1(nc, sch, sb_factory, phase,
                  dict(x=x, lng=lng, lnb=lnb, wkv=wkv,
                       kvT_s=kvT_d, vtok_s=vtok_d, sqq_s=sqq_d),
                  cv, onesbt, identb)
        wcoll = phase()
        sch.op("g", lambda g: g.collective_compute(
            "AllGather", ALU.bypass, replica_groups=[[0, 1, 2, 3], [4, 5, 6, 7]],
            ins=[kvT_d.opt()], outs=[kvg.opt()]), wait=wcoll, inc=("cc", 1))
        sch.op("g", lambda g: g.collective_compute(
            "AllGather", ALU.bypass, replica_groups=[[0, 1, 2, 3], [4, 5, 6, 7]],
            ins=[vtok_d.opt()], outs=[vtg.opt()]), inc=("cc", 1))
        sch.op("g", lambda g: g.collective_compute(
            "AllGather", ALU.bypass, replica_groups=[[0, 1, 2, 3], [4, 5, 6, 7]],
            ins=[sqq_d.opt()], outs=[sqg.opt()]), inc=("cc", 1))
        ccm = sch.mark("cc")

        _fused_u2(nc, sch, sb_factory, phase, ccm,
                  dict(kqT=kvT_d, kvg=kvg, vtg=vtg, sqq_own=sqq_d, sqg=sqg,
                       wmix=wmix, wo0=wo0, x0T=x0T, x1T_s=x1T_d, scr=scr),
                  cv, onesbt, onesft, identb, identft)
        _fused_u3(nc, sch, sb_factory, phase,
                  dict(x1T=x1T_d, wrT=wrT, wiT=wiT, Gr=Gr, Gi=Gi, Gin=Gin,
                       Fr=Fr, Fin=Fin, biasc=biasc, x2T_s=x2T_d),
                  cv, onesbt, onesft)
        _fused_u4(nc, sch, sb_factory, phase,
                  dict(x2T=x2T_d, WqT=WqT, WkT=WkT, WvT=WvT, nwc=nwc,
                       bqc=bqc, bkc=bkc, bvr=bvr, qT_s=qT_d, kT_s=kT_d,
                       vtok1_s=vt1_d, xnT_s=xnT_d),
                  cv, onesbt, onesft)
        wcoll = phase()
        sch.op("g", lambda g: g.collective_compute(
            "AllGather", ALU.bypass, replica_groups=[[0, 1, 2, 3], [4, 5, 6, 7]],
            ins=[qT_d.opt()], outs=[qTg.opt()]), wait=wcoll, inc=("cc", 1))
        sch.op("g", lambda g: g.collective_compute(
            "AllGather", ALU.bypass, replica_groups=[[0, 1, 2, 3], [4, 5, 6, 7]],
            ins=[kT_d.opt()], outs=[kTg.opt()]), inc=("cc", 1))
        sch.op("g", lambda g: g.collective_compute(
            "AllGather", ALU.bypass, replica_groups=[[0, 1, 2, 3], [4, 5, 6, 7]],
            ins=[vt1_d.opt()], outs=[vt1g.opt()]), inc=("cc", 1))
        ccm2 = sch.mark("cc")

        _fused_u5(nc, sch, sb_factory, phase, ccm2,
                  dict(qTg=qTg, kTg=kTg, vt1g=vt1g, msk=msk, selh=selh,
                       oT_s=oT_d),
                  onesbt, onesft)
        wcoll = phase()
        sch.op("g", lambda g: g.collective_compute(
            "AllGather", ALU.bypass, replica_groups=[[0, 1, 2, 3], [4, 5, 6, 7]],
            ins=[oT_d.opt()], outs=[oTg.opt()]), wait=wcoll, inc=("cc", 1))
        ccm3 = sch.mark("cc")

        _fused_u6a(nc, sch, sb_factory, phase, ccm3,
                   dict(oTg=oTg, xnT=xnT_d, x2T=x2T_d, WoT=WoT, boc=boc,
                        selq=selq, x3T_s=x3T_d, edge=edge_d),
                   cv, onesbt, onesft)
        wcoll = phase()
        sch.op("g", lambda g: g.collective_compute(
            "AllGather", ALU.bypass, replica_groups=[[0, 1, 2, 3], [4, 5, 6, 7]],
            ins=[edge_d.opt()], outs=[edgeg.opt()]), wait=wcoll, inc=("cc", 1))
        ccm4 = sch.mark("cc")

        _fused_u6b1(nc, sch, sb_factory, phase, ccm4,
                    dict(x3T=x3T_d, edgeg=edgeg, selL=selL, selR=selR,
                         mk0=mk0, mk1=mk1, mk2=mk2, u3T_s=u3T_d, xmT_s=xmT_d),
                    cv, onesbt, onesft)
        _fused_u6b2(nc, sch, sb_factory, phase,
                    dict(xmT=xmT_d, u3T=u3T_d, x3T=x3T_d, W1T=W1T, W2T=W2T,
                         b1c=b1c, b2r=b2r, Wo1=Wo1, bo1c=bo1c, Wo2=Wo2,
                         bo2c=bo2c, gamc=gamc, betc=betc, selb=selbc,
                         x4_s=x4_s),
                    cv, onesbt, onesft, identft)

        sch.nop("y", phase())
        block = ctx.enter_context(nc.Block())
        sch.attach(block, lazysems)
    return nc


def kernel_fused(inputs):
    C = _consts()
    x0 = np.ascontiguousarray(inputs["L0_attn_x"], dtype=np.float32)
    Wx = np.asarray(inputs["L0_attn_W_x"], np.float32)
    wmix = np.zeros((128, 128), np.float32)
    for g in range(H):
        for h in range(H):
            wmix[h * 8:h * 8 + 8, g * 8:g * 8 + 8][np.arange(8), np.arange(8)] = Wx[g, h]
    b1 = np.asarray(inputs["L1_ffn_b1"], np.float32)
    b1cols = np.zeros((128, 18), np.float32)
    for p in range(9):
        for mc in range(2):
            b1cols[:, p * 2 + mc] = b1[p, mc * 128:(mc + 1) * 128]
    bo2c = np.zeros((128, 1), np.float32)
    bo2c[0:9, 0] = np.asarray(inputs["L1_ffn_bo2"], np.float32)
    mk = np.asarray(inputs["L1_ffn_mix_kernel"], np.float32)

    shared = {
        "lng": _rep(inputs["L0_attn_ln_g"]), "lnb": _rep(inputs["L0_attn_ln_b"]),
        "wkv": np.asarray(inputs["L0_attn_W_kv"], np.float32),
        "wmix": wmix, "wo0": np.asarray(inputs["L0_attn_W_out"], np.float32),
        "wrT": np.ascontiguousarray(np.asarray(inputs["L0_ffn_w_freq_real"]).T),
        "wiT": np.ascontiguousarray(np.asarray(inputs["L0_ffn_w_freq_imag"]).T),
        "Gr": C["Gr"], "Gi": C["Gi"], "Gin": C["Gin"],
        "Fr": C["Fr"], "Fin": C["Fin"],
        "biasc": _fcols(inputs["L0_ffn_bias"]),
        "WqT": np.ascontiguousarray(np.asarray(inputs["L1_attn_Wq"]).T),
        "WkT": np.ascontiguousarray(np.asarray(inputs["L1_attn_Wk"]).T),
        "WvT": np.ascontiguousarray(np.asarray(inputs["L1_attn_Wv"]).T),
        "nwc": _fcols(inputs["L1_attn_norm_w"]),
        "bqc": _fcols(inputs["L1_attn_bq"]), "bkc": _fcols(inputs["L1_attn_bk"]),
        "bvr": _rep(inputs["L1_attn_bv"]),
        "msk": C["msk"],
        "WoT": np.ascontiguousarray(np.asarray(inputs["L1_attn_Wo"]).T),
        "boc": _fcols(inputs["L1_attn_bo"]),
        "mk0c": _fcols(mk[0]), "mk1c": _fcols(mk[1]), "mk2c": _fcols(mk[2]),
        "W1T": np.ascontiguousarray(
            np.asarray(inputs["L1_ffn_W1"], np.float32).reshape(9 * D, 256)),
        "W2T": np.ascontiguousarray(
            np.asarray(inputs["L1_ffn_W2"], np.float32).reshape(9 * 256, D)),
        "b1c": b1cols, "b2r": np.asarray(inputs["L1_ffn_b2"], np.float32),
        "Wo1": np.asarray(inputs["L1_ffn_Wo1"], np.float32),
        "bo1c": np.ascontiguousarray(
            np.asarray(inputs["L1_ffn_bo1"], np.float32).reshape(2, 128).T),
        "Wo2": np.asarray(inputs["L1_ffn_Wo2"], np.float32),
        "bo2c": bo2c,
        "gamc": _fcols(inputs["L1_ffn_gamma"]), "betc": _fcols(inputs["L1_ffn_beta"]),
        "ident": C["ident"], "ident_f": C["ident_f"], "selb": C["selb"],
        "ones_bf": C["ones_bf"], "ones_f": C["ones_f"], "cvals": C["cvals"],
    }

    def onehot(i):
        v = np.zeros((128, 4), np.float32)
        if 0 <= i < 4:
            v[:, i] = 1.0
        return v

    maps = []
    for c in range(NCORE):
        b, qq = c // 4, c % 4
        sl = slice(qq * TPC, (qq + 1) * TPC)
        m = dict(shared)
        m["x_tok"] = np.ascontiguousarray(x0[b, sl])
        m["x0T_s"] = np.ascontiguousarray(x0[b, sl].T)
        m["selh"] = onehot(qq)
        m["selq"] = onehot(qq)
        m["selL"] = onehot(qq - 1)
        m["selR"] = onehot(qq + 1) if qq < 3 else onehot(-1)
        maps.append(m)

    nc = _get("fused", build_fused)
    res = run_bass_kernel_spmd(nc, maps, core_ids=list(range(NCORE)))
    out = np.empty((B, N, D), np.float32)
    for c in range(NCORE):
        b, qq = c // 4, c % 4
        out[b, qq * TPC:(qq + 1) * TPC] = res.results[c]["x4_s"]
    return out
